# version 14
# baseline (speedup 1.0000x reference)
import contextlib
import numpy as np
import ml_dtypes
import concourse.bass as bass
import concourse.mybir as mybir
from concourse.bass_utils import run_bass_kernel_spmd
from concourse.alu_op_type import AluOpType

F32 = mybir.dt.float32
BF16 = mybir.dt.bfloat16
AF = mybir.ActivationFunctionType
AX = mybir.AxisListType
OP = AluOpType

T = 4096
C = 1024
NCH = 8
TSL = 512
NSL = 8
AH = 16
AD = 64
FH = 8
FHID = 2730
F2H = 5460
LN_EPS = 1e-5

DEBUG = []
bfdt = ml_dtypes.bfloat16


def host_constants():
    cst = {}
    cst["ident"] = np.eye(128, dtype=np.float32)
    sel2 = np.zeros((128, 2), np.float32)
    sel2[0:64, 0] = 1.0
    sel2[64:128, 1] = 1.0
    cst["sel2"] = sel2.astype(bfdt)
    brd2 = np.zeros((2, 128), np.float32)
    brd2[0, 0:64] = 1.0
    brd2[1, 64:128] = 1.0
    cst["brd2"] = brd2
    i = np.arange(1, AD // 2 + 1, dtype=np.float64)
    thetas = np.repeat(1000.0 ** (-2.0 * i / AD), 2)
    pos = np.arange(1, T + 1, dtype=np.float64)
    args = pos[:, None] * thetas[None, :]
    perm = np.concatenate([np.arange(0, AD, 2), np.arange(1, AD, 2)])
    cst["rp_cos"] = np.ascontiguousarray(np.cos(args)[:, perm].T).astype(bfdt)
    cst["rp_sin"] = np.ascontiguousarray(np.sin(args)[:, perm].T).astype(bfdt)
    half = np.arange(0, AD, 2, dtype=np.float64)
    inv_freq = 1.0 / 10000.0 ** (half / AD)
    t = np.arange(T, dtype=np.float64)
    freqs = np.concatenate([t[:, None] * inv_freq[None, :]] * 2, axis=-1)
    base = (half + 0.4 * AD) / (1.4 * AD)
    power = (t - T // 2) / 256.0
    scale = np.concatenate([base[None, :] ** power[:, None]] * 2, axis=-1)
    for k, v in [("xp_qc", np.cos(freqs) * scale), ("xp_qs", np.sin(freqs) * scale),
                 ("xp_kc", np.cos(freqs) / scale), ("xp_ks", np.sin(freqs) / scale)]:
        cst[k] = np.ascontiguousarray(v.T).astype(bfdt)
    i_idx = np.arange(128)[:, None]
    j_idx = np.arange(512)[None, :]
    m = np.zeros((8, 128, 512), dtype=np.float32)
    for a, r in enumerate(range(-512, 385, 128)):
        vis = (j_idx - 511 <= r + i_idx) & (r + i_idx <= j_idx)
        m[a] = np.where(vis, 0.0, -30000.0)
    cst["lmask"] = m.astype(bfdt)
    return cst


class Ctr:
    __slots__ = ("h", "n")

    def __init__(self, h):
        self.h = h
        self.n = 0

    def inc(self, instr, k=1):
        instr.then_inc(self.h, k)
        self.n += k
        return instr


class P:
    pass


def build_program():
    nc = bass.Bass("TRN2", target_bir_lowering=False)
    p = P()
    p.nc = nc
    p.gates = {}
    p.rope_slot = {}
    p.plane_slot = {}
    p.plain_slot = {}
    p.dnT_bank = {}
    p.dnT_slot = {}
    p.avhist = {}

    def gate(key):
        pass

    din = lambda name, shape, dt=F32: nc.dram_tensor(name, shape, dt, kind="ExternalInput").ap()
    scr = lambda name, shape, dt=F32: nc.dram_tensor(name, shape, dt, kind="Internal").ap()

    p.x_in = din("x", [T, C])
    p.w_inI = din("w_in", [3 * C, C])
    p.w_outI = din("w_out", [C, C])
    p.gamma_t = din("gamma_t", [128, NCH])
    p.beta_t = din("beta_t", [128, NCH])
    p.wqkvI = [din(f"wqkv{l}", [C, 3 * 512]) for l in range(2)]
    p.woI = [din(f"wo{l}", [512, C]) for l in range(2)]
    p.w1I = [din(f"w1{l}", [C, F2H]) for l in range(2)]
    p.w2I = [din(f"w2{l}", [FHID, C]) for l in range(2)]
    p.b1pI = [din(f"b1p{l}", [128, 44]) for l in range(2)]
    p.b2tI = [din(f"b2t{l}", [128, NCH]) for l in range(2)]
    p.lgtI = [din(f"lgt{l}", [128, NCH]) for l in range(2)]
    p.lbtI = [din(f"lbt{l}", [128, NCH]) for l in range(2)]
    p.qscI = [din(f"qsc{l}", [AD, 1]) for l in range(2)]
    p.kscI = [din(f"ksc{l}", [AD, 1]) for l in range(2)]
    p.identI = din("ident", [128, 128])
    p.sel2I = din("sel2", [128, 2], BF16)
    p.brd2I = din("brd2", [2, 128])
    p.rp_cosI = din("rp_cos", [AD, T], BF16)
    p.rp_sinI = din("rp_sin", [AD, T], BF16)
    p.xpI = {k: din(k, [AD, T], BF16) for k in ["xp_qc", "xp_qs", "xp_kc", "xp_ks"]}
    p.lmaskI = din("lmask", [8, 128, TSL], BF16)

    p.out_t = nc.dram_tensor("out", [T, C], F32, kind="ExternalOutput").ap()
    p.dbg = {}
    for name, shape, dt in DEBUG:
        p.dbg[name] = nc.dram_tensor(name, shape, dt, kind="ExternalOutput").ap()

    p.scr_x0 = scr("scr_x0", [NCH, 128, T])
    p.scr_xh = scr("scr_xh", [NCH, 128, T])
    p.scr_hbq = scr("scr_hbq", [AH, 128, T], BF16)
    p.scr_hbk = scr("scr_hbk", [AH, 128, T], BF16)
    p.scr_v = scr("scr_v", [8, 128, T], BF16)
    p.scr_h = scr("scr_h", [NCH, 128, T])
    p.scr_ao = scr("scr_ao", [NCH, 128, T])
    p.scr_x1 = scr("scr_x1", [NCH, 128, T])
    p.scr_fA = scr("scr_fA", [NCH, 128, T])
    p.scr_fB = scr("scr_fB", [NCH, 128, T])
    p.scr_dnb = scr("scr_dnb", [AH, 32, 128])
    p.scr_bb = scr("scr_bb", [FH, 32, 128])

    stack = contextlib.ExitStack()
    p.stack = stack
    p.sb = lambda nm, shp, dt=F32: stack.enter_context(nc.sbuf_tensor(nm, shp, dt))
    mksem = lambda nm: Ctr(stack.enter_context(nc.semaphore(nm)))
    p.scopes = {}

    def sbS(scope, nm, shp, dt=F32, side=None):
        if scope not in p.scopes:
            p.scopes[scope] = contextlib.ExitStack()
        return p.scopes[scope].enter_context(nc.sbuf_tensor(nm, shp, dt, side=side))

    def close_scope(scope):
        if scope in p.scopes:
            p.scopes.pop(scope).close()

    p.sbS = sbS
    p.close_scope = close_scope

    with stack:
        p.cW = mksem("cW")
        p.cA = mksem("cA")
        p.cG = mksem("cG")
        p.cB = mksem("cB")
        p.cO = mksem("cO")
        p.cT = mksem("cT")
        p.cV = mksem("cV")
        p.cS = mksem("cS")
        p.cX = mksem("cX")
        p.cY = mksem("cY")
        p.cZ = mksem("cZ")
        p.cD = [mksem(f"cD{i}") for i in range(6)]
        p.cE = [mksem(f"cE{i}") for i in range(4)]
        p.cF = [mksem(f"cF{i}") for i in range(4)]
        p.cSL = mksem("cSL")
        p.cRX = mksem("cRX")
        p.cVc = mksem("cVc")
        p.cSc = mksem("cSc")
        p.all_ctrs = [p.cW, p.cA, p.cG, p.cB, p.cO, p.cT, p.cV, p.cS, p.cX, p.cY,
                      p.cZ, p.cVc, p.cSc, p.cSL, p.cRX] + p.cD + p.cE + p.cF

        p.ident_f = p.sb("ident_f", [128, 128])
        p.ident_b = p.sb("ident_b", [128, 128], BF16)
        p.inv_c = p.sb("inv_c", [128, 1], BF16)
        p.ones_rf = p.sb("ones_rf", [1, 128])
        p.ones_rb = p.sb("ones_rb", [1, 128], BF16)
        p.ones128 = p.sb("ones128", [128, 1], BF16)
        p.eps_col = p.sb("eps_col", [128, 1])
        p.epsl2_col = p.sb("epsl2_col", [128, 1])
        p.gam = p.sb("gam", [128, NCH])
        p.bet = p.sb("bet", [128, NCH])
        p.sel2 = p.sb("sel2_sb", [128, 2], BF16)
        p.brd2 = p.sb("brd2_sb", [2, 128])


        p.psF = stack.enter_context(nc.psum_tensor("psF", [128, 3072], F32))
        p.psB = stack.enter_context(nc.psum_tensor("psB", [128, 2048], BF16))
        p.bank = lambda i, w=1: p.psF[:, i * 512:(i + w) * 512]
        p.bslot = lambda i: p.psB[:, (i % 2) * 1024:(i % 2) * 1024 + 128]

        p.srow = [p.sb(f"srow{i}", [1, TSL]) for i in range(4)]

        p.rp_cos = p.sbS("AB", "rp_cos_sb", [AD, T], BF16)
        p.rp_sin = p.sbS("AB", "rp_sin_sb", [AD, T], BF16)
        p.w_inT = [p.sbS("AB", f"w_inT{c}", [128, 3 * C], BF16) for c in range(NCH)]
        p.xhb = [p.sbS("AB", f"xhb{c}", [128, T], BF16) for c in range(NCH)]

        with nc.Block("init") as blk:
            @blk.sync
            def _(s):
                for c0 in p.all_ctrs:
                    s.sem_clear(c0.h)

        with nc.Block("main") as blk:
            p.blk = blk
            emit_const_loads(p)
            emit_winT(p)
            emit_phaseA(p)
            emit_phaseB(p)
            emit_attention(p)
            emit_wout_lnchain(p)
            for l in range(2):
                emit_ffn_layer(p, l)
            emit_final(p)

        for sc in reversed(list(p.scopes)):
            p.close_scope(sc)

    return nc, p


def VI(p, ins):
    p.cVc.inc(ins, 1)
    return ins


def VW(p, v):
    v.wait_ge(p.cVc.h, p.cVc.n)


def SI(p, ins):
    p.cSc.inc(ins, 1)
    return ins


def SW(p, s):
    s.wait_ge(p.cSc.h, p.cSc.n)


def g_set(p, key, ctr):
    p.gates[key] = (ctr, ctr.n)


def g_wait(p, eng, key):
    ctr, v = p.gates[key]
    eng.wait_ge(ctr.h, v)


def barrier(p, eng):
    for c in p.all_ctrs:
        eng.wait_ge(c.h, c.n)


def emit_const_loads(p):
    blk = p.blk
    cW, cG = p.cW, p.cG

    @blk.sync
    def _(s):
        for dst, src in [(p.ident_f, p.identI), (p.gam, p.gamma_t), (p.bet, p.beta_t),
                         (p.rp_cos, p.rp_cosI), (p.rp_sin, p.rp_sinI),
                         (p.sel2, p.sel2I), (p.brd2, p.brd2I)]:
            cW.inc(s.dma_start(out=dst[:, :], in_=src[:, :]), 16)


    @blk.gpsimd
    def _(g):
        p.cD[4].inc(g.dma_start(out=p.ident_b[:, :], in_=p.identI[:, :]), 16)
        g.memset(p.inv_c[:, :], 1.0 / C)
        g.memset(p.ones_rf[:, :], 1.0)
        g.memset(p.ones_rb[:, :], 1.0)
        g.memset(p.ones128[:, :], 1.0)
        g.memset(p.eps_col[:, :], LN_EPS)
        p.cZ.inc(g.memset(p.epsl2_col[:, :], 1e-18), 1)
    g_set(p, "consts", p.cZ)
    g_set(p, "identb", p.cD[4])
    g_set(p, "constsW", cW)


def emit_winT(p):
    blk = p.blk
    stg = [p.sbS("W", f"winstg{i}", [128, C], BF16) for i in range(4)]
    cG, cT, cV = p.cG, p.cT, p.cV

    for ch in range(24):
        r0 = ch * 128
        dst = stg[ch % 4]

        @blk.gpsimd
        def _(g, ch=ch, r0=r0, dst=dst):
            cs = p.cD[ch % 4]
            if ch >= 4:
                g_wait(p, g, ("winT_ev", ch - 4))
            if ch < 16:
                for hb in range(2):
                    rr = r0 + hb * 64
                    cs.inc(g.dma_start(out=dst[hb * 64:hb * 64 + 32, :],
                                       in_=p.w_inI[rr:rr + 64:2, :]), 16)
                    cs.inc(g.dma_start(out=dst[hb * 64 + 32:hb * 64 + 64, :],
                                       in_=p.w_inI[rr + 1:rr + 64:2, :]), 16)
            else:
                cs.inc(g.dma_start(out=dst[:, :], in_=p.w_inI[r0:r0 + 128, :]), 16)
        g_set(p, ("winT_ld", ch), p.cD[ch % 4])

        for c in range(NCH):
            k = ch * 8 + c

            @blk.tensor
            def _(t, ch=ch, dst=dst, c=c, k=k):
                if c == 0:
                    g_wait(p, t, ("winT_ld", ch))
                if ch == 0 and c == 0:
                    g_wait(p, t, "consts")
                    g_wait(p, t, "identb")
                if k >= 2:
                    g_wait(p, t, ("winT_evc", k - 2))
                cT.inc(t.transpose(p.bslot(k), dst[:, c * 128:(c + 1) * 128],
                                   p.ident_b[:, :]), 1)
            g_set(p, ("winT_tr", k), cT)

            @blk.vector
            def _(v, ch=ch, c=c, k=k):
                g_wait(p, v, ("winT_tr", k))
                VI(p, v.tensor_copy(p.w_inT[c][:, ch * 128:(ch + 1) * 128], p.bslot(k)))
            g_set(p, ("winT_evc", k), p.cVc)
        g_set(p, ("winT_ev", ch), p.cVc)
    g_set(p, "winT_done", p.cVc)
    p.close_scope("W")


def emit_phaseA(p):
    blk = p.blk
    tm_in = [p.sbS("A", f"tm_in{i}", [128, C]) for i in range(4)]
    x0T = [p.sbS("A", f"x0T{c}", [128, TSL]) for c in range(NCH)]
    xb = [p.sbS("A", f"xbA{c}", [128, TSL], BF16) for c in range(NCH)]
    sq = [p.sbS("A", f"sqA{c}", [128, TSL], BF16) for c in range(NCH)]
    cA, cT, cV, cS, cG = p.cA, p.cT, p.cV, p.cS, p.cG

    for sl in range(NSL):
        ssl = slice(sl * TSL, (sl + 1) * TSL)

        @blk.sync
        def _(s, sl=sl):
            if sl == 0:
                barrier(p, s)
            else:
                g_wait(p, s, ("A_tr_done", sl - 1))
            for r in range(4):
                cA.inc(s.dma_start(out=tm_in[r][:, :],
                                   in_=p.x_in[sl * TSL + r * 128: sl * TSL + (r + 1) * 128, :]), 16)
        g_set(p, ("A_ld", sl), cA)

        for c in range(NCH):
            for r in range(4):
                k = c * 4 + r
                kg = sl * 32 + k

                @blk.tensor
                def _(t, sl=sl, c=c, r=r, k=k, kg=kg):
                    if k == 0:
                        g_wait(p, t, ("A_ld", sl))
                        if sl == 0:
                            g_wait(p, t, "winT_done")
                    if kg >= 2:
                        g_wait(p, t, ("A_ev", kg - 2))
                    pt = p.bank(k % 2)[:, 0:128]
                    cT.inc(t.transpose(pt, tm_in[r][:, c * 128:(c + 1) * 128],
                                       p.ident_f[:, :]), 1)
                g_set(p, ("A_tr", kg), cT)

                @blk.vector
                def _(v, sl=sl, c=c, r=r, k=k, kg=kg):
                    g_wait(p, v, ("A_tr", kg))
                    if k == 0 and sl >= 1:
                        g_wait(p, v, ("A_spill_xh", sl - 1))
                    pt = p.bank(k % 2)[:, 0:128]
                    VI(p, v.tensor_copy(x0T[c][:, r * 128:(r + 1) * 128], pt))
                g_set(p, ("A_ev", kg), p.cVc)
        g_set(p, ("A_tr_done", sl), cT)
        g_set(p, ("A_ev_done", sl), p.cVc)

        @blk.gpsimd
        def _(g, sl=sl, ssl=ssl):
            g_wait(p, g, ("A_ev_done", sl))
            for c in range(NCH):
                cG.inc(g.dma_start(out=p.scr_x0[c, :, ssl], in_=x0T[c][:, :]), 16)
        g_set(p, ("A_spill_x0", sl), cG)

        @blk.vector
        def _(v, sl=sl):
            VW(p, v)
            for c in range(NCH):
                if c == 0 and sl >= 1:
                    g_wait(p, v, ("A_stats", sl - 1))
                VI(p, v.tensor_copy(xb[c][:, :], x0T[c][:, :]))
                VW(p, v)
                VI(p, v.tensor_tensor(sq[c][:, :], xb[c][:, :], xb[c][:, :], op=OP.mult))
        g_set(p, ("A_xbsq", sl), p.cVc)

        @blk.tensor
        def _(t, sl=sl):
            g_wait(p, t, ("A_xbsq", sl))
            if sl >= 1:
                g_wait(p, t, ("A_var", sl - 1))
            for c in range(NCH):
                t.matmul(p.bank(2)[0:1, :], p.inv_c[:, :], xb[c][:, :],
                         start=(c == 0), stop=(c == NCH - 1))
            for c in range(NCH):
                ins = t.matmul(p.bank(3)[0:1, :], p.inv_c[:, :], sq[c][:, :],
                               start=(c == 0), stop=(c == NCH - 1))
            cT.inc(ins, 1)
        g_set(p, ("A_stats", sl), cT)

        @blk.vector
        def _(v, sl=sl):
            g_wait(p, v, ("A_stats", sl))
            VI(p, v.tensor_copy(p.srow[0][:, :], p.bank(2)[0:1, :]))
            VW(p, v)
            VI(p, v.tensor_tensor(p.srow[1][:, :], p.srow[0][:, :], p.srow[0][:, :],
                                  op=OP.mult))
            VW(p, v)
            VI(p, v.tensor_tensor(p.srow[1][:, :], p.bank(3)[0:1, :], p.srow[1][:, :],
                                  op=OP.subtract))
        g_set(p, ("A_var", sl), p.cVc)

        @blk.scalar
        def _(s, sl=sl):
            g_wait(p, s, ("A_var", sl))
            SI(p, s.activation(p.srow[1][:, :], p.srow[1][:, :], AF.Ln, bias=p.eps_col[0:1, :]))
            SW(p, s)
            SI(p, s.activation(p.srow[2][:, :], p.srow[1][:, :], AF.Exp, scale=-0.5))
        g_set(p, ("A_rstd", sl), p.cSc)

        @blk.vector
        def _(v, sl=sl):
            g_wait(p, v, ("A_rstd", sl))
            VI(p, v.scalar_tensor_tensor(p.srow[3][:, :], p.srow[0][:, :], -1.0,
                                         p.srow[2][:, :], op0=OP.mult, op1=OP.mult))
        g_set(p, ("A_mr", sl), p.cVc)

        @blk.tensor
        def _(t, sl=sl):
            g_wait(p, t, ("A_mr", sl))
            if sl >= 1:
                g_wait(p, t, ("A_apply", sl - 1))
            t.matmul(p.bank(4), p.ones_rf[:, :], p.srow[2][:, :], start=True, stop=True)
            ins = t.matmul(p.bank(5), p.ones_rf[:, :], p.srow[3][:, :], start=True, stop=True)
            cT.inc(ins, 1)
        g_set(p, ("A_planes", sl), cT)

        @blk.vector
        def _(v, sl=sl, ssl=ssl):
            g_wait(p, v, ("A_planes", sl))
            g_wait(p, v, ("A_spill_x0", sl))
            for c in range(NCH):
                VI(p, v.tensor_tensor(x0T[c][:, :], x0T[c][:, :], p.bank(4), op=OP.mult))
                VW(p, v)
                VI(p, v.tensor_tensor(x0T[c][:, :], x0T[c][:, :], p.bank(5), op=OP.add))
                VW(p, v)
                VI(p, v.tensor_copy(p.xhb[c][:, ssl], x0T[c][:, :]))
        g_set(p, ("A_apply", sl), p.cVc)

        @blk.gpsimd
        def _(g, sl=sl, ssl=ssl):
            g_wait(p, g, ("A_apply", sl))
            for c in range(NCH):
                cG.inc(g.dma_start(out=p.scr_xh[c, :, ssl], in_=x0T[c][:, :]), 16)
        g_set(p, ("A_spill_xh", sl), cG)
    g_set(p, "A_done", cG)
    p.close_scope("A")

    if "d_xhb" in p.dbg:
        @blk.sync
        def _(s):
            barrier(p, s)
            for c in range(NCH):
                p.cO.inc(s.dma_start(out=p.dbg["d_xhb"][c, :, :], in_=p.xhb[c][:, :]), 16)


def emit_phaseB(p):
    blk = p.blk
    hb = [p.sbS("B", f"hbB{i}", [128, TSL], BF16) for i in range(4)]
    vst = [p.sbS("B", f"vstB{i}", [128, C], BF16) for i in range(8)]
    rsc = p.sbS("B", "rscB", [128, TSL], BF16)
    rsc2 = p.sbS("B", "rsc2B", [64, TSL], BF16)
    cT, cV, cG = p.cT, p.cV, p.cG

    for sl in range(NSL):
        ssl = slice(sl * TSL, (sl + 1) * TSL)
        for fc in range(16):
            k = sl * 16 + fc
            par = k % 2

            @blk.tensor
            def _(t, sl=sl, fc=fc, k=k, par=par, ssl=ssl):
                if k == 0:
                    g_wait(p, t, "A_done")
                if k >= 2:
                    g_wait(p, t, ("B_rope", k - 2))
                bk = p.bank(par)
                for c in range(NCH):
                    ins = t.matmul(bk, p.w_inT[c][:, fc * 128:(fc + 1) * 128],
                                   p.xhb[c][:, ssl], start=(c == 0), stop=(c == NCH - 1))
                cT.inc(ins, 1)
            g_set(p, ("B_qk", k), cT)

            @blk.vector
            def _(v, sl=sl, fc=fc, k=k, par=par, ssl=ssl):
                g_wait(p, v, ("B_qk", k))
                if k == 0:
                    g_wait(p, v, "constsW")
                if k >= 2:
                    g_wait(p, v, ("B_hspill", k - 2))
                bk = p.bank(par)
                A, B = hb[par * 2], hb[par * 2 + 1]
                VI(p, v.tensor_copy(A[64:128, :], bk[0:64, :]))
                VI(p, v.tensor_copy(B[64:128, :], bk[64:128, :]))
                for X in (A, B):
                    VW(p, v)
                    VI(p, v.tensor_copy(rsc[0:64, :], X[64:128, :]))
                    VW(p, v)
                    VI(p, v.tensor_scalar(rsc2[0:32, :], rsc[32:64, :], -1.0, None, op0=OP.mult))
                    VI(p, v.tensor_copy(rsc2[32:64, :], rsc[0:32, :]))
                    VW(p, v)
                    VI(p, v.tensor_tensor(X[0:64, :], rsc[0:64, :], p.rp_cos[0:64, ssl], op=OP.mult))
                    VI(p, v.tensor_tensor(rsc2[0:64, :], rsc2[0:64, :], p.rp_sin[0:64, ssl], op=OP.mult))
                    VW(p, v)
                    VI(p, v.tensor_tensor(X[0:64, :], X[0:64, :], rsc2[0:64, :], op=OP.add))
            g_set(p, ("B_rope", k), p.cVc)

            @blk.gpsimd
            def _(g, sl=sl, fc=fc, k=k, par=par, ssl=ssl):
                g_wait(p, g, ("B_rope", k))
                cs = p.cD[par]
                A, B = hb[par * 2], hb[par * 2 + 1]
                tgt = p.scr_hbq if fc < 8 else p.scr_hbk
                h2 = (fc % 8) * 2
                cs.inc(g.dma_start(out=tgt[h2, :, ssl], in_=A[:, :]), 16)
                cs.inc(g.dma_start(out=tgt[h2 + 1, :, ssl], in_=B[:, :]), 16)
            g_set(p, ("B_hspill", k), p.cD[par])

        for tsub in range(4):
            kv = sl * 4 + tsub

            @blk.tensor
            def _(t, sl=sl, tsub=tsub, kv=kv, ssl=ssl):
                if kv >= 2:
                    g_wait(p, t, ("B_vev", kv - 2))
                for nn in range(2):
                    bk = p.bank(2 + (kv % 2) * 2 + nn)
                    for c in range(NCH):
                        ins = t.matmul(
                            bk,
                            p.xhb[c][:, sl * TSL + tsub * 128: sl * TSL + (tsub + 1) * 128],
                            p.w_inT[c][:, 2048 + nn * 512: 2048 + (nn + 1) * 512],
                            start=(c == 0), stop=(c == NCH - 1))
                cT.inc(ins, 1)
            g_set(p, ("B_v", kv), cT)

            @blk.vector
            def _(v, sl=sl, tsub=tsub, kv=kv):
                g_wait(p, v, ("B_v", kv))
                if kv >= 2:
                    g_wait(p, v, ("B_vspill", kv - 2))
                dst = vst[(kv % 2) * 4 + tsub]
                VI(p, v.tensor_copy(dst[:, 0:512], p.bank(2 + (kv % 2) * 2)))
                VI(p, v.tensor_copy(dst[:, 512:1024], p.bank(3 + (kv % 2) * 2)))
            g_set(p, ("B_vev", kv), p.cVc)

            @blk.gpsimd
            def _(g, sl=sl, tsub=tsub, kv=kv):
                g_wait(p, g, ("B_vev", kv))
                cs = p.cD[2 + kv % 2]
                src = vst[(kv % 2) * 4 + tsub]
                for pr in range(8):
                    cs.inc(g.dma_start(out=p.scr_v[pr, :, kv * 128:(kv + 1) * 128],
                                       in_=src[:, pr * 128:(pr + 1) * 128]), 16)
            g_set(p, ("B_vspill", kv), p.cD[2 + kv % 2])
    for i in range(4):
        g_set(p, f"B_done{i}", p.cD[i])
    g_set(p, "B_done", p.cVc)
    p.close_scope("B")
    p.close_scope("AB")

    if "d_hbq" in p.dbg:
        @blk.sync
        def _(s):
            barrier(p, s)
            for h in range(AH):
                p.cO.inc(s.dma_start(out=p.dbg["d_hbq"][h, :, :], in_=p.scr_hbq[h, :, :]), 16)
                p.cO.inc(s.dma_start(out=p.dbg["d_hbk"][h, :, :], in_=p.scr_hbk[h, :, :]), 16)
            for i in range(T // 128):
                p.cO.inc(s.dma_start(out=p.dbg["d_v"][i, :, :], in_=p.scr_v[i, :, :]), 16)


def emit_attention(p):
    blk = p.blk
    cT, cG, cA, cB = p.cT, p.cG, p.cA, p.cB
    cSL, cRX = p.cSL, p.cRX

    p.ybuf = [p.sbS("SY", f"ybuf{i}", [128, T], BF16) for i in range(8)]
    hq = [p.sbS("S", f"hqS{i}", [128, T], BF16) for i in range(4)]
    hk = [p.sbS("S", f"hkS{i}", [128, T], BF16) for i in range(4)]
    vpair = [p.sbS("S", f"vpairS{i}", [128, T], BF16) for i in range(2)]
    attb = [p.sbS("S", f"attbS{i}", [128, 1024], BF16) for i in range(4)]
    ytmp = p.sbS("S", "ytmpS", [128, TSL])
    denp = [p.sbS("S", f"denpS{i}", [128, 128]) for i in range(2)]
    den = [p.sbS("S", f"denS{i}", [128, 32]) for i in range(2)]
    invd = [p.sbS("S", f"invdS{i}", [128, 32]) for i in range(2)]
    dnstg = [p.sbS("S", f"dnstgS{i}", [32, 128]) for i in range(2)]
    invsl = [p.sbS("S", f"invslS{i}", [1, TSL]) for i in range(2)]

    g = [0]
    p.slot_pair = {}
    pair_last = {}

    gp = [0]

    def next_slot(t, pair=None, plane=False):
        if plane:
            gp[0] += 1
            ordv, ctr = gp[0], cRX
            if pair is None:
                pair = gp[0] % 2
        else:
            g[0] += 1
            ordv, ctr = g[0], cSL
            if pair is None:
                pair = g[0] % 2
        last = pair_last.get(pair)
        if last is not None:
            t.wait_ge(last[0].h, last[1])
        pair_last[pair] = (ctr, ordv)
        if not plane:
            p.slot_pair[g[0]] = pair
        return p.bank(pair * 2, 2), ordv, pair

    for hp in range(8):
        h0, h1 = 2 * hp, 2 * hp + 1

        @blk.sync
        def _(s, hp=hp, h0=h0, h1=h1):
            if hp == 0:
                for i in range(4):
                    g_wait(p, s, f"B_done{i}")
            if hp >= 2:
                g_wait(p, s, ("S_pair_done", hp - 2))
            cs = p.cE[hp % 2]
            par = (hp % 2) * 2
            cs.inc(s.dma_start(out=hq[par][:, :], in_=p.scr_hbq[h0, :, :]), 16)
            cs.inc(s.dma_start(out=hq[par + 1][:, :], in_=p.scr_hbq[h1, :, :]), 16)
            cs.inc(s.dma_start(out=hk[par][:, :], in_=p.scr_hbk[h0, :, :]), 16)
            cs.inc(s.dma_start(out=hk[par + 1][:, :], in_=p.scr_hbk[h1, :, :]), 16)
            cs.inc(s.dma_start(out=vpair[hp % 2][:, :], in_=p.scr_v[hp, :, :]), 16)
        g_set(p, ("S_ld", hp), p.cE[hp % 2])

        slot0 = {}
        for hh in range(2):
            Q = hq[(hp % 2) * 2 + hh]
            K = hk[(hp % 2) * 2 + hh]

            @blk.tensor
            def _(t, hp=hp, hh=hh, Q=Q, K=K):
                if hh == 0:
                    g_wait(p, t, ("S_ld", hp))
                    if hp > 0:
                        g_wait(p, t, ("S_ysc", (hp - 1, 7)))
                else:
                    g_wait(p, t, ("S_dnev", (hp, 0)))
                for qt in range(32):
                    for ks in range(4):
                        bk, _, _ = next_slot(t, pair=g[0] % 3)
                        if qt == 0 and ks == 0:
                            slot0[hh] = g[0]
                        for j in range(2):
                            ins = t.matmul(
                                bk[:, j * 512:(j + 1) * 512],
                                Q[64:128, qt * 128:(qt + 1) * 128],
                                K[64:128, ks * 1024 + j * 512: ks * 1024 + (j + 1) * 512],
                                start=True, stop=True)
                        cT.inc(ins, 1)
            g_set(p, ("S_pmm", (hp, hh)), cT)

            @blk.scalar
            def _(s, hp=hp, hh=hh):
                base_t = p.gates[("S_pmm", (hp, hh))][1] - 128
                g0 = slot0[hh]
                if hh == 0 and hp >= 2:
                    g_wait(p, s, ("S_invd", (hp - 2, 1)))
                for i in range(128):
                    if i % 2 == 0:
                        s.wait_ge(cT.h, base_t + min(i + 2, 128))
                    gg = g0 + i
                    bk = p.bank(p.slot_pair[gg] * 2, 2)
                    cSL.inc(s.activation(bk, bk, AF.Exp, scale=0.125,
                                         accum_out=denp[hh][:, i:i + 1]), 1)
            g_set(p, ("S_pexp", (hp, hh)), cSL)

            @blk.vector
            def _(v, hp=hp, hh=hh):
                g_wait(p, v, ("S_pexp", (hp, hh)))
                VW(p, v)
                VI(p, v.tensor_tensor(den[hh][:, :], denp[hh][:, 0:128:4],
                                      denp[hh][:, 1:128:4], op=OP.add))
                VI(p, v.tensor_tensor(invd[hh][:, :], denp[hh][:, 2:128:4],
                                      denp[hh][:, 3:128:4], op=OP.add))
                VW(p, v)
                VI(p, v.tensor_tensor(den[hh][:, :], den[hh][:, :], invd[hh][:, :], op=OP.add))
                VW(p, v)
                VI(p, v.reciprocal(invd[hh][:, :], den[hh][:, :]))
            g_set(p, ("S_invd", (hp, hh)), p.cVc)

            @blk.tensor
            def _(t, hp=hp, hh=hh):
                g_wait(p, t, ("S_invd", (hp, hh)))
                bk, _, _ = next_slot(t)
                p.dnT_slot[(hp, hh)] = g[0]
                cT.inc(t.transpose(bk[0:32, 0:128], invd[hh][:, :], p.ident_f[:, :]), 1)
            g_set(p, ("S_dnT", (hp, hh)), cT)

            @blk.vector
            def _(v, hp=hp, hh=hh):
                g_wait(p, v, ("S_dnT", (hp, hh)))
                gg = p.dnT_slot[(hp, hh)]
                bk = p.bank(p.slot_pair[gg] * 2, 2)
                cSL.inc(v.tensor_copy(dnstg[hh][:, :], bk[0:32, 0:128]), 1)
            g_set(p, ("S_dnev", (hp, hh)), cSL)

            @blk.gpsimd
            def _(gp, hp=hp, hh=hh):
                g_wait(p, gp, ("S_dnev", (hp, hh)))
                cbs = [cB, p.cD[4], p.cD[5], p.cD[0]][(hp % 2) * 2 + hh]
                cbs.inc(gp.dma_start(out=p.scr_dnb[2 * hp + hh, :, :], in_=dnstg[hh][:, :]), 16)
            g_set(p, ("S_bounceout", (hp, hh)),
                  [cB, p.cD[4], p.cD[5], p.cD[0]][(hp % 2) * 2 + hh])

        for qb in range(8):
            qsl = slice(qb * 512, (qb + 1) * 512)

            @blk.sync
            def _(s, hp=hp, qb=qb):
                if qb > 0:
                    g_wait(p, s, ("S_plane", (hp, qb - 1)))
                elif hp > 0:
                    g_wait(p, s, ("S_plane", (hp - 1, 7)))
                for hh in range(2):
                    g_wait(p, s, ("S_bounceout", (hp, hh)))
                    rowv = p.scr_dnb[2 * hp + hh].rearrange("a b -> (a b)")
                    p.cE[2].inc(s.dma_start(out=invsl[hh][:, :],
                                in_=rowv[None, qb * 512:(qb + 1) * 512]), 16)
            g_set(p, ("S_invsl", (hp, qb)), p.cE[2])

            slot_of = {}

            @blk.tensor
            def _(t, hp=hp, qb=qb, qsl=qsl):
                K0, K1 = hk[(hp % 2) * 2], hk[(hp % 2) * 2 + 1]
                Q0, Q1 = hq[(hp % 2) * 2], hq[(hp % 2) * 2 + 1]
                V = vpair[hp % 2]
                if qb == 0:
                    g_wait(p, t, ("S_dnev", (hp, 1)))
                    if hp > 0:
                        g_wait(p, t, ("S_yev", (hp - 1, 7)))
                for kt in range(33):
                    if kt < 32:
                        bk, _, _ = next_slot(t)
                        slot_of[kt] = g[0]
                        t.matmul(bk[:, 0:512], K0[0:64, kt * 128:(kt + 1) * 128],
                                 Q0[0:64, qsl], start=True, stop=True)
                        ins = t.matmul(bk[:, 512:1024], K1[0:64, kt * 128:(kt + 1) * 128],
                                       Q1[0:64, qsl], start=True, stop=True)
                        cT.inc(ins, 1)
                    if kt >= 1:
                        k2 = kt - 1
                        t.wait_ge(cSL.h, slot_of[k2])
                        if k2 == 0 and (qb >= 2 or (hp > 0 and qb < 2)):
                            prev = (hp, qb - 2) if qb >= 2 else (hp - 1, 6 + qb)
                            g_wait(p, t, ("S_yev", prev))
                        avb = p.bank(4 + qb % 2)
                        att = attb[k2 % 4]
                        t.matmul(avb[0:64, :], V[:, k2 * 128: k2 * 128 + 64],
                                 att[:, 0:512], start=(k2 == 0), stop=(k2 == 31),
                                 tile_position=(0, 0), skip_group_check=True)
                        ins = t.matmul(avb[64:128, :], V[:, k2 * 128 + 64: k2 * 128 + 128],
                                       att[:, 512:1024], start=(k2 == 0), stop=(k2 == 31),
                                       tile_position=(0, 64), skip_group_check=True)
                        cT.inc(ins, 1)
                        g_set(p, ("S_avk", (hp, qb, k2)), cT)
            g_set(p, ("S_av", (hp, qb)), cT)

            @blk.scalar
            def _(s, hp=hp, qb=qb):
                base_t = p.gates[("S_av", (hp, qb))][1] - 64
                for kt in range(32):
                    nmm = base_t + (1 if kt == 0 else 2 * kt)
                    s.wait_ge(cT.h, nmm)
                    if kt >= 4:
                        g_wait(p, s, ("S_avk", (hp, qb, kt - 4)))
                    elif qb > 0:
                        g_wait(p, s, ("S_avk", (hp, qb - 1, 28 + kt)))
                    elif hp > 0:
                        g_wait(p, s, ("S_avk", (hp - 1, 7, 28 + kt)))
                    gg = slot_of[kt]
                    bk = p.bank(p.slot_pair[gg] * 2, 2)
                    cSL.inc(s.activation(attb[kt % 4][:, :], bk, AF.Exp, scale=0.125), 1)
            g_set(p, ("S_rexp", (hp, qb)), cSL)

            @blk.tensor
            def _(t, hp=hp, qb=qb):
                g_wait(p, t, ("S_invsl", (hp, qb)))
                bk, ordv, pr = next_slot(t, plane=True)
                p.plane_slot[(hp, qb)] = (ordv, pr)
                t.matmul(bk[:, 0:512], p.ones_rf[:, :], invsl[0][:, :], start=True, stop=True)
                ins = t.matmul(bk[:, 512:1024], p.ones_rf[:, :], invsl[1][:, :],
                               start=True, stop=True)
                cT.inc(ins, 1)
            g_set(p, ("S_plane", (hp, qb)), cT)

            @blk.vector
            def _(v, hp=hp, qb=qb, qsl=qsl):
                g_wait(p, v, ("S_av", (hp, qb)))
                avb = p.bank(4 + qb % 2)
                VI(p, v.tensor_copy(ytmp[:, :], avb))
                g_set(p, ("S_yev", (hp, qb)), p.cVc)
                VW(p, v)
                g_wait(p, v, ("S_plane", (hp, qb)))
                gg, pr = p.plane_slot[(hp, qb)]
                bk = p.bank(pr * 2, 2)
                VI(p, v.tensor_tensor(p.ybuf[hp][0:64, qsl], ytmp[0:64, :],
                                      bk[0:64, 0:512], op=OP.mult))
                cRX.inc(v.tensor_tensor(p.ybuf[hp][64:128, qsl], ytmp[64:128, :],
                                        bk[64:128, 512:1024], op=OP.mult), 1)
            g_set(p, ("S_ysc", (hp, qb)), cRX)
        g_set(p, ("S_pair_done", hp), cRX)
    g_set(p, "S_done", cRX)
    g_set(p, "S_doneB", cSL)
    p.close_scope("S")

    if "d_ybuf" in p.dbg:
        @blk.sync
        def _(s):
            barrier(p, s)
            for i in range(8):
                p.cO.inc(s.dma_start(out=p.dbg["d_ybuf"][i, :, :], in_=p.ybuf[i][:, :]), 16)
    if "d_invd" in p.dbg:
        @blk.sync
        def _(s):
            barrier(p, s)
            for h in range(AH):
                p.cO.inc(s.dma_start(out=p.dbg["d_invd"][h, :, :], in_=p.scr_dnb[h, :, :]), 16)


def emit_wout_lnchain(p):
    blk = p.blk
    cT, cA, cG = p.cT, p.cA, p.cG
    p.rowA = [p.sbS("LN", "rowA_rstd", [1, 2 * TSL], side="right"),
              p.sbS("LN", "rowA_mr", [1, 2 * TSL], side="right")]
    p.rowB = [p.sbS("LN", "rowB_rstd", [1, 2 * TSL], side="right"),
              p.sbS("LN", "rowB_mr", [1, 2 * TSL], side="right")]
    p.w_outT = [p.sbS("SO", f"w_outT{c}", [128, C], BF16) for c in range(NCH)]
    wstg = [p.sbS("SO", f"wostg{i}", [128, C], BF16) for i in range(2)]
    main2 = [[p.sbS("SO", f"mainW{j}_{c}", [128, TSL]) for c in range(NCH)]
             for j in range(2)]
    aux2 = [[p.sbS("SO", f"auxW{j}_{c}", [128, TSL]) for c in range(NCH)]
            for j in range(2)]
    xb = [p.sbS("SO", f"xbW{c}", [128, TSL], BF16) for c in range(NCH)]
    sq = [p.sbS("SO", f"sqW{c}", [128, TSL], BF16) for c in range(NCH)]

    for ch in range(8):
        @blk.gpsimd
        def _(g, ch=ch):
            if ch == 0:
                g_wait(p, g, "S_done")
                g_wait(p, g, "S_doneB")
            if ch >= 2:
                g_wait(p, g, ("WO_tr", ch - 2))
            p.cD[ch % 2].inc(g.dma_start(out=wstg[ch % 2][:, :],
                                         in_=p.w_outI[ch * 128:(ch + 1) * 128, :]), 16)
        g_set(p, ("WO_ld", ch), p.cD[ch % 2])

        for c in range(NCH):
            k = ch * 8 + c

            @blk.tensor
            def _(t, ch=ch, c=c, k=k):
                if c == 0:
                    g_wait(p, t, ("WO_ld", ch))
                if k == 0:
                    g_wait(p, t, "S_done")
                    g_wait(p, t, "S_doneB")
                if k >= 2:
                    g_wait(p, t, ("WO_evc", k - 2))
                cT.inc(t.transpose(p.bslot(k), wstg[ch % 2][:, c * 128:(c + 1) * 128],
                                   p.ident_b[:, :]), 1)
            g_set(p, ("WO_trk", k), cT)

            @blk.vector
            def _(v, ch=ch, c=c, k=k):
                g_wait(p, v, ("WO_trk", k))
                VI(p, v.tensor_copy(p.w_outT[c][:, ch * 128:(ch + 1) * 128], p.bslot(k)))
            g_set(p, ("WO_evc", k), p.cVc)
        g_set(p, ("WO_tr", ch), p.cVc)
    g_set(p, "WO_done", p.cVc)

    stages = ["2", "3", "4"]
    rows_of = {"2": p.rowA, "3": p.rowB, "4": p.rowA}

    for sl in range(NSL):
        ssl = slice(sl * TSL, (sl + 1) * TSL)
        rsl = slice((sl % 2) * TSL, (sl % 2 + 1) * TSL)
        main = main2[sl % 2]
        aux = aux2[sl % 2]

        @blk.sync
        def _(s, sl=sl, ssl=ssl, main=main):
            if sl == 0:
                g_wait(p, s, "S_done")
                g_wait(p, s, "S_doneB")
            if sl >= 2:
                g_wait(p, s, ("W_x1sp", sl - 2))
                g_wait(p, s, ("W_apl", (sl - 2, "4")))
            cs = [cA, p.cE[0]][sl % 2]
            for c in range(NCH):
                cs.inc(s.dma_start(out=main[c][:, :], in_=p.scr_xh[c, :, ssl]), 16)
        g_set(p, ("W_ldm", sl), [cA, p.cE[0]][sl % 2])

        @blk.scalar
        def _(s, sl=sl, ssl=ssl, aux=aux):
            if sl >= 2:
                g_wait(p, s, ("W_fasp", sl - 2))
            cs = [p.cE[1], p.cE[2]][sl % 2]
            for c in range(NCH):
                cs.inc(s.dma_start(out=aux[c][:, :], in_=p.scr_x0[c, :, ssl]), 16)
        g_set(p, ("W_ldx", sl), [p.cE[1], p.cE[2]][sl % 2])

        for oc in range(NCH):
            k = sl * 8 + oc

            @blk.tensor
            def _(t, sl=sl, oc=oc, k=k, ssl=ssl):
                if sl == 0 and oc == 0:
                    g_wait(p, t, "WO_done")
                if k >= 2:
                    g_wait(p, t, ("W_hev", k - 2))
                bk = p.bank(k % 2)
                for c in range(NCH):
                    ins = t.matmul(bk, p.w_outT[c][:, oc * 128:(oc + 1) * 128],
                                   p.ybuf[c][:, ssl], start=(c == 0), stop=(c == NCH - 1))
                cT.inc(ins, 1)
            g_set(p, ("W_mm", k), cT)

            @blk.vector
            def _(v, sl=sl, oc=oc, k=k, main=main):
                g_wait(p, v, ("W_mm", k))
                if oc == 0:
                    g_wait(p, v, ("W_ldm", sl))
                VI(p, v.tensor_tensor(main[oc][:, :], p.bank(k % 2), main[oc][:, :],
                                      op=OP.add))
            g_set(p, ("W_hev", k), p.cVc)

        for si, st in enumerate(stages):
            @blk.vector
            def _(v, sl=sl, st=st, si=si, main=main):
                VW(p, v)
                prev = (sl, stages[si - 1]) if si > 0 else \
                    ((sl - 1, "4") if sl > 0 else None)
                if prev is not None:
                    g_wait(p, v, ("W_st", prev))
                for c in range(NCH):
                    VI(p, v.tensor_copy(xb[c][:, :], main[c][:, :]))
                    VW(p, v)
                    VI(p, v.tensor_tensor(sq[c][:, :], xb[c][:, :], xb[c][:, :],
                                          op=OP.mult))
            g_set(p, ("W_xb", (sl, st)), p.cVc)

            @blk.tensor
            def _(t, sl=sl, st=st, si=si):
                g_wait(p, t, ("W_xb", (sl, st)))
                prev = (sl, stages[si - 1]) if si > 0 else \
                    ((sl - 1, "4") if sl > 0 else None)
                if prev is not None:
                    g_wait(p, t, (("W" + prev[1], prev[0]), "rows"))
                for c in range(NCH):
                    t.matmul(p.bank(2)[0:1, :], p.inv_c[:, :], xb[c][:, :],
                             start=(c == 0), stop=(c == NCH - 1))
                for c in range(NCH):
                    ins = t.matmul(p.bank(3)[0:1, :], p.inv_c[:, :], sq[c][:, :],
                                   start=(c == 0), stop=(c == NCH - 1))
                cT.inc(ins, 1)
            g_set(p, ("W_st", (sl, st)), cT)
            g_set(p, ("W" + st + "_stats", sl), cT)
            extra = []
            if st == "4":
                extra.append(("W_pl", (sl, "2")))
            if sl >= 2:
                extra.append(("W_pl", (sl - 2, "4" if st == "2" else st)))
            emit_row_finish(p, ("W" + st, sl), rows_of[st], rsl, extra_scalar=extra)

            @blk.tensor
            def _(t, sl=sl, st=st, si=si, ssl=ssl):
                g_wait(p, t, (("W" + st, sl), "rows"))
                prevu = (sl, stages[si - 1]) if si > 0 else \
                    ((sl - 1, "4") if sl > 0 else None)
                if prevu is not None:
                    g_wait(p, t, ("W_apl", prevu))
                rows = rows_of[st]
                rsl = slice((sl % 2) * TSL, (sl % 2 + 1) * TSL)
                t.matmul(p.bank(4), p.ones_rf[:, :], rows[0][0:1, rsl],
                         start=True, stop=True)
                ins = t.matmul(p.bank(5), p.ones_rf[:, :], rows[1][0:1, rsl],
                               start=True, stop=True)
                cT.inc(ins, 1)
            g_set(p, ("W_pl", (sl, st)), cT)

            @blk.vector
            def _(v, sl=sl, st=st, main=main, aux=aux):
                g_wait(p, v, ("W_pl", (sl, st)))
                if st == "3":
                    g_wait(p, v, ("W_ldx", sl))
                for c in range(NCH):
                    if st == "4":
                        VI(p, v.tensor_tensor(aux[c][:, :], main[c][:, :], p.bank(4),
                                              op=OP.mult))
                        VW(p, v)
                        VI(p, v.tensor_tensor(aux[c][:, :], aux[c][:, :], p.bank(5),
                                              op=OP.add))
                        continue
                    VI(p, v.tensor_tensor(main[c][:, :], main[c][:, :], p.bank(4),
                                          op=OP.mult))
                    VW(p, v)
                    VI(p, v.tensor_tensor(main[c][:, :], main[c][:, :], p.bank(5),
                                          op=OP.add))
                    VW(p, v)
                    if st == "2":
                        VI(p, v.tensor_scalar(main[c][:, :], main[c][:, :],
                                              p.gam[:, c:c + 1], p.bet[:, c:c + 1],
                                              op0=OP.mult, op1=OP.add))
                    else:
                        VI(p, v.tensor_tensor(main[c][:, :], main[c][:, :],
                                              aux[c][:, :], op=OP.add))
            g_set(p, ("W_apl", (sl, st)), p.cVc)

            if st == "3":
                @blk.gpsimd
                def _(g, sl=sl, ssl=ssl, main=main):
                    g_wait(p, g, ("W_apl", (sl, "3")))
                    cs = [p.cD[2], p.cD[3]][sl % 2]
                    for c in range(NCH):
                        cs.inc(g.dma_start(out=p.scr_x1[c, :, ssl],
                                           in_=main[c][:, :]), 16)
                g_set(p, ("W_x1sp", sl), [p.cD[2], p.cD[3]][sl % 2])

        @blk.gpsimd
        def _(g, sl=sl, ssl=ssl, aux=aux):
            g_wait(p, g, ("W_apl", (sl, "4")))
            cs = [cG, p.cD[1]][sl % 2]
            for c in range(NCH):
                cs.inc(g.dma_start(out=p.scr_fA[c, :, ssl], in_=aux[c][:, :]), 16)
        g_set(p, ("W_fasp", sl), [cG, p.cD[1]][sl % 2])

    g_set(p, "LC_done", cG)
    g_set(p, "LC_done_odd", p.cD[1])
    g_set(p, "X1_done_e", p.cD[2])
    g_set(p, "X1_done_o", p.cD[3])
    p.close_scope("SO")
    p.close_scope("SY")
    p.close_scope("LN")

    if "d_xf" in p.dbg:
        @blk.sync
        def _(s):
            barrier(p, s)
            for c in range(NCH):
                p.cO.inc(s.dma_start(out=p.dbg["d_xf"][c, :, :], in_=p.scr_fA[c, :, :]), 16)
            for c in range(NCH):
                p.cO.inc(s.dma_start(out=p.dbg["d_x1"][c, :, :], in_=p.scr_x1[c, :, :]), 16)
        g_set(p, "DXF_dump", p.cO)


def emit_row_finish(p, tag, rows, ssl, extra_scalar=()):
    blk = p.blk

    @blk.vector
    def _(v):
        g_wait(p, v, (tag[0] + "_stats", tag[1]) if isinstance(tag, tuple) else tag)
        VI(p, v.tensor_copy(p.srow[0][:, :], p.bank(2)[0:1, :]))
        VW(p, v)
        VI(p, v.tensor_tensor(p.srow[1][:, :], p.srow[0][:, :], p.srow[0][:, :], op=OP.mult))
        VW(p, v)
        VI(p, v.tensor_tensor(p.srow[1][:, :], p.bank(3)[0:1, :], p.srow[1][:, :],
                              op=OP.subtract))
    g_set(p, (tag, "var"), p.cVc)

    @blk.scalar
    def _(s):
        g_wait(p, s, (tag, "var"))
        for eg in extra_scalar:
            g_wait(p, s, eg)
        SI(p, s.activation(p.srow[1][:, :], p.srow[1][:, :], AF.Ln, bias=p.eps_col[0:1, :]))
        SW(p, s)
        SI(p, s.activation(rows[0][0:1, ssl], p.srow[1][:, :], AF.Exp, scale=-0.5))
    g_set(p, (tag, "rstd"), p.cSc)

    @blk.vector
    def _(v):
        g_wait(p, v, (tag, "rstd"))
        VI(p, v.scalar_tensor_tensor(rows[1][0:1, ssl], p.srow[0][:, :], -1.0,
                                     rows[0][0:1, ssl], op0=OP.mult, op1=OP.mult))
    g_set(p, (tag, "rows"), p.cVc)


def emit_ffn_layer(p, l):
    blk = p.blk
    cT, cA, cG, cB = p.cT, p.cA, p.cG, p.cB
    L = f"L{l}_"
    vis_k = lambda w: list(range(max(0, 4 * w - 4), 4 * w + 4))

    qT = [p.sbS(L + "B1", L + f"qT{i}", [128, T], BF16) for i in range(4)]
    kT = [p.sbS(L + "B1", L + f"kT{i}", [128, T], BF16) for i in range(4)]
    vtm = [p.sbS(L + "B1", L + f"vtm{i}", [128, 520], BF16) for i in range(32)]

    wq = [p.sbS(L + "A", L + f"wq{c}", [128, 1536], BF16) for c in range(NCH)]
    fin13 = [p.sbS(L + "A", L + f"fin13{c}", [128, 513]) for c in range(NCH)]
    finb2 = [[p.sbS(L + "A", L + f"finb{j}_{c}", [128, 513], BF16) for c in range(NCH)]
             for j in range(2)]
    xps = [p.sbS(L + "A", L + f"xps{i}", [64, TSL], BF16) for i in range(4)]
    l2a8 = [p.sbS(L + "A", L + f"l2a{i}", [64, TSL], BF16) for i in range(8)]
    sqf = [p.sbS(L + "A", L + f"sqf{i}", [128, TSL], BF16) for i in range(4)]
    srw = [p.sbS(L + "A", L + f"srw{i}", [2, TSL]) for i in range(4)]
    l2r = p.sbS(L + "A", L + "l2r", [64, TSL], BF16)
    l2t = p.sbS(L + "A", L + "l2t", [64, TSL], BF16)
    qscol = p.sbS(L + "A", L + "qscol", [128, 1])
    kscol = p.sbS(L + "A", L + "kscol", [128, 1])
    prev_done = "LC_done" if l == 0 else "L0_F4_done"

    @blk.sync
    def _(s):
        g_wait(p, s, prev_done)
        if l == 0:
            g_wait(p, s, "LC_done_odd")
            g_wait(p, s, "X1_done_e")
            g_wait(p, s, "X1_done_o")
        p.cE[3].inc(s.dma_start(out=qscol[0:64, :], in_=p.qscI[l][:, :]), 16)
        p.cE[3].inc(s.dma_start(out=qscol[64:128, :], in_=p.qscI[l][:, :]), 16)
        p.cE[3].inc(s.dma_start(out=kscol[0:64, :], in_=p.kscI[l][:, :]), 16)
        p.cE[3].inc(s.dma_start(out=kscol[64:128, :], in_=p.kscI[l][:, :]), 16)
    g_set(p, (L, "qsld"), p.cE[3])

    @blk.gpsimd
    def _(g):
        g_wait(p, g, prev_done)
        if l == 0:
            g_wait(p, g, "LC_done_odd")
            g_wait(p, g, "X1_done_e")
            g_wait(p, g, "X1_done_o")
        else:
            g_wait(p, g, "L0_F4_done_d1")
        for c in range(NCH):
            p.cD[1].inc(g.dma_start(out=wq[c][:, :],
                                    in_=p.wqkvI[l][c * 128:(c + 1) * 128, :]), 16)
        for i in range(32):
            ins = g.memset(vtm[i][:, :], 1.0)
        for j in range(2):
            for c in range(NCH):
                ins = g.memset(finb2[j][c][:, 0:1], 0.0)
        for c in range(NCH):
            ins = g.memset(fin13[c][:, 0:1], 0.0)
        p.cZ.inc(ins, 1)
    g_set(p, (L, "wqld"), p.cD[1])
    g_set(p, (L, "vtmz"), p.cZ)

    for sl in range(NSL):
        ssl = slice(sl * TSL, (sl + 1) * TSL)

        finb = finb2[sl % 2]

        @blk.sync
        def _(s, sl=sl):
            if sl == 0:
                g_wait(p, s, prev_done)
            if sl >= 1:
                g_wait(p, s, (L + "F1_cp", sl - 1))
            for c in range(NCH):
                if sl == 0:
                    p.cA.inc(s.dma_start(out=fin13[c][:, 1:513],
                                         in_=p.scr_fA[c, :, 0:512]), 16)
                else:
                    p.cA.inc(s.dma_start(out=fin13[c][:, 0:513],
                                         in_=p.scr_fA[c, :, sl * 512 - 1: (sl + 1) * 512]), 16)
        g_set(p, (L + "F1_ldf", sl), p.cA)

        @blk.vector
        def _(v, sl=sl, finb=finb):
            g_wait(p, v, (L + "F1_ldf", sl))
            if sl == 0:
                g_wait(p, v, (L, "vtmz"))
            if sl >= 2:
                g_wait(p, v, (L + "F1_vdone", sl - 2))
            for c in range(NCH):
                VI(p, v.tensor_copy(finb[c][:, :], fin13[c][:, :]))
        g_set(p, (L + "F1_cp", sl), p.cVc)
        g_set(p, (L + "F1_ld", sl), p.cVc)

        @blk.sync
        def _(s, sl=sl):
            if sl >= 1:
                g_wait(p, s, (L + "F1_xpos", sl * 16 - 1))
            for i, nm in enumerate(["xp_qc", "xp_qs", "xp_kc", "xp_ks"]):
                p.cE[1].inc(s.dma_start(out=xps[i][:, :],
                                        in_=p.xpI[nm][:, sl * TSL:(sl + 1) * TSL]), 16)
        g_set(p, (L + "F1_xpld", sl), p.cE[1])

        def rhsv(c, w=TSL, t0=0):
            off = (1 if c < 4 else 0) + t0
            return finb[c][:, off:off + w]

        for fc in range(8):
            @blk.tensor
            def _(t, sl=sl, fc=fc):
                k = sl * 8 + fc
                if k == 0:
                    g_wait(p, t, (L, "wqld"))
                    g_wait(p, t, "LC_done" if l == 0 else "L0_F4_done")
                g_wait(p, t, (L + "F1_ld", sl))
                if k >= 2:
                    g_wait(p, t, (L + "F1_qkev", k - 2))
                bk = p.bank(fc % 2)
                for c in range(NCH):
                    ins = t.matmul(bk, wq[c][:, fc * 128:(fc + 1) * 128],
                                   rhsv(c), start=(c == 0), stop=(c == NCH - 1))
                cT.inc(ins, 1)
            g_set(p, (L + "F1_qk", sl * 8 + fc), cT)

            @blk.vector
            def _(v, sl=sl, fc=fc, ssl=ssl):
                k = sl * 8 + fc
                g_wait(p, v, (L + "F1_qk", k))
                dst = qT[fc] if fc < 4 else kT[fc - 4]
                VI(p, v.tensor_copy(dst[:, ssl], p.bank(fc % 2)))
            g_set(p, (L + "F1_qkev", sl * 8 + fc), p.cVc)

        for tsub in range(4):
            @blk.tensor
            def _(t, sl=sl, tsub=tsub):
                kv = sl * 4 + tsub
                if kv >= 2:
                    g_wait(p, t, (L + "F1_vev", kv - 2))
                bk = p.bank(2 + kv % 2)
                for c in range(NCH):
                    ins = t.matmul(bk, rhsv(c, 128, tsub * 128),
                                   wq[c][:, 1024:1536], start=(c == 0), stop=(c == NCH - 1))
                cT.inc(ins, 1)
            g_set(p, (L + "F1_v", sl * 4 + tsub), cT)

            @blk.vector
            def _(v, sl=sl, tsub=tsub):
                kv = sl * 4 + tsub
                g_wait(p, v, (L + "F1_v", kv))
                bk = p.bank(2 + kv % 2)
                for h in range(FH):
                    VI(p, v.tensor_copy(vtm[kv][:, h * 65: h * 65 + 64],
                                        bk[:, h * 64:(h + 1) * 64]))
            g_set(p, (L + "F1_vev", sl * 4 + tsub), p.cVc)

        for wg in range(2):
            prs = [wg * 4 + j for j in range(4)]

            @blk.vector
            def _(v, sl=sl, wg=wg, prs=prs, ssl=ssl):
                if wg == 0:
                    g_wait(p, v, (L + "F1_qkev", sl * 8 + 7))
                    g_wait(p, v, (L + "F1_xpld", sl))
                    if sl == 0:
                        g_wait(p, v, (L, "qsld"))
                for j, pr in enumerate(prs):
                    src = qT[pr] if pr < 4 else kT[pr - 4]
                    scol = qscol if pr < 4 else kscol
                    if sl > 0 or wg > 0:
                        pw = (sl, wg - 1) if wg > 0 else (sl - 1, 1)
                        g_wait(p, v, (L + "F1_stmm", (pw[0], pw[1], j)))
                    VW(p, v)
                    VI(p, v.tensor_scalar(l2a8[2 * j][:, :], src[0:64, ssl],
                                          scol[0:64, :], None, op0=OP.mult))
                    VI(p, v.tensor_scalar(l2a8[2 * j + 1][:, :], src[64:128, ssl],
                                          scol[64:128, :], None, op0=OP.mult))
                    VW(p, v)
                    VI(p, v.tensor_tensor(sqf[j][0:64, :], l2a8[2 * j][:, :],
                                          l2a8[2 * j][:, :], op=OP.mult))
                    VI(p, v.tensor_tensor(sqf[j][64:128, :], l2a8[2 * j + 1][:, :],
                                          l2a8[2 * j + 1][:, :], op=OP.mult))
            g_set(p, (L + "F1_sqw", (sl, wg)), p.cVc)

            @blk.tensor
            def _(t, sl=sl, wg=wg, prs=prs):
                g_wait(p, t, (L + "F1_sqw", (sl, wg)))
                if l == 0 and sl == 0 and wg == 0:
                    g_wait(p, t, "consts")
                    g_wait(p, t, "constsW")
                for j, pr in enumerate(prs):
                    if sl > 0 or wg > 0:
                        pw = (sl, wg - 1) if wg > 0 else (sl - 1, 1)
                        g_wait(p, t, (L + "F1_xposw", pw))
                    cT.inc(t.matmul(p.bank(2 + j)[0:2, :], p.sel2[:, :],
                                    sqf[j][:, :], start=True, stop=True), 1)
                    g_set(p, (L + "F1_stmm", (sl, wg, j)), cT)

            @blk.scalar
            def _(s, sl=sl, wg=wg, prs=prs):
                for j, pr in enumerate(prs):
                    g_wait(p, s, (L + "F1_stmm", (sl, wg, j)))
                    if sl > 0 or wg > 0:
                        pw = (sl, wg - 1) if wg > 0 else (sl - 1, 1)
                        g_wait(p, s, (L + "F1_plw", pw))
                    SW(p, s)
                    SI(p, s.activation(srw[j][:, :], p.bank(2 + j)[0:2, :], AF.Ln,
                                       bias=p.epsl2_col[0:2, :]))
                    SW(p, s)
                    SI(p, s.activation(srw[j][:, :], srw[j][:, :], AF.Exp, scale=-0.5))
            g_set(p, (L + "F1_rstdw", (sl, wg)), p.cSc)

            @blk.tensor
            def _(t, sl=sl, wg=wg, prs=prs):
                g_wait(p, t, (L + "F1_rstdw", (sl, wg)))
                for j, pr in enumerate(prs):
                    cT.inc(t.matmul(p.bank(2 + j), p.brd2[:, :], srw[j][:, :],
                                    start=True, stop=True), 1)
            g_set(p, (L + "F1_plw", (sl, wg)), cT)

            @blk.vector
            def _(v, sl=sl, wg=wg, prs=prs, ssl=ssl):
                g_wait(p, v, (L + "F1_plw", (sl, wg)))
                for j, pr in enumerate(prs):
                    src = qT[pr] if pr < 4 else kT[pr - 4]
                    xc, xs = (xps[0], xps[1]) if pr < 4 else (xps[2], xps[3])
                    bk = p.bank(2 + j)
                    for h in range(2):
                        a = l2a8[2 * j + h]
                        r0 = h * 64
                        VW(p, v)
                        VI(p, v.tensor_scalar(l2r[0:32, :], a[32:64, :], -1.0, None,
                                              op0=OP.mult))
                        VI(p, v.tensor_copy(l2r[32:64, :], a[0:32, :]))
                        VW(p, v)
                        VI(p, v.tensor_tensor(l2t[:, :], a[:, :], xc[:, :], op=OP.mult))
                        VI(p, v.tensor_tensor(l2r[:, :], l2r[:, :], xs[:, :], op=OP.mult))
                        VW(p, v)
                        VI(p, v.tensor_tensor(l2t[:, :], l2t[:, :], l2r[:, :], op=OP.add))
                        VW(p, v)
                        VI(p, v.tensor_tensor(src[r0:r0 + 64, ssl], l2t[:, :],
                                              bk[r0:r0 + 64, :], op=OP.mult))
            g_set(p, (L + "F1_xposw", (sl, wg)), p.cVc)
        g_set(p, (L + "F1_xpos", sl * 16 + 15), p.cVc)
        @blk.vector
        def _(v, sl=sl):
            VW(p, v)
            VI(p, v.tensor_copy(l2t[0:1, 0:1], l2t[0:1, 0:1]))
        g_set(p, (L + "F1_vdone", sl), p.cVc)
    g_set(p, L + "F1_done", p.cVc)
    p.close_scope(L + "A")

    attf = [p.sbS(L + "C", L + f"attf{i}", [128, TSL], BF16) for i in range(4)]
    p.lmask_sb = [p.sbS(L + "C", L + f"lmask{i}", [128, TSL], BF16) for i in range(8)]
    negB = [p.sbS(L + "C", L + f"negB{i}", [1, T], BF16) for i in range(2)]
    Bbuf = [p.sbS(L + "C", L + f"Bbuf{i}", [128, 32]) for i in range(2)]
    bstg = [p.sbS(L + "C", L + f"bstg{i}", [32, 128]) for i in range(2)]
    yt65 = p.sbS(L + "C", L + "yt65", [128, TSL])
    invloc = p.sbS(L + "C", L + "invloc", [1, TSL])
    ostg = [p.sbS(L + "C", L + f"ostg{i}", [64, TSL], BF16) for i in range(2)]

    @blk.sync
    def _(s):
        g_wait(p, s, L + "F1_done")
        for i in range(8):
            p.cE[3].inc(s.dma_start(out=p.lmask_sb[i][:, :], in_=p.lmaskI[i, :, :]), 16)
    g_set(p, (L, "lmaskld"), p.cE[3])

    g2 = [p.cSL.n]
    f2slot = {}
    f2avval = {}
    Bbuf2 = [p.sbS(L + "C", L + f"Bbuf2_{i}", [128, 64]) for i in range(2)]

    F2B = [2, 3, 0]

    def nslot2(t):
        g2[0] += 1
        t.wait_ge(p.cSL.h, g2[0] - 3)
        return p.bank(F2B[g2[0] % 3])

    gp2 = [p.cRX.n]

    def nslotP(t):
        gp2[0] += 1
        if gp2[0] - 1 > 0:
            t.wait_ge(p.cRX.h, gp2[0] - 1)
        return p.bank(1)

    def run_P1(h):
        Qs, Ks, r0 = qT[h // 2], kT[h // 2], (h % 2) * 64
        for w in range(8):
            kk = vis_k(w)
            nb = len(kk) // 4
            slots_w = {}

            @blk.tensor
            def _(t, h=h, w=w, kk=kk, nb=nb, Qs=Qs, Ks=Ks, r0=r0, slots_w=slots_w):
                if h == 0 and w == 0:
                    g_wait(p, t, L + "F1_done")
                for qt in range(4):
                    for j in range(nb):
                        bk = nslot2(t)
                        slots_w[(qt, j)] = g2[0]
                        cT.inc(t.matmul(
                            bk,
                            Qs[r0:r0 + 64, w * 512 + qt * 128: w * 512 + (qt + 1) * 128],
                            Ks[r0:r0 + 64, kk[0] * 128 + j * 512: kk[0] * 128 + (j + 1) * 512],
                            start=True, stop=True), 1)
                        g_set(p, (L + "F2_p1mm", (h, w, qt, j)), cT)

            @blk.vector
            def _(v, h=h, w=w, nb=nb, slots_w=slots_w):
                for qt in range(4):
                    for j in range(nb):
                        g_wait(p, v, (L + "F2_p1mm", (h, w, qt, j)))
                        if w == 0 and qt == 0 and j == 0 and h >= 2:
                            g_wait(p, v, (L + "F2_bev", h - 2))
                        gg = slots_w[(qt, j)]
                        bk = p.bank(F2B[gg % 3])
                        dstcol = Bbuf2[h % 2][:, (w * 4 + qt) * 2 + j: (w * 4 + qt) * 2 + j + 1]
                        p.cSL.inc(v.reduce_max(dstcol, bk, axis=AX.X), 1)
                    if nb == 1:
                        v.wait_ge(p.cSL.h, p.cSL.n)
                        VI(p, v.tensor_copy(
                            Bbuf2[h % 2][:, (w * 4 + qt) * 2 + 1: (w * 4 + qt) * 2 + 2],
                            Bbuf2[h % 2][:, (w * 4 + qt) * 2: (w * 4 + qt) * 2 + 1]))
            g_set(p, (L + "F2_maxw", (h, w)), p.cSL)

    def emit_bfinish(h):
        @blk.vector
        def _(v, h=h):
            g_wait(p, v, (L + "F2_maxw", (h, 7)))
            VW(p, v)
            VI(p, v.tensor_tensor(Bbuf[h % 2][:, :], Bbuf2[h % 2][:, 0:64:2],
                                  Bbuf2[h % 2][:, 1:64:2], op=OP.max))
            VW(p, v)
            VI(p, v.tensor_scalar(Bbuf[h % 2][:, :], Bbuf[h % 2][:, :], -1.0, None,
                                  op0=OP.mult))
        g_set(p, (L + "F2_bneg", h), p.cVc)

        @blk.tensor
        def _(t, h=h):
            g_wait(p, t, (L + "F2_bneg", h))
            bk = nslotP(t)
            cT.inc(t.transpose(bk[0:32, 0:128], Bbuf[h % 2][:, :], p.ident_f[:, :]), 1)
        g_set(p, (L + "F2_bT", h), cT)

        @blk.vector
        def _(v, h=h):
            g_wait(p, v, (L + "F2_bT", h))
            bk = p.bank(1)
            p.cRX.inc(v.tensor_copy(bstg[h % 2][:, :], bk[0:32, 0:128]), 1)
        g_set(p, (L + "F2_bev", h), p.cRX)

        @blk.gpsimd
        def _(g, h=h):
            g_wait(p, g, (L + "F2_bev", h))
            cc = p.cD[2 + h % 2]
            cc.inc(g.dma_start(out=p.scr_bb[h, :, :], in_=bstg[h % 2][:, :]), 16)
            g.wait_ge(cc.h, cc.n)
            cc.inc(g.dma_start(
                out=negB[h % 2][0:1, :],
                in_=p.scr_bb[h].rearrange("a b -> (a b)")[None, :]), 16)
        g_set(p, (L + "F2_brow", h), p.cD[2 + h % 2])

    def emit_P2(h):
        Qs, Ks, r0 = qT[h // 2], kT[h // 2], (h % 2) * 64
        for w in range(8):
            kk = vis_k(w)
            nk = len(kk)

            @blk.tensor
            def _(t, h=h, w=w, kk=kk, nk=nk, Qs=Qs, Ks=Ks, r0=r0):
                if w == 0:
                    g_wait(p, t, (L + "F2_brow", h))
                    if h == 0:
                        g_wait(p, t, (L, "lmaskld"))
                    if h + 2 < FH:
                        g_wait(p, t, (L + "F2_maxw", (h + 2, 7)))
                for ki in range(nk + 1):
                    if ki < nk:
                        ktg = kk[ki]
                        bk = nslot2(t)
                        f2slot[(h, w, ki)] = g2[0]
                        t.matmul(bk, Ks[r0:r0 + 64, ktg * 128:(ktg + 1) * 128],
                                 Qs[r0:r0 + 64, w * 512:(w + 1) * 512],
                                 start=True, stop=False)
                        t.matmul(bk, p.ones_rb[0:1, :],
                                 negB[h % 2][0:1, w * 512:(w + 1) * 512],
                                 start=False, stop=False)
                        ridx = (ktg * 128 - w * 512 + 512) // 128
                        ins = t.matmul(bk, p.ident_b[:, :], p.lmask_sb[ridx][:, :],
                                       start=False, stop=True)
                        cT.inc(ins, 1)
                        g_set(p, (L + "F2_p2mm", (h, w, ki)), cT)
                    if ki >= 1:
                        k2 = ki - 1
                        t.wait_ge(p.cSL.h, f2slot[(h, w, k2)])
                        if k2 == 0 and not (h == 0 and w <= 1):
                            pw = (h, w - 2) if w >= 2 else (h - 1, 6 + w)
                            g_wait(p, t, (L + "F2_yev", pw))
                        avb = p.bank(4 + w % 2)
                        ins = t.matmul(avb[0:65, :], vtm[kk[k2]][:, h * 65: h * 65 + 65],
                                       attf[f2slot[(h, w, k2)] % 4][:, :],
                                       start=(k2 == 0), stop=(k2 == nk - 1))
                        cT.inc(ins, 1)
                        g_set(p, (L + "F2_avk", (h, w, k2)), cT)
            g_set(p, (L + "F2_av", (h, w)), cT)

            @blk.scalar
            def _(s, h=h, w=w, nk=nk):
                for ki in range(nk):
                    gg = f2slot[(h, w, ki)]
                    g_wait(p, s, (L + "F2_p2mm", (h, w, ki)))
                    pk = (h, w, ki - 4)
                    if ki < 4:
                        if w > 0:
                            nkp = len(vis_k(w - 1))
                            pk = (h, w - 1, nkp - 4 + ki)
                        elif h > 0:
                            pk = (h - 1, 7, 4 + ki)
                        else:
                            pk = None
                    if pk is not None:
                        g_wait(p, s, (L + "F2_avk", pk))
                    bk = p.bank(F2B[gg % 3])
                    p.cSL.inc(s.activation(attf[gg % 4][:, :], bk, AF.Exp, scale=8.0), 1)
            g_set(p, (L + "F2_exp", (h, w)), p.cSL)

            @blk.vector
            def _(v, h=h, w=w):
                g_wait(p, v, (L + "F2_av", (h, w)))
                if w > 0:
                    g_wait(p, v, (L + "F2_ysc", (h, w - 1)))
                elif h > 0:
                    g_wait(p, v, (L + "F2_ysc", (h - 1, 7)))
                avb = p.bank(4 + w % 2)
                VI(p, v.tensor_copy(yt65[0:65, :], avb[0:65, :]))
                g_set(p, (L + "F2_yev", (h, w)), p.cVc)
                VW(p, v)
                VI(p, v.reciprocal(invloc[:, :], yt65[64:65, :]))
            g_set(p, (L + "F2_invl", (h, w)), p.cVc)

            @blk.tensor
            def _(t, h=h, w=w):
                g_wait(p, t, (L + "F2_invl", (h, w)))
                bk = nslotP(t)
                cT.inc(t.matmul(bk, p.ones_rf[:, :], invloc[:, :], start=True, stop=True), 1)
            g_set(p, (L + "F2_ypl", (h, w)), cT)

            @blk.vector
            def _(v, h=h, w=w, r0=r0):
                g_wait(p, v, (L + "F2_ypl", (h, w)))
                nw = h * 8 + w
                if nw >= 2:
                    g_wait(p, v, (L + "F2_osp", ((nw - 2) // 8, (nw - 2) % 8)))
                bk = p.bank(1)
                p.cRX.inc(v.tensor_tensor(ostg[nw % 2][:, :],
                                          yt65[0:64, :], bk[0:64, :], op=OP.mult), 1)
            g_set(p, (L + "F2_ysc", (h, w)), p.cRX)

            @blk.gpsimd
            def _(g, h=h, w=w, r0=r0):
                g_wait(p, g, (L + "F2_ysc", (h, w)))
                nw = h * 8 + w
                cs = p.cD[nw % 2]
                cs.inc(g.dma_start(
                    out=p.scr_v[h // 2, r0:r0 + 64, w * 512:(w + 1) * 512],
                    in_=ostg[nw % 2][:, :]), 16)
            g_set(p, (L + "F2_osp", (h, w)), p.cD[nw % 2] if False else p.cD[(h * 8 + w) % 2])

    p.f2_yslot = {}
    p.f2_bslot = {}
    run_P1(0)
    run_P1(1)
    for h in range(FH):
        emit_bfinish(h)
        if h + 2 < FH:
            run_P1(h + 2)
        emit_P2(h)
    g_set(p, L + "F2_done", p.cSL)

    @blk.gpsimd
    def _(g):
        for h in range(FH):
            for w in range(8):
                g_wait(p, g, (L + "F2_osp", (h, w)))
        p.cD[0].inc(g.dma_start(out=bstg[0][0:1, 0:1].rearrange("a b -> (a b)")[None, :]
                                if False else p.scr_bb[0, 0:1, 0:1], in_=bstg[0][0:1, 0:1]), 16)
    g_set(p, (L, "oTspill"), p.cD[0])
    p.close_scope(L + "C")
    p.close_scope(L + "B1")

    if l == 1:
        p.row5 = [p.sbS("LN5", "row5_rstd", [1, T], side="right"),
                  p.sbS("LN5", "row5_mr", [1, T], side="right")]
    p.rowF = [p.sbS(L + "LN", L + "rowF_rstd", [1, 2 * TSL], side="right"),
              p.sbS(L + "LN", L + "rowF_mr", [1, 2 * TSL], side="right")]
    hb16 = [p.sbS(L + "D4", L + f"hb16{c}", [128, TSL], BF16) for c in range(NCH)]
    agb = [p.sbS(L + "D4", L + f"agb{i}", [128, TSL], BF16) for i in range(22)]
    gt = p.sbS(L + "D4", L + "gt", [128, TSL], BF16)
    at = p.sbS(L + "D4", L + "at", [128, TSL])
    fm13 = [p.sbS(L + "D4", L + f"fm13{c}", [128, 513]) for c in range(NCH)]
    w1sb = [p.sbS(L + "D4", L + f"w1sb{i}", [128, 8 * 128], BF16) for i in range(4)]
    w2stb = [p.sbS(L + "D4", L + f"w2stb{i}", [128, 22 * 128], BF16) for i in range(2)]
    b1sb = p.sbS(L + "D4", L + "b1sb", [128, 44])
    b2sb = p.sbS(L + "D4", L + "b2sb", [128, NCH])
    lgsb = p.sbS(L + "D4", L + "lgsb", [128, NCH])
    lbsb = p.sbS(L + "D4", L + "lbsb", [128, NCH])
    xb5 = [p.sbS(L + "D4", L + f"xb5{c}", [128, TSL], BF16) for c in range(NCH)]
    sq5 = [p.sbS(L + "D4", L + f"sq5{c}", [128, TSL], BF16) for c in range(NCH)]
    wo_sb = [p.sbS(L + "D3", L + f"wo{i}", [128, C], BF16) for i in range(4)]
    oTs2 = [p.sbS(L + "D3", L + f"oTs{i}", [128, 4 * TSL], BF16) for i in range(2)]
    finD2 = [[p.sbS(L + "D3", L + f"finD{j}_{c}", [128, TSL]) for c in range(NCH)]
             for j in range(2)]
    carry = [p.sbS(L + "D3", L + f"carry{c}", [128, 1]) for c in range(NCH)]
    carryF = [p.sbS(L + "D3", L + f"carryF{c}", [128, 1]) for c in range(NCH)]
    xbD = [p.sbS(L + "D3", L + f"xbD{c}", [128, TSL], BF16) for c in range(NCH)]
    sqD = [p.sbS(L + "D3", L + f"sqD{c}", [128, TSL], BF16) for c in range(NCH)]

    @blk.sync
    def _(s):
        g_wait(p, s, L + "F2_done")
        for dst, srcI in [(b1sb, p.b1pI[l]), (b2sb, p.b2tI[l]),
                          (lgsb, p.lgtI[l]), (lbsb, p.lbtI[l])]:
            p.cE[3].inc(s.dma_start(out=dst[:, :], in_=srcI[:, :]), 16)
    g_set(p, (L, "bld"), p.cE[3])

    @blk.gpsimd
    def _(g):
        g_wait(p, g, L + "F2_done")
        for i in range(4):
            p.cD[1].inc(g.dma_start(out=wo_sb[i][:, :],
                                    in_=p.woI[l][i * 128:(i + 1) * 128, :]), 16)
        for c in range(NCH):
            ins = g.memset(carry[c][:, :], 0.0)
        p.cZ.inc(ins, 1)
    g_set(p, (L, "wold"), p.cD[1])

    @blk.gpsimd
    def _(g):
        for c in range(NCH):
            ins = g.memset(fm13[c][:, 0:1], 0.0)
        p.cZ.inc(ins, 1)
    g_set(p, (L, "fm13z"), p.cZ)
    g_set(p, (L, "carryz"), p.cZ)

    for sl in range(NSL):
        ssl = slice(sl * TSL, (sl + 1) * TSL)
        finD = finD2[sl % 2]
        fmD = finD
        oTs = oTs2[sl % 2]

        @blk.sync
        def _(s, sl=sl, ssl=ssl, finD=finD, oTs=oTs):
            g_wait(p, s, L + "F2_done")
            g_wait(p, s, (L, "oTspill"))
            if sl >= 2:
                g_wait(p, s, (L + "F3_mmdone", sl - 2))
                g_wait(p, s, (L + "F3_xbsq", sl - 2))
                g_wait(p, s, (L + "F3_fm13", sl - 2))
            cs = [p.cA, p.cE[2]][sl % 2]
            for c in range(NCH):
                cs.inc(s.dma_start(out=finD[c][:, :], in_=p.scr_fA[c, :, ssl]), 16)
            for i in range(4):
                cs.inc(s.dma_start(out=oTs[0 * 0][:, i * TSL:(i + 1) * TSL]
                                   if False else oTs[:, i * TSL:(i + 1) * TSL],
                                   in_=p.scr_v[i, :, ssl]), 16)
        g_set(p, (L + "F3_ld", sl), [p.cA, p.cE[2]][sl % 2])

        for oc in range(NCH):
            @blk.tensor
            def _(t, sl=sl, oc=oc, ssl=ssl, oTs=oTs):
                k = sl * 8 + oc
                if k == 0:
                    g_wait(p, t, (L, "wold"))
                if oc == 0 and sl >= 1:
                    g_wait(p, t, (L + "F4_ag", (sl - 1, 21)))
                if k >= 2:
                    g_wait(p, t, (L + "F3_hev", k - 2))
                g_wait(p, t, (L + "F3_ld", sl))
                bk = p.bank(k % 2)
                for i in range(4):
                    ins = t.matmul(bk, wo_sb[i][:, oc * 128:(oc + 1) * 128],
                                   oTs[:, i * TSL:(i + 1) * TSL],
                                   start=(i == 0), stop=(i == 3))
                cT.inc(ins, 1)
            g_set(p, (L + "F3_mm", sl * 8 + oc), cT)
            if oc == NCH - 1:
                g_set(p, (L + "F3_mmdone", sl), cT)

            @blk.vector
            def _(v, sl=sl, oc=oc, finD=finD):
                k = sl * 8 + oc
                g_wait(p, v, (L + "F3_mm", k))
                if oc == 0:
                    g_wait(p, v, (L + "F3_ld", sl))
                VI(p, v.tensor_tensor(finD[oc][:, :], p.bank(k % 2), finD[oc][:, :], op=OP.add))
            g_set(p, (L + "F3_hev", sl * 8 + oc), p.cVc)
        g_set(p, (L + "F3_h", sl), p.cVc)

        @blk.vector
        def _(v, sl=sl, fmD=fmD):
            VW(p, v)
            if sl == 0:
                g_wait(p, v, (L, "carryz"))
            for c in range(NCH):
                if c == 0 and sl >= 1:
                    g_wait(p, v, (L + "F3r_stats", sl - 1))
                if c < 4:
                    VI(p, v.tensor_copy(xbD[c][:, :], fmD[c][:, :]))
                else:
                    VI(p, v.tensor_copy(xbD[c][:, 0:1], carry[c][:, :]))
                    VI(p, v.tensor_copy(xbD[c][:, 1:TSL], fmD[c][:, 0:TSL - 1]))
                    VW(p, v)
                    VI(p, v.tensor_copy(carry[c][:, :], fmD[c][:, TSL - 1:TSL]))
                VW(p, v)
                VI(p, v.tensor_tensor(sqD[c][:, :], xbD[c][:, :], xbD[c][:, :], op=OP.mult))
        g_set(p, (L + "F3_xbsq", sl), p.cVc)

        @blk.tensor
        def _(t, sl=sl):
            g_wait(p, t, (L + "F3_xbsq", sl))
            if sl >= 1:
                g_wait(p, t, ((L + "F3r", sl - 1), "var"))
            for c in range(NCH):
                t.matmul(p.bank(2)[0:1, :], p.inv_c[:, :], xbD[c][:, :],
                         start=(c == 0), stop=(c == NCH - 1))
            for c in range(NCH):
                ins = t.matmul(p.bank(3)[0:1, :], p.inv_c[:, :], sqD[c][:, :],
                               start=(c == 0), stop=(c == NCH - 1))
            cT.inc(ins, 1)
        g_set(p, (L + "F3r_stats", sl), cT)
        extraF = [(L + "F4_planes", sl - 2)] if sl >= 2 else []
        emit_row_finish(p, (L + "F3r", sl), p.rowF,
                        slice((sl % 2) * TSL, (sl % 2 + 1) * TSL), extra_scalar=extraF)

        @blk.vector
        def _(v, sl=sl, fmD=fmD):
            VW(p, v)
            if sl >= 1:
                g_wait(p, v, (L + "F4_fod", sl - 1))
                g_wait(p, v, (L + "F4_spill", sl - 1))
                if l == 1:
                    g_wait(p, v, (L + "F5_xbsq", sl - 1))
            for c in range(NCH):
                if c >= 4 and sl >= 1:
                    VI(p, v.tensor_copy(fm13[c][:, 0:1], carryF[c][:, :]))
                VI(p, v.tensor_copy(fm13[c][:, 1:513], fmD[c][:, :]))
                if c >= 4:
                    VW(p, v)
                    VI(p, v.tensor_copy(carryF[c][:, :], fmD[c][:, TSL - 1:TSL]))
                    VW(p, v)
        g_set(p, (L + "F3_fm13", sl), p.cVc)

        @blk.tensor
        def _(t, sl=sl, ssl=ssl):
            g_wait(p, t, ((L + "F3r", sl), "rows"))
            if sl >= 1:
                g_wait(p, t, (L + "F4_w2evdone", sl - 1))
            rslF = slice((sl % 2) * TSL, (sl % 2 + 1) * TSL)
            t.matmul(p.bank(4), p.ones_rf[:, :], p.rowF[0][0:1, rslF], start=True, stop=True)
            ins = t.matmul(p.bank(5), p.ones_rf[:, :], p.rowF[1][0:1, rslF], start=True, stop=True)
            cT.inc(ins, 1)
        g_set(p, (L + "F4_planes", sl), cT)

        @blk.vector
        def _(v, sl=sl):
            g_wait(p, v, (L + "F4_planes", sl))
            g_wait(p, v, (L + "F3_fm13", sl))
            if sl == 0:
                g_wait(p, v, (L, "fm13z"))
                g_wait(p, v, (L, "bld"))
            if sl >= 1:
                g_wait(p, v, (L + "F4_w1", (sl - 1, 21)))
            for c in range(NCH):
                off = 1 if c < 4 else 0
                sh = fm13[c][:, off:off + TSL]
                VI(p, v.tensor_tensor(hb16[c][:, :], sh, p.bank(4), op=OP.mult))
                VW(p, v)
                VI(p, v.tensor_tensor(hb16[c][:, :], hb16[c][:, :], p.bank(5), op=OP.add))
                VW(p, v)
                VI(p, v.tensor_scalar(hb16[c][:, :], hb16[c][:, :], lgsb[:, c:c + 1],
                                      lbsb[:, c:c + 1], op0=OP.mult, op1=OP.add))
        g_set(p, (L + "F4_apply", sl), p.cVc)

        for mp in range(22):
            rr = min(128, FHID - mp * 128)
            kk = sl * 22 + mp

            @blk.gpsimd
            def _(g, sl=sl, mp=mp, rr=rr, kk=kk):
                if kk >= 2:
                    g_wait(p, g, (L + "F4_w1", (sl, mp - 2) if mp >= 2 else (sl - 1, 20 + mp)))
                buf_a = w1sb[(kk % 2) * 2]
                buf_g = w1sb[(kk % 2) * 2 + 1]
                srcA = p.w1I[l][:, mp * 128: mp * 128 + rr].rearrange(
                    "(c p) n -> p c n", p=128)
                srcG = p.w1I[l][:, FHID + mp * 128: FHID + mp * 128 + rr].rearrange(
                    "(c p) n -> p c n", p=128)
                cw1 = p.cF[kk % 2]
                cw1.inc(g.dma_start(
                    out=buf_a.rearrange("p (c n) -> p c n", c=NCH)[:, :, 0:rr], in_=srcA), 16)
                cw1.inc(g.dma_start(
                    out=buf_g.rearrange("p (c n) -> p c n", c=NCH)[:, :, 0:rr], in_=srcG), 16)
            g_set(p, (L + "F4_w1ld", (sl, mp)), p.cF[kk % 2])

            @blk.tensor
            def _(t, sl=sl, mp=mp, rr=rr, kk=kk):
                g_wait(p, t, (L + "F4_apply", sl))
                g_wait(p, t, (L + "F4_w1ld", (sl, mp)))
                if mp >= 2:
                    g_wait(p, t, (L + "F4_ag", (sl, mp - 2)))
                elif sl >= 1:
                    g_wait(p, t, (L + "F4_ag", (sl - 1, 20 + mp)))
                ba = p.bank((mp % 2) * 2)
                bg = p.bank((mp % 2) * 2 + 1)
                buf_a = w1sb[(kk % 2) * 2]
                buf_g = w1sb[(kk % 2) * 2 + 1]
                for c in range(NCH):
                    t.matmul(ba[0:rr, :], buf_a[:, c * 128: c * 128 + rr],
                             hb16[c][:, :], start=(c == 0), stop=(c == NCH - 1))
                for c in range(NCH):
                    ins = t.matmul(bg[0:rr, :], buf_g[:, c * 128: c * 128 + rr],
                                   hb16[c][:, :], start=(c == 0), stop=(c == NCH - 1))
                cT.inc(ins, 1)
            g_set(p, (L + "F4_w1", (sl, mp)), cT)

            @blk.scalar
            def _(s, sl=sl, mp=mp, rr=rr):
                g_wait(p, s, (L + "F4_w1", (sl, mp)))
                if mp >= 1:
                    g_wait(p, s, (L + "F4_ag", (sl, mp - 1)))
                elif sl >= 1:
                    g_wait(p, s, (L + "F4_ag", (sl - 1, 21)))
                bg = p.bank((mp % 2) * 2 + 1)
                SW(p, s)
                SI(p, s.activation(gt[0:rr, :], bg[0:rr, :], AF.Gelu,
                                   bias=b1sb[0:rr, 22 + mp: 23 + mp]))
            g_set(p, (L + "F4_gelu", (sl, mp)), p.cSc)

            @blk.vector
            def _(v, sl=sl, mp=mp, rr=rr):
                g_wait(p, v, (L + "F4_gelu", (sl, mp)))
                ba = p.bank((mp % 2) * 2)
                VI(p, v.scalar_tensor_tensor(agb[mp][0:rr, :], ba[0:rr, :],
                                             b1sb[0:rr, mp:mp + 1], gt[0:rr, :],
                                             op0=OP.add, op1=OP.mult))
            g_set(p, (L + "F4_ag", (sl, mp)), p.cVc)

        for oc in range(NCH):
            kk2 = sl * 8 + oc

            @blk.gpsimd
            def _(g, sl=sl, oc=oc, kk2=kk2):
                kk = kk2
                if kk >= 2:
                    g_wait(p, g, (L + "F4_w2", (sl, oc - 2) if oc >= 2 else (sl - 1, 6 + oc)))
                buf = w2stb[kk % 2]
                src = p.w2I[l][:, oc * 128:(oc + 1) * 128]
                cw2 = p.cF[2 + kk2 % 2]
                cw2.inc(g.dma_start(
                    out=buf.rearrange("p (c n) -> p c n", c=22)[0:128, 0:21, :],
                    in_=p.w2I[l][0:2688, oc * 128:(oc + 1) * 128].rearrange(
                        "(c p) n -> p c n", p=128)), 16)
                cw2.inc(g.dma_start(
                    out=buf[0:42, 21 * 128: 22 * 128],
                    in_=p.w2I[l][2688:2730, oc * 128:(oc + 1) * 128]), 16)
            g_set(p, (L + "F4_w2ld", (sl, oc)), p.cF[2 + kk2 % 2])

            @blk.tensor
            def _(t, sl=sl, oc=oc):
                g_wait(p, t, (L + "F4_ag", (sl, 21)))
                g_wait(p, t, (L + "F4_w2ld", (sl, oc)))
                k = sl * 8 + oc
                if k >= 2:
                    g_wait(p, t, (L + "F4_w2ev", k - 2))
                bk = p.bank(4 + oc % 2)
                buf = w2stb[k % 2]
                for i in range(22):
                    rr = min(128, FHID - i * 128)
                    ins = t.matmul(bk, buf[0:rr, i * 128: i * 128 + 128] if False else
                                   buf[0:rr, i * 128: (i + 1) * 128],
                                   agb[i][0:rr, :], start=(i == 0), stop=(i == 21))
                cT.inc(ins, 1)
            g_set(p, (L + "F4_w2", sl * 8 + oc if False else (sl, oc)), cT)

            @blk.vector
            def _(v, sl=sl, oc=oc):
                g_wait(p, v, (L + "F4_w2", (sl, oc)))
                bk = p.bank(4 + oc % 2)
                VW(p, v)
                VI(p, v.tensor_scalar(at[:, :], bk, b2sb[:, oc:oc + 1], None, op0=OP.add))
                VW(p, v)
                VI(p, v.tensor_tensor(fm13[oc][:, 1:513], at[:, :],
                                      fm13[oc][:, 1:513], op=OP.add))
            g_set(p, ((L + "F4_w2ev", sl * 8 + oc)) if False else (L + "F4_w2ev", sl * 8 + oc), p.cVc)
        g_set(p, (L + "F4_fod", sl), p.cVc)
        g_set(p, (L + "F4_w2evdone", sl), p.cVc)

        if l == 1:
            @blk.vector
            def _(v, sl=sl):
                VW(p, v)
                for c in range(NCH):
                    if c == 0 and sl >= 1:
                        g_wait(p, v, (L + "F5r_stats", sl - 1))
                    VI(p, v.tensor_copy(xb5[c][:, :], fm13[c][:, 1:513]))
                    VW(p, v)
                    VI(p, v.tensor_tensor(sq5[c][:, :], xb5[c][:, :], xb5[c][:, :], op=OP.mult))
            g_set(p, (L + "F5_xbsq", sl), p.cVc)

            @blk.tensor
            def _(t, sl=sl):
                g_wait(p, t, (L + "F5_xbsq", sl))
                if sl >= 1:
                    g_wait(p, t, ((L + "F5r", sl - 1), "var"))
                for c in range(NCH):
                    t.matmul(p.bank(2)[0:1, :], p.inv_c[:, :], xb5[c][:, :],
                             start=(c == 0), stop=(c == NCH - 1))
                for c in range(NCH):
                    ins = t.matmul(p.bank(3)[0:1, :], p.inv_c[:, :], sq5[c][:, :],
                                   start=(c == 0), stop=(c == NCH - 1))
                cT.inc(ins, 1)
            g_set(p, (L + "F5r_stats", sl), cT)
            emit_row_finish(p, (L + "F5r", sl), p.row5, slice(sl * TSL, (sl + 1) * TSL))

        @blk.gpsimd
        def _(g, sl=sl, ssl=ssl):
            g_wait(p, g, (L + "F4_fod", sl))
            if l == 1:
                g_wait(p, g, (L + "F5_xbsq", sl))
            if "d_xf" in p.dbg and l == 0 and sl == 0:
                g_wait(p, g, "DXF_dump")
            if "d_f0" in p.dbg and l == 1 and sl == 0:
                g_wait(p, g, "DF0_dump")
            for c in range(NCH):
                p.cD[4].inc(g.dma_start(out=p.scr_fA[c, :, ssl], in_=fm13[c][:, 1:513]), 16)
        g_set(p, (L + "F4_spill", sl), p.cD[4])
        g_set(p, (L + "F4_fod", sl), p.cVc) if False else None
    g_set(p, L + "F4_done", p.cD[4])
    g_set(p, L + "F4_done_d1", p.cD[1])
    g_set(p, L + "F4_rows", p.cVc)
    if "d_f0" in p.dbg and l == 0:
        @blk.sync
        def _(s):
            barrier(p, s)
            for c in range(NCH):
                p.cO.inc(s.dma_start(out=p.dbg["d_f0"][c, :, :], in_=p.scr_fA[c, :, :]), 16)
        g_set(p, "DF0_dump", p.cO)
    p.close_scope(L + "D3")
    p.close_scope(L + "D4")
    p.close_scope(L + "LN")


def emit_final(p):
    blk = p.blk
    cT, cA, cG = p.cT, p.cA, p.cG
    src2 = [[p.sbS("Z", f"srcZ{j}_{c}", [128, TSL]) for c in range(NCH)] for j in range(2)]
    aux2 = [[p.sbS("Z", f"auxZ{j}_{c}", [128, TSL]) for c in range(NCH)] for j in range(2)]
    tmout2 = [[p.sbS("Z", f"tmoutZ{j}_{i}", [128, C]) for i in range(4)] for j in range(2)]

    for sl in range(NSL):
        ssl = slice(sl * TSL, (sl + 1) * TSL)
        src = src2[sl % 2]
        aux = aux2[sl % 2]
        tmout = tmout2[sl % 2]

        @blk.sync
        def _(s, sl=sl, ssl=ssl, src=src, aux=aux):
            if sl == 0:
                g_wait(p, s, "L1_F4_done")
                g_wait(p, s, "L1_F4_rows")
                g_wait(p, s, "X1_done_e")
                g_wait(p, s, "X1_done_o")
            if sl == 1:
                g_wait(p, s, "L1_F4_done_d1")
            if sl >= 2:
                g_wait(p, s, ("Z_trdone", sl - 2))
            cs = [p.cA, p.cE[0]][sl % 2]
            for c in range(NCH):
                cs.inc(s.dma_start(out=src[c][:, :], in_=p.scr_fA[c, :, ssl]), 16)
                cs.inc(s.dma_start(out=aux[c][:, :], in_=p.scr_x1[c, :, ssl]), 16)
        g_set(p, ("Z_ld", sl), [p.cA, p.cE[0]][sl % 2])

        @blk.tensor
        def _(t, sl=sl, ssl=ssl):
            if sl == 0:
                g_wait(p, t, "L1_F4_rows")
            if sl >= 1:
                g_wait(p, t, ("Z_apply", sl - 1))
            t.matmul(p.bank(4), p.ones_rf[:, :], p.row5[0][0:1, ssl], start=True, stop=True)
            ins = t.matmul(p.bank(5), p.ones_rf[:, :], p.row5[1][0:1, ssl], start=True, stop=True)
            cT.inc(ins, 1)
        g_set(p, ("Z_planes", sl), cT)

        @blk.vector
        def _(v, sl=sl, src=src, aux=aux):
            g_wait(p, v, ("Z_planes", sl))
            g_wait(p, v, ("Z_ld", sl))
            for c in range(NCH):
                VI(p, v.tensor_tensor(src[c][:, :], src[c][:, :], p.bank(4), op=OP.mult))
                VW(p, v)
                VI(p, v.tensor_tensor(src[c][:, :], src[c][:, :], p.bank(5), op=OP.add))
                VW(p, v)
                VI(p, v.tensor_tensor(src[c][:, :], src[c][:, :], aux[c][:, :], op=OP.add))
        g_set(p, ("Z_apply", sl), p.cVc)

        for r in range(4):
            for c in range(NCH):
                k = sl * 32 + r * 8 + c

                @blk.tensor
                def _(t, sl=sl, r=r, c=c, k=k, src=src):
                    if r == 0 and c == 0:
                        g_wait(p, t, ("Z_apply", sl))
                    if k >= 2:
                        g_wait(p, t, ("Z_ev", k - 2))
                    pt = p.bank(k % 2)[:, 0:128]
                    cT.inc(t.transpose(pt, src[c][:, r * 128:(r + 1) * 128],
                                       p.ident_f[:, :]), 1)
                g_set(p, ("Z_tr", k), cT)

                @blk.vector
                def _(v, sl=sl, r=r, c=c, k=k, tmout=tmout):
                    g_wait(p, v, ("Z_tr", k))
                    if r == 0 and c == 0 and sl >= 2:
                        g_wait(p, v, ("Z_st", sl - 2))
                    pt = p.bank(k % 2)[:, 0:128]
                    VI(p, v.tensor_copy(tmout[r][:, c * 128:(c + 1) * 128], pt))
                g_set(p, ("Z_ev", k), p.cVc)
        g_set(p, ("Z_trdone", sl), p.cVc)

        @blk.sync
        def _(s, sl=sl, tmout=tmout):
            g_wait(p, s, ("Z_trdone", sl))
            cs = [p.cO, p.cE[3]][sl % 2]
            for r in range(4):
                cs.inc(s.dma_start(
                    out=p.out_t[sl * TSL + r * 128: sl * TSL + (r + 1) * 128, :],
                    in_=tmout[r][:, :]), 16)
        g_set(p, ("Z_st", sl), [p.cO, p.cE[3]][sl % 2])

    @blk.sync
    def _(s):
        s.wait_ge(p.cO.h, p.cO.n)
        s.wait_ge(p.cE[3].h, p.cE[3].n)


def prep_inputs(inputs, batch_idx, cst):
    m = {
        "x": np.ascontiguousarray(inputs["L0_attn_x"][batch_idx]),
        "w_in": np.ascontiguousarray(inputs["L0_attn_w_in"]),
        "w_out": np.ascontiguousarray(inputs["L0_attn_w_out"]),
        "gamma_t": np.ascontiguousarray(
            np.asarray(inputs["L0_attn_ln_gamma"]).reshape(NCH, 128).T),
        "beta_t": np.ascontiguousarray(
            np.asarray(inputs["L0_attn_ln_beta"]).reshape(NCH, 128).T),
        "ident": cst["ident"],
        "sel2": cst["sel2"], "brd2": cst["brd2"],
        "rp_cos": cst["rp_cos"], "rp_sin": cst["rp_sin"],
        "xp_qc": cst["xp_qc"], "xp_qs": cst["xp_qs"],
        "xp_kc": cst["xp_kc"], "xp_ks": cst["xp_ks"],
        "lmask": cst["lmask"],
    }
    for l in range(2):
        b1 = np.asarray(inputs["L0_ffn_b1"][l])
        b1p = np.zeros((44, 128), dtype=np.float32)
        for mm in range(22):
            a = b1[128 * mm: min(128 * (mm + 1), FHID)]
            b1p[mm, :len(a)] = a
            g = b1[FHID + 128 * mm: FHID + min(128 * (mm + 1), FHID)]
            b1p[22 + mm, :len(g)] = g
        m[f"b1p{l}"] = np.ascontiguousarray(b1p.T)
        m[f"b2t{l}"] = np.ascontiguousarray(np.asarray(inputs["L0_ffn_b2"][l]).reshape(NCH, 128).T)
        m[f"lgt{l}"] = np.ascontiguousarray(np.asarray(inputs["L0_ffn_ln_g"][l]).reshape(NCH, 128).T)
        m[f"lbt{l}"] = np.ascontiguousarray(np.asarray(inputs["L0_ffn_ln_b"][l]).reshape(NCH, 128).T)
        m[f"qsc{l}"] = np.ascontiguousarray(np.asarray(inputs["L0_ffn_q_scale"][l]).reshape(AD, 1))
        m[f"ksc{l}"] = np.ascontiguousarray(np.asarray(inputs["L0_ffn_k_scale"][l]).reshape(AD, 1))
        m[f"wqkv{l}"] = np.ascontiguousarray(inputs["L0_ffn_wqkv"][l])
        m[f"wo{l}"] = np.ascontiguousarray(inputs["L0_ffn_wo"][l])
        m[f"w1{l}"] = np.ascontiguousarray(inputs["L0_ffn_w1"][l])
        m[f"w2{l}"] = np.ascontiguousarray(inputs["L0_ffn_w2"][l])
    return m


def kernel(**inputs):
    cst = host_constants()
    nc, p = build_program()
    in_maps = [prep_inputs(inputs, b, cst) for b in range(8)]
    res = run_bass_kernel_spmd(nc, in_maps, core_ids=list(range(8)))
    return np.stack([res.results[b]["out"] for b in range(8)], axis=0)


def run_debug(inputs, cores=(0,), trace=False):
    cst = host_constants()
    nc, p = build_program()
    in_maps = [prep_inputs(inputs, b, cst) for b in cores]
    res = run_bass_kernel_spmd(nc, in_maps, core_ids=list(range(len(cores))), trace=trace)
    return res


# version 15
# speedup vs baseline: 1.0219x; 1.0219x over previous
import contextlib
import numpy as np
import ml_dtypes
import concourse.bass as bass
import concourse.mybir as mybir
from concourse.bass_utils import run_bass_kernel_spmd
from concourse.alu_op_type import AluOpType

F32 = mybir.dt.float32
BF16 = mybir.dt.bfloat16
AF = mybir.ActivationFunctionType
AX = mybir.AxisListType
OP = AluOpType

T = 4096
C = 1024
NCH = 8
TSL = 512
NSL = 8
AH = 16
AD = 64
FH = 8
FHID = 2730
F2H = 5460
LN_EPS = 1e-5

DEBUG = []
bfdt = ml_dtypes.bfloat16


def host_constants():
    cst = {}
    cst["ident"] = np.eye(128, dtype=np.float32)
    sel2 = np.zeros((128, 2), np.float32)
    sel2[0:64, 0] = 1.0
    sel2[64:128, 1] = 1.0
    cst["sel2"] = sel2.astype(bfdt)
    brd2 = np.zeros((2, 128), np.float32)
    brd2[0, 0:64] = 1.0
    brd2[1, 64:128] = 1.0
    cst["brd2"] = brd2
    i = np.arange(1, AD // 2 + 1, dtype=np.float64)
    thetas = np.repeat(1000.0 ** (-2.0 * i / AD), 2)
    pos = np.arange(1, T + 1, dtype=np.float64)
    args = pos[:, None] * thetas[None, :]
    perm = np.concatenate([np.arange(0, AD, 2), np.arange(1, AD, 2)])
    cst["rp_cos"] = np.ascontiguousarray(np.cos(args)[:, perm].T).astype(bfdt)
    cst["rp_sin"] = np.ascontiguousarray(np.sin(args)[:, perm].T).astype(bfdt)
    half = np.arange(0, AD, 2, dtype=np.float64)
    inv_freq = 1.0 / 10000.0 ** (half / AD)
    t = np.arange(T, dtype=np.float64)
    freqs = np.concatenate([t[:, None] * inv_freq[None, :]] * 2, axis=-1)
    base = (half + 0.4 * AD) / (1.4 * AD)
    power = (t - T // 2) / 256.0
    scale = np.concatenate([base[None, :] ** power[:, None]] * 2, axis=-1)
    for k, v in [("xp_qc", np.cos(freqs) * scale), ("xp_qs", np.sin(freqs) * scale),
                 ("xp_kc", np.cos(freqs) / scale), ("xp_ks", np.sin(freqs) / scale)]:
        cst[k] = np.ascontiguousarray(v.T).astype(bfdt)
    i_idx = np.arange(128)[:, None]
    j_idx = np.arange(512)[None, :]
    m = np.zeros((8, 128, 512), dtype=np.float32)
    for a, r in enumerate(range(-512, 385, 128)):
        vis = (j_idx - 511 <= r + i_idx) & (r + i_idx <= j_idx)
        m[a] = np.where(vis, 0.0, -30000.0)
    cst["lmask"] = m.astype(bfdt)
    return cst


class Ctr:
    __slots__ = ("h", "n")

    def __init__(self, h):
        self.h = h
        self.n = 0

    def inc(self, instr, k=1):
        instr.then_inc(self.h, k)
        self.n += k
        return instr


class P:
    pass


def build_program():
    nc = bass.Bass("TRN2", target_bir_lowering=False)
    p = P()
    p.nc = nc
    p.gates = {}
    p.rope_slot = {}
    p.plane_slot = {}
    p.plain_slot = {}
    p.dnT_bank = {}
    p.dnT_slot = {}
    p.avhist = {}

    def gate(key):
        pass

    din = lambda name, shape, dt=F32: nc.dram_tensor(name, shape, dt, kind="ExternalInput").ap()
    scr = lambda name, shape, dt=F32: nc.dram_tensor(name, shape, dt, kind="Internal").ap()

    p.x_in = din("x", [T, C])
    p.w_inI = din("w_in", [3 * C, C])
    p.w_outI = din("w_out", [C, C])
    p.gamma_t = din("gamma_t", [128, NCH])
    p.beta_t = din("beta_t", [128, NCH])
    p.wqkvI = [din(f"wqkv{l}", [C, 3 * 512]) for l in range(2)]
    p.woI = [din(f"wo{l}", [512, C]) for l in range(2)]
    p.w1I = [din(f"w1{l}", [C, F2H]) for l in range(2)]
    p.w2I = [din(f"w2{l}", [FHID, C]) for l in range(2)]
    p.b1pI = [din(f"b1p{l}", [128, 44]) for l in range(2)]
    p.b2tI = [din(f"b2t{l}", [128, NCH]) for l in range(2)]
    p.lgtI = [din(f"lgt{l}", [128, NCH]) for l in range(2)]
    p.lbtI = [din(f"lbt{l}", [128, NCH]) for l in range(2)]
    p.qscI = [din(f"qsc{l}", [AD, 1]) for l in range(2)]
    p.kscI = [din(f"ksc{l}", [AD, 1]) for l in range(2)]
    p.identI = din("ident", [128, 128])
    p.sel2I = din("sel2", [128, 2], BF16)
    p.brd2I = din("brd2", [2, 128])
    p.rp_cosI = din("rp_cos", [AD, T], BF16)
    p.rp_sinI = din("rp_sin", [AD, T], BF16)
    p.xpI = {k: din(k, [AD, T], BF16) for k in ["xp_qc", "xp_qs", "xp_kc", "xp_ks"]}
    p.lmaskI = din("lmask", [8, 128, TSL], BF16)

    p.out_t = nc.dram_tensor("out", [T, C], F32, kind="ExternalOutput").ap()
    p.dbg = {}
    for name, shape, dt in DEBUG:
        p.dbg[name] = nc.dram_tensor(name, shape, dt, kind="ExternalOutput").ap()

    p.w1bfI = [scr(f"w1bf{l}", [C, F2H], BF16) for l in range(2)]
    p.w2bfI = [scr(f"w2bf{l}", [FHID, C], BF16) for l in range(2)]
    p.scr_x0 = scr("scr_x0", [NCH, 128, T])
    p.scr_xh = scr("scr_xh", [NCH, 128, T])
    p.scr_hbq = scr("scr_hbq", [AH, 128, T], BF16)
    p.scr_hbk = scr("scr_hbk", [AH, 128, T], BF16)
    p.scr_v = scr("scr_v", [8, 128, T], BF16)
    p.scr_h = scr("scr_h", [NCH, 128, T])
    p.scr_ao = scr("scr_ao", [NCH, 128, T])
    p.scr_x1 = scr("scr_x1", [NCH, 128, T])
    p.scr_fA = scr("scr_fA", [NCH, 128, T])
    p.scr_fB = scr("scr_fB", [NCH, 128, T])
    p.scr_dnb = scr("scr_dnb", [AH, 32, 128])
    p.scr_bb = scr("scr_bb", [FH, 32, 128])

    stack = contextlib.ExitStack()
    p.stack = stack
    p.sb = lambda nm, shp, dt=F32: stack.enter_context(nc.sbuf_tensor(nm, shp, dt))
    mksem = lambda nm: Ctr(stack.enter_context(nc.semaphore(nm)))
    p.scopes = {}

    def sbS(scope, nm, shp, dt=F32, side=None):
        if scope not in p.scopes:
            p.scopes[scope] = contextlib.ExitStack()
        return p.scopes[scope].enter_context(nc.sbuf_tensor(nm, shp, dt, side=side))

    def close_scope(scope):
        if scope in p.scopes:
            p.scopes.pop(scope).close()

    p.sbS = sbS
    p.close_scope = close_scope

    with stack:
        p.cW = mksem("cW")
        p.cA = mksem("cA")
        p.cG = mksem("cG")
        p.cB = mksem("cB")
        p.cO = mksem("cO")
        p.cT = mksem("cT")
        p.cV = mksem("cV")
        p.cS = mksem("cS")
        p.cX = mksem("cX")
        p.cY = mksem("cY")
        p.cZ = mksem("cZ")
        p.cD = [mksem(f"cD{i}") for i in range(6)]
        p.cE = [mksem(f"cE{i}") for i in range(4)]
        p.cF = [mksem(f"cF{i}") for i in range(4)]
        p.cSL = mksem("cSL")
        p.cRX = mksem("cRX")
        p.cVc = mksem("cVc")
        p.cSc = mksem("cSc")
        p.all_ctrs = [p.cW, p.cA, p.cG, p.cB, p.cO, p.cT, p.cV, p.cS, p.cX, p.cY,
                      p.cZ, p.cVc, p.cSc, p.cSL, p.cRX] + p.cD + p.cE + p.cF

        p.ident_f = p.sb("ident_f", [128, 128])
        p.ident_b = p.sb("ident_b", [128, 128], BF16)
        p.inv_c = p.sb("inv_c", [128, 1], BF16)
        p.ones_rf = p.sb("ones_rf", [1, 128])
        p.ones_rb = p.sb("ones_rb", [1, 128], BF16)
        p.ones128 = p.sb("ones128", [128, 1], BF16)
        p.eps_col = p.sb("eps_col", [128, 1])
        p.epsl2_col = p.sb("epsl2_col", [128, 1])
        p.gam = p.sb("gam", [128, NCH])
        p.bet = p.sb("bet", [128, NCH])
        p.sel2 = p.sb("sel2_sb", [128, 2], BF16)
        p.brd2 = p.sb("brd2_sb", [2, 128])


        p.psF = stack.enter_context(nc.psum_tensor("psF", [128, 3072], F32))
        p.psB = stack.enter_context(nc.psum_tensor("psB", [128, 2048], BF16))
        p.bank = lambda i, w=1: p.psF[:, i * 512:(i + w) * 512]
        p.bslot = lambda i: p.psB[:, (i % 2) * 1024:(i % 2) * 1024 + 128]

        p.srow = [p.sb(f"srow{i}", [1, TSL]) for i in range(4)]

        p.rp_cos = p.sbS("AB", "rp_cos_sb", [AD, T], BF16)
        p.rp_sin = p.sbS("AB", "rp_sin_sb", [AD, T], BF16)
        p.w_inT = [p.sbS("AB", f"w_inT{c}", [128, 3 * C], BF16) for c in range(NCH)]
        p.xhb = [p.sbS("AB", f"xhb{c}", [128, T], BF16) for c in range(NCH)]

        with nc.Block("init") as blk:
            @blk.sync
            def _(s):
                for c0 in p.all_ctrs:
                    s.sem_clear(c0.h)

        with nc.Block("main") as blk:
            p.blk = blk
            emit_const_loads(p)
            emit_winT(p)
            emit_phaseA(p)
            emit_phaseB(p)
            emit_attention(p)
            emit_wout_lnchain(p)
            for l in range(2):
                emit_ffn_layer(p, l)
            emit_final(p)

        for sc in reversed(list(p.scopes)):
            p.close_scope(sc)

    return nc, p


def VI(p, ins):
    p.cVc.inc(ins, 1)
    return ins


def VW(p, v):
    v.wait_ge(p.cVc.h, p.cVc.n)


def SI(p, ins):
    p.cSc.inc(ins, 1)
    return ins


def SW(p, s):
    s.wait_ge(p.cSc.h, p.cSc.n)


def g_set(p, key, ctr):
    p.gates[key] = (ctr, ctr.n)


def g_wait(p, eng, key):
    ctr, v = p.gates[key]
    eng.wait_ge(ctr.h, v)


def barrier(p, eng):
    for c in p.all_ctrs:
        eng.wait_ge(c.h, c.n)


def emit_const_loads(p):
    blk = p.blk
    cW, cG = p.cW, p.cG

    @blk.sync
    def _(s):
        for dst, src in [(p.ident_f, p.identI), (p.gam, p.gamma_t), (p.bet, p.beta_t),
                         (p.rp_cos, p.rp_cosI), (p.rp_sin, p.rp_sinI),
                         (p.sel2, p.sel2I), (p.brd2, p.brd2I)]:
            cW.inc(s.dma_start(out=dst[:, :], in_=src[:, :]), 16)


    @blk.gpsimd
    def _(g):
        p.cD[4].inc(g.dma_start(out=p.ident_b[:, :], in_=p.identI[:, :]), 16)
        g.memset(p.inv_c[:, :], 1.0 / C)
        g.memset(p.ones_rf[:, :], 1.0)
        g.memset(p.ones_rb[:, :], 1.0)
        g.memset(p.ones128[:, :], 1.0)
        g.memset(p.eps_col[:, :], LN_EPS)
        p.cZ.inc(g.memset(p.epsl2_col[:, :], 1e-18), 1)
        for l in range(2):
            p.cY.inc(g.dma_start(out=p.w1bfI[l][:, :], in_=p.w1I[l][:, :]), 16)
            p.cY.inc(g.dma_start(out=p.w2bfI[l][:, :], in_=p.w2I[l][:, :]), 16)
    g_set(p, "wcast", p.cY)
    g_set(p, "consts", p.cZ)
    g_set(p, "identb", p.cD[4])
    g_set(p, "constsW", cW)


def emit_winT(p):
    blk = p.blk
    stg = [p.sbS("W", f"winstg{i}", [128, C], BF16) for i in range(4)]
    cG, cT, cV = p.cG, p.cT, p.cV

    for ch in range(24):
        r0 = ch * 128
        dst = stg[ch % 4]

        @blk.gpsimd
        def _(g, ch=ch, r0=r0, dst=dst):
            cs = p.cD[ch % 4]
            if ch >= 4:
                g_wait(p, g, ("winT_ev", ch - 4))
            if ch < 16:
                for hb in range(2):
                    rr = r0 + hb * 64
                    cs.inc(g.dma_start(out=dst[hb * 64:hb * 64 + 32, :],
                                       in_=p.w_inI[rr:rr + 64:2, :]), 16)
                    cs.inc(g.dma_start(out=dst[hb * 64 + 32:hb * 64 + 64, :],
                                       in_=p.w_inI[rr + 1:rr + 64:2, :]), 16)
            else:
                cs.inc(g.dma_start(out=dst[:, :], in_=p.w_inI[r0:r0 + 128, :]), 16)
        g_set(p, ("winT_ld", ch), p.cD[ch % 4])

        for c in range(NCH):
            k = ch * 8 + c

            @blk.tensor
            def _(t, ch=ch, dst=dst, c=c, k=k):
                if c == 0:
                    g_wait(p, t, ("winT_ld", ch))
                if ch == 0 and c == 0:
                    g_wait(p, t, "consts")
                    g_wait(p, t, "identb")
                if k >= 2:
                    g_wait(p, t, ("winT_evc", k - 2))
                cT.inc(t.transpose(p.bslot(k), dst[:, c * 128:(c + 1) * 128],
                                   p.ident_b[:, :]), 1)
            g_set(p, ("winT_tr", k), cT)

            @blk.vector
            def _(v, ch=ch, c=c, k=k):
                g_wait(p, v, ("winT_tr", k))
                VI(p, v.tensor_copy(p.w_inT[c][:, ch * 128:(ch + 1) * 128], p.bslot(k)))
            g_set(p, ("winT_evc", k), p.cVc)
        g_set(p, ("winT_ev", ch), p.cVc)
    g_set(p, "winT_done", p.cVc)
    p.close_scope("W")


def emit_phaseA(p):
    blk = p.blk
    tm_in = [p.sbS("A", f"tm_in{i}", [128, C]) for i in range(4)]
    x0T = [p.sbS("A", f"x0T{c}", [128, TSL]) for c in range(NCH)]
    xb = [p.sbS("A", f"xbA{c}", [128, TSL], BF16) for c in range(NCH)]
    sq = [p.sbS("A", f"sqA{c}", [128, TSL], BF16) for c in range(NCH)]
    cA, cT, cV, cS, cG = p.cA, p.cT, p.cV, p.cS, p.cG

    for sl in range(NSL):
        ssl = slice(sl * TSL, (sl + 1) * TSL)

        @blk.sync
        def _(s, sl=sl):
            if sl == 0:
                barrier(p, s)
            else:
                g_wait(p, s, ("A_tr_done", sl - 1))
            for r in range(4):
                cA.inc(s.dma_start(out=tm_in[r][:, :],
                                   in_=p.x_in[sl * TSL + r * 128: sl * TSL + (r + 1) * 128, :]), 16)
        g_set(p, ("A_ld", sl), cA)

        for c in range(NCH):
            for r in range(4):
                k = c * 4 + r
                kg = sl * 32 + k

                @blk.tensor
                def _(t, sl=sl, c=c, r=r, k=k, kg=kg):
                    if k == 0:
                        g_wait(p, t, ("A_ld", sl))
                        if sl == 0:
                            g_wait(p, t, "winT_done")
                    if kg >= 2:
                        g_wait(p, t, ("A_ev", kg - 2))
                    pt = p.bank(k % 2)[:, 0:128]
                    cT.inc(t.transpose(pt, tm_in[r][:, c * 128:(c + 1) * 128],
                                       p.ident_f[:, :]), 1)
                g_set(p, ("A_tr", kg), cT)

                @blk.vector
                def _(v, sl=sl, c=c, r=r, k=k, kg=kg):
                    g_wait(p, v, ("A_tr", kg))
                    if k == 0 and sl >= 1:
                        g_wait(p, v, ("A_spill_xh", sl - 1))
                    pt = p.bank(k % 2)[:, 0:128]
                    VI(p, v.tensor_copy(x0T[c][:, r * 128:(r + 1) * 128], pt))
                g_set(p, ("A_ev", kg), p.cVc)
        g_set(p, ("A_tr_done", sl), cT)
        g_set(p, ("A_ev_done", sl), p.cVc)

        @blk.gpsimd
        def _(g, sl=sl, ssl=ssl):
            g_wait(p, g, ("A_ev_done", sl))
            for c in range(NCH):
                cG.inc(g.dma_start(out=p.scr_x0[c, :, ssl], in_=x0T[c][:, :]), 16)
        g_set(p, ("A_spill_x0", sl), cG)

        @blk.vector
        def _(v, sl=sl):
            VW(p, v)
            for c in range(NCH):
                if c == 0 and sl >= 1:
                    g_wait(p, v, ("A_stats", sl - 1))
                VI(p, v.tensor_copy(xb[c][:, :], x0T[c][:, :]))
                VW(p, v)
                VI(p, v.tensor_tensor(sq[c][:, :], xb[c][:, :], xb[c][:, :], op=OP.mult))
        g_set(p, ("A_xbsq", sl), p.cVc)

        @blk.tensor
        def _(t, sl=sl):
            g_wait(p, t, ("A_xbsq", sl))
            if sl >= 1:
                g_wait(p, t, ("A_var", sl - 1))
            for c in range(NCH):
                t.matmul(p.bank(2)[0:1, :], p.inv_c[:, :], xb[c][:, :],
                         start=(c == 0), stop=(c == NCH - 1))
            for c in range(NCH):
                ins = t.matmul(p.bank(3)[0:1, :], p.inv_c[:, :], sq[c][:, :],
                               start=(c == 0), stop=(c == NCH - 1))
            cT.inc(ins, 1)
        g_set(p, ("A_stats", sl), cT)

        @blk.vector
        def _(v, sl=sl):
            g_wait(p, v, ("A_stats", sl))
            VI(p, v.tensor_copy(p.srow[0][:, :], p.bank(2)[0:1, :]))
            VW(p, v)
            VI(p, v.tensor_tensor(p.srow[1][:, :], p.srow[0][:, :], p.srow[0][:, :],
                                  op=OP.mult))
            VW(p, v)
            VI(p, v.tensor_tensor(p.srow[1][:, :], p.bank(3)[0:1, :], p.srow[1][:, :],
                                  op=OP.subtract))
        g_set(p, ("A_var", sl), p.cVc)

        @blk.scalar
        def _(s, sl=sl):
            g_wait(p, s, ("A_var", sl))
            SI(p, s.activation(p.srow[1][:, :], p.srow[1][:, :], AF.Ln, bias=p.eps_col[0:1, :]))
            SW(p, s)
            SI(p, s.activation(p.srow[2][:, :], p.srow[1][:, :], AF.Exp, scale=-0.5))
        g_set(p, ("A_rstd", sl), p.cSc)

        @blk.vector
        def _(v, sl=sl):
            g_wait(p, v, ("A_rstd", sl))
            VI(p, v.scalar_tensor_tensor(p.srow[3][:, :], p.srow[0][:, :], -1.0,
                                         p.srow[2][:, :], op0=OP.mult, op1=OP.mult))
        g_set(p, ("A_mr", sl), p.cVc)

        @blk.tensor
        def _(t, sl=sl):
            g_wait(p, t, ("A_mr", sl))
            if sl >= 1:
                g_wait(p, t, ("A_apply", sl - 1))
            t.matmul(p.bank(4), p.ones_rf[:, :], p.srow[2][:, :], start=True, stop=True)
            ins = t.matmul(p.bank(5), p.ones_rf[:, :], p.srow[3][:, :], start=True, stop=True)
            cT.inc(ins, 1)
        g_set(p, ("A_planes", sl), cT)

        @blk.vector
        def _(v, sl=sl, ssl=ssl):
            g_wait(p, v, ("A_planes", sl))
            g_wait(p, v, ("A_spill_x0", sl))
            for c in range(NCH):
                VI(p, v.tensor_tensor(x0T[c][:, :], x0T[c][:, :], p.bank(4), op=OP.mult))
                VW(p, v)
                VI(p, v.tensor_tensor(x0T[c][:, :], x0T[c][:, :], p.bank(5), op=OP.add))
                VW(p, v)
                VI(p, v.tensor_copy(p.xhb[c][:, ssl], x0T[c][:, :]))
        g_set(p, ("A_apply", sl), p.cVc)

        @blk.gpsimd
        def _(g, sl=sl, ssl=ssl):
            g_wait(p, g, ("A_apply", sl))
            for c in range(NCH):
                cG.inc(g.dma_start(out=p.scr_xh[c, :, ssl], in_=x0T[c][:, :]), 16)
        g_set(p, ("A_spill_xh", sl), cG)
    g_set(p, "A_done", cG)
    p.close_scope("A")

    if "d_xhb" in p.dbg:
        @blk.sync
        def _(s):
            barrier(p, s)
            for c in range(NCH):
                p.cO.inc(s.dma_start(out=p.dbg["d_xhb"][c, :, :], in_=p.xhb[c][:, :]), 16)


def emit_phaseB(p):
    blk = p.blk
    hb = [p.sbS("B", f"hbB{i}", [128, TSL], BF16) for i in range(4)]
    vst = [p.sbS("B", f"vstB{i}", [128, C], BF16) for i in range(8)]
    rsc = p.sbS("B", "rscB", [128, TSL], BF16)
    rsc2 = p.sbS("B", "rsc2B", [64, TSL], BF16)
    cT, cV, cG = p.cT, p.cV, p.cG

    for sl in range(NSL):
        ssl = slice(sl * TSL, (sl + 1) * TSL)
        for fc in range(16):
            k = sl * 16 + fc
            par = k % 2

            @blk.tensor
            def _(t, sl=sl, fc=fc, k=k, par=par, ssl=ssl):
                if k == 0:
                    g_wait(p, t, "A_done")
                if k >= 2:
                    g_wait(p, t, ("B_rope", k - 2))
                bk = p.bank(par)
                for c in range(NCH):
                    ins = t.matmul(bk, p.w_inT[c][:, fc * 128:(fc + 1) * 128],
                                   p.xhb[c][:, ssl], start=(c == 0), stop=(c == NCH - 1))
                cT.inc(ins, 1)
            g_set(p, ("B_qk", k), cT)

            @blk.vector
            def _(v, sl=sl, fc=fc, k=k, par=par, ssl=ssl):
                g_wait(p, v, ("B_qk", k))
                if k == 0:
                    g_wait(p, v, "constsW")
                if k >= 2:
                    g_wait(p, v, ("B_hspill", k - 2))
                bk = p.bank(par)
                A, B = hb[par * 2], hb[par * 2 + 1]
                VI(p, v.tensor_copy(A[64:128, :], bk[0:64, :]))
                VI(p, v.tensor_copy(B[64:128, :], bk[64:128, :]))
                for X in (A, B):
                    VW(p, v)
                    VI(p, v.tensor_copy(rsc[0:64, :], X[64:128, :]))
                    VW(p, v)
                    VI(p, v.tensor_scalar(rsc2[0:32, :], rsc[32:64, :], -1.0, None, op0=OP.mult))
                    VI(p, v.tensor_copy(rsc2[32:64, :], rsc[0:32, :]))
                    VW(p, v)
                    VI(p, v.tensor_tensor(X[0:64, :], rsc[0:64, :], p.rp_cos[0:64, ssl], op=OP.mult))
                    VI(p, v.tensor_tensor(rsc2[0:64, :], rsc2[0:64, :], p.rp_sin[0:64, ssl], op=OP.mult))
                    VW(p, v)
                    VI(p, v.tensor_tensor(X[0:64, :], X[0:64, :], rsc2[0:64, :], op=OP.add))
            g_set(p, ("B_rope", k), p.cVc)

            @blk.gpsimd
            def _(g, sl=sl, fc=fc, k=k, par=par, ssl=ssl):
                g_wait(p, g, ("B_rope", k))
                cs = p.cD[par]
                A, B = hb[par * 2], hb[par * 2 + 1]
                tgt = p.scr_hbq if fc < 8 else p.scr_hbk
                h2 = (fc % 8) * 2
                cs.inc(g.dma_start(out=tgt[h2, :, ssl], in_=A[:, :]), 16)
                cs.inc(g.dma_start(out=tgt[h2 + 1, :, ssl], in_=B[:, :]), 16)
            g_set(p, ("B_hspill", k), p.cD[par])

        for tsub in range(4):
            kv = sl * 4 + tsub

            @blk.tensor
            def _(t, sl=sl, tsub=tsub, kv=kv, ssl=ssl):
                if kv >= 2:
                    g_wait(p, t, ("B_vev", kv - 2))
                for nn in range(2):
                    bk = p.bank(2 + (kv % 2) * 2 + nn)
                    for c in range(NCH):
                        ins = t.matmul(
                            bk,
                            p.xhb[c][:, sl * TSL + tsub * 128: sl * TSL + (tsub + 1) * 128],
                            p.w_inT[c][:, 2048 + nn * 512: 2048 + (nn + 1) * 512],
                            start=(c == 0), stop=(c == NCH - 1))
                cT.inc(ins, 1)
            g_set(p, ("B_v", kv), cT)

            @blk.vector
            def _(v, sl=sl, tsub=tsub, kv=kv):
                g_wait(p, v, ("B_v", kv))
                if kv >= 2:
                    g_wait(p, v, ("B_vspill", kv - 2))
                dst = vst[(kv % 2) * 4 + tsub]
                VI(p, v.tensor_copy(dst[:, 0:512], p.bank(2 + (kv % 2) * 2)))
                VI(p, v.tensor_copy(dst[:, 512:1024], p.bank(3 + (kv % 2) * 2)))
            g_set(p, ("B_vev", kv), p.cVc)

            @blk.gpsimd
            def _(g, sl=sl, tsub=tsub, kv=kv):
                g_wait(p, g, ("B_vev", kv))
                cs = p.cD[2 + kv % 2]
                src = vst[(kv % 2) * 4 + tsub]
                for pr in range(8):
                    cs.inc(g.dma_start(out=p.scr_v[pr, :, kv * 128:(kv + 1) * 128],
                                       in_=src[:, pr * 128:(pr + 1) * 128]), 16)
            g_set(p, ("B_vspill", kv), p.cD[2 + kv % 2])
    for i in range(4):
        g_set(p, f"B_done{i}", p.cD[i])
    g_set(p, "B_done", p.cVc)
    p.close_scope("B")
    p.close_scope("AB")

    if "d_hbq" in p.dbg:
        @blk.sync
        def _(s):
            barrier(p, s)
            for h in range(AH):
                p.cO.inc(s.dma_start(out=p.dbg["d_hbq"][h, :, :], in_=p.scr_hbq[h, :, :]), 16)
                p.cO.inc(s.dma_start(out=p.dbg["d_hbk"][h, :, :], in_=p.scr_hbk[h, :, :]), 16)
            for i in range(T // 128):
                p.cO.inc(s.dma_start(out=p.dbg["d_v"][i, :, :], in_=p.scr_v[i, :, :]), 16)


def emit_attention(p):
    blk = p.blk
    cT, cG, cA, cB = p.cT, p.cG, p.cA, p.cB
    cSL, cRX = p.cSL, p.cRX

    p.ybuf = [p.sbS("SY", f"ybuf{i}", [128, T], BF16) for i in range(8)]
    hq = [p.sbS("S", f"hqS{i}", [128, T], BF16) for i in range(4)]
    hk = [p.sbS("S", f"hkS{i}", [128, T], BF16) for i in range(4)]
    vpair = [p.sbS("S", f"vpairS{i}", [128, T], BF16) for i in range(2)]
    attb = [p.sbS("S", f"attbS{i}", [128, 1024], BF16) for i in range(4)]
    ytmp = p.sbS("S", "ytmpS", [128, TSL])
    denp = [p.sbS("S", f"denpS{i}", [128, 128]) for i in range(2)]
    den = [p.sbS("S", f"denS{i}", [128, 32]) for i in range(2)]
    invd = [p.sbS("S", f"invdS{i}", [128, 32]) for i in range(2)]
    dnstg = [p.sbS("S", f"dnstgS{i}", [32, 128]) for i in range(2)]
    invsl = [p.sbS("S", f"invslS{i}", [1, TSL]) for i in range(2)]

    g = [0]
    p.slot_pair = {}
    pair_last = {}

    gp = [0]

    def next_slot(t, pair=None, plane=False):
        if plane:
            gp[0] += 1
            ordv, ctr = gp[0], cRX
            if pair is None:
                pair = gp[0] % 2
        else:
            g[0] += 1
            ordv, ctr = g[0], cSL
            if pair is None:
                pair = g[0] % 2
        last = pair_last.get(pair)
        if last is not None:
            t.wait_ge(last[0].h, last[1])
        pair_last[pair] = (ctr, ordv)
        if not plane:
            p.slot_pair[g[0]] = pair
        return p.bank(pair * 2, 2), ordv, pair

    for hp in range(8):
        h0, h1 = 2 * hp, 2 * hp + 1

        @blk.sync
        def _(s, hp=hp, h0=h0, h1=h1):
            if hp == 0:
                for i in range(4):
                    g_wait(p, s, f"B_done{i}")
            if hp >= 2:
                g_wait(p, s, ("S_pair_done", hp - 2))
            cs = p.cE[hp % 2]
            par = (hp % 2) * 2
            cs.inc(s.dma_start(out=hq[par][:, :], in_=p.scr_hbq[h0, :, :]), 16)
            cs.inc(s.dma_start(out=hq[par + 1][:, :], in_=p.scr_hbq[h1, :, :]), 16)
            cs.inc(s.dma_start(out=hk[par][:, :], in_=p.scr_hbk[h0, :, :]), 16)
            cs.inc(s.dma_start(out=hk[par + 1][:, :], in_=p.scr_hbk[h1, :, :]), 16)
            cs.inc(s.dma_start(out=vpair[hp % 2][:, :], in_=p.scr_v[hp, :, :]), 16)
        g_set(p, ("S_ld", hp), p.cE[hp % 2])

        slot0 = {}
        for hh in range(2):
            Q = hq[(hp % 2) * 2 + hh]
            K = hk[(hp % 2) * 2 + hh]

            @blk.tensor
            def _(t, hp=hp, hh=hh, Q=Q, K=K):
                if hh == 0:
                    g_wait(p, t, ("S_ld", hp))
                    if hp > 0:
                        g_wait(p, t, ("S_ysc", (hp - 1, 7)))
                else:
                    g_wait(p, t, ("S_dnev", (hp, 0)))
                for qt in range(32):
                    for ks in range(4):
                        bk, _, _ = next_slot(t, pair=g[0] % 3)
                        if qt == 0 and ks == 0:
                            slot0[hh] = g[0]
                        for j in range(2):
                            ins = t.matmul(
                                bk[:, j * 512:(j + 1) * 512],
                                Q[64:128, qt * 128:(qt + 1) * 128],
                                K[64:128, ks * 1024 + j * 512: ks * 1024 + (j + 1) * 512],
                                start=True, stop=True)
                        cT.inc(ins, 1)
            g_set(p, ("S_pmm", (hp, hh)), cT)

            @blk.scalar
            def _(s, hp=hp, hh=hh):
                base_t = p.gates[("S_pmm", (hp, hh))][1] - 128
                g0 = slot0[hh]
                if hh == 0 and hp >= 2:
                    g_wait(p, s, ("S_invd", (hp - 2, 1)))
                for i in range(128):
                    if i % 2 == 0:
                        s.wait_ge(cT.h, base_t + min(i + 2, 128))
                    gg = g0 + i
                    bk = p.bank(p.slot_pair[gg] * 2, 2)
                    cSL.inc(s.activation(bk, bk, AF.Exp, scale=0.125,
                                         accum_out=denp[hh][:, i:i + 1]), 1)
            g_set(p, ("S_pexp", (hp, hh)), cSL)

            @blk.vector
            def _(v, hp=hp, hh=hh):
                g_wait(p, v, ("S_pexp", (hp, hh)))
                VW(p, v)
                VI(p, v.tensor_tensor(den[hh][:, :], denp[hh][:, 0:128:4],
                                      denp[hh][:, 1:128:4], op=OP.add))
                VI(p, v.tensor_tensor(invd[hh][:, :], denp[hh][:, 2:128:4],
                                      denp[hh][:, 3:128:4], op=OP.add))
                VW(p, v)
                VI(p, v.tensor_tensor(den[hh][:, :], den[hh][:, :], invd[hh][:, :], op=OP.add))
                VW(p, v)
                VI(p, v.reciprocal(invd[hh][:, :], den[hh][:, :]))
            g_set(p, ("S_invd", (hp, hh)), p.cVc)

            @blk.tensor
            def _(t, hp=hp, hh=hh):
                g_wait(p, t, ("S_invd", (hp, hh)))
                bk, _, _ = next_slot(t)
                p.dnT_slot[(hp, hh)] = g[0]
                cT.inc(t.transpose(bk[0:32, 0:128], invd[hh][:, :], p.ident_f[:, :]), 1)
            g_set(p, ("S_dnT", (hp, hh)), cT)

            @blk.vector
            def _(v, hp=hp, hh=hh):
                g_wait(p, v, ("S_dnT", (hp, hh)))
                gg = p.dnT_slot[(hp, hh)]
                bk = p.bank(p.slot_pair[gg] * 2, 2)
                cSL.inc(v.tensor_copy(dnstg[hh][:, :], bk[0:32, 0:128]), 1)
            g_set(p, ("S_dnev", (hp, hh)), cSL)

            @blk.gpsimd
            def _(gp, hp=hp, hh=hh):
                g_wait(p, gp, ("S_dnev", (hp, hh)))
                cbs = [cB, p.cD[4], p.cD[5], p.cD[0]][(hp % 2) * 2 + hh]
                cbs.inc(gp.dma_start(out=p.scr_dnb[2 * hp + hh, :, :], in_=dnstg[hh][:, :]), 16)
            g_set(p, ("S_bounceout", (hp, hh)),
                  [cB, p.cD[4], p.cD[5], p.cD[0]][(hp % 2) * 2 + hh])

        for qb in range(8):
            qsl = slice(qb * 512, (qb + 1) * 512)

            @blk.sync
            def _(s, hp=hp, qb=qb):
                if qb > 0:
                    g_wait(p, s, ("S_plane", (hp, qb - 1)))
                elif hp > 0:
                    g_wait(p, s, ("S_plane", (hp - 1, 7)))
                for hh in range(2):
                    g_wait(p, s, ("S_bounceout", (hp, hh)))
                    rowv = p.scr_dnb[2 * hp + hh].rearrange("a b -> (a b)")
                    p.cE[2].inc(s.dma_start(out=invsl[hh][:, :],
                                in_=rowv[None, qb * 512:(qb + 1) * 512]), 16)
            g_set(p, ("S_invsl", (hp, qb)), p.cE[2])

            slot_of = {}

            @blk.tensor
            def _(t, hp=hp, qb=qb, qsl=qsl):
                K0, K1 = hk[(hp % 2) * 2], hk[(hp % 2) * 2 + 1]
                Q0, Q1 = hq[(hp % 2) * 2], hq[(hp % 2) * 2 + 1]
                V = vpair[hp % 2]
                if qb == 0:
                    g_wait(p, t, ("S_dnev", (hp, 1)))
                    if hp > 0:
                        g_wait(p, t, ("S_yev", (hp - 1, 7)))
                for kt in range(33):
                    if kt < 32:
                        bk, _, _ = next_slot(t)
                        slot_of[kt] = g[0]
                        t.matmul(bk[:, 0:512], K0[0:64, kt * 128:(kt + 1) * 128],
                                 Q0[0:64, qsl], start=True, stop=True)
                        ins = t.matmul(bk[:, 512:1024], K1[0:64, kt * 128:(kt + 1) * 128],
                                       Q1[0:64, qsl], start=True, stop=True)
                        cT.inc(ins, 1)
                    if kt >= 1:
                        k2 = kt - 1
                        t.wait_ge(cSL.h, slot_of[k2])
                        if k2 == 0 and (qb >= 2 or (hp > 0 and qb < 2)):
                            prev = (hp, qb - 2) if qb >= 2 else (hp - 1, 6 + qb)
                            g_wait(p, t, ("S_yev", prev))
                        avb = p.bank(4 + qb % 2)
                        att = attb[k2 % 4]
                        t.matmul(avb[0:64, :], V[:, k2 * 128: k2 * 128 + 64],
                                 att[:, 0:512], start=(k2 == 0), stop=(k2 == 31),
                                 tile_position=(0, 0), skip_group_check=True)
                        ins = t.matmul(avb[64:128, :], V[:, k2 * 128 + 64: k2 * 128 + 128],
                                       att[:, 512:1024], start=(k2 == 0), stop=(k2 == 31),
                                       tile_position=(0, 64), skip_group_check=True)
                        cT.inc(ins, 1)
                        g_set(p, ("S_avk", (hp, qb, k2)), cT)
            g_set(p, ("S_av", (hp, qb)), cT)

            @blk.scalar
            def _(s, hp=hp, qb=qb):
                base_t = p.gates[("S_av", (hp, qb))][1] - 64
                for kt in range(32):
                    nmm = base_t + (1 if kt == 0 else 2 * kt)
                    s.wait_ge(cT.h, nmm)
                    if kt >= 4:
                        g_wait(p, s, ("S_avk", (hp, qb, kt - 4)))
                    elif qb > 0:
                        g_wait(p, s, ("S_avk", (hp, qb - 1, 28 + kt)))
                    elif hp > 0:
                        g_wait(p, s, ("S_avk", (hp - 1, 7, 28 + kt)))
                    gg = slot_of[kt]
                    bk = p.bank(p.slot_pair[gg] * 2, 2)
                    cSL.inc(s.activation(attb[kt % 4][:, :], bk, AF.Exp, scale=0.125), 1)
            g_set(p, ("S_rexp", (hp, qb)), cSL)

            @blk.tensor
            def _(t, hp=hp, qb=qb):
                g_wait(p, t, ("S_invsl", (hp, qb)))
                bk, ordv, pr = next_slot(t, plane=True)
                p.plane_slot[(hp, qb)] = (ordv, pr)
                t.matmul(bk[:, 0:512], p.ones_rf[:, :], invsl[0][:, :], start=True, stop=True)
                ins = t.matmul(bk[:, 512:1024], p.ones_rf[:, :], invsl[1][:, :],
                               start=True, stop=True)
                cT.inc(ins, 1)
            g_set(p, ("S_plane", (hp, qb)), cT)

            @blk.vector
            def _(v, hp=hp, qb=qb, qsl=qsl):
                g_wait(p, v, ("S_av", (hp, qb)))
                avb = p.bank(4 + qb % 2)
                VI(p, v.tensor_copy(ytmp[:, :], avb))
                g_set(p, ("S_yev", (hp, qb)), p.cVc)
                VW(p, v)
                g_wait(p, v, ("S_plane", (hp, qb)))
                gg, pr = p.plane_slot[(hp, qb)]
                bk = p.bank(pr * 2, 2)
                VI(p, v.tensor_tensor(p.ybuf[hp][0:64, qsl], ytmp[0:64, :],
                                      bk[0:64, 0:512], op=OP.mult))
                cRX.inc(v.tensor_tensor(p.ybuf[hp][64:128, qsl], ytmp[64:128, :],
                                        bk[64:128, 512:1024], op=OP.mult), 1)
            g_set(p, ("S_ysc", (hp, qb)), cRX)
        g_set(p, ("S_pair_done", hp), cRX)
    g_set(p, "S_done", cRX)
    g_set(p, "S_doneB", cSL)
    p.close_scope("S")

    if "d_ybuf" in p.dbg:
        @blk.sync
        def _(s):
            barrier(p, s)
            for i in range(8):
                p.cO.inc(s.dma_start(out=p.dbg["d_ybuf"][i, :, :], in_=p.ybuf[i][:, :]), 16)
    if "d_invd" in p.dbg:
        @blk.sync
        def _(s):
            barrier(p, s)
            for h in range(AH):
                p.cO.inc(s.dma_start(out=p.dbg["d_invd"][h, :, :], in_=p.scr_dnb[h, :, :]), 16)


def emit_wout_lnchain(p):
    blk = p.blk
    cT, cA, cG = p.cT, p.cA, p.cG
    p.rowA = [p.sbS("LN", "rowA_rstd", [1, 2 * TSL], side="right"),
              p.sbS("LN", "rowA_mr", [1, 2 * TSL], side="right")]
    p.rowB = [p.sbS("LN", "rowB_rstd", [1, 2 * TSL], side="right"),
              p.sbS("LN", "rowB_mr", [1, 2 * TSL], side="right")]
    p.w_outT = [p.sbS("SO", f"w_outT{c}", [128, C], BF16) for c in range(NCH)]
    wstg = [p.sbS("SO", f"wostg{i}", [128, C], BF16) for i in range(2)]
    main2 = [[p.sbS("SO", f"mainW{j}_{c}", [128, TSL]) for c in range(NCH)]
             for j in range(2)]
    aux2 = [[p.sbS("SO", f"auxW{j}_{c}", [128, TSL]) for c in range(NCH)]
            for j in range(2)]
    xb = [p.sbS("SO", f"xbW{c}", [128, TSL], BF16) for c in range(NCH)]
    sq = [p.sbS("SO", f"sqW{c}", [128, TSL], BF16) for c in range(NCH)]

    for ch in range(8):
        @blk.gpsimd
        def _(g, ch=ch):
            if ch == 0:
                g_wait(p, g, "S_done")
                g_wait(p, g, "S_doneB")
            if ch >= 2:
                g_wait(p, g, ("WO_tr", ch - 2))
            p.cD[ch % 2].inc(g.dma_start(out=wstg[ch % 2][:, :],
                                         in_=p.w_outI[ch * 128:(ch + 1) * 128, :]), 16)
        g_set(p, ("WO_ld", ch), p.cD[ch % 2])

        for c in range(NCH):
            k = ch * 8 + c

            @blk.tensor
            def _(t, ch=ch, c=c, k=k):
                if c == 0:
                    g_wait(p, t, ("WO_ld", ch))
                if k == 0:
                    g_wait(p, t, "S_done")
                    g_wait(p, t, "S_doneB")
                if k >= 2:
                    g_wait(p, t, ("WO_evc", k - 2))
                cT.inc(t.transpose(p.bslot(k), wstg[ch % 2][:, c * 128:(c + 1) * 128],
                                   p.ident_b[:, :]), 1)
            g_set(p, ("WO_trk", k), cT)

            @blk.vector
            def _(v, ch=ch, c=c, k=k):
                g_wait(p, v, ("WO_trk", k))
                VI(p, v.tensor_copy(p.w_outT[c][:, ch * 128:(ch + 1) * 128], p.bslot(k)))
            g_set(p, ("WO_evc", k), p.cVc)
        g_set(p, ("WO_tr", ch), p.cVc)
    g_set(p, "WO_done", p.cVc)

    stages = ["2", "3", "4"]
    rows_of = {"2": p.rowA, "3": p.rowB, "4": p.rowA}

    for sl in range(NSL):
        ssl = slice(sl * TSL, (sl + 1) * TSL)
        rsl = slice((sl % 2) * TSL, (sl % 2 + 1) * TSL)
        main = main2[sl % 2]
        aux = aux2[sl % 2]

        @blk.sync
        def _(s, sl=sl, ssl=ssl, main=main):
            if sl == 0:
                g_wait(p, s, "S_done")
                g_wait(p, s, "S_doneB")
            if sl >= 2:
                g_wait(p, s, ("W_x1sp", sl - 2))
                g_wait(p, s, ("W_apl", (sl - 2, "4")))
            cs = [cA, p.cE[0]][sl % 2]
            for c in range(NCH):
                cs.inc(s.dma_start(out=main[c][:, :], in_=p.scr_xh[c, :, ssl]), 16)
        g_set(p, ("W_ldm", sl), [cA, p.cE[0]][sl % 2])

        @blk.scalar
        def _(s, sl=sl, ssl=ssl, aux=aux):
            if sl >= 2:
                g_wait(p, s, ("W_fasp", sl - 2))
            cs = [p.cE[1], p.cE[2]][sl % 2]
            for c in range(NCH):
                cs.inc(s.dma_start(out=aux[c][:, :], in_=p.scr_x0[c, :, ssl]), 16)
        g_set(p, ("W_ldx", sl), [p.cE[1], p.cE[2]][sl % 2])

        for oc in range(NCH):
            k = sl * 8 + oc

            @blk.tensor
            def _(t, sl=sl, oc=oc, k=k, ssl=ssl):
                if sl == 0 and oc == 0:
                    g_wait(p, t, "WO_done")
                if k >= 2:
                    g_wait(p, t, ("W_hev", k - 2))
                bk = p.bank(k % 2)
                for c in range(NCH):
                    ins = t.matmul(bk, p.w_outT[c][:, oc * 128:(oc + 1) * 128],
                                   p.ybuf[c][:, ssl], start=(c == 0), stop=(c == NCH - 1))
                cT.inc(ins, 1)
            g_set(p, ("W_mm", k), cT)

            @blk.vector
            def _(v, sl=sl, oc=oc, k=k, main=main):
                g_wait(p, v, ("W_mm", k))
                if oc == 0:
                    g_wait(p, v, ("W_ldm", sl))
                VI(p, v.tensor_tensor(main[oc][:, :], p.bank(k % 2), main[oc][:, :],
                                      op=OP.add))
            g_set(p, ("W_hev", k), p.cVc)

        for si, st in enumerate(stages):
            @blk.vector
            def _(v, sl=sl, st=st, si=si, main=main):
                VW(p, v)
                prev = (sl, stages[si - 1]) if si > 0 else \
                    ((sl - 1, "4") if sl > 0 else None)
                if prev is not None:
                    g_wait(p, v, ("W_st", prev))
                for c in range(NCH):
                    VI(p, v.tensor_copy(xb[c][:, :], main[c][:, :]))
                    VW(p, v)
                    VI(p, v.tensor_tensor(sq[c][:, :], xb[c][:, :], xb[c][:, :],
                                          op=OP.mult))
            g_set(p, ("W_xb", (sl, st)), p.cVc)

            @blk.tensor
            def _(t, sl=sl, st=st, si=si):
                g_wait(p, t, ("W_xb", (sl, st)))
                prev = (sl, stages[si - 1]) if si > 0 else \
                    ((sl - 1, "4") if sl > 0 else None)
                if prev is not None:
                    g_wait(p, t, (("W" + prev[1], prev[0]), "rows"))
                for c in range(NCH):
                    t.matmul(p.bank(2)[0:1, :], p.inv_c[:, :], xb[c][:, :],
                             start=(c == 0), stop=(c == NCH - 1))
                for c in range(NCH):
                    ins = t.matmul(p.bank(3)[0:1, :], p.inv_c[:, :], sq[c][:, :],
                                   start=(c == 0), stop=(c == NCH - 1))
                cT.inc(ins, 1)
            g_set(p, ("W_st", (sl, st)), cT)
            g_set(p, ("W" + st + "_stats", sl), cT)
            extra = []
            if st == "4":
                extra.append(("W_pl", (sl, "2")))
            if sl >= 2:
                extra.append(("W_pl", (sl - 2, "4" if st == "2" else st)))
            emit_row_finish(p, ("W" + st, sl), rows_of[st], rsl, extra_scalar=extra)

            @blk.tensor
            def _(t, sl=sl, st=st, si=si, ssl=ssl):
                g_wait(p, t, (("W" + st, sl), "rows"))
                prevu = (sl, stages[si - 1]) if si > 0 else \
                    ((sl - 1, "4") if sl > 0 else None)
                if prevu is not None:
                    g_wait(p, t, ("W_apl", prevu))
                rows = rows_of[st]
                rsl = slice((sl % 2) * TSL, (sl % 2 + 1) * TSL)
                t.matmul(p.bank(4), p.ones_rf[:, :], rows[0][0:1, rsl],
                         start=True, stop=True)
                ins = t.matmul(p.bank(5), p.ones_rf[:, :], rows[1][0:1, rsl],
                               start=True, stop=True)
                cT.inc(ins, 1)
            g_set(p, ("W_pl", (sl, st)), cT)

            @blk.vector
            def _(v, sl=sl, st=st, main=main, aux=aux):
                g_wait(p, v, ("W_pl", (sl, st)))
                if st == "3":
                    g_wait(p, v, ("W_ldx", sl))
                for c in range(NCH):
                    if st == "4":
                        VI(p, v.tensor_tensor(aux[c][:, :], main[c][:, :], p.bank(4),
                                              op=OP.mult))
                        VW(p, v)
                        VI(p, v.tensor_tensor(aux[c][:, :], aux[c][:, :], p.bank(5),
                                              op=OP.add))
                        continue
                    VI(p, v.tensor_tensor(main[c][:, :], main[c][:, :], p.bank(4),
                                          op=OP.mult))
                    VW(p, v)
                    VI(p, v.tensor_tensor(main[c][:, :], main[c][:, :], p.bank(5),
                                          op=OP.add))
                    VW(p, v)
                    if st == "2":
                        VI(p, v.tensor_scalar(main[c][:, :], main[c][:, :],
                                              p.gam[:, c:c + 1], p.bet[:, c:c + 1],
                                              op0=OP.mult, op1=OP.add))
                    else:
                        VI(p, v.tensor_tensor(main[c][:, :], main[c][:, :],
                                              aux[c][:, :], op=OP.add))
            g_set(p, ("W_apl", (sl, st)), p.cVc)

            if st == "3":
                @blk.gpsimd
                def _(g, sl=sl, ssl=ssl, main=main):
                    g_wait(p, g, ("W_apl", (sl, "3")))
                    cs = [p.cD[2], p.cD[3]][sl % 2]
                    for c in range(NCH):
                        cs.inc(g.dma_start(out=p.scr_x1[c, :, ssl],
                                           in_=main[c][:, :]), 16)
                g_set(p, ("W_x1sp", sl), [p.cD[2], p.cD[3]][sl % 2])

        @blk.gpsimd
        def _(g, sl=sl, ssl=ssl, aux=aux):
            g_wait(p, g, ("W_apl", (sl, "4")))
            cs = [cG, p.cD[1]][sl % 2]
            for c in range(NCH):
                cs.inc(g.dma_start(out=p.scr_fA[c, :, ssl], in_=aux[c][:, :]), 16)
        g_set(p, ("W_fasp", sl), [cG, p.cD[1]][sl % 2])

    g_set(p, "LC_done", cG)
    g_set(p, "LC_done_odd", p.cD[1])
    g_set(p, "X1_done_e", p.cD[2])
    g_set(p, "X1_done_o", p.cD[3])
    p.close_scope("SO")
    p.close_scope("SY")
    p.close_scope("LN")

    if "d_xf" in p.dbg:
        @blk.sync
        def _(s):
            barrier(p, s)
            for c in range(NCH):
                p.cO.inc(s.dma_start(out=p.dbg["d_xf"][c, :, :], in_=p.scr_fA[c, :, :]), 16)
            for c in range(NCH):
                p.cO.inc(s.dma_start(out=p.dbg["d_x1"][c, :, :], in_=p.scr_x1[c, :, :]), 16)
        g_set(p, "DXF_dump", p.cO)


def emit_row_finish(p, tag, rows, ssl, extra_scalar=()):
    blk = p.blk

    @blk.vector
    def _(v):
        g_wait(p, v, (tag[0] + "_stats", tag[1]) if isinstance(tag, tuple) else tag)
        VI(p, v.tensor_copy(p.srow[0][:, :], p.bank(2)[0:1, :]))
        VW(p, v)
        VI(p, v.tensor_tensor(p.srow[1][:, :], p.srow[0][:, :], p.srow[0][:, :], op=OP.mult))
        VW(p, v)
        VI(p, v.tensor_tensor(p.srow[1][:, :], p.bank(3)[0:1, :], p.srow[1][:, :],
                              op=OP.subtract))
    g_set(p, (tag, "var"), p.cVc)

    @blk.scalar
    def _(s):
        g_wait(p, s, (tag, "var"))
        for eg in extra_scalar:
            g_wait(p, s, eg)
        SI(p, s.activation(p.srow[1][:, :], p.srow[1][:, :], AF.Ln, bias=p.eps_col[0:1, :]))
        SW(p, s)
        SI(p, s.activation(rows[0][0:1, ssl], p.srow[1][:, :], AF.Exp, scale=-0.5))
    g_set(p, (tag, "rstd"), p.cSc)

    @blk.vector
    def _(v):
        g_wait(p, v, (tag, "rstd"))
        VI(p, v.scalar_tensor_tensor(rows[1][0:1, ssl], p.srow[0][:, :], -1.0,
                                     rows[0][0:1, ssl], op0=OP.mult, op1=OP.mult))
    g_set(p, (tag, "rows"), p.cVc)


def emit_ffn_layer(p, l):
    blk = p.blk
    cT, cA, cG, cB = p.cT, p.cA, p.cG, p.cB
    L = f"L{l}_"
    vis_k = lambda w: list(range(max(0, 4 * w - 4), 4 * w + 4))

    qT = [p.sbS(L + "B1", L + f"qT{i}", [128, T], BF16) for i in range(4)]
    kT = [p.sbS(L + "B1", L + f"kT{i}", [128, T], BF16) for i in range(4)]
    vtm = [p.sbS(L + "B1", L + f"vtm{i}", [128, 520], BF16) for i in range(32)]

    wq = [p.sbS(L + "A", L + f"wq{c}", [128, 1536], BF16) for c in range(NCH)]
    fin13 = [p.sbS(L + "A", L + f"fin13{c}", [128, 513]) for c in range(NCH)]
    finb2 = [[p.sbS(L + "A", L + f"finb{j}_{c}", [128, 513], BF16) for c in range(NCH)]
             for j in range(2)]
    xps = [p.sbS(L + "A", L + f"xps{i}", [64, TSL], BF16) for i in range(4)]
    l2a8 = [p.sbS(L + "A", L + f"l2a{i}", [64, TSL], BF16) for i in range(8)]
    sqf = [p.sbS(L + "A", L + f"sqf{i}", [128, TSL], BF16) for i in range(4)]
    srw = [p.sbS(L + "A", L + f"srw{i}", [2, TSL]) for i in range(4)]
    l2r = p.sbS(L + "A", L + "l2r", [64, TSL], BF16)
    l2t = p.sbS(L + "A", L + "l2t", [64, TSL], BF16)
    qscol = p.sbS(L + "A", L + "qscol", [128, 1])
    kscol = p.sbS(L + "A", L + "kscol", [128, 1])
    prev_done = "LC_done" if l == 0 else "L0_F4_done"

    @blk.sync
    def _(s):
        g_wait(p, s, prev_done)
        if l == 0:
            g_wait(p, s, "LC_done_odd")
            g_wait(p, s, "X1_done_e")
            g_wait(p, s, "X1_done_o")
        p.cE[3].inc(s.dma_start(out=qscol[0:64, :], in_=p.qscI[l][:, :]), 16)
        p.cE[3].inc(s.dma_start(out=qscol[64:128, :], in_=p.qscI[l][:, :]), 16)
        p.cE[3].inc(s.dma_start(out=kscol[0:64, :], in_=p.kscI[l][:, :]), 16)
        p.cE[3].inc(s.dma_start(out=kscol[64:128, :], in_=p.kscI[l][:, :]), 16)
    g_set(p, (L, "qsld"), p.cE[3])

    @blk.gpsimd
    def _(g):
        g_wait(p, g, prev_done)
        if l == 0:
            g_wait(p, g, "LC_done_odd")
            g_wait(p, g, "X1_done_e")
            g_wait(p, g, "X1_done_o")
        else:
            g_wait(p, g, "L0_F4_done_d1")
        for c in range(NCH):
            p.cD[1].inc(g.dma_start(out=wq[c][:, :],
                                    in_=p.wqkvI[l][c * 128:(c + 1) * 128, :]), 16)
        for i in range(32):
            ins = g.memset(vtm[i][:, :], 1.0)
        for j in range(2):
            for c in range(NCH):
                ins = g.memset(finb2[j][c][:, 0:1], 0.0)
        for c in range(NCH):
            ins = g.memset(fin13[c][:, 0:1], 0.0)
        p.cZ.inc(ins, 1)
    g_set(p, (L, "wqld"), p.cD[1])
    g_set(p, (L, "vtmz"), p.cZ)

    for sl in range(NSL):
        ssl = slice(sl * TSL, (sl + 1) * TSL)

        finb = finb2[sl % 2]

        @blk.sync
        def _(s, sl=sl):
            if sl == 0:
                g_wait(p, s, prev_done)
            if sl >= 1:
                g_wait(p, s, (L + "F1_cp", sl - 1))
            for c in range(NCH):
                if sl == 0:
                    p.cA.inc(s.dma_start(out=fin13[c][:, 1:513],
                                         in_=p.scr_fA[c, :, 0:512]), 16)
                else:
                    p.cA.inc(s.dma_start(out=fin13[c][:, 0:513],
                                         in_=p.scr_fA[c, :, sl * 512 - 1: (sl + 1) * 512]), 16)
        g_set(p, (L + "F1_ldf", sl), p.cA)

        @blk.vector
        def _(v, sl=sl, finb=finb):
            g_wait(p, v, (L + "F1_ldf", sl))
            if sl == 0:
                g_wait(p, v, (L, "vtmz"))
            if sl >= 2:
                g_wait(p, v, (L + "F1_vdone", sl - 2))
            for c in range(NCH):
                VI(p, v.tensor_copy(finb[c][:, :], fin13[c][:, :]))
        g_set(p, (L + "F1_cp", sl), p.cVc)
        g_set(p, (L + "F1_ld", sl), p.cVc)

        @blk.sync
        def _(s, sl=sl):
            if sl >= 1:
                g_wait(p, s, (L + "F1_xpos", sl * 16 - 1))
            for i, nm in enumerate(["xp_qc", "xp_qs", "xp_kc", "xp_ks"]):
                p.cE[1].inc(s.dma_start(out=xps[i][:, :],
                                        in_=p.xpI[nm][:, sl * TSL:(sl + 1) * TSL]), 16)
        g_set(p, (L + "F1_xpld", sl), p.cE[1])

        def rhsv(c, w=TSL, t0=0):
            off = (1 if c < 4 else 0) + t0
            return finb[c][:, off:off + w]

        for fc in range(8):
            @blk.tensor
            def _(t, sl=sl, fc=fc):
                k = sl * 8 + fc
                if k == 0:
                    g_wait(p, t, (L, "wqld"))
                    g_wait(p, t, "LC_done" if l == 0 else "L0_F4_done")
                g_wait(p, t, (L + "F1_ld", sl))
                if k >= 2:
                    g_wait(p, t, (L + "F1_qkev", k - 2))
                bk = p.bank(fc % 2)
                for c in range(NCH):
                    ins = t.matmul(bk, wq[c][:, fc * 128:(fc + 1) * 128],
                                   rhsv(c), start=(c == 0), stop=(c == NCH - 1))
                cT.inc(ins, 1)
            g_set(p, (L + "F1_qk", sl * 8 + fc), cT)

            @blk.vector
            def _(v, sl=sl, fc=fc, ssl=ssl):
                k = sl * 8 + fc
                g_wait(p, v, (L + "F1_qk", k))
                dst = qT[fc] if fc < 4 else kT[fc - 4]
                VI(p, v.tensor_copy(dst[:, ssl], p.bank(fc % 2)))
            g_set(p, (L + "F1_qkev", sl * 8 + fc), p.cVc)

        for tsub in range(4):
            @blk.tensor
            def _(t, sl=sl, tsub=tsub):
                kv = sl * 4 + tsub
                if kv >= 2:
                    g_wait(p, t, (L + "F1_vev", kv - 2))
                bk = p.bank(2 + kv % 2)
                for c in range(NCH):
                    ins = t.matmul(bk, rhsv(c, 128, tsub * 128),
                                   wq[c][:, 1024:1536], start=(c == 0), stop=(c == NCH - 1))
                cT.inc(ins, 1)
            g_set(p, (L + "F1_v", sl * 4 + tsub), cT)

            @blk.vector
            def _(v, sl=sl, tsub=tsub):
                kv = sl * 4 + tsub
                g_wait(p, v, (L + "F1_v", kv))
                bk = p.bank(2 + kv % 2)
                for h in range(FH):
                    VI(p, v.tensor_copy(vtm[kv][:, h * 65: h * 65 + 64],
                                        bk[:, h * 64:(h + 1) * 64]))
            g_set(p, (L + "F1_vev", sl * 4 + tsub), p.cVc)

        for wg in range(2):
            prs = [wg * 4 + j for j in range(4)]

            @blk.vector
            def _(v, sl=sl, wg=wg, prs=prs, ssl=ssl):
                if wg == 0:
                    g_wait(p, v, (L + "F1_qkev", sl * 8 + 7))
                    g_wait(p, v, (L + "F1_xpld", sl))
                    if sl == 0:
                        g_wait(p, v, (L, "qsld"))
                for j, pr in enumerate(prs):
                    src = qT[pr] if pr < 4 else kT[pr - 4]
                    scol = qscol if pr < 4 else kscol
                    if sl > 0 or wg > 0:
                        pw = (sl, wg - 1) if wg > 0 else (sl - 1, 1)
                        g_wait(p, v, (L + "F1_stmm", (pw[0], pw[1], j)))
                    VW(p, v)
                    VI(p, v.tensor_scalar(l2a8[2 * j][:, :], src[0:64, ssl],
                                          scol[0:64, :], None, op0=OP.mult))
                    VI(p, v.tensor_scalar(l2a8[2 * j + 1][:, :], src[64:128, ssl],
                                          scol[64:128, :], None, op0=OP.mult))
                    VW(p, v)
                    VI(p, v.tensor_tensor(sqf[j][0:64, :], l2a8[2 * j][:, :],
                                          l2a8[2 * j][:, :], op=OP.mult))
                    VI(p, v.tensor_tensor(sqf[j][64:128, :], l2a8[2 * j + 1][:, :],
                                          l2a8[2 * j + 1][:, :], op=OP.mult))
            g_set(p, (L + "F1_sqw", (sl, wg)), p.cVc)

            @blk.tensor
            def _(t, sl=sl, wg=wg, prs=prs):
                g_wait(p, t, (L + "F1_sqw", (sl, wg)))
                if l == 0 and sl == 0 and wg == 0:
                    g_wait(p, t, "consts")
                    g_wait(p, t, "constsW")
                for j, pr in enumerate(prs):
                    if sl > 0 or wg > 0:
                        pw = (sl, wg - 1) if wg > 0 else (sl - 1, 1)
                        g_wait(p, t, (L + "F1_xposw", pw))
                    cT.inc(t.matmul(p.bank(2 + j)[0:2, :], p.sel2[:, :],
                                    sqf[j][:, :], start=True, stop=True), 1)
                    g_set(p, (L + "F1_stmm", (sl, wg, j)), cT)

            @blk.scalar
            def _(s, sl=sl, wg=wg, prs=prs):
                for j, pr in enumerate(prs):
                    g_wait(p, s, (L + "F1_stmm", (sl, wg, j)))
                    if sl > 0 or wg > 0:
                        pw = (sl, wg - 1) if wg > 0 else (sl - 1, 1)
                        g_wait(p, s, (L + "F1_plw", pw))
                    SW(p, s)
                    SI(p, s.activation(srw[j][:, :], p.bank(2 + j)[0:2, :], AF.Ln,
                                       bias=p.epsl2_col[0:2, :]))
                    SW(p, s)
                    SI(p, s.activation(srw[j][:, :], srw[j][:, :], AF.Exp, scale=-0.5))
            g_set(p, (L + "F1_rstdw", (sl, wg)), p.cSc)

            @blk.tensor
            def _(t, sl=sl, wg=wg, prs=prs):
                g_wait(p, t, (L + "F1_rstdw", (sl, wg)))
                for j, pr in enumerate(prs):
                    cT.inc(t.matmul(p.bank(2 + j), p.brd2[:, :], srw[j][:, :],
                                    start=True, stop=True), 1)
            g_set(p, (L + "F1_plw", (sl, wg)), cT)

            @blk.vector
            def _(v, sl=sl, wg=wg, prs=prs, ssl=ssl):
                g_wait(p, v, (L + "F1_plw", (sl, wg)))
                for j, pr in enumerate(prs):
                    src = qT[pr] if pr < 4 else kT[pr - 4]
                    xc, xs = (xps[0], xps[1]) if pr < 4 else (xps[2], xps[3])
                    bk = p.bank(2 + j)
                    for h in range(2):
                        a = l2a8[2 * j + h]
                        r0 = h * 64
                        VW(p, v)
                        VI(p, v.tensor_scalar(l2r[0:32, :], a[32:64, :], -1.0, None,
                                              op0=OP.mult))
                        VI(p, v.tensor_copy(l2r[32:64, :], a[0:32, :]))
                        VW(p, v)
                        VI(p, v.tensor_tensor(l2t[:, :], a[:, :], xc[:, :], op=OP.mult))
                        VI(p, v.tensor_tensor(l2r[:, :], l2r[:, :], xs[:, :], op=OP.mult))
                        VW(p, v)
                        VI(p, v.tensor_tensor(l2t[:, :], l2t[:, :], l2r[:, :], op=OP.add))
                        VW(p, v)
                        VI(p, v.tensor_tensor(src[r0:r0 + 64, ssl], l2t[:, :],
                                              bk[r0:r0 + 64, :], op=OP.mult))
            g_set(p, (L + "F1_xposw", (sl, wg)), p.cVc)
        g_set(p, (L + "F1_xpos", sl * 16 + 15), p.cVc)
        @blk.vector
        def _(v, sl=sl):
            VW(p, v)
            VI(p, v.tensor_copy(l2t[0:1, 0:1], l2t[0:1, 0:1]))
        g_set(p, (L + "F1_vdone", sl), p.cVc)
    g_set(p, L + "F1_done", p.cVc)
    p.close_scope(L + "A")

    attf = [p.sbS(L + "C", L + f"attf{i}", [128, TSL], BF16) for i in range(4)]
    p.lmask_sb = [p.sbS(L + "C", L + f"lmask{i}", [128, TSL], BF16) for i in range(8)]
    negB = [p.sbS(L + "C", L + f"negB{i}", [1, T], BF16) for i in range(2)]
    Bbuf = [p.sbS(L + "C", L + f"Bbuf{i}", [128, 32]) for i in range(2)]
    bstg = [p.sbS(L + "C", L + f"bstg{i}", [32, 128]) for i in range(2)]
    yt65 = p.sbS(L + "C", L + "yt65", [128, TSL])
    invloc = p.sbS(L + "C", L + "invloc", [1, TSL])
    ostg = [p.sbS(L + "C", L + f"ostg{i}", [64, TSL], BF16) for i in range(2)]

    @blk.sync
    def _(s):
        g_wait(p, s, L + "F1_done")
        for i in range(8):
            p.cE[3].inc(s.dma_start(out=p.lmask_sb[i][:, :], in_=p.lmaskI[i, :, :]), 16)
    g_set(p, (L, "lmaskld"), p.cE[3])

    g2 = [p.cSL.n]
    f2slot = {}
    f2avval = {}
    Bbuf2 = [p.sbS(L + "C", L + f"Bbuf2_{i}", [128, 64]) for i in range(2)]

    F2B = [2, 3, 0]

    def nslot2(t):
        g2[0] += 1
        t.wait_ge(p.cSL.h, g2[0] - 3)
        return p.bank(F2B[g2[0] % 3])

    gp2 = [p.cRX.n]

    def nslotP(t):
        gp2[0] += 1
        if gp2[0] - 1 > 0:
            t.wait_ge(p.cRX.h, gp2[0] - 1)
        return p.bank(1)

    def run_P1(h):
        Qs, Ks, r0 = qT[h // 2], kT[h // 2], (h % 2) * 64
        for w in range(8):
            kk = vis_k(w)
            nb = len(kk) // 4
            slots_w = {}

            @blk.tensor
            def _(t, h=h, w=w, kk=kk, nb=nb, Qs=Qs, Ks=Ks, r0=r0, slots_w=slots_w):
                if h == 0 and w == 0:
                    g_wait(p, t, L + "F1_done")
                for qt in range(4):
                    for j in range(nb):
                        bk = nslot2(t)
                        slots_w[(qt, j)] = g2[0]
                        cT.inc(t.matmul(
                            bk,
                            Qs[r0:r0 + 64, w * 512 + qt * 128: w * 512 + (qt + 1) * 128],
                            Ks[r0:r0 + 64, kk[0] * 128 + j * 512: kk[0] * 128 + (j + 1) * 512],
                            start=True, stop=True), 1)
                        g_set(p, (L + "F2_p1mm", (h, w, qt, j)), cT)

            @blk.vector
            def _(v, h=h, w=w, nb=nb, slots_w=slots_w):
                for qt in range(4):
                    for j in range(nb):
                        g_wait(p, v, (L + "F2_p1mm", (h, w, qt, j)))
                        if w == 0 and qt == 0 and j == 0 and h >= 2:
                            g_wait(p, v, (L + "F2_bev", h - 2))
                        gg = slots_w[(qt, j)]
                        bk = p.bank(F2B[gg % 3])
                        dstcol = Bbuf2[h % 2][:, (w * 4 + qt) * 2 + j: (w * 4 + qt) * 2 + j + 1]
                        p.cSL.inc(v.reduce_max(dstcol, bk, axis=AX.X), 1)
                    if nb == 1:
                        v.wait_ge(p.cSL.h, p.cSL.n)
                        VI(p, v.tensor_copy(
                            Bbuf2[h % 2][:, (w * 4 + qt) * 2 + 1: (w * 4 + qt) * 2 + 2],
                            Bbuf2[h % 2][:, (w * 4 + qt) * 2: (w * 4 + qt) * 2 + 1]))
            g_set(p, (L + "F2_maxw", (h, w)), p.cSL)

    def emit_bfinish(h):
        @blk.vector
        def _(v, h=h):
            g_wait(p, v, (L + "F2_maxw", (h, 7)))
            VW(p, v)
            VI(p, v.tensor_tensor(Bbuf[h % 2][:, :], Bbuf2[h % 2][:, 0:64:2],
                                  Bbuf2[h % 2][:, 1:64:2], op=OP.max))
            VW(p, v)
            VI(p, v.tensor_scalar(Bbuf[h % 2][:, :], Bbuf[h % 2][:, :], -1.0, None,
                                  op0=OP.mult))
        g_set(p, (L + "F2_bneg", h), p.cVc)

        @blk.tensor
        def _(t, h=h):
            g_wait(p, t, (L + "F2_bneg", h))
            bk = nslotP(t)
            cT.inc(t.transpose(bk[0:32, 0:128], Bbuf[h % 2][:, :], p.ident_f[:, :]), 1)
        g_set(p, (L + "F2_bT", h), cT)

        @blk.vector
        def _(v, h=h):
            g_wait(p, v, (L + "F2_bT", h))
            bk = p.bank(1)
            p.cRX.inc(v.tensor_copy(bstg[h % 2][:, :], bk[0:32, 0:128]), 1)
        g_set(p, (L + "F2_bev", h), p.cRX)

        @blk.gpsimd
        def _(g, h=h):
            g_wait(p, g, (L + "F2_bev", h))
            cc = p.cD[2 + h % 2]
            cc.inc(g.dma_start(out=p.scr_bb[h, :, :], in_=bstg[h % 2][:, :]), 16)
            g.wait_ge(cc.h, cc.n)
            cc.inc(g.dma_start(
                out=negB[h % 2][0:1, :],
                in_=p.scr_bb[h].rearrange("a b -> (a b)")[None, :]), 16)
        g_set(p, (L + "F2_brow", h), p.cD[2 + h % 2])

    def emit_P2(h):
        Qs, Ks, r0 = qT[h // 2], kT[h // 2], (h % 2) * 64
        for w in range(8):
            kk = vis_k(w)
            nk = len(kk)

            @blk.tensor
            def _(t, h=h, w=w, kk=kk, nk=nk, Qs=Qs, Ks=Ks, r0=r0):
                if w == 0:
                    g_wait(p, t, (L + "F2_brow", h))
                    if h == 0:
                        g_wait(p, t, (L, "lmaskld"))
                    if h + 2 < FH:
                        g_wait(p, t, (L + "F2_maxw", (h + 2, 7)))
                for ki in range(nk + 1):
                    if ki < nk:
                        ktg = kk[ki]
                        bk = nslot2(t)
                        f2slot[(h, w, ki)] = g2[0]
                        t.matmul(bk, Ks[r0:r0 + 64, ktg * 128:(ktg + 1) * 128],
                                 Qs[r0:r0 + 64, w * 512:(w + 1) * 512],
                                 start=True, stop=False)
                        t.matmul(bk, p.ones_rb[0:1, :],
                                 negB[h % 2][0:1, w * 512:(w + 1) * 512],
                                 start=False, stop=False)
                        ridx = (ktg * 128 - w * 512 + 512) // 128
                        ins = t.matmul(bk, p.ident_b[:, :], p.lmask_sb[ridx][:, :],
                                       start=False, stop=True)
                        cT.inc(ins, 1)
                        g_set(p, (L + "F2_p2mm", (h, w, ki)), cT)
                    if ki >= 1:
                        k2 = ki - 1
                        t.wait_ge(p.cSL.h, f2slot[(h, w, k2)])
                        if k2 == 0 and not (h == 0 and w <= 1):
                            pw = (h, w - 2) if w >= 2 else (h - 1, 6 + w)
                            g_wait(p, t, (L + "F2_yev", pw))
                        avb = p.bank(4 + w % 2)
                        ins = t.matmul(avb[0:65, :], vtm[kk[k2]][:, h * 65: h * 65 + 65],
                                       attf[f2slot[(h, w, k2)] % 4][:, :],
                                       start=(k2 == 0), stop=(k2 == nk - 1))
                        cT.inc(ins, 1)
                        g_set(p, (L + "F2_avk", (h, w, k2)), cT)
            g_set(p, (L + "F2_av", (h, w)), cT)

            @blk.scalar
            def _(s, h=h, w=w, nk=nk):
                for ki in range(nk):
                    gg = f2slot[(h, w, ki)]
                    g_wait(p, s, (L + "F2_p2mm", (h, w, ki)))
                    pk = (h, w, ki - 4)
                    if ki < 4:
                        if w > 0:
                            nkp = len(vis_k(w - 1))
                            pk = (h, w - 1, nkp - 4 + ki)
                        elif h > 0:
                            pk = (h - 1, 7, 4 + ki)
                        else:
                            pk = None
                    if pk is not None:
                        g_wait(p, s, (L + "F2_avk", pk))
                    bk = p.bank(F2B[gg % 3])
                    p.cSL.inc(s.activation(attf[gg % 4][:, :], bk, AF.Exp, scale=8.0), 1)
            g_set(p, (L + "F2_exp", (h, w)), p.cSL)

            @blk.vector
            def _(v, h=h, w=w):
                g_wait(p, v, (L + "F2_av", (h, w)))
                if w > 0:
                    g_wait(p, v, (L + "F2_ysc", (h, w - 1)))
                elif h > 0:
                    g_wait(p, v, (L + "F2_ysc", (h - 1, 7)))
                avb = p.bank(4 + w % 2)
                VI(p, v.tensor_copy(yt65[0:65, :], avb[0:65, :]))
                g_set(p, (L + "F2_yev", (h, w)), p.cVc)
                VW(p, v)
                VI(p, v.reciprocal(invloc[:, :], yt65[64:65, :]))
            g_set(p, (L + "F2_invl", (h, w)), p.cVc)

            @blk.tensor
            def _(t, h=h, w=w):
                g_wait(p, t, (L + "F2_invl", (h, w)))
                bk = nslotP(t)
                cT.inc(t.matmul(bk, p.ones_rf[:, :], invloc[:, :], start=True, stop=True), 1)
            g_set(p, (L + "F2_ypl", (h, w)), cT)

            @blk.vector
            def _(v, h=h, w=w, r0=r0):
                g_wait(p, v, (L + "F2_ypl", (h, w)))
                nw = h * 8 + w
                if nw >= 2:
                    g_wait(p, v, (L + "F2_osp", ((nw - 2) // 8, (nw - 2) % 8)))
                bk = p.bank(1)
                p.cRX.inc(v.tensor_tensor(ostg[nw % 2][:, :],
                                          yt65[0:64, :], bk[0:64, :], op=OP.mult), 1)
            g_set(p, (L + "F2_ysc", (h, w)), p.cRX)

            @blk.gpsimd
            def _(g, h=h, w=w, r0=r0):
                g_wait(p, g, (L + "F2_ysc", (h, w)))
                nw = h * 8 + w
                cs = p.cD[nw % 2]
                cs.inc(g.dma_start(
                    out=p.scr_v[h // 2, r0:r0 + 64, w * 512:(w + 1) * 512],
                    in_=ostg[nw % 2][:, :]), 16)
            g_set(p, (L + "F2_osp", (h, w)), p.cD[nw % 2] if False else p.cD[(h * 8 + w) % 2])

    p.f2_yslot = {}
    p.f2_bslot = {}
    run_P1(0)
    run_P1(1)
    for h in range(FH):
        emit_bfinish(h)
        if h + 2 < FH:
            run_P1(h + 2)
        emit_P2(h)
    g_set(p, L + "F2_done", p.cSL)

    @blk.gpsimd
    def _(g):
        for h in range(FH):
            for w in range(8):
                g_wait(p, g, (L + "F2_osp", (h, w)))
        p.cD[0].inc(g.dma_start(out=bstg[0][0:1, 0:1].rearrange("a b -> (a b)")[None, :]
                                if False else p.scr_bb[0, 0:1, 0:1], in_=bstg[0][0:1, 0:1]), 16)
    g_set(p, (L, "oTspill"), p.cD[0])
    p.close_scope(L + "C")
    p.close_scope(L + "B1")

    if l == 1:
        p.row5 = [p.sbS("LN5", "row5_rstd", [1, T], side="right"),
                  p.sbS("LN5", "row5_mr", [1, T], side="right")]
    p.rowF = [p.sbS(L + "LN", L + "rowF_rstd", [1, 2 * TSL], side="right"),
              p.sbS(L + "LN", L + "rowF_mr", [1, 2 * TSL], side="right")]
    hb16 = [p.sbS(L + "D4", L + f"hb16{c}", [128, TSL], BF16) for c in range(NCH)]
    agb = [p.sbS(L + "D4", L + f"agb{i}", [128, TSL], BF16) for i in range(22)]
    gt = p.sbS(L + "D4", L + "gt", [128, TSL], BF16)
    at = p.sbS(L + "D4", L + "at", [128, TSL])
    fm13 = [p.sbS(L + "D4", L + f"fm13{c}", [128, 513]) for c in range(NCH)]
    w1sb = [p.sbS(L + "D4", L + f"w1sb{i}", [128, 8 * 128], BF16) for i in range(4)]
    w2stb = [p.sbS(L + "D4", L + f"w2stb{i}", [128, 22 * 128], BF16) for i in range(2)]
    b1sb = p.sbS(L + "D4", L + "b1sb", [128, 44])
    b2sb = p.sbS(L + "D4", L + "b2sb", [128, NCH])
    lgsb = p.sbS(L + "D4", L + "lgsb", [128, NCH])
    lbsb = p.sbS(L + "D4", L + "lbsb", [128, NCH])
    xb5 = [p.sbS(L + "D4", L + f"xb5{c}", [128, TSL], BF16) for c in range(NCH)]
    sq5 = [p.sbS(L + "D4", L + f"sq5{c}", [128, TSL], BF16) for c in range(NCH)]
    wo_sb = [p.sbS(L + "D3", L + f"wo{i}", [128, C], BF16) for i in range(4)]
    oTs2 = [p.sbS(L + "D3", L + f"oTs{i}", [128, 4 * TSL], BF16) for i in range(2)]
    finD2 = [[p.sbS(L + "D3", L + f"finD{j}_{c}", [128, TSL]) for c in range(NCH)]
             for j in range(2)]
    carry = [p.sbS(L + "D3", L + f"carry{c}", [128, 1]) for c in range(NCH)]
    carryF = [p.sbS(L + "D3", L + f"carryF{c}", [128, 1]) for c in range(NCH)]
    xbD = [p.sbS(L + "D3", L + f"xbD{c}", [128, TSL], BF16) for c in range(NCH)]
    sqD = [p.sbS(L + "D3", L + f"sqD{c}", [128, TSL], BF16) for c in range(NCH)]

    @blk.sync
    def _(s):
        g_wait(p, s, L + "F2_done")
        for dst, srcI in [(b1sb, p.b1pI[l]), (b2sb, p.b2tI[l]),
                          (lgsb, p.lgtI[l]), (lbsb, p.lbtI[l])]:
            p.cE[3].inc(s.dma_start(out=dst[:, :], in_=srcI[:, :]), 16)
    g_set(p, (L, "bld"), p.cE[3])

    @blk.gpsimd
    def _(g):
        g_wait(p, g, L + "F2_done")
        for i in range(4):
            p.cD[1].inc(g.dma_start(out=wo_sb[i][:, :],
                                    in_=p.woI[l][i * 128:(i + 1) * 128, :]), 16)
        for c in range(NCH):
            ins = g.memset(carry[c][:, :], 0.0)
        p.cZ.inc(ins, 1)
    g_set(p, (L, "wold"), p.cD[1])

    @blk.gpsimd
    def _(g):
        for c in range(NCH):
            ins = g.memset(fm13[c][:, 0:1], 0.0)
        p.cZ.inc(ins, 1)
    g_set(p, (L, "fm13z"), p.cZ)
    g_set(p, (L, "carryz"), p.cZ)

    for sl in range(NSL):
        ssl = slice(sl * TSL, (sl + 1) * TSL)
        finD = finD2[sl % 2]
        fmD = finD
        oTs = oTs2[sl % 2]

        @blk.sync
        def _(s, sl=sl, ssl=ssl, finD=finD, oTs=oTs):
            g_wait(p, s, L + "F2_done")
            g_wait(p, s, (L, "oTspill"))
            if sl >= 2:
                g_wait(p, s, (L + "F3_mmdone", sl - 2))
                g_wait(p, s, (L + "F3_xbsq", sl - 2))
                g_wait(p, s, (L + "F3_fm13", sl - 2))
            cs = [p.cA, p.cE[2]][sl % 2]
            for c in range(NCH):
                cs.inc(s.dma_start(out=finD[c][:, :], in_=p.scr_fA[c, :, ssl]), 16)
            for i in range(4):
                cs.inc(s.dma_start(out=oTs[0 * 0][:, i * TSL:(i + 1) * TSL]
                                   if False else oTs[:, i * TSL:(i + 1) * TSL],
                                   in_=p.scr_v[i, :, ssl]), 16)
        g_set(p, (L + "F3_ld", sl), [p.cA, p.cE[2]][sl % 2])

        for oc in range(NCH):
            @blk.tensor
            def _(t, sl=sl, oc=oc, ssl=ssl, oTs=oTs):
                k = sl * 8 + oc
                if k == 0:
                    g_wait(p, t, (L, "wold"))
                if oc == 0 and sl >= 1:
                    g_wait(p, t, (L + "F4_ag", (sl - 1, 21)))
                if k >= 2:
                    g_wait(p, t, (L + "F3_hev", k - 2))
                g_wait(p, t, (L + "F3_ld", sl))
                bk = p.bank(k % 2)
                for i in range(4):
                    ins = t.matmul(bk, wo_sb[i][:, oc * 128:(oc + 1) * 128],
                                   oTs[:, i * TSL:(i + 1) * TSL],
                                   start=(i == 0), stop=(i == 3))
                cT.inc(ins, 1)
            g_set(p, (L + "F3_mm", sl * 8 + oc), cT)
            if oc == NCH - 1:
                g_set(p, (L + "F3_mmdone", sl), cT)

            @blk.vector
            def _(v, sl=sl, oc=oc, finD=finD):
                k = sl * 8 + oc
                g_wait(p, v, (L + "F3_mm", k))
                if oc == 0:
                    g_wait(p, v, (L + "F3_ld", sl))
                VI(p, v.tensor_tensor(finD[oc][:, :], p.bank(k % 2), finD[oc][:, :], op=OP.add))
            g_set(p, (L + "F3_hev", sl * 8 + oc), p.cVc)
        g_set(p, (L + "F3_h", sl), p.cVc)

        @blk.vector
        def _(v, sl=sl, fmD=fmD):
            VW(p, v)
            if sl == 0:
                g_wait(p, v, (L, "carryz"))
            for c in range(NCH):
                if c == 0 and sl >= 1:
                    g_wait(p, v, (L + "F3r_stats", sl - 1))
                if c < 4:
                    VI(p, v.tensor_copy(xbD[c][:, :], fmD[c][:, :]))
                else:
                    VI(p, v.tensor_copy(xbD[c][:, 0:1], carry[c][:, :]))
                    VI(p, v.tensor_copy(xbD[c][:, 1:TSL], fmD[c][:, 0:TSL - 1]))
                    VW(p, v)
                    VI(p, v.tensor_copy(carry[c][:, :], fmD[c][:, TSL - 1:TSL]))
                VW(p, v)
                VI(p, v.tensor_tensor(sqD[c][:, :], xbD[c][:, :], xbD[c][:, :], op=OP.mult))
        g_set(p, (L + "F3_xbsq", sl), p.cVc)

        @blk.tensor
        def _(t, sl=sl):
            g_wait(p, t, (L + "F3_xbsq", sl))
            if sl >= 1:
                g_wait(p, t, ((L + "F3r", sl - 1), "var"))
            for c in range(NCH):
                t.matmul(p.bank(2)[0:1, :], p.inv_c[:, :], xbD[c][:, :],
                         start=(c == 0), stop=(c == NCH - 1))
            for c in range(NCH):
                ins = t.matmul(p.bank(3)[0:1, :], p.inv_c[:, :], sqD[c][:, :],
                               start=(c == 0), stop=(c == NCH - 1))
            cT.inc(ins, 1)
        g_set(p, (L + "F3r_stats", sl), cT)
        extraF = [(L + "F4_planes", sl - 2)] if sl >= 2 else []
        emit_row_finish(p, (L + "F3r", sl), p.rowF,
                        slice((sl % 2) * TSL, (sl % 2 + 1) * TSL), extra_scalar=extraF)

        @blk.vector
        def _(v, sl=sl, fmD=fmD):
            VW(p, v)
            if sl >= 1:
                g_wait(p, v, (L + "F4_fod", sl - 1))
                g_wait(p, v, (L + "F4_spill", sl - 1))
                if l == 1:
                    g_wait(p, v, (L + "F5_xbsq", sl - 1))
            for c in range(NCH):
                if c >= 4 and sl >= 1:
                    VI(p, v.tensor_copy(fm13[c][:, 0:1], carryF[c][:, :]))
                VI(p, v.tensor_copy(fm13[c][:, 1:513], fmD[c][:, :]))
                if c >= 4:
                    VW(p, v)
                    VI(p, v.tensor_copy(carryF[c][:, :], fmD[c][:, TSL - 1:TSL]))
                    VW(p, v)
        g_set(p, (L + "F3_fm13", sl), p.cVc)

        @blk.tensor
        def _(t, sl=sl, ssl=ssl):
            g_wait(p, t, ((L + "F3r", sl), "rows"))
            if sl >= 1:
                g_wait(p, t, (L + "F4_w2evdone", sl - 1))
            rslF = slice((sl % 2) * TSL, (sl % 2 + 1) * TSL)
            t.matmul(p.bank(4), p.ones_rf[:, :], p.rowF[0][0:1, rslF], start=True, stop=True)
            ins = t.matmul(p.bank(5), p.ones_rf[:, :], p.rowF[1][0:1, rslF], start=True, stop=True)
            cT.inc(ins, 1)
        g_set(p, (L + "F4_planes", sl), cT)

        @blk.vector
        def _(v, sl=sl):
            g_wait(p, v, (L + "F4_planes", sl))
            g_wait(p, v, (L + "F3_fm13", sl))
            if sl == 0:
                g_wait(p, v, (L, "fm13z"))
                g_wait(p, v, (L, "bld"))
            if sl >= 1:
                g_wait(p, v, (L + "F4_w1", (sl - 1, 21)))
            for c in range(NCH):
                off = 1 if c < 4 else 0
                sh = fm13[c][:, off:off + TSL]
                VI(p, v.tensor_tensor(hb16[c][:, :], sh, p.bank(4), op=OP.mult))
                VW(p, v)
                VI(p, v.tensor_tensor(hb16[c][:, :], hb16[c][:, :], p.bank(5), op=OP.add))
                VW(p, v)
                VI(p, v.tensor_scalar(hb16[c][:, :], hb16[c][:, :], lgsb[:, c:c + 1],
                                      lbsb[:, c:c + 1], op0=OP.mult, op1=OP.add))
        g_set(p, (L + "F4_apply", sl), p.cVc)

        for mp in range(22):
            rr = min(128, FHID - mp * 128)
            kk = sl * 22 + mp

            @blk.gpsimd
            def _(g, sl=sl, mp=mp, rr=rr, kk=kk):
                if kk >= 2:
                    g_wait(p, g, (L + "F4_w1", (sl, mp - 2) if mp >= 2 else (sl - 1, 20 + mp)))
                if kk < 2:
                    g_wait(p, g, "wcast")
                buf_a = w1sb[(kk % 2) * 2]
                buf_g = w1sb[(kk % 2) * 2 + 1]
                srcA = p.w1bfI[l][:, mp * 128: mp * 128 + rr].rearrange(
                    "(c p) n -> p c n", p=128)
                srcG = p.w1bfI[l][:, FHID + mp * 128: FHID + mp * 128 + rr].rearrange(
                    "(c p) n -> p c n", p=128)
                cw1 = p.cF[kk % 2]
                cw1.inc(g.dma_start(
                    out=buf_a.rearrange("p (c n) -> p c n", c=NCH)[:, :, 0:rr], in_=srcA), 16)
                cw1.inc(g.dma_start(
                    out=buf_g.rearrange("p (c n) -> p c n", c=NCH)[:, :, 0:rr], in_=srcG), 16)
            g_set(p, (L + "F4_w1ld", (sl, mp)), p.cF[kk % 2])

            @blk.tensor
            def _(t, sl=sl, mp=mp, rr=rr, kk=kk):
                g_wait(p, t, (L + "F4_apply", sl))
                g_wait(p, t, (L + "F4_w1ld", (sl, mp)))
                if mp >= 2:
                    g_wait(p, t, (L + "F4_ag", (sl, mp - 2)))
                elif sl >= 1:
                    g_wait(p, t, (L + "F4_ag", (sl - 1, 20 + mp)))
                ba = p.bank((mp % 2) * 2)
                bg = p.bank((mp % 2) * 2 + 1)
                buf_a = w1sb[(kk % 2) * 2]
                buf_g = w1sb[(kk % 2) * 2 + 1]
                for c in range(NCH):
                    t.matmul(ba[0:rr, :], buf_a[:, c * 128: c * 128 + rr],
                             hb16[c][:, :], start=(c == 0), stop=(c == NCH - 1))
                for c in range(NCH):
                    ins = t.matmul(bg[0:rr, :], buf_g[:, c * 128: c * 128 + rr],
                                   hb16[c][:, :], start=(c == 0), stop=(c == NCH - 1))
                cT.inc(ins, 1)
            g_set(p, (L + "F4_w1", (sl, mp)), cT)

            @blk.scalar
            def _(s, sl=sl, mp=mp, rr=rr):
                g_wait(p, s, (L + "F4_w1", (sl, mp)))
                if mp >= 1:
                    g_wait(p, s, (L + "F4_ag", (sl, mp - 1)))
                elif sl >= 1:
                    g_wait(p, s, (L + "F4_ag", (sl - 1, 21)))
                bg = p.bank((mp % 2) * 2 + 1)
                SW(p, s)
                SI(p, s.activation(gt[0:rr, :], bg[0:rr, :], AF.Gelu,
                                   bias=b1sb[0:rr, 22 + mp: 23 + mp]))
            g_set(p, (L + "F4_gelu", (sl, mp)), p.cSc)

            @blk.vector
            def _(v, sl=sl, mp=mp, rr=rr):
                g_wait(p, v, (L + "F4_gelu", (sl, mp)))
                ba = p.bank((mp % 2) * 2)
                VI(p, v.scalar_tensor_tensor(agb[mp][0:rr, :], ba[0:rr, :],
                                             b1sb[0:rr, mp:mp + 1], gt[0:rr, :],
                                             op0=OP.add, op1=OP.mult))
            g_set(p, (L + "F4_ag", (sl, mp)), p.cVc)

        for oc in range(NCH):
            kk2 = sl * 8 + oc

            @blk.gpsimd
            def _(g, sl=sl, oc=oc, kk2=kk2):
                kk = kk2
                if kk >= 2:
                    g_wait(p, g, (L + "F4_w2", (sl, oc - 2) if oc >= 2 else (sl - 1, 6 + oc)))
                if kk2 < 2:
                    g_wait(p, g, "wcast")
                buf = w2stb[kk % 2]
                cw2 = p.cF[2 + kk2 % 2]
                cw2.inc(g.dma_start(
                    out=buf.rearrange("p (c n) -> p c n", c=22)[0:128, 0:21, :],
                    in_=p.w2bfI[l][0:2688, oc * 128:(oc + 1) * 128].rearrange(
                        "(c p) n -> p c n", p=128)), 16)
                cw2.inc(g.dma_start(
                    out=buf[0:42, 21 * 128: 22 * 128],
                    in_=p.w2bfI[l][2688:2730, oc * 128:(oc + 1) * 128]), 16)
            g_set(p, (L + "F4_w2ld", (sl, oc)), p.cF[2 + kk2 % 2])

            @blk.tensor
            def _(t, sl=sl, oc=oc):
                g_wait(p, t, (L + "F4_ag", (sl, 21)))
                g_wait(p, t, (L + "F4_w2ld", (sl, oc)))
                k = sl * 8 + oc
                if k >= 2:
                    g_wait(p, t, (L + "F4_w2ev", k - 2))
                bk = p.bank(4 + oc % 2)
                buf = w2stb[k % 2]
                for i in range(22):
                    rr = min(128, FHID - i * 128)
                    ins = t.matmul(bk, buf[0:rr, i * 128: i * 128 + 128] if False else
                                   buf[0:rr, i * 128: (i + 1) * 128],
                                   agb[i][0:rr, :], start=(i == 0), stop=(i == 21))
                cT.inc(ins, 1)
            g_set(p, (L + "F4_w2", sl * 8 + oc if False else (sl, oc)), cT)

            @blk.vector
            def _(v, sl=sl, oc=oc):
                g_wait(p, v, (L + "F4_w2", (sl, oc)))
                bk = p.bank(4 + oc % 2)
                VW(p, v)
                VI(p, v.tensor_scalar(at[:, :], bk, b2sb[:, oc:oc + 1], None, op0=OP.add))
                VW(p, v)
                VI(p, v.tensor_tensor(fm13[oc][:, 1:513], at[:, :],
                                      fm13[oc][:, 1:513], op=OP.add))
            g_set(p, ((L + "F4_w2ev", sl * 8 + oc)) if False else (L + "F4_w2ev", sl * 8 + oc), p.cVc)
        g_set(p, (L + "F4_fod", sl), p.cVc)
        g_set(p, (L + "F4_w2evdone", sl), p.cVc)

        if l == 1:
            @blk.vector
            def _(v, sl=sl):
                VW(p, v)
                for c in range(NCH):
                    if c == 0 and sl >= 1:
                        g_wait(p, v, (L + "F5r_stats", sl - 1))
                    VI(p, v.tensor_copy(xb5[c][:, :], fm13[c][:, 1:513]))
                    VW(p, v)
                    VI(p, v.tensor_tensor(sq5[c][:, :], xb5[c][:, :], xb5[c][:, :], op=OP.mult))
            g_set(p, (L + "F5_xbsq", sl), p.cVc)

            @blk.tensor
            def _(t, sl=sl):
                g_wait(p, t, (L + "F5_xbsq", sl))
                if sl >= 1:
                    g_wait(p, t, ((L + "F5r", sl - 1), "var"))
                for c in range(NCH):
                    t.matmul(p.bank(2)[0:1, :], p.inv_c[:, :], xb5[c][:, :],
                             start=(c == 0), stop=(c == NCH - 1))
                for c in range(NCH):
                    ins = t.matmul(p.bank(3)[0:1, :], p.inv_c[:, :], sq5[c][:, :],
                                   start=(c == 0), stop=(c == NCH - 1))
                cT.inc(ins, 1)
            g_set(p, (L + "F5r_stats", sl), cT)
            emit_row_finish(p, (L + "F5r", sl), p.row5, slice(sl * TSL, (sl + 1) * TSL))

        @blk.gpsimd
        def _(g, sl=sl, ssl=ssl):
            g_wait(p, g, (L + "F4_fod", sl))
            if l == 1:
                g_wait(p, g, (L + "F5_xbsq", sl))
            if "d_xf" in p.dbg and l == 0 and sl == 0:
                g_wait(p, g, "DXF_dump")
            if "d_f0" in p.dbg and l == 1 and sl == 0:
                g_wait(p, g, "DF0_dump")
            for c in range(NCH):
                p.cD[4].inc(g.dma_start(out=p.scr_fA[c, :, ssl], in_=fm13[c][:, 1:513]), 16)
        g_set(p, (L + "F4_spill", sl), p.cD[4])
        g_set(p, (L + "F4_fod", sl), p.cVc) if False else None
    g_set(p, L + "F4_done", p.cD[4])
    g_set(p, L + "F4_done_d1", p.cD[1])
    g_set(p, L + "F4_rows", p.cVc)
    if "d_f0" in p.dbg and l == 0:
        @blk.sync
        def _(s):
            barrier(p, s)
            for c in range(NCH):
                p.cO.inc(s.dma_start(out=p.dbg["d_f0"][c, :, :], in_=p.scr_fA[c, :, :]), 16)
        g_set(p, "DF0_dump", p.cO)
    p.close_scope(L + "D3")
    p.close_scope(L + "D4")
    p.close_scope(L + "LN")


def emit_final(p):
    blk = p.blk
    cT, cA, cG = p.cT, p.cA, p.cG
    src2 = [[p.sbS("Z", f"srcZ{j}_{c}", [128, TSL]) for c in range(NCH)] for j in range(2)]
    aux2 = [[p.sbS("Z", f"auxZ{j}_{c}", [128, TSL]) for c in range(NCH)] for j in range(2)]
    tmout2 = [[p.sbS("Z", f"tmoutZ{j}_{i}", [128, C]) for i in range(4)] for j in range(2)]

    for sl in range(NSL):
        ssl = slice(sl * TSL, (sl + 1) * TSL)
        src = src2[sl % 2]
        aux = aux2[sl % 2]
        tmout = tmout2[sl % 2]

        @blk.sync
        def _(s, sl=sl, ssl=ssl, src=src, aux=aux):
            if sl == 0:
                g_wait(p, s, "L1_F4_done")
                g_wait(p, s, "L1_F4_rows")
                g_wait(p, s, "X1_done_e")
                g_wait(p, s, "X1_done_o")
            if sl == 1:
                g_wait(p, s, "L1_F4_done_d1")
            if sl >= 2:
                g_wait(p, s, ("Z_trdone", sl - 2))
            cs = [p.cA, p.cE[0]][sl % 2]
            for c in range(NCH):
                cs.inc(s.dma_start(out=src[c][:, :], in_=p.scr_fA[c, :, ssl]), 16)
                cs.inc(s.dma_start(out=aux[c][:, :], in_=p.scr_x1[c, :, ssl]), 16)
        g_set(p, ("Z_ld", sl), [p.cA, p.cE[0]][sl % 2])

        @blk.tensor
        def _(t, sl=sl, ssl=ssl):
            if sl == 0:
                g_wait(p, t, "L1_F4_rows")
            if sl >= 1:
                g_wait(p, t, ("Z_apply", sl - 1))
            t.matmul(p.bank(4), p.ones_rf[:, :], p.row5[0][0:1, ssl], start=True, stop=True)
            ins = t.matmul(p.bank(5), p.ones_rf[:, :], p.row5[1][0:1, ssl], start=True, stop=True)
            cT.inc(ins, 1)
        g_set(p, ("Z_planes", sl), cT)

        @blk.vector
        def _(v, sl=sl, src=src, aux=aux):
            g_wait(p, v, ("Z_planes", sl))
            g_wait(p, v, ("Z_ld", sl))
            for c in range(NCH):
                VI(p, v.tensor_tensor(src[c][:, :], src[c][:, :], p.bank(4), op=OP.mult))
                VW(p, v)
                VI(p, v.tensor_tensor(src[c][:, :], src[c][:, :], p.bank(5), op=OP.add))
                VW(p, v)
                VI(p, v.tensor_tensor(src[c][:, :], src[c][:, :], aux[c][:, :], op=OP.add))
        g_set(p, ("Z_apply", sl), p.cVc)

        for r in range(4):
            for c in range(NCH):
                k = sl * 32 + r * 8 + c

                @blk.tensor
                def _(t, sl=sl, r=r, c=c, k=k, src=src):
                    if r == 0 and c == 0:
                        g_wait(p, t, ("Z_apply", sl))
                    if k >= 2:
                        g_wait(p, t, ("Z_ev", k - 2))
                    pt = p.bank(k % 2)[:, 0:128]
                    cT.inc(t.transpose(pt, src[c][:, r * 128:(r + 1) * 128],
                                       p.ident_f[:, :]), 1)
                g_set(p, ("Z_tr", k), cT)

                @blk.vector
                def _(v, sl=sl, r=r, c=c, k=k, tmout=tmout):
                    g_wait(p, v, ("Z_tr", k))
                    if r == 0 and c == 0 and sl >= 2:
                        g_wait(p, v, ("Z_st", sl - 2))
                    pt = p.bank(k % 2)[:, 0:128]
                    VI(p, v.tensor_copy(tmout[r][:, c * 128:(c + 1) * 128], pt))
                g_set(p, ("Z_ev", k), p.cVc)
        g_set(p, ("Z_trdone", sl), p.cVc)

        @blk.sync
        def _(s, sl=sl, tmout=tmout):
            g_wait(p, s, ("Z_trdone", sl))
            cs = [p.cO, p.cE[3]][sl % 2]
            for r in range(4):
                cs.inc(s.dma_start(
                    out=p.out_t[sl * TSL + r * 128: sl * TSL + (r + 1) * 128, :],
                    in_=tmout[r][:, :]), 16)
        g_set(p, ("Z_st", sl), [p.cO, p.cE[3]][sl % 2])

    @blk.sync
    def _(s):
        s.wait_ge(p.cO.h, p.cO.n)
        s.wait_ge(p.cE[3].h, p.cE[3].n)


def prep_inputs(inputs, batch_idx, cst):
    m = {
        "x": np.ascontiguousarray(inputs["L0_attn_x"][batch_idx]),
        "w_in": np.ascontiguousarray(inputs["L0_attn_w_in"]),
        "w_out": np.ascontiguousarray(inputs["L0_attn_w_out"]),
        "gamma_t": np.ascontiguousarray(
            np.asarray(inputs["L0_attn_ln_gamma"]).reshape(NCH, 128).T),
        "beta_t": np.ascontiguousarray(
            np.asarray(inputs["L0_attn_ln_beta"]).reshape(NCH, 128).T),
        "ident": cst["ident"],
        "sel2": cst["sel2"], "brd2": cst["brd2"],
        "rp_cos": cst["rp_cos"], "rp_sin": cst["rp_sin"],
        "xp_qc": cst["xp_qc"], "xp_qs": cst["xp_qs"],
        "xp_kc": cst["xp_kc"], "xp_ks": cst["xp_ks"],
        "lmask": cst["lmask"],
    }
    for l in range(2):
        b1 = np.asarray(inputs["L0_ffn_b1"][l])
        b1p = np.zeros((44, 128), dtype=np.float32)
        for mm in range(22):
            a = b1[128 * mm: min(128 * (mm + 1), FHID)]
            b1p[mm, :len(a)] = a
            g = b1[FHID + 128 * mm: FHID + min(128 * (mm + 1), FHID)]
            b1p[22 + mm, :len(g)] = g
        m[f"b1p{l}"] = np.ascontiguousarray(b1p.T)
        m[f"b2t{l}"] = np.ascontiguousarray(np.asarray(inputs["L0_ffn_b2"][l]).reshape(NCH, 128).T)
        m[f"lgt{l}"] = np.ascontiguousarray(np.asarray(inputs["L0_ffn_ln_g"][l]).reshape(NCH, 128).T)
        m[f"lbt{l}"] = np.ascontiguousarray(np.asarray(inputs["L0_ffn_ln_b"][l]).reshape(NCH, 128).T)
        m[f"qsc{l}"] = np.ascontiguousarray(np.asarray(inputs["L0_ffn_q_scale"][l]).reshape(AD, 1))
        m[f"ksc{l}"] = np.ascontiguousarray(np.asarray(inputs["L0_ffn_k_scale"][l]).reshape(AD, 1))
        m[f"wqkv{l}"] = np.ascontiguousarray(inputs["L0_ffn_wqkv"][l])
        m[f"wo{l}"] = np.ascontiguousarray(inputs["L0_ffn_wo"][l])
        m[f"w1{l}"] = np.ascontiguousarray(inputs["L0_ffn_w1"][l])
        m[f"w2{l}"] = np.ascontiguousarray(inputs["L0_ffn_w2"][l])
    return m


def kernel(**inputs):
    cst = host_constants()
    nc, p = build_program()
    in_maps = [prep_inputs(inputs, b, cst) for b in range(8)]
    res = run_bass_kernel_spmd(nc, in_maps, core_ids=list(range(8)))
    return np.stack([res.results[b]["out"] for b in range(8)], axis=0)


def run_debug(inputs, cores=(0,), trace=False):
    cst = host_constants()
    nc, p = build_program()
    in_maps = [prep_inputs(inputs, b, cst) for b in cores]
    res = run_bass_kernel_spmd(nc, in_maps, core_ids=list(range(len(cores))), trace=trace)
    return res


# version 16
# speedup vs baseline: 1.0253x; 1.0034x over previous
import contextlib
import numpy as np
import ml_dtypes
import concourse.bass as bass
import concourse.mybir as mybir
from concourse.bass_utils import run_bass_kernel_spmd
from concourse.alu_op_type import AluOpType

F32 = mybir.dt.float32
BF16 = mybir.dt.bfloat16
AF = mybir.ActivationFunctionType
AX = mybir.AxisListType
OP = AluOpType

T = 4096
C = 1024
NCH = 8
TSL = 512
NSL = 8
AH = 16
AD = 64
FH = 8
FHID = 2730
F2H = 5460
LN_EPS = 1e-5

DEBUG = []
bfdt = ml_dtypes.bfloat16


def host_constants():
    cst = {}
    cst["ident"] = np.eye(128, dtype=np.float32)
    sel2 = np.zeros((128, 2), np.float32)
    sel2[0:64, 0] = 1.0
    sel2[64:128, 1] = 1.0
    cst["sel2"] = sel2.astype(bfdt)
    brd2 = np.zeros((2, 128), np.float32)
    brd2[0, 0:64] = 1.0
    brd2[1, 64:128] = 1.0
    cst["brd2"] = brd2
    i = np.arange(1, AD // 2 + 1, dtype=np.float64)
    thetas = np.repeat(1000.0 ** (-2.0 * i / AD), 2)
    pos = np.arange(1, T + 1, dtype=np.float64)
    args = pos[:, None] * thetas[None, :]
    perm = np.concatenate([np.arange(0, AD, 2), np.arange(1, AD, 2)])
    cst["rp_cos"] = np.ascontiguousarray(np.cos(args)[:, perm].T).astype(bfdt)
    cst["rp_sin"] = np.ascontiguousarray(np.sin(args)[:, perm].T).astype(bfdt)
    half = np.arange(0, AD, 2, dtype=np.float64)
    inv_freq = 1.0 / 10000.0 ** (half / AD)
    t = np.arange(T, dtype=np.float64)
    freqs = np.concatenate([t[:, None] * inv_freq[None, :]] * 2, axis=-1)
    base = (half + 0.4 * AD) / (1.4 * AD)
    power = (t - T // 2) / 256.0
    scale = np.concatenate([base[None, :] ** power[:, None]] * 2, axis=-1)
    for k, v in [("xp_qc", np.cos(freqs) * scale), ("xp_qs", np.sin(freqs) * scale),
                 ("xp_kc", np.cos(freqs) / scale), ("xp_ks", np.sin(freqs) / scale)]:
        cst[k] = np.ascontiguousarray(v.T).astype(bfdt)
    i_idx = np.arange(128)[:, None]
    j_idx = np.arange(512)[None, :]
    m = np.zeros((8, 128, 512), dtype=np.float32)
    for a, r in enumerate(range(-512, 385, 128)):
        vis = (j_idx - 511 <= r + i_idx) & (r + i_idx <= j_idx)
        m[a] = np.where(vis, 0.0, -30000.0)
    cst["lmask"] = m.astype(bfdt)
    return cst


class Ctr:
    __slots__ = ("h", "n")

    def __init__(self, h):
        self.h = h
        self.n = 0

    def inc(self, instr, k=1):
        instr.then_inc(self.h, k)
        self.n += k
        return instr


class P:
    pass


def build_program():
    nc = bass.Bass("TRN2", target_bir_lowering=False)
    p = P()
    p.nc = nc
    p.gates = {}
    p.rope_slot = {}
    p.plane_slot = {}
    p.plain_slot = {}
    p.dnT_bank = {}
    p.dnT_slot = {}
    p.avhist = {}

    def gate(key):
        pass

    din = lambda name, shape, dt=F32: nc.dram_tensor(name, shape, dt, kind="ExternalInput").ap()
    scr = lambda name, shape, dt=F32: nc.dram_tensor(name, shape, dt, kind="Internal").ap()

    p.x_in = din("x", [T, C])
    p.w_inI = din("w_in", [3 * C, C])
    p.w_outI = din("w_out", [C, C])
    p.gamma_t = din("gamma_t", [128, NCH])
    p.beta_t = din("beta_t", [128, NCH])
    p.wqkvI = [din(f"wqkv{l}", [C, 3 * 512]) for l in range(2)]
    p.woI = [din(f"wo{l}", [512, C]) for l in range(2)]
    p.w1I = [din(f"w1{l}", [C, F2H]) for l in range(2)]
    p.w2I = [din(f"w2{l}", [FHID, C]) for l in range(2)]
    p.b1pI = [din(f"b1p{l}", [128, 44]) for l in range(2)]
    p.b2tI = [din(f"b2t{l}", [128, NCH]) for l in range(2)]
    p.lgtI = [din(f"lgt{l}", [128, NCH]) for l in range(2)]
    p.lbtI = [din(f"lbt{l}", [128, NCH]) for l in range(2)]
    p.qscI = [din(f"qsc{l}", [AD, 1]) for l in range(2)]
    p.kscI = [din(f"ksc{l}", [AD, 1]) for l in range(2)]
    p.identI = din("ident", [128, 128])
    p.sel2I = din("sel2", [128, 2], BF16)
    p.brd2I = din("brd2", [2, 128])
    p.rp_cosI = din("rp_cos", [AD, T], BF16)
    p.rp_sinI = din("rp_sin", [AD, T], BF16)
    p.xpI = {k: din(k, [AD, T], BF16) for k in ["xp_qc", "xp_qs", "xp_kc", "xp_ks"]}
    p.lmaskI = din("lmask", [8, 128, TSL], BF16)

    p.out_t = nc.dram_tensor("out", [T, C], F32, kind="ExternalOutput").ap()
    p.dbg = {}
    for name, shape, dt in DEBUG:
        p.dbg[name] = nc.dram_tensor(name, shape, dt, kind="ExternalOutput").ap()

    p.w1bfI = [scr(f"w1bf{l}", [C, F2H], BF16) for l in range(2)]
    p.w2bfI = [scr(f"w2bf{l}", [FHID, C], BF16) for l in range(2)]
    p.scr_r5 = scr("scr_r5", [2, T])
    p.scr_x0 = scr("scr_x0", [NCH, 128, T])
    p.scr_xh = scr("scr_xh", [NCH, 128, T])
    p.scr_hbq = scr("scr_hbq", [AH, 128, T], BF16)
    p.scr_hbk = scr("scr_hbk", [AH, 128, T], BF16)
    p.scr_v = scr("scr_v", [8, 128, T], BF16)
    p.scr_h = scr("scr_h", [NCH, 128, T])
    p.scr_ao = scr("scr_ao", [NCH, 128, T])
    p.scr_x1 = scr("scr_x1", [NCH, 128, T])
    p.scr_fA = scr("scr_fA", [NCH, 128, T])
    p.scr_fB = scr("scr_fB", [NCH, 128, T])
    p.scr_dnb = scr("scr_dnb", [AH, 32, 128])
    p.scr_bb = scr("scr_bb", [FH, 32, 128])

    stack = contextlib.ExitStack()
    p.stack = stack
    p.sb = lambda nm, shp, dt=F32: stack.enter_context(nc.sbuf_tensor(nm, shp, dt))
    mksem = lambda nm: Ctr(stack.enter_context(nc.semaphore(nm)))
    p.scopes = {}

    def sbS(scope, nm, shp, dt=F32, side=None):
        if scope not in p.scopes:
            p.scopes[scope] = contextlib.ExitStack()
        return p.scopes[scope].enter_context(nc.sbuf_tensor(nm, shp, dt, side=side))

    def close_scope(scope):
        if scope in p.scopes:
            p.scopes.pop(scope).close()

    p.sbS = sbS
    p.close_scope = close_scope

    with stack:
        p.cW = mksem("cW")
        p.cA = mksem("cA")
        p.cG = mksem("cG")
        p.cB = mksem("cB")
        p.cO = mksem("cO")
        p.cT = mksem("cT")
        p.cV = mksem("cV")
        p.cS = mksem("cS")
        p.cX = mksem("cX")
        p.cY = mksem("cY")
        p.cZ = mksem("cZ")
        p.cD = [mksem(f"cD{i}") for i in range(6)]
        p.cE = [mksem(f"cE{i}") for i in range(4)]
        p.cF = [mksem(f"cF{i}") for i in range(4)]
        p.cSL = mksem("cSL")
        p.cRX = mksem("cRX")
        p.cVc = mksem("cVc")
        p.cSc = mksem("cSc")
        p.all_ctrs = [p.cW, p.cA, p.cG, p.cB, p.cO, p.cT, p.cV, p.cS, p.cX, p.cY,
                      p.cZ, p.cVc, p.cSc, p.cSL, p.cRX] + p.cD + p.cE + p.cF

        p.ident_f = p.sb("ident_f", [128, 128])
        p.ident_b = p.sb("ident_b", [128, 128], BF16)
        p.inv_c = p.sb("inv_c", [128, 1], BF16)
        p.ones_rf = p.sb("ones_rf", [1, 128])
        p.ones_rb = p.sb("ones_rb", [1, 128], BF16)
        p.ones128 = p.sb("ones128", [128, 1], BF16)
        p.eps_col = p.sb("eps_col", [128, 1])
        p.epsl2_col = p.sb("epsl2_col", [128, 1])
        p.gam = p.sb("gam", [128, NCH])
        p.bet = p.sb("bet", [128, NCH])
        p.sel2 = p.sb("sel2_sb", [128, 2], BF16)
        p.brd2 = p.sb("brd2_sb", [2, 128])


        p.psF = stack.enter_context(nc.psum_tensor("psF", [128, 3072], F32))
        p.psB = stack.enter_context(nc.psum_tensor("psB", [128, 2048], BF16))
        p.bank = lambda i, w=1: p.psF[:, i * 512:(i + w) * 512]
        p.bslot = lambda i: p.psB[:, (i % 2) * 1024:(i % 2) * 1024 + 128]

        p.srow = [p.sb(f"srow{i}", [1, TSL]) for i in range(4)]

        p.rp_cos = p.sbS("AB", "rp_cos_sb", [AD, T], BF16)
        p.rp_sin = p.sbS("AB", "rp_sin_sb", [AD, T], BF16)
        p.w_inT = [p.sbS("AB", f"w_inT{c}", [128, 3 * C], BF16) for c in range(NCH)]
        p.xhb = [p.sbS("AB", f"xhb{c}", [128, T], BF16) for c in range(NCH)]

        with nc.Block("init") as blk:
            @blk.sync
            def _(s):
                for c0 in p.all_ctrs:
                    s.sem_clear(c0.h)

        with nc.Block("main") as blk:
            p.blk = blk
            emit_const_loads(p)
            emit_winT(p)
            emit_phaseA(p)
            emit_phaseB(p)
            emit_attention(p)
            emit_wout_lnchain(p)
            for l in range(2):
                emit_ffn_layer(p, l)
            emit_final(p)

        for sc in reversed(list(p.scopes)):
            p.close_scope(sc)

    return nc, p


def VI(p, ins):
    p.cVc.inc(ins, 1)
    return ins


def VW(p, v):
    v.wait_ge(p.cVc.h, p.cVc.n)


def SI(p, ins):
    p.cSc.inc(ins, 1)
    return ins


def SW(p, s):
    s.wait_ge(p.cSc.h, p.cSc.n)


def g_set(p, key, ctr):
    p.gates[key] = (ctr, ctr.n)


def g_wait(p, eng, key):
    ctr, v = p.gates[key]
    eng.wait_ge(ctr.h, v)


def barrier(p, eng):
    for c in p.all_ctrs:
        eng.wait_ge(c.h, c.n)


def emit_const_loads(p):
    blk = p.blk
    cW, cG = p.cW, p.cG

    @blk.sync
    def _(s):
        for dst, src in [(p.ident_f, p.identI), (p.gam, p.gamma_t), (p.bet, p.beta_t),
                         (p.rp_cos, p.rp_cosI), (p.rp_sin, p.rp_sinI),
                         (p.sel2, p.sel2I), (p.brd2, p.brd2I)]:
            cW.inc(s.dma_start(out=dst[:, :], in_=src[:, :]), 16)


    @blk.gpsimd
    def _(g):
        p.cD[4].inc(g.dma_start(out=p.ident_b[:, :], in_=p.identI[:, :]), 16)
        g.memset(p.inv_c[:, :], 1.0 / C)
        g.memset(p.ones_rf[:, :], 1.0)
        g.memset(p.ones_rb[:, :], 1.0)
        g.memset(p.ones128[:, :], 1.0)
        g.memset(p.eps_col[:, :], LN_EPS)
        p.cZ.inc(g.memset(p.epsl2_col[:, :], 1e-18), 1)
        for l in range(2):
            p.cY.inc(g.dma_start(out=p.w1bfI[l][:, :], in_=p.w1I[l][:, :]), 16)
            p.cY.inc(g.dma_start(out=p.w2bfI[l][:, :], in_=p.w2I[l][:, :]), 16)
    g_set(p, "wcast", p.cY)
    g_set(p, "consts", p.cZ)
    g_set(p, "identb", p.cD[4])
    g_set(p, "constsW", cW)


def emit_winT(p):
    blk = p.blk
    stg = [p.sbS("W", f"winstg{i}", [128, C], BF16) for i in range(4)]
    cG, cT, cV = p.cG, p.cT, p.cV

    for ch in range(24):
        r0 = ch * 128
        dst = stg[ch % 4]

        @blk.gpsimd
        def _(g, ch=ch, r0=r0, dst=dst):
            cs = p.cD[ch % 4]
            if ch >= 4:
                g_wait(p, g, ("winT_ev", ch - 4))
            if ch < 16:
                for hb in range(2):
                    rr = r0 + hb * 64
                    cs.inc(g.dma_start(out=dst[hb * 64:hb * 64 + 32, :],
                                       in_=p.w_inI[rr:rr + 64:2, :]), 16)
                    cs.inc(g.dma_start(out=dst[hb * 64 + 32:hb * 64 + 64, :],
                                       in_=p.w_inI[rr + 1:rr + 64:2, :]), 16)
            else:
                cs.inc(g.dma_start(out=dst[:, :], in_=p.w_inI[r0:r0 + 128, :]), 16)
        g_set(p, ("winT_ld", ch), p.cD[ch % 4])

        for c in range(NCH):
            k = ch * 8 + c

            @blk.tensor
            def _(t, ch=ch, dst=dst, c=c, k=k):
                if c == 0:
                    g_wait(p, t, ("winT_ld", ch))
                if ch == 0 and c == 0:
                    g_wait(p, t, "consts")
                    g_wait(p, t, "identb")
                if k >= 2:
                    g_wait(p, t, ("winT_evc", k - 2))
                cT.inc(t.transpose(p.bslot(k), dst[:, c * 128:(c + 1) * 128],
                                   p.ident_b[:, :]), 1)
            g_set(p, ("winT_tr", k), cT)

            @blk.vector
            def _(v, ch=ch, c=c, k=k):
                g_wait(p, v, ("winT_tr", k))
                VI(p, v.tensor_copy(p.w_inT[c][:, ch * 128:(ch + 1) * 128], p.bslot(k)))
            g_set(p, ("winT_evc", k), p.cVc)
        g_set(p, ("winT_ev", ch), p.cVc)
    g_set(p, "winT_done", p.cVc)
    p.close_scope("W")


def emit_phaseA(p):
    blk = p.blk
    tm_in = [p.sbS("A", f"tm_in{i}", [128, C]) for i in range(4)]
    x0T = [p.sbS("A", f"x0T{c}", [128, TSL]) for c in range(NCH)]
    xb = [p.sbS("A", f"xbA{c}", [128, TSL], BF16) for c in range(NCH)]
    sq = [p.sbS("A", f"sqA{c}", [128, TSL], BF16) for c in range(NCH)]
    cA, cT, cV, cS, cG = p.cA, p.cT, p.cV, p.cS, p.cG

    for sl in range(NSL):
        ssl = slice(sl * TSL, (sl + 1) * TSL)

        @blk.sync
        def _(s, sl=sl):
            if sl == 0:
                barrier(p, s)
            else:
                g_wait(p, s, ("A_tr_done", sl - 1))
            for r in range(4):
                cA.inc(s.dma_start(out=tm_in[r][:, :],
                                   in_=p.x_in[sl * TSL + r * 128: sl * TSL + (r + 1) * 128, :]), 16)
        g_set(p, ("A_ld", sl), cA)

        for c in range(NCH):
            for r in range(4):
                k = c * 4 + r
                kg = sl * 32 + k

                @blk.tensor
                def _(t, sl=sl, c=c, r=r, k=k, kg=kg):
                    if k == 0:
                        g_wait(p, t, ("A_ld", sl))
                        if sl == 0:
                            g_wait(p, t, "winT_done")
                    if kg >= 2:
                        g_wait(p, t, ("A_ev", kg - 2))
                    pt = p.bank(k % 2)[:, 0:128]
                    cT.inc(t.transpose(pt, tm_in[r][:, c * 128:(c + 1) * 128],
                                       p.ident_f[:, :]), 1)
                g_set(p, ("A_tr", kg), cT)

                @blk.vector
                def _(v, sl=sl, c=c, r=r, k=k, kg=kg):
                    g_wait(p, v, ("A_tr", kg))
                    if k == 0 and sl >= 1:
                        g_wait(p, v, ("A_spill_xh", sl - 1))
                    pt = p.bank(k % 2)[:, 0:128]
                    VI(p, v.tensor_copy(x0T[c][:, r * 128:(r + 1) * 128], pt))
                g_set(p, ("A_ev", kg), p.cVc)
        g_set(p, ("A_tr_done", sl), cT)
        g_set(p, ("A_ev_done", sl), p.cVc)

        @blk.gpsimd
        def _(g, sl=sl, ssl=ssl):
            g_wait(p, g, ("A_ev_done", sl))
            for c in range(NCH):
                cG.inc(g.dma_start(out=p.scr_x0[c, :, ssl], in_=x0T[c][:, :]), 16)
        g_set(p, ("A_spill_x0", sl), cG)

        @blk.vector
        def _(v, sl=sl):
            VW(p, v)
            for c in range(NCH):
                if c == 0 and sl >= 1:
                    g_wait(p, v, ("A_stats", sl - 1))
                VI(p, v.tensor_copy(xb[c][:, :], x0T[c][:, :]))
                VW(p, v)
                VI(p, v.tensor_tensor(sq[c][:, :], xb[c][:, :], xb[c][:, :], op=OP.mult))
        g_set(p, ("A_xbsq", sl), p.cVc)

        @blk.tensor
        def _(t, sl=sl):
            g_wait(p, t, ("A_xbsq", sl))
            if sl >= 1:
                g_wait(p, t, ("A_var", sl - 1))
            for c in range(NCH):
                t.matmul(p.bank(2)[0:1, :], p.inv_c[:, :], xb[c][:, :],
                         start=(c == 0), stop=(c == NCH - 1))
            for c in range(NCH):
                ins = t.matmul(p.bank(3)[0:1, :], p.inv_c[:, :], sq[c][:, :],
                               start=(c == 0), stop=(c == NCH - 1))
            cT.inc(ins, 1)
        g_set(p, ("A_stats", sl), cT)

        @blk.vector
        def _(v, sl=sl):
            g_wait(p, v, ("A_stats", sl))
            VI(p, v.tensor_copy(p.srow[0][:, :], p.bank(2)[0:1, :]))
            VW(p, v)
            VI(p, v.tensor_tensor(p.srow[1][:, :], p.srow[0][:, :], p.srow[0][:, :],
                                  op=OP.mult))
            VW(p, v)
            VI(p, v.tensor_tensor(p.srow[1][:, :], p.bank(3)[0:1, :], p.srow[1][:, :],
                                  op=OP.subtract))
        g_set(p, ("A_var", sl), p.cVc)

        @blk.scalar
        def _(s, sl=sl):
            g_wait(p, s, ("A_var", sl))
            SI(p, s.activation(p.srow[1][:, :], p.srow[1][:, :], AF.Ln, bias=p.eps_col[0:1, :]))
            SW(p, s)
            SI(p, s.activation(p.srow[2][:, :], p.srow[1][:, :], AF.Exp, scale=-0.5))
        g_set(p, ("A_rstd", sl), p.cSc)

        @blk.vector
        def _(v, sl=sl):
            g_wait(p, v, ("A_rstd", sl))
            VI(p, v.scalar_tensor_tensor(p.srow[3][:, :], p.srow[0][:, :], -1.0,
                                         p.srow[2][:, :], op0=OP.mult, op1=OP.mult))
        g_set(p, ("A_mr", sl), p.cVc)

        @blk.tensor
        def _(t, sl=sl):
            g_wait(p, t, ("A_mr", sl))
            if sl >= 1:
                g_wait(p, t, ("A_apply", sl - 1))
            t.matmul(p.bank(4), p.ones_rf[:, :], p.srow[2][:, :], start=True, stop=True)
            ins = t.matmul(p.bank(5), p.ones_rf[:, :], p.srow[3][:, :], start=True, stop=True)
            cT.inc(ins, 1)
        g_set(p, ("A_planes", sl), cT)

        @blk.vector
        def _(v, sl=sl, ssl=ssl):
            g_wait(p, v, ("A_planes", sl))
            g_wait(p, v, ("A_spill_x0", sl))
            for c in range(NCH):
                VI(p, v.tensor_tensor(x0T[c][:, :], x0T[c][:, :], p.bank(4), op=OP.mult))
                VW(p, v)
                VI(p, v.tensor_tensor(x0T[c][:, :], x0T[c][:, :], p.bank(5), op=OP.add))
                VW(p, v)
                VI(p, v.tensor_copy(p.xhb[c][:, ssl], x0T[c][:, :]))
        g_set(p, ("A_apply", sl), p.cVc)

        @blk.gpsimd
        def _(g, sl=sl, ssl=ssl):
            g_wait(p, g, ("A_apply", sl))
            for c in range(NCH):
                cG.inc(g.dma_start(out=p.scr_xh[c, :, ssl], in_=x0T[c][:, :]), 16)
        g_set(p, ("A_spill_xh", sl), cG)
    g_set(p, "A_done", cG)
    p.close_scope("A")

    if "d_xhb" in p.dbg:
        @blk.sync
        def _(s):
            barrier(p, s)
            for c in range(NCH):
                p.cO.inc(s.dma_start(out=p.dbg["d_xhb"][c, :, :], in_=p.xhb[c][:, :]), 16)


def emit_phaseB(p):
    blk = p.blk
    hb = [p.sbS("B", f"hbB{i}", [128, TSL], BF16) for i in range(4)]
    vst = [p.sbS("B", f"vstB{i}", [128, C], BF16) for i in range(8)]
    rsc = p.sbS("B", "rscB", [128, TSL], BF16)
    rsc2 = p.sbS("B", "rsc2B", [64, TSL], BF16)
    cT, cV, cG = p.cT, p.cV, p.cG

    for sl in range(NSL):
        ssl = slice(sl * TSL, (sl + 1) * TSL)
        for fc in range(16):
            k = sl * 16 + fc
            par = k % 2

            @blk.tensor
            def _(t, sl=sl, fc=fc, k=k, par=par, ssl=ssl):
                if k == 0:
                    g_wait(p, t, "A_done")
                if k >= 2:
                    g_wait(p, t, ("B_rope", k - 2))
                bk = p.bank(par)
                for c in range(NCH):
                    ins = t.matmul(bk, p.w_inT[c][:, fc * 128:(fc + 1) * 128],
                                   p.xhb[c][:, ssl], start=(c == 0), stop=(c == NCH - 1))
                cT.inc(ins, 1)
            g_set(p, ("B_qk", k), cT)

            @blk.vector
            def _(v, sl=sl, fc=fc, k=k, par=par, ssl=ssl):
                g_wait(p, v, ("B_qk", k))
                if k == 0:
                    g_wait(p, v, "constsW")
                if k >= 2:
                    g_wait(p, v, ("B_hspill", k - 2))
                bk = p.bank(par)
                A, B = hb[par * 2], hb[par * 2 + 1]
                VI(p, v.tensor_copy(A[64:128, :], bk[0:64, :]))
                VI(p, v.tensor_copy(B[64:128, :], bk[64:128, :]))
                for X in (A, B):
                    VW(p, v)
                    VI(p, v.tensor_copy(rsc[0:64, :], X[64:128, :]))
                    VW(p, v)
                    VI(p, v.tensor_scalar(rsc2[0:32, :], rsc[32:64, :], -1.0, None, op0=OP.mult))
                    VI(p, v.tensor_copy(rsc2[32:64, :], rsc[0:32, :]))
                    VW(p, v)
                    VI(p, v.tensor_tensor(X[0:64, :], rsc[0:64, :], p.rp_cos[0:64, ssl], op=OP.mult))
                    VI(p, v.tensor_tensor(rsc2[0:64, :], rsc2[0:64, :], p.rp_sin[0:64, ssl], op=OP.mult))
                    VW(p, v)
                    VI(p, v.tensor_tensor(X[0:64, :], X[0:64, :], rsc2[0:64, :], op=OP.add))
            g_set(p, ("B_rope", k), p.cVc)

            @blk.gpsimd
            def _(g, sl=sl, fc=fc, k=k, par=par, ssl=ssl):
                g_wait(p, g, ("B_rope", k))
                cs = p.cD[par]
                A, B = hb[par * 2], hb[par * 2 + 1]
                tgt = p.scr_hbq if fc < 8 else p.scr_hbk
                h2 = (fc % 8) * 2
                cs.inc(g.dma_start(out=tgt[h2, :, ssl], in_=A[:, :]), 16)
                cs.inc(g.dma_start(out=tgt[h2 + 1, :, ssl], in_=B[:, :]), 16)
            g_set(p, ("B_hspill", k), p.cD[par])

        for tsub in range(4):
            kv = sl * 4 + tsub

            @blk.tensor
            def _(t, sl=sl, tsub=tsub, kv=kv, ssl=ssl):
                if kv >= 2:
                    g_wait(p, t, ("B_vev", kv - 2))
                for nn in range(2):
                    bk = p.bank(2 + (kv % 2) * 2 + nn)
                    for c in range(NCH):
                        ins = t.matmul(
                            bk,
                            p.xhb[c][:, sl * TSL + tsub * 128: sl * TSL + (tsub + 1) * 128],
                            p.w_inT[c][:, 2048 + nn * 512: 2048 + (nn + 1) * 512],
                            start=(c == 0), stop=(c == NCH - 1))
                cT.inc(ins, 1)
            g_set(p, ("B_v", kv), cT)

            @blk.vector
            def _(v, sl=sl, tsub=tsub, kv=kv):
                g_wait(p, v, ("B_v", kv))
                if kv >= 2:
                    g_wait(p, v, ("B_vspill", kv - 2))
                dst = vst[(kv % 2) * 4 + tsub]
                VI(p, v.tensor_copy(dst[:, 0:512], p.bank(2 + (kv % 2) * 2)))
                VI(p, v.tensor_copy(dst[:, 512:1024], p.bank(3 + (kv % 2) * 2)))
            g_set(p, ("B_vev", kv), p.cVc)

            @blk.gpsimd
            def _(g, sl=sl, tsub=tsub, kv=kv):
                g_wait(p, g, ("B_vev", kv))
                cs = p.cD[2 + kv % 2]
                src = vst[(kv % 2) * 4 + tsub]
                for pr in range(8):
                    cs.inc(g.dma_start(out=p.scr_v[pr, :, kv * 128:(kv + 1) * 128],
                                       in_=src[:, pr * 128:(pr + 1) * 128]), 16)
            g_set(p, ("B_vspill", kv), p.cD[2 + kv % 2])
    for i in range(4):
        g_set(p, f"B_done{i}", p.cD[i])
    g_set(p, "B_done", p.cVc)
    p.close_scope("B")
    p.close_scope("AB")

    if "d_hbq" in p.dbg:
        @blk.sync
        def _(s):
            barrier(p, s)
            for h in range(AH):
                p.cO.inc(s.dma_start(out=p.dbg["d_hbq"][h, :, :], in_=p.scr_hbq[h, :, :]), 16)
                p.cO.inc(s.dma_start(out=p.dbg["d_hbk"][h, :, :], in_=p.scr_hbk[h, :, :]), 16)
            for i in range(T // 128):
                p.cO.inc(s.dma_start(out=p.dbg["d_v"][i, :, :], in_=p.scr_v[i, :, :]), 16)


def emit_attention(p):
    blk = p.blk
    cT, cG, cA, cB = p.cT, p.cG, p.cA, p.cB
    cSL, cRX = p.cSL, p.cRX

    p.ybuf = [p.sbS("SY", f"ybuf{i}", [128, T], BF16) for i in range(8)]
    hq = [p.sbS("S", f"hqS{i}", [128, T], BF16) for i in range(4)]
    hk = [p.sbS("S", f"hkS{i}", [128, T], BF16) for i in range(4)]
    vpair = [p.sbS("S", f"vpairS{i}", [128, T], BF16) for i in range(2)]
    attb = [p.sbS("S", f"attbS{i}", [128, 1024], BF16) for i in range(4)]
    ytmp = p.sbS("S", "ytmpS", [128, TSL])
    denp = [p.sbS("S", f"denpS{i}", [128, 128]) for i in range(2)]
    den = [p.sbS("S", f"denS{i}", [128, 32]) for i in range(2)]
    invd = [p.sbS("S", f"invdS{i}", [128, 32]) for i in range(2)]
    dnstg = [p.sbS("S", f"dnstgS{i}", [32, 128]) for i in range(2)]
    invsl = [p.sbS("S", f"invslS{i}", [1, TSL]) for i in range(2)]

    g = [0]
    p.slot_pair = {}
    pair_last = {}

    gp = [0]

    def next_slot(t, pair=None, plane=False):
        if plane:
            gp[0] += 1
            ordv, ctr = gp[0], cRX
            if pair is None:
                pair = gp[0] % 2
        else:
            g[0] += 1
            ordv, ctr = g[0], cSL
            if pair is None:
                pair = g[0] % 2
        last = pair_last.get(pair)
        if last is not None:
            t.wait_ge(last[0].h, last[1])
        pair_last[pair] = (ctr, ordv)
        if not plane:
            p.slot_pair[g[0]] = pair
        return p.bank(pair * 2, 2), ordv, pair

    for hp in range(8):
        h0, h1 = 2 * hp, 2 * hp + 1

        @blk.sync
        def _(s, hp=hp, h0=h0, h1=h1):
            if hp == 0:
                for i in range(4):
                    g_wait(p, s, f"B_done{i}")
            if hp >= 2:
                g_wait(p, s, ("S_pair_done", hp - 2))
            cs = p.cE[hp % 2]
            par = (hp % 2) * 2
            cs.inc(s.dma_start(out=hq[par][:, :], in_=p.scr_hbq[h0, :, :]), 16)
            cs.inc(s.dma_start(out=hq[par + 1][:, :], in_=p.scr_hbq[h1, :, :]), 16)
            cs.inc(s.dma_start(out=hk[par][:, :], in_=p.scr_hbk[h0, :, :]), 16)
            cs.inc(s.dma_start(out=hk[par + 1][:, :], in_=p.scr_hbk[h1, :, :]), 16)
            cs.inc(s.dma_start(out=vpair[hp % 2][:, :], in_=p.scr_v[hp, :, :]), 16)
        g_set(p, ("S_ld", hp), p.cE[hp % 2])

        slot0 = {}
        for hh in range(2):
            Q = hq[(hp % 2) * 2 + hh]
            K = hk[(hp % 2) * 2 + hh]

            @blk.tensor
            def _(t, hp=hp, hh=hh, Q=Q, K=K):
                if hh == 0:
                    g_wait(p, t, ("S_ld", hp))
                    if hp > 0:
                        g_wait(p, t, ("S_ysc", (hp - 1, 7)))
                else:
                    g_wait(p, t, ("S_dnev", (hp, 0)))
                for qt in range(32):
                    for ks in range(4):
                        bk, _, _ = next_slot(t, pair=g[0] % 3)
                        if qt == 0 and ks == 0:
                            slot0[hh] = g[0]
                        for j in range(2):
                            ins = t.matmul(
                                bk[:, j * 512:(j + 1) * 512],
                                Q[64:128, qt * 128:(qt + 1) * 128],
                                K[64:128, ks * 1024 + j * 512: ks * 1024 + (j + 1) * 512],
                                start=True, stop=True)
                        cT.inc(ins, 1)
            g_set(p, ("S_pmm", (hp, hh)), cT)

            @blk.scalar
            def _(s, hp=hp, hh=hh):
                base_t = p.gates[("S_pmm", (hp, hh))][1] - 128
                g0 = slot0[hh]
                if hh == 0 and hp >= 2:
                    g_wait(p, s, ("S_invd", (hp - 2, 1)))
                for i in range(128):
                    if i % 2 == 0:
                        s.wait_ge(cT.h, base_t + min(i + 2, 128))
                    gg = g0 + i
                    bk = p.bank(p.slot_pair[gg] * 2, 2)
                    cSL.inc(s.activation(bk, bk, AF.Exp, scale=0.125,
                                         accum_out=denp[hh][:, i:i + 1]), 1)
            g_set(p, ("S_pexp", (hp, hh)), cSL)

            @blk.vector
            def _(v, hp=hp, hh=hh):
                g_wait(p, v, ("S_pexp", (hp, hh)))
                VW(p, v)
                VI(p, v.tensor_tensor(den[hh][:, :], denp[hh][:, 0:128:4],
                                      denp[hh][:, 1:128:4], op=OP.add))
                VI(p, v.tensor_tensor(invd[hh][:, :], denp[hh][:, 2:128:4],
                                      denp[hh][:, 3:128:4], op=OP.add))
                VW(p, v)
                VI(p, v.tensor_tensor(den[hh][:, :], den[hh][:, :], invd[hh][:, :], op=OP.add))
                VW(p, v)
                VI(p, v.reciprocal(invd[hh][:, :], den[hh][:, :]))
            g_set(p, ("S_invd", (hp, hh)), p.cVc)

            @blk.tensor
            def _(t, hp=hp, hh=hh):
                g_wait(p, t, ("S_invd", (hp, hh)))
                bk, _, _ = next_slot(t)
                p.dnT_slot[(hp, hh)] = g[0]
                cT.inc(t.transpose(bk[0:32, 0:128], invd[hh][:, :], p.ident_f[:, :]), 1)
            g_set(p, ("S_dnT", (hp, hh)), cT)

            @blk.vector
            def _(v, hp=hp, hh=hh):
                g_wait(p, v, ("S_dnT", (hp, hh)))
                gg = p.dnT_slot[(hp, hh)]
                bk = p.bank(p.slot_pair[gg] * 2, 2)
                cSL.inc(v.tensor_copy(dnstg[hh][:, :], bk[0:32, 0:128]), 1)
            g_set(p, ("S_dnev", (hp, hh)), cSL)

            @blk.gpsimd
            def _(gp, hp=hp, hh=hh):
                g_wait(p, gp, ("S_dnev", (hp, hh)))
                cbs = [cB, p.cD[4], p.cD[5], p.cD[0]][(hp % 2) * 2 + hh]
                cbs.inc(gp.dma_start(out=p.scr_dnb[2 * hp + hh, :, :], in_=dnstg[hh][:, :]), 16)
            g_set(p, ("S_bounceout", (hp, hh)),
                  [cB, p.cD[4], p.cD[5], p.cD[0]][(hp % 2) * 2 + hh])

        for qb in range(8):
            qsl = slice(qb * 512, (qb + 1) * 512)

            @blk.sync
            def _(s, hp=hp, qb=qb):
                if qb > 0:
                    g_wait(p, s, ("S_plane", (hp, qb - 1)))
                elif hp > 0:
                    g_wait(p, s, ("S_plane", (hp - 1, 7)))
                for hh in range(2):
                    g_wait(p, s, ("S_bounceout", (hp, hh)))
                    rowv = p.scr_dnb[2 * hp + hh].rearrange("a b -> (a b)")
                    p.cE[2].inc(s.dma_start(out=invsl[hh][:, :],
                                in_=rowv[None, qb * 512:(qb + 1) * 512]), 16)
            g_set(p, ("S_invsl", (hp, qb)), p.cE[2])

            slot_of = {}

            @blk.tensor
            def _(t, hp=hp, qb=qb, qsl=qsl):
                K0, K1 = hk[(hp % 2) * 2], hk[(hp % 2) * 2 + 1]
                Q0, Q1 = hq[(hp % 2) * 2], hq[(hp % 2) * 2 + 1]
                V = vpair[hp % 2]
                if qb == 0:
                    g_wait(p, t, ("S_dnev", (hp, 1)))
                    if hp > 0:
                        g_wait(p, t, ("S_yev", (hp - 1, 7)))
                for kt in range(33):
                    if kt < 32:
                        bk, _, _ = next_slot(t)
                        slot_of[kt] = g[0]
                        t.matmul(bk[:, 0:512], K0[0:64, kt * 128:(kt + 1) * 128],
                                 Q0[0:64, qsl], start=True, stop=True)
                        ins = t.matmul(bk[:, 512:1024], K1[0:64, kt * 128:(kt + 1) * 128],
                                       Q1[0:64, qsl], start=True, stop=True)
                        cT.inc(ins, 1)
                    if kt >= 1:
                        k2 = kt - 1
                        t.wait_ge(cSL.h, slot_of[k2])
                        if k2 == 0 and (qb >= 2 or (hp > 0 and qb < 2)):
                            prev = (hp, qb - 2) if qb >= 2 else (hp - 1, 6 + qb)
                            g_wait(p, t, ("S_yev", prev))
                        avb = p.bank(4 + qb % 2)
                        att = attb[k2 % 4]
                        t.matmul(avb[0:64, :], V[:, k2 * 128: k2 * 128 + 64],
                                 att[:, 0:512], start=(k2 == 0), stop=(k2 == 31),
                                 tile_position=(0, 0), skip_group_check=True)
                        ins = t.matmul(avb[64:128, :], V[:, k2 * 128 + 64: k2 * 128 + 128],
                                       att[:, 512:1024], start=(k2 == 0), stop=(k2 == 31),
                                       tile_position=(0, 64), skip_group_check=True)
                        cT.inc(ins, 1)
                        g_set(p, ("S_avk", (hp, qb, k2)), cT)
            g_set(p, ("S_av", (hp, qb)), cT)

            @blk.scalar
            def _(s, hp=hp, qb=qb):
                base_t = p.gates[("S_av", (hp, qb))][1] - 64
                for kt in range(32):
                    nmm = base_t + (1 if kt == 0 else 2 * kt)
                    s.wait_ge(cT.h, nmm)
                    if kt >= 4:
                        g_wait(p, s, ("S_avk", (hp, qb, kt - 4)))
                    elif qb > 0:
                        g_wait(p, s, ("S_avk", (hp, qb - 1, 28 + kt)))
                    elif hp > 0:
                        g_wait(p, s, ("S_avk", (hp - 1, 7, 28 + kt)))
                    gg = slot_of[kt]
                    bk = p.bank(p.slot_pair[gg] * 2, 2)
                    cSL.inc(s.activation(attb[kt % 4][:, :], bk, AF.Exp, scale=0.125), 1)
            g_set(p, ("S_rexp", (hp, qb)), cSL)

            @blk.tensor
            def _(t, hp=hp, qb=qb):
                g_wait(p, t, ("S_invsl", (hp, qb)))
                bk, ordv, pr = next_slot(t, plane=True)
                p.plane_slot[(hp, qb)] = (ordv, pr)
                t.matmul(bk[:, 0:512], p.ones_rf[:, :], invsl[0][:, :], start=True, stop=True)
                ins = t.matmul(bk[:, 512:1024], p.ones_rf[:, :], invsl[1][:, :],
                               start=True, stop=True)
                cT.inc(ins, 1)
            g_set(p, ("S_plane", (hp, qb)), cT)

            @blk.vector
            def _(v, hp=hp, qb=qb, qsl=qsl):
                g_wait(p, v, ("S_av", (hp, qb)))
                avb = p.bank(4 + qb % 2)
                VI(p, v.tensor_copy(ytmp[:, :], avb))
                g_set(p, ("S_yev", (hp, qb)), p.cVc)
                VW(p, v)
                g_wait(p, v, ("S_plane", (hp, qb)))
                gg, pr = p.plane_slot[(hp, qb)]
                bk = p.bank(pr * 2, 2)
                VI(p, v.tensor_tensor(p.ybuf[hp][0:64, qsl], ytmp[0:64, :],
                                      bk[0:64, 0:512], op=OP.mult))
                cRX.inc(v.tensor_tensor(p.ybuf[hp][64:128, qsl], ytmp[64:128, :],
                                        bk[64:128, 512:1024], op=OP.mult), 1)
            g_set(p, ("S_ysc", (hp, qb)), cRX)
        g_set(p, ("S_pair_done", hp), cRX)
    g_set(p, "S_done", cRX)
    g_set(p, "S_doneB", cSL)
    p.close_scope("S")

    if "d_ybuf" in p.dbg:
        @blk.sync
        def _(s):
            barrier(p, s)
            for i in range(8):
                p.cO.inc(s.dma_start(out=p.dbg["d_ybuf"][i, :, :], in_=p.ybuf[i][:, :]), 16)
    if "d_invd" in p.dbg:
        @blk.sync
        def _(s):
            barrier(p, s)
            for h in range(AH):
                p.cO.inc(s.dma_start(out=p.dbg["d_invd"][h, :, :], in_=p.scr_dnb[h, :, :]), 16)


def emit_wout_lnchain(p):
    blk = p.blk
    cT, cA, cG = p.cT, p.cA, p.cG
    p.rowA = [p.sbS("LN", "rowA_rstd", [1, 2 * TSL], side="right"),
              p.sbS("LN", "rowA_mr", [1, 2 * TSL], side="right")]
    p.rowB = [p.sbS("LN", "rowB_rstd", [1, 2 * TSL], side="right"),
              p.sbS("LN", "rowB_mr", [1, 2 * TSL], side="right")]
    p.w_outT = [p.sbS("SO", f"w_outT{c}", [128, C], BF16) for c in range(NCH)]
    wstg = [p.sbS("SO", f"wostg{i}", [128, C], BF16) for i in range(2)]
    main2 = [[p.sbS("SO", f"mainW{j}_{c}", [128, TSL]) for c in range(NCH)]
             for j in range(2)]
    aux2 = [[p.sbS("SO", f"auxW{j}_{c}", [128, TSL]) for c in range(NCH)]
            for j in range(2)]
    xb = [p.sbS("SO", f"xbW{c}", [128, TSL], BF16) for c in range(NCH)]
    sq = [p.sbS("SO", f"sqW{c}", [128, TSL], BF16) for c in range(NCH)]

    for ch in range(8):
        @blk.gpsimd
        def _(g, ch=ch):
            if ch == 0:
                g_wait(p, g, "S_done")
                g_wait(p, g, "S_doneB")
            if ch >= 2:
                g_wait(p, g, ("WO_tr", ch - 2))
            p.cD[ch % 2].inc(g.dma_start(out=wstg[ch % 2][:, :],
                                         in_=p.w_outI[ch * 128:(ch + 1) * 128, :]), 16)
        g_set(p, ("WO_ld", ch), p.cD[ch % 2])

        for c in range(NCH):
            k = ch * 8 + c

            @blk.tensor
            def _(t, ch=ch, c=c, k=k):
                if c == 0:
                    g_wait(p, t, ("WO_ld", ch))
                if k == 0:
                    g_wait(p, t, "S_done")
                    g_wait(p, t, "S_doneB")
                if k >= 2:
                    g_wait(p, t, ("WO_evc", k - 2))
                cT.inc(t.transpose(p.bslot(k), wstg[ch % 2][:, c * 128:(c + 1) * 128],
                                   p.ident_b[:, :]), 1)
            g_set(p, ("WO_trk", k), cT)

            @blk.vector
            def _(v, ch=ch, c=c, k=k):
                g_wait(p, v, ("WO_trk", k))
                VI(p, v.tensor_copy(p.w_outT[c][:, ch * 128:(ch + 1) * 128], p.bslot(k)))
            g_set(p, ("WO_evc", k), p.cVc)
        g_set(p, ("WO_tr", ch), p.cVc)
    g_set(p, "WO_done", p.cVc)

    stages = ["2", "3", "4"]
    rows_of = {"2": p.rowA, "3": p.rowB, "4": p.rowA}

    for sl in range(NSL):
        ssl = slice(sl * TSL, (sl + 1) * TSL)
        rsl = slice((sl % 2) * TSL, (sl % 2 + 1) * TSL)
        main = main2[sl % 2]
        aux = aux2[sl % 2]

        @blk.sync
        def _(s, sl=sl, ssl=ssl, main=main):
            if sl == 0:
                g_wait(p, s, "S_done")
                g_wait(p, s, "S_doneB")
            if sl >= 2:
                g_wait(p, s, ("W_x1sp", sl - 2))
                g_wait(p, s, ("W_apl", (sl - 2, "4")))
            cs = [cA, p.cE[0]][sl % 2]
            for c in range(NCH):
                cs.inc(s.dma_start(out=main[c][:, :], in_=p.scr_xh[c, :, ssl]), 16)
        g_set(p, ("W_ldm", sl), [cA, p.cE[0]][sl % 2])

        @blk.scalar
        def _(s, sl=sl, ssl=ssl, aux=aux):
            if sl >= 2:
                g_wait(p, s, ("W_fasp", sl - 2))
            cs = [p.cE[1], p.cE[2]][sl % 2]
            for c in range(NCH):
                cs.inc(s.dma_start(out=aux[c][:, :], in_=p.scr_x0[c, :, ssl]), 16)
        g_set(p, ("W_ldx", sl), [p.cE[1], p.cE[2]][sl % 2])

        for oc in range(NCH):
            k = sl * 8 + oc

            @blk.tensor
            def _(t, sl=sl, oc=oc, k=k, ssl=ssl):
                if sl == 0 and oc == 0:
                    g_wait(p, t, "WO_done")
                if k >= 2:
                    g_wait(p, t, ("W_hev", k - 2))
                bk = p.bank(k % 2)
                for c in range(NCH):
                    ins = t.matmul(bk, p.w_outT[c][:, oc * 128:(oc + 1) * 128],
                                   p.ybuf[c][:, ssl], start=(c == 0), stop=(c == NCH - 1))
                cT.inc(ins, 1)
            g_set(p, ("W_mm", k), cT)

            @blk.vector
            def _(v, sl=sl, oc=oc, k=k, main=main):
                g_wait(p, v, ("W_mm", k))
                if oc == 0:
                    g_wait(p, v, ("W_ldm", sl))
                VI(p, v.tensor_tensor(main[oc][:, :], p.bank(k % 2), main[oc][:, :],
                                      op=OP.add))
            g_set(p, ("W_hev", k), p.cVc)

        for si, st in enumerate(stages):
            @blk.vector
            def _(v, sl=sl, st=st, si=si, main=main):
                VW(p, v)
                prev = (sl, stages[si - 1]) if si > 0 else \
                    ((sl - 1, "4") if sl > 0 else None)
                if prev is not None:
                    g_wait(p, v, ("W_st", prev))
                for c in range(NCH):
                    VI(p, v.tensor_copy(xb[c][:, :], main[c][:, :]))
                    VW(p, v)
                    VI(p, v.tensor_tensor(sq[c][:, :], xb[c][:, :], xb[c][:, :],
                                          op=OP.mult))
            g_set(p, ("W_xb", (sl, st)), p.cVc)

            @blk.tensor
            def _(t, sl=sl, st=st, si=si):
                g_wait(p, t, ("W_xb", (sl, st)))
                prev = (sl, stages[si - 1]) if si > 0 else \
                    ((sl - 1, "4") if sl > 0 else None)
                if prev is not None:
                    g_wait(p, t, (("W" + prev[1], prev[0]), "rows"))
                for c in range(NCH):
                    t.matmul(p.bank(2)[0:1, :], p.inv_c[:, :], xb[c][:, :],
                             start=(c == 0), stop=(c == NCH - 1))
                for c in range(NCH):
                    ins = t.matmul(p.bank(3)[0:1, :], p.inv_c[:, :], sq[c][:, :],
                                   start=(c == 0), stop=(c == NCH - 1))
                cT.inc(ins, 1)
            g_set(p, ("W_st", (sl, st)), cT)
            g_set(p, ("W" + st + "_stats", sl), cT)
            extra = []
            if st == "4":
                extra.append(("W_pl", (sl, "2")))
            if sl >= 2:
                extra.append(("W_pl", (sl - 2, "4" if st == "2" else st)))
            emit_row_finish(p, ("W" + st, sl), rows_of[st], rsl, extra_scalar=extra)

            @blk.tensor
            def _(t, sl=sl, st=st, si=si, ssl=ssl):
                g_wait(p, t, (("W" + st, sl), "rows"))
                prevu = (sl, stages[si - 1]) if si > 0 else \
                    ((sl - 1, "4") if sl > 0 else None)
                if prevu is not None:
                    g_wait(p, t, ("W_apl", prevu))
                rows = rows_of[st]
                rsl = slice((sl % 2) * TSL, (sl % 2 + 1) * TSL)
                t.matmul(p.bank(4), p.ones_rf[:, :], rows[0][0:1, rsl],
                         start=True, stop=True)
                ins = t.matmul(p.bank(5), p.ones_rf[:, :], rows[1][0:1, rsl],
                               start=True, stop=True)
                cT.inc(ins, 1)
            g_set(p, ("W_pl", (sl, st)), cT)

            @blk.vector
            def _(v, sl=sl, st=st, main=main, aux=aux):
                g_wait(p, v, ("W_pl", (sl, st)))
                if st == "3":
                    g_wait(p, v, ("W_ldx", sl))
                for c in range(NCH):
                    if st == "4":
                        VI(p, v.tensor_tensor(aux[c][:, :], main[c][:, :], p.bank(4),
                                              op=OP.mult))
                        VW(p, v)
                        VI(p, v.tensor_tensor(aux[c][:, :], aux[c][:, :], p.bank(5),
                                              op=OP.add))
                        continue
                    VI(p, v.tensor_tensor(main[c][:, :], main[c][:, :], p.bank(4),
                                          op=OP.mult))
                    VW(p, v)
                    VI(p, v.tensor_tensor(main[c][:, :], main[c][:, :], p.bank(5),
                                          op=OP.add))
                    VW(p, v)
                    if st == "2":
                        VI(p, v.tensor_scalar(main[c][:, :], main[c][:, :],
                                              p.gam[:, c:c + 1], p.bet[:, c:c + 1],
                                              op0=OP.mult, op1=OP.add))
                    else:
                        VI(p, v.tensor_tensor(main[c][:, :], main[c][:, :],
                                              aux[c][:, :], op=OP.add))
            g_set(p, ("W_apl", (sl, st)), p.cVc)

            if st == "3":
                @blk.gpsimd
                def _(g, sl=sl, ssl=ssl, main=main):
                    g_wait(p, g, ("W_apl", (sl, "3")))
                    cs = [p.cD[2], p.cD[3]][sl % 2]
                    for c in range(NCH):
                        cs.inc(g.dma_start(out=p.scr_x1[c, :, ssl],
                                           in_=main[c][:, :]), 16)
                g_set(p, ("W_x1sp", sl), [p.cD[2], p.cD[3]][sl % 2])

        @blk.gpsimd
        def _(g, sl=sl, ssl=ssl, aux=aux):
            g_wait(p, g, ("W_apl", (sl, "4")))
            cs = [cG, p.cD[1]][sl % 2]
            for c in range(NCH):
                cs.inc(g.dma_start(out=p.scr_fA[c, :, ssl], in_=aux[c][:, :]), 16)
        g_set(p, ("W_fasp", sl), [cG, p.cD[1]][sl % 2])

    g_set(p, "LC_done", cG)
    g_set(p, "LC_done_odd", p.cD[1])
    g_set(p, "X1_done_e", p.cD[2])
    g_set(p, "X1_done_o", p.cD[3])
    p.close_scope("SO")
    p.close_scope("SY")
    p.close_scope("LN")

    if "d_xf" in p.dbg:
        @blk.sync
        def _(s):
            barrier(p, s)
            for c in range(NCH):
                p.cO.inc(s.dma_start(out=p.dbg["d_xf"][c, :, :], in_=p.scr_fA[c, :, :]), 16)
            for c in range(NCH):
                p.cO.inc(s.dma_start(out=p.dbg["d_x1"][c, :, :], in_=p.scr_x1[c, :, :]), 16)
        g_set(p, "DXF_dump", p.cO)


def emit_row_finish(p, tag, rows, ssl, extra_scalar=()):
    blk = p.blk

    @blk.vector
    def _(v):
        g_wait(p, v, (tag[0] + "_stats", tag[1]) if isinstance(tag, tuple) else tag)
        VI(p, v.tensor_copy(p.srow[0][:, :], p.bank(2)[0:1, :]))
        VW(p, v)
        VI(p, v.tensor_tensor(p.srow[1][:, :], p.srow[0][:, :], p.srow[0][:, :], op=OP.mult))
        VW(p, v)
        VI(p, v.tensor_tensor(p.srow[1][:, :], p.bank(3)[0:1, :], p.srow[1][:, :],
                              op=OP.subtract))
    g_set(p, (tag, "var"), p.cVc)

    @blk.scalar
    def _(s):
        g_wait(p, s, (tag, "var"))
        for eg in extra_scalar:
            g_wait(p, s, eg)
        SI(p, s.activation(p.srow[1][:, :], p.srow[1][:, :], AF.Ln, bias=p.eps_col[0:1, :]))
        SW(p, s)
        SI(p, s.activation(rows[0][0:1, ssl], p.srow[1][:, :], AF.Exp, scale=-0.5))
    g_set(p, (tag, "rstd"), p.cSc)

    @blk.vector
    def _(v):
        g_wait(p, v, (tag, "rstd"))
        VI(p, v.scalar_tensor_tensor(rows[1][0:1, ssl], p.srow[0][:, :], -1.0,
                                     rows[0][0:1, ssl], op0=OP.mult, op1=OP.mult))
    g_set(p, (tag, "rows"), p.cVc)


def emit_ffn_layer(p, l):
    blk = p.blk
    cT, cA, cG, cB = p.cT, p.cA, p.cG, p.cB
    L = f"L{l}_"
    vis_k = lambda w: list(range(max(0, 4 * w - 4), 4 * w + 4))

    qT = [p.sbS(L + "B1", L + f"qT{i}", [128, T], BF16) for i in range(4)]
    kT = [p.sbS(L + "B1", L + f"kT{i}", [128, T], BF16) for i in range(4)]
    vtm = [p.sbS(L + "B1", L + f"vtm{i}", [128, 520], BF16) for i in range(32)]

    wq = [p.sbS(L + "A", L + f"wq{c}", [128, 1536], BF16) for c in range(NCH)]
    fin13 = [p.sbS(L + "A", L + f"fin13{c}", [128, 513]) for c in range(NCH)]
    finb2 = [[p.sbS(L + "A", L + f"finb{j}_{c}", [128, 513], BF16) for c in range(NCH)]
             for j in range(2)]
    xps = [p.sbS(L + "A", L + f"xps{i}", [64, TSL], BF16) for i in range(4)]
    l2a8 = [p.sbS(L + "A", L + f"l2a{i}", [64, TSL], BF16) for i in range(8)]
    sqf = [p.sbS(L + "A", L + f"sqf{i}", [128, TSL], BF16) for i in range(4)]
    srw = [p.sbS(L + "A", L + f"srw{i}", [2, TSL]) for i in range(4)]
    l2r = p.sbS(L + "A", L + "l2r", [64, TSL], BF16)
    l2t = p.sbS(L + "A", L + "l2t", [64, TSL], BF16)
    qscol = p.sbS(L + "A", L + "qscol", [128, 1])
    kscol = p.sbS(L + "A", L + "kscol", [128, 1])
    prev_done = "LC_done" if l == 0 else "L0_F4_done"

    @blk.sync
    def _(s):
        g_wait(p, s, prev_done)
        if l == 0:
            g_wait(p, s, "LC_done_odd")
            g_wait(p, s, "X1_done_e")
            g_wait(p, s, "X1_done_o")
        p.cE[3].inc(s.dma_start(out=qscol[0:64, :], in_=p.qscI[l][:, :]), 16)
        p.cE[3].inc(s.dma_start(out=qscol[64:128, :], in_=p.qscI[l][:, :]), 16)
        p.cE[3].inc(s.dma_start(out=kscol[0:64, :], in_=p.kscI[l][:, :]), 16)
        p.cE[3].inc(s.dma_start(out=kscol[64:128, :], in_=p.kscI[l][:, :]), 16)
    g_set(p, (L, "qsld"), p.cE[3])

    @blk.gpsimd
    def _(g):
        g_wait(p, g, prev_done)
        if l == 0:
            g_wait(p, g, "LC_done_odd")
            g_wait(p, g, "X1_done_e")
            g_wait(p, g, "X1_done_o")
        else:
            g_wait(p, g, "L0_F4_done_d1")
        for c in range(NCH):
            p.cD[1].inc(g.dma_start(out=wq[c][:, :],
                                    in_=p.wqkvI[l][c * 128:(c + 1) * 128, :]), 16)
        for i in range(32):
            ins = g.memset(vtm[i][:, :], 1.0)
        for j in range(2):
            for c in range(NCH):
                ins = g.memset(finb2[j][c][:, 0:1], 0.0)
        for c in range(NCH):
            ins = g.memset(fin13[c][:, 0:1], 0.0)
        p.cZ.inc(ins, 1)
    g_set(p, (L, "wqld"), p.cD[1])
    g_set(p, (L, "vtmz"), p.cZ)

    for sl in range(NSL):
        ssl = slice(sl * TSL, (sl + 1) * TSL)

        finb = finb2[sl % 2]

        @blk.sync
        def _(s, sl=sl):
            if sl == 0:
                g_wait(p, s, prev_done)
            if sl >= 1:
                g_wait(p, s, (L + "F1_cp", sl - 1))
            for c in range(NCH):
                if sl == 0:
                    p.cA.inc(s.dma_start(out=fin13[c][:, 1:513],
                                         in_=p.scr_fA[c, :, 0:512]), 16)
                else:
                    p.cA.inc(s.dma_start(out=fin13[c][:, 0:513],
                                         in_=p.scr_fA[c, :, sl * 512 - 1: (sl + 1) * 512]), 16)
        g_set(p, (L + "F1_ldf", sl), p.cA)

        @blk.vector
        def _(v, sl=sl, finb=finb):
            g_wait(p, v, (L + "F1_ldf", sl))
            if sl == 0:
                g_wait(p, v, (L, "vtmz"))
            if sl >= 2:
                g_wait(p, v, (L + "F1_vdone", sl - 2))
            for c in range(NCH):
                VI(p, v.tensor_copy(finb[c][:, :], fin13[c][:, :]))
        g_set(p, (L + "F1_cp", sl), p.cVc)
        g_set(p, (L + "F1_ld", sl), p.cVc)

        @blk.sync
        def _(s, sl=sl):
            if sl >= 1:
                g_wait(p, s, (L + "F1_xpos", sl * 16 - 1))
            for i, nm in enumerate(["xp_qc", "xp_qs", "xp_kc", "xp_ks"]):
                p.cE[1].inc(s.dma_start(out=xps[i][:, :],
                                        in_=p.xpI[nm][:, sl * TSL:(sl + 1) * TSL]), 16)
        g_set(p, (L + "F1_xpld", sl), p.cE[1])

        def rhsv(c, w=TSL, t0=0):
            off = (1 if c < 4 else 0) + t0
            return finb[c][:, off:off + w]

        for fc in range(8):
            @blk.tensor
            def _(t, sl=sl, fc=fc):
                k = sl * 8 + fc
                if k == 0:
                    g_wait(p, t, (L, "wqld"))
                    g_wait(p, t, "LC_done" if l == 0 else "L0_F4_done")
                g_wait(p, t, (L + "F1_ld", sl))
                if k >= 2:
                    g_wait(p, t, (L + "F1_qkev", k - 2))
                bk = p.bank(fc % 2)
                for c in range(NCH):
                    ins = t.matmul(bk, wq[c][:, fc * 128:(fc + 1) * 128],
                                   rhsv(c), start=(c == 0), stop=(c == NCH - 1))
                cT.inc(ins, 1)
            g_set(p, (L + "F1_qk", sl * 8 + fc), cT)

            @blk.vector
            def _(v, sl=sl, fc=fc, ssl=ssl):
                k = sl * 8 + fc
                g_wait(p, v, (L + "F1_qk", k))
                dst = qT[fc] if fc < 4 else kT[fc - 4]
                VI(p, v.tensor_copy(dst[:, ssl], p.bank(fc % 2)))
            g_set(p, (L + "F1_qkev", sl * 8 + fc), p.cVc)

        for tsub in range(4):
            @blk.tensor
            def _(t, sl=sl, tsub=tsub):
                kv = sl * 4 + tsub
                if kv >= 2:
                    g_wait(p, t, (L + "F1_vev", kv - 2))
                bk = p.bank(2 + kv % 2)
                for c in range(NCH):
                    ins = t.matmul(bk, rhsv(c, 128, tsub * 128),
                                   wq[c][:, 1024:1536], start=(c == 0), stop=(c == NCH - 1))
                cT.inc(ins, 1)
            g_set(p, (L + "F1_v", sl * 4 + tsub), cT)

            @blk.vector
            def _(v, sl=sl, tsub=tsub):
                kv = sl * 4 + tsub
                g_wait(p, v, (L + "F1_v", kv))
                bk = p.bank(2 + kv % 2)
                for h in range(FH):
                    VI(p, v.tensor_copy(vtm[kv][:, h * 65: h * 65 + 64],
                                        bk[:, h * 64:(h + 1) * 64]))
            g_set(p, (L + "F1_vev", sl * 4 + tsub), p.cVc)

        for wg in range(2):
            prs = [wg * 4 + j for j in range(4)]

            @blk.vector
            def _(v, sl=sl, wg=wg, prs=prs, ssl=ssl):
                if wg == 0:
                    g_wait(p, v, (L + "F1_qkev", sl * 8 + 7))
                    g_wait(p, v, (L + "F1_xpld", sl))
                    if sl == 0:
                        g_wait(p, v, (L, "qsld"))
                for j, pr in enumerate(prs):
                    src = qT[pr] if pr < 4 else kT[pr - 4]
                    scol = qscol if pr < 4 else kscol
                    if sl > 0 or wg > 0:
                        pw = (sl, wg - 1) if wg > 0 else (sl - 1, 1)
                        g_wait(p, v, (L + "F1_stmm", (pw[0], pw[1], j)))
                    VW(p, v)
                    VI(p, v.tensor_scalar(l2a8[2 * j][:, :], src[0:64, ssl],
                                          scol[0:64, :], None, op0=OP.mult))
                    VI(p, v.tensor_scalar(l2a8[2 * j + 1][:, :], src[64:128, ssl],
                                          scol[64:128, :], None, op0=OP.mult))
                    VW(p, v)
                    VI(p, v.tensor_tensor(sqf[j][0:64, :], l2a8[2 * j][:, :],
                                          l2a8[2 * j][:, :], op=OP.mult))
                    VI(p, v.tensor_tensor(sqf[j][64:128, :], l2a8[2 * j + 1][:, :],
                                          l2a8[2 * j + 1][:, :], op=OP.mult))
            g_set(p, (L + "F1_sqw", (sl, wg)), p.cVc)

            @blk.tensor
            def _(t, sl=sl, wg=wg, prs=prs):
                g_wait(p, t, (L + "F1_sqw", (sl, wg)))
                if l == 0 and sl == 0 and wg == 0:
                    g_wait(p, t, "consts")
                    g_wait(p, t, "constsW")
                for j, pr in enumerate(prs):
                    if sl > 0 or wg > 0:
                        pw = (sl, wg - 1) if wg > 0 else (sl - 1, 1)
                        g_wait(p, t, (L + "F1_xposw", pw))
                    cT.inc(t.matmul(p.bank(2 + j)[0:2, :], p.sel2[:, :],
                                    sqf[j][:, :], start=True, stop=True), 1)
                    g_set(p, (L + "F1_stmm", (sl, wg, j)), cT)

            @blk.scalar
            def _(s, sl=sl, wg=wg, prs=prs):
                for j, pr in enumerate(prs):
                    g_wait(p, s, (L + "F1_stmm", (sl, wg, j)))
                    if sl > 0 or wg > 0:
                        pw = (sl, wg - 1) if wg > 0 else (sl - 1, 1)
                        g_wait(p, s, (L + "F1_plw", pw))
                    SW(p, s)
                    SI(p, s.activation(srw[j][:, :], p.bank(2 + j)[0:2, :], AF.Ln,
                                       bias=p.epsl2_col[0:2, :]))
                    SW(p, s)
                    SI(p, s.activation(srw[j][:, :], srw[j][:, :], AF.Exp, scale=-0.5))
            g_set(p, (L + "F1_rstdw", (sl, wg)), p.cSc)

            @blk.tensor
            def _(t, sl=sl, wg=wg, prs=prs):
                g_wait(p, t, (L + "F1_rstdw", (sl, wg)))
                for j, pr in enumerate(prs):
                    cT.inc(t.matmul(p.bank(2 + j), p.brd2[:, :], srw[j][:, :],
                                    start=True, stop=True), 1)
            g_set(p, (L + "F1_plw", (sl, wg)), cT)

            @blk.vector
            def _(v, sl=sl, wg=wg, prs=prs, ssl=ssl):
                g_wait(p, v, (L + "F1_plw", (sl, wg)))
                for j, pr in enumerate(prs):
                    src = qT[pr] if pr < 4 else kT[pr - 4]
                    xc, xs = (xps[0], xps[1]) if pr < 4 else (xps[2], xps[3])
                    bk = p.bank(2 + j)
                    for h in range(2):
                        a = l2a8[2 * j + h]
                        r0 = h * 64
                        VW(p, v)
                        VI(p, v.tensor_scalar(l2r[0:32, :], a[32:64, :], -1.0, None,
                                              op0=OP.mult))
                        VI(p, v.tensor_copy(l2r[32:64, :], a[0:32, :]))
                        VW(p, v)
                        VI(p, v.tensor_tensor(l2t[:, :], a[:, :], xc[:, :], op=OP.mult))
                        VI(p, v.tensor_tensor(l2r[:, :], l2r[:, :], xs[:, :], op=OP.mult))
                        VW(p, v)
                        VI(p, v.tensor_tensor(l2t[:, :], l2t[:, :], l2r[:, :], op=OP.add))
                        VW(p, v)
                        VI(p, v.tensor_tensor(src[r0:r0 + 64, ssl], l2t[:, :],
                                              bk[r0:r0 + 64, :], op=OP.mult))
            g_set(p, (L + "F1_xposw", (sl, wg)), p.cVc)
        g_set(p, (L + "F1_xpos", sl * 16 + 15), p.cVc)
        @blk.vector
        def _(v, sl=sl):
            VW(p, v)
            VI(p, v.tensor_copy(l2t[0:1, 0:1], l2t[0:1, 0:1]))
        g_set(p, (L + "F1_vdone", sl), p.cVc)
    g_set(p, L + "F1_done", p.cVc)
    p.close_scope(L + "A")

    attf = [p.sbS(L + "C", L + f"attf{i}", [128, TSL], BF16) for i in range(4)]
    p.lmask_sb = [p.sbS(L + "C", L + f"lmask{i}", [128, TSL], BF16) for i in range(8)]
    negB = [p.sbS(L + "C", L + f"negB{i}", [1, T], BF16) for i in range(2)]
    Bbuf = [p.sbS(L + "C", L + f"Bbuf{i}", [128, 32]) for i in range(2)]
    bstg = [p.sbS(L + "C", L + f"bstg{i}", [32, 128]) for i in range(2)]
    yt65 = p.sbS(L + "C", L + "yt65", [128, TSL])
    invloc = p.sbS(L + "C", L + "invloc", [1, TSL])
    ostg = [p.sbS(L + "C", L + f"ostg{i}", [64, TSL], BF16) for i in range(2)]

    @blk.sync
    def _(s):
        g_wait(p, s, L + "F1_done")
        for i in range(8):
            p.cE[3].inc(s.dma_start(out=p.lmask_sb[i][:, :], in_=p.lmaskI[i, :, :]), 16)
    g_set(p, (L, "lmaskld"), p.cE[3])

    g2 = [p.cSL.n]
    f2slot = {}
    f2avval = {}
    Bbuf2 = [p.sbS(L + "C", L + f"Bbuf2_{i}", [128, 64]) for i in range(2)]

    F2B = [2, 3, 0]

    def nslot2(t):
        g2[0] += 1
        t.wait_ge(p.cSL.h, g2[0] - 3)
        return p.bank(F2B[g2[0] % 3])

    gp2 = [p.cRX.n]

    def nslotP(t):
        gp2[0] += 1
        if gp2[0] - 1 > 0:
            t.wait_ge(p.cRX.h, gp2[0] - 1)
        return p.bank(1)

    def run_P1(h):
        Qs, Ks, r0 = qT[h // 2], kT[h // 2], (h % 2) * 64
        for w in range(8):
            kk = vis_k(w)
            nb = len(kk) // 4
            slots_w = {}

            @blk.tensor
            def _(t, h=h, w=w, kk=kk, nb=nb, Qs=Qs, Ks=Ks, r0=r0, slots_w=slots_w):
                if h == 0 and w == 0:
                    g_wait(p, t, L + "F1_done")
                for qt in range(4):
                    for j in range(nb):
                        bk = nslot2(t)
                        slots_w[(qt, j)] = g2[0]
                        cT.inc(t.matmul(
                            bk,
                            Qs[r0:r0 + 64, w * 512 + qt * 128: w * 512 + (qt + 1) * 128],
                            Ks[r0:r0 + 64, kk[0] * 128 + j * 512: kk[0] * 128 + (j + 1) * 512],
                            start=True, stop=True), 1)
                        g_set(p, (L + "F2_p1mm", (h, w, qt, j)), cT)

            @blk.vector
            def _(v, h=h, w=w, nb=nb, slots_w=slots_w):
                for qt in range(4):
                    for j in range(nb):
                        g_wait(p, v, (L + "F2_p1mm", (h, w, qt, j)))
                        if w == 0 and qt == 0 and j == 0 and h >= 2:
                            g_wait(p, v, (L + "F2_bev", h - 2))
                        gg = slots_w[(qt, j)]
                        bk = p.bank(F2B[gg % 3])
                        dstcol = Bbuf2[h % 2][:, (w * 4 + qt) * 2 + j: (w * 4 + qt) * 2 + j + 1]
                        p.cSL.inc(v.reduce_max(dstcol, bk, axis=AX.X), 1)
                    if nb == 1:
                        v.wait_ge(p.cSL.h, p.cSL.n)
                        VI(p, v.tensor_copy(
                            Bbuf2[h % 2][:, (w * 4 + qt) * 2 + 1: (w * 4 + qt) * 2 + 2],
                            Bbuf2[h % 2][:, (w * 4 + qt) * 2: (w * 4 + qt) * 2 + 1]))
            g_set(p, (L + "F2_maxw", (h, w)), p.cSL)

    def emit_bfinish(h):
        @blk.vector
        def _(v, h=h):
            g_wait(p, v, (L + "F2_maxw", (h, 7)))
            VW(p, v)
            VI(p, v.tensor_tensor(Bbuf[h % 2][:, :], Bbuf2[h % 2][:, 0:64:2],
                                  Bbuf2[h % 2][:, 1:64:2], op=OP.max))
            VW(p, v)
            VI(p, v.tensor_scalar(Bbuf[h % 2][:, :], Bbuf[h % 2][:, :], -1.0, None,
                                  op0=OP.mult))
        g_set(p, (L + "F2_bneg", h), p.cVc)

        @blk.tensor
        def _(t, h=h):
            g_wait(p, t, (L + "F2_bneg", h))
            bk = nslotP(t)
            cT.inc(t.transpose(bk[0:32, 0:128], Bbuf[h % 2][:, :], p.ident_f[:, :]), 1)
        g_set(p, (L + "F2_bT", h), cT)

        @blk.vector
        def _(v, h=h):
            g_wait(p, v, (L + "F2_bT", h))
            bk = p.bank(1)
            p.cRX.inc(v.tensor_copy(bstg[h % 2][:, :], bk[0:32, 0:128]), 1)
        g_set(p, (L + "F2_bev", h), p.cRX)

        @blk.gpsimd
        def _(g, h=h):
            g_wait(p, g, (L + "F2_bev", h))
            cc = p.cD[2 + h % 2]
            cc.inc(g.dma_start(out=p.scr_bb[h, :, :], in_=bstg[h % 2][:, :]), 16)
            g.wait_ge(cc.h, cc.n)
            cc.inc(g.dma_start(
                out=negB[h % 2][0:1, :],
                in_=p.scr_bb[h].rearrange("a b -> (a b)")[None, :]), 16)
        g_set(p, (L + "F2_brow", h), p.cD[2 + h % 2])

    def emit_P2(h):
        Qs, Ks, r0 = qT[h // 2], kT[h // 2], (h % 2) * 64
        for w in range(8):
            kk = vis_k(w)
            nk = len(kk)

            @blk.tensor
            def _(t, h=h, w=w, kk=kk, nk=nk, Qs=Qs, Ks=Ks, r0=r0):
                if w == 0:
                    g_wait(p, t, (L + "F2_brow", h))
                    if h == 0:
                        g_wait(p, t, (L, "lmaskld"))
                    if h + 2 < FH:
                        g_wait(p, t, (L + "F2_maxw", (h + 2, 7)))
                for ki in range(nk + 1):
                    if ki < nk:
                        ktg = kk[ki]
                        bk = nslot2(t)
                        f2slot[(h, w, ki)] = g2[0]
                        t.matmul(bk, Ks[r0:r0 + 64, ktg * 128:(ktg + 1) * 128],
                                 Qs[r0:r0 + 64, w * 512:(w + 1) * 512],
                                 start=True, stop=False)
                        t.matmul(bk, p.ones_rb[0:1, :],
                                 negB[h % 2][0:1, w * 512:(w + 1) * 512],
                                 start=False, stop=False)
                        ridx = (ktg * 128 - w * 512 + 512) // 128
                        ins = t.matmul(bk, p.ident_b[:, :], p.lmask_sb[ridx][:, :],
                                       start=False, stop=True)
                        cT.inc(ins, 1)
                        g_set(p, (L + "F2_p2mm", (h, w, ki)), cT)
                    if ki >= 1:
                        k2 = ki - 1
                        t.wait_ge(p.cSL.h, f2slot[(h, w, k2)])
                        if k2 == 0 and not (h == 0 and w <= 1):
                            pw = (h, w - 2) if w >= 2 else (h - 1, 6 + w)
                            g_wait(p, t, (L + "F2_yev", pw))
                        avb = p.bank(4 + w % 2)
                        ins = t.matmul(avb[0:65, :], vtm[kk[k2]][:, h * 65: h * 65 + 65],
                                       attf[f2slot[(h, w, k2)] % 4][:, :],
                                       start=(k2 == 0), stop=(k2 == nk - 1))
                        cT.inc(ins, 1)
                        g_set(p, (L + "F2_avk", (h, w, k2)), cT)
            g_set(p, (L + "F2_av", (h, w)), cT)

            @blk.scalar
            def _(s, h=h, w=w, nk=nk):
                for ki in range(nk):
                    gg = f2slot[(h, w, ki)]
                    g_wait(p, s, (L + "F2_p2mm", (h, w, ki)))
                    pk = (h, w, ki - 4)
                    if ki < 4:
                        if w > 0:
                            nkp = len(vis_k(w - 1))
                            pk = (h, w - 1, nkp - 4 + ki)
                        elif h > 0:
                            pk = (h - 1, 7, 4 + ki)
                        else:
                            pk = None
                    if pk is not None:
                        g_wait(p, s, (L + "F2_avk", pk))
                    bk = p.bank(F2B[gg % 3])
                    p.cSL.inc(s.activation(attf[gg % 4][:, :], bk, AF.Exp, scale=8.0), 1)
            g_set(p, (L + "F2_exp", (h, w)), p.cSL)

            @blk.vector
            def _(v, h=h, w=w):
                g_wait(p, v, (L + "F2_av", (h, w)))
                if w > 0:
                    g_wait(p, v, (L + "F2_ysc", (h, w - 1)))
                elif h > 0:
                    g_wait(p, v, (L + "F2_ysc", (h - 1, 7)))
                avb = p.bank(4 + w % 2)
                VI(p, v.tensor_copy(yt65[0:65, :], avb[0:65, :]))
                g_set(p, (L + "F2_yev", (h, w)), p.cVc)
                VW(p, v)
                VI(p, v.reciprocal(invloc[:, :], yt65[64:65, :]))
            g_set(p, (L + "F2_invl", (h, w)), p.cVc)

            @blk.tensor
            def _(t, h=h, w=w):
                g_wait(p, t, (L + "F2_invl", (h, w)))
                bk = nslotP(t)
                cT.inc(t.matmul(bk, p.ones_rf[:, :], invloc[:, :], start=True, stop=True), 1)
            g_set(p, (L + "F2_ypl", (h, w)), cT)

            @blk.vector
            def _(v, h=h, w=w, r0=r0):
                g_wait(p, v, (L + "F2_ypl", (h, w)))
                nw = h * 8 + w
                if nw >= 2:
                    g_wait(p, v, (L + "F2_osp", ((nw - 2) // 8, (nw - 2) % 8)))
                bk = p.bank(1)
                p.cRX.inc(v.tensor_tensor(ostg[nw % 2][:, :],
                                          yt65[0:64, :], bk[0:64, :], op=OP.mult), 1)
            g_set(p, (L + "F2_ysc", (h, w)), p.cRX)

            @blk.gpsimd
            def _(g, h=h, w=w, r0=r0):
                g_wait(p, g, (L + "F2_ysc", (h, w)))
                nw = h * 8 + w
                cs = p.cD[nw % 2]
                cs.inc(g.dma_start(
                    out=p.scr_v[h // 2, r0:r0 + 64, w * 512:(w + 1) * 512],
                    in_=ostg[nw % 2][:, :]), 16)
            g_set(p, (L + "F2_osp", (h, w)), p.cD[nw % 2] if False else p.cD[(h * 8 + w) % 2])

    p.f2_yslot = {}
    p.f2_bslot = {}
    run_P1(0)
    run_P1(1)
    for h in range(FH):
        emit_bfinish(h)
        if h + 2 < FH:
            run_P1(h + 2)
        emit_P2(h)
    g_set(p, L + "F2_done", p.cSL)

    @blk.gpsimd
    def _(g):
        for h in range(FH):
            for w in range(8):
                g_wait(p, g, (L + "F2_osp", (h, w)))
        p.cD[0].inc(g.dma_start(out=bstg[0][0:1, 0:1].rearrange("a b -> (a b)")[None, :]
                                if False else p.scr_bb[0, 0:1, 0:1], in_=bstg[0][0:1, 0:1]), 16)
    g_set(p, (L, "oTspill"), p.cD[0])
    p.close_scope(L + "C")
    p.close_scope(L + "B1")

    if l == 1:
        p.row5 = [p.sbS("LN5", "row5_rstd", [1, 2 * TSL], side="right"),
                  p.sbS("LN5", "row5_mr", [1, 2 * TSL], side="right")]
    p.rowF = [p.sbS(L + "LN", L + "rowF_rstd", [1, 2 * TSL], side="right"),
              p.sbS(L + "LN", L + "rowF_mr", [1, 2 * TSL], side="right")]
    hb16 = [p.sbS(L + "D4", L + f"hb16{c}", [128, TSL], BF16) for c in range(NCH)]
    agb = [p.sbS(L + "D4", L + f"agb{i}", [128, TSL], BF16) for i in range(22)]
    gt = p.sbS(L + "D4", L + "gt", [128, TSL], BF16)
    at = p.sbS(L + "D4", L + "at", [128, TSL])
    fm13 = [p.sbS(L + "D4", L + f"fm13{c}", [128, 513]) for c in range(NCH)]
    w1sb = [p.sbS(L + "D4", L + f"w1sb{i}", [128, 8 * 128], BF16) for i in range(4)]
    w2res = [p.sbS(L + "D4", L + f"w2res{i}", [128, 22 * 128], BF16) for i in range(NCH)]
    b1sb = p.sbS(L + "D4", L + "b1sb", [128, 44])
    b2sb = p.sbS(L + "D4", L + "b2sb", [128, NCH])
    lgsb = p.sbS(L + "D4", L + "lgsb", [128, NCH])
    lbsb = p.sbS(L + "D4", L + "lbsb", [128, NCH])
    xb5 = [p.sbS(L + "D4", L + f"xb5{c}", [128, TSL], BF16) for c in range(NCH)]
    sq5 = [p.sbS(L + "D4", L + f"sq5{c}", [128, TSL], BF16) for c in range(NCH)]
    wo_sb = [p.sbS(L + "D3", L + f"wo{i}", [128, C], BF16) for i in range(4)]
    oTs2 = [p.sbS(L + "D3", L + f"oTs{i}", [128, 4 * TSL], BF16) for i in range(1)]
    finD2 = [[p.sbS(L + "D3", L + f"finD{j}_{c}", [128, TSL]) for c in range(NCH)]
             for j in range(1)]
    carry = [p.sbS(L + "D3", L + f"carry{c}", [128, 1]) for c in range(NCH)]
    carryF = [p.sbS(L + "D3", L + f"carryF{c}", [128, 1]) for c in range(NCH)]
    xbD = [p.sbS(L + "D3", L + f"xbD{c}", [128, TSL], BF16) for c in range(NCH)]
    sqD = [p.sbS(L + "D3", L + f"sqD{c}", [128, TSL], BF16) for c in range(NCH)]

    @blk.sync
    def _(s):
        g_wait(p, s, L + "F2_done")
        for dst, srcI in [(b1sb, p.b1pI[l]), (b2sb, p.b2tI[l]),
                          (lgsb, p.lgtI[l]), (lbsb, p.lbtI[l])]:
            p.cE[3].inc(s.dma_start(out=dst[:, :], in_=srcI[:, :]), 16)
    g_set(p, (L, "bld"), p.cE[3])

    @blk.gpsimd
    def _(g):
        g_wait(p, g, L + "F2_done")
        g_wait(p, g, (L, "oTspill"))
        for i in range(4):
            p.cD[1].inc(g.dma_start(out=wo_sb[i][:, :],
                                    in_=p.woI[l][i * 128:(i + 1) * 128, :]), 16)
        for c in range(NCH):
            ins = g.memset(carry[c][:, :], 0.0)
        p.cZ.inc(ins, 1)
    g_set(p, (L, "wold"), p.cD[1])

    @blk.gpsimd
    def _(g):
        g_wait(p, g, "wcast")
        for oc in range(NCH):
            p.cF[2].inc(g.dma_start(
                out=w2res[oc].rearrange("p (c n) -> p c n", c=22)[0:128, 0:21, :],
                in_=p.w2bfI[l][0:2688, oc * 128:(oc + 1) * 128].rearrange(
                    "(c p) n -> p c n", p=128)), 16)
            p.cF[2].inc(g.dma_start(
                out=w2res[oc][0:42, 21 * 128: 22 * 128],
                in_=p.w2bfI[l][2688:2730, oc * 128:(oc + 1) * 128]), 16)
    g_set(p, (L, "w2resld"), p.cF[2])

    @blk.gpsimd
    def _(g):
        for c in range(NCH):
            ins = g.memset(fm13[c][:, 0:1], 0.0)
        p.cZ.inc(ins, 1)
    g_set(p, (L, "fm13z"), p.cZ)
    g_set(p, (L, "carryz"), p.cZ)

    for sl in range(NSL):
        ssl = slice(sl * TSL, (sl + 1) * TSL)
        finD = finD2[0]
        fmD = finD
        oTs = oTs2[0]

        @blk.sync
        def _(s, sl=sl, ssl=ssl, finD=finD, oTs=oTs):
            g_wait(p, s, L + "F2_done")
            g_wait(p, s, (L, "oTspill"))
            if sl >= 1:
                g_wait(p, s, (L + "F3_mmdone", sl - 1))
                g_wait(p, s, (L + "F3_xbsq", sl - 1))
                g_wait(p, s, (L + "F3_fm13", sl - 1))
            cs = [p.cA, p.cE[2]][sl % 2]
            for c in range(NCH):
                cs.inc(s.dma_start(out=finD[c][:, :], in_=p.scr_fA[c, :, ssl]), 16)
            for i in range(4):
                cs.inc(s.dma_start(out=oTs[0 * 0][:, i * TSL:(i + 1) * TSL]
                                   if False else oTs[:, i * TSL:(i + 1) * TSL],
                                   in_=p.scr_v[i, :, ssl]), 16)
        g_set(p, (L + "F3_ld", sl), [p.cA, p.cE[2]][sl % 2])

        for oc in range(NCH):
            @blk.tensor
            def _(t, sl=sl, oc=oc, ssl=ssl, oTs=oTs):
                k = sl * 8 + oc
                if k == 0:
                    g_wait(p, t, (L, "wold"))
                if oc == 0 and sl >= 1:
                    g_wait(p, t, (L + "F4_ag", (sl - 1, 21)))
                if k >= 2:
                    g_wait(p, t, (L + "F3_hev", k - 2))
                g_wait(p, t, (L + "F3_ld", sl))
                bk = p.bank(k % 2)
                for i in range(4):
                    ins = t.matmul(bk, wo_sb[i][:, oc * 128:(oc + 1) * 128],
                                   oTs[:, i * TSL:(i + 1) * TSL],
                                   start=(i == 0), stop=(i == 3))
                cT.inc(ins, 1)
            g_set(p, (L + "F3_mm", sl * 8 + oc), cT)
            if oc == NCH - 1:
                g_set(p, (L + "F3_mmdone", sl), cT)

            @blk.vector
            def _(v, sl=sl, oc=oc, finD=finD):
                k = sl * 8 + oc
                g_wait(p, v, (L + "F3_mm", k))
                if oc == 0:
                    g_wait(p, v, (L + "F3_ld", sl))
                VI(p, v.tensor_tensor(finD[oc][:, :], p.bank(k % 2), finD[oc][:, :], op=OP.add))
            g_set(p, (L + "F3_hev", sl * 8 + oc), p.cVc)
        g_set(p, (L + "F3_h", sl), p.cVc)

        @blk.vector
        def _(v, sl=sl, fmD=fmD):
            VW(p, v)
            if sl == 0:
                g_wait(p, v, (L, "carryz"))
            for c in range(NCH):
                if c == 0 and sl >= 1:
                    g_wait(p, v, (L + "F3r_stats", sl - 1))
                if c < 4:
                    VI(p, v.tensor_copy(xbD[c][:, :], fmD[c][:, :]))
                else:
                    VI(p, v.tensor_copy(xbD[c][:, 0:1], carry[c][:, :]))
                    VI(p, v.tensor_copy(xbD[c][:, 1:TSL], fmD[c][:, 0:TSL - 1]))
                    VW(p, v)
                    VI(p, v.tensor_copy(carry[c][:, :], fmD[c][:, TSL - 1:TSL]))
                VW(p, v)
                VI(p, v.tensor_tensor(sqD[c][:, :], xbD[c][:, :], xbD[c][:, :], op=OP.mult))
        g_set(p, (L + "F3_xbsq", sl), p.cVc)

        @blk.tensor
        def _(t, sl=sl):
            g_wait(p, t, (L + "F3_xbsq", sl))
            if sl >= 1:
                g_wait(p, t, ((L + "F3r", sl - 1), "var"))
            for c in range(NCH):
                t.matmul(p.bank(2)[0:1, :], p.inv_c[:, :], xbD[c][:, :],
                         start=(c == 0), stop=(c == NCH - 1))
            for c in range(NCH):
                ins = t.matmul(p.bank(3)[0:1, :], p.inv_c[:, :], sqD[c][:, :],
                               start=(c == 0), stop=(c == NCH - 1))
            cT.inc(ins, 1)
        g_set(p, (L + "F3r_stats", sl), cT)
        extraF = [(L + "F4_planes", sl - 2)] if sl >= 2 else []
        emit_row_finish(p, (L + "F3r", sl), p.rowF,
                        slice((sl % 2) * TSL, (sl % 2 + 1) * TSL), extra_scalar=extraF)

        @blk.vector
        def _(v, sl=sl, fmD=fmD):
            VW(p, v)
            if sl >= 1:
                g_wait(p, v, (L + "F4_fod", sl - 1))
                g_wait(p, v, (L + "F4_spill", sl - 1))
                if l == 1:
                    g_wait(p, v, (L + "F5_xbsq", sl - 1))
            for c in range(NCH):
                if c >= 4 and sl >= 1:
                    VI(p, v.tensor_copy(fm13[c][:, 0:1], carryF[c][:, :]))
                VI(p, v.tensor_copy(fm13[c][:, 1:513], fmD[c][:, :]))
                if c >= 4:
                    VW(p, v)
                    VI(p, v.tensor_copy(carryF[c][:, :], fmD[c][:, TSL - 1:TSL]))
                    VW(p, v)
        g_set(p, (L + "F3_fm13", sl), p.cVc)

        @blk.tensor
        def _(t, sl=sl, ssl=ssl):
            g_wait(p, t, ((L + "F3r", sl), "rows"))
            if sl >= 1:
                g_wait(p, t, (L + "F4_w2evdone", sl - 1))
            rslF = slice((sl % 2) * TSL, (sl % 2 + 1) * TSL)
            t.matmul(p.bank(4), p.ones_rf[:, :], p.rowF[0][0:1, rslF], start=True, stop=True)
            ins = t.matmul(p.bank(5), p.ones_rf[:, :], p.rowF[1][0:1, rslF], start=True, stop=True)
            cT.inc(ins, 1)
        g_set(p, (L + "F4_planes", sl), cT)

        @blk.vector
        def _(v, sl=sl):
            g_wait(p, v, (L + "F4_planes", sl))
            g_wait(p, v, (L + "F3_fm13", sl))
            if sl == 0:
                g_wait(p, v, (L, "fm13z"))
                g_wait(p, v, (L, "bld"))
            if sl >= 1:
                g_wait(p, v, (L + "F4_w1", (sl - 1, 21)))
            for c in range(NCH):
                off = 1 if c < 4 else 0
                sh = fm13[c][:, off:off + TSL]
                VI(p, v.tensor_tensor(hb16[c][:, :], sh, p.bank(4), op=OP.mult))
                VW(p, v)
                VI(p, v.tensor_tensor(hb16[c][:, :], hb16[c][:, :], p.bank(5), op=OP.add))
                VW(p, v)
                VI(p, v.tensor_scalar(hb16[c][:, :], hb16[c][:, :], lgsb[:, c:c + 1],
                                      lbsb[:, c:c + 1], op0=OP.mult, op1=OP.add))
        g_set(p, (L + "F4_apply", sl), p.cVc)

        for mp in range(22):
            rr = min(128, FHID - mp * 128)
            kk = sl * 22 + mp

            @blk.gpsimd
            def _(g, sl=sl, mp=mp, rr=rr, kk=kk):
                if kk >= 2:
                    g_wait(p, g, (L + "F4_w1", (sl, mp - 2) if mp >= 2 else (sl - 1, 20 + mp)))
                if kk < 2:
                    g_wait(p, g, "wcast")
                buf_a = w1sb[(kk % 2) * 2]
                buf_g = w1sb[(kk % 2) * 2 + 1]
                srcA = p.w1bfI[l][:, mp * 128: mp * 128 + rr].rearrange(
                    "(c p) n -> p c n", p=128)
                srcG = p.w1bfI[l][:, FHID + mp * 128: FHID + mp * 128 + rr].rearrange(
                    "(c p) n -> p c n", p=128)
                cw1 = p.cF[kk % 2]
                cw1.inc(g.dma_start(
                    out=buf_a.rearrange("p (c n) -> p c n", c=NCH)[:, :, 0:rr], in_=srcA), 16)
                cw1.inc(g.dma_start(
                    out=buf_g.rearrange("p (c n) -> p c n", c=NCH)[:, :, 0:rr], in_=srcG), 16)
            g_set(p, (L + "F4_w1ld", (sl, mp)), p.cF[kk % 2])

            @blk.tensor
            def _(t, sl=sl, mp=mp, rr=rr, kk=kk):
                g_wait(p, t, (L + "F4_apply", sl))
                g_wait(p, t, (L + "F4_w1ld", (sl, mp)))
                if mp >= 2:
                    g_wait(p, t, (L + "F4_ag", (sl, mp - 2)))
                elif sl >= 1:
                    g_wait(p, t, (L + "F4_ag", (sl - 1, 20 + mp)))
                ba = p.bank((mp % 2) * 2)
                bg = p.bank((mp % 2) * 2 + 1)
                buf_a = w1sb[(kk % 2) * 2]
                buf_g = w1sb[(kk % 2) * 2 + 1]
                for c in range(NCH):
                    t.matmul(ba[0:rr, :], buf_a[:, c * 128: c * 128 + rr],
                             hb16[c][:, :], start=(c == 0), stop=(c == NCH - 1))
                for c in range(NCH):
                    ins = t.matmul(bg[0:rr, :], buf_g[:, c * 128: c * 128 + rr],
                                   hb16[c][:, :], start=(c == 0), stop=(c == NCH - 1))
                cT.inc(ins, 1)
            g_set(p, (L + "F4_w1", (sl, mp)), cT)

            @blk.scalar
            def _(s, sl=sl, mp=mp, rr=rr):
                g_wait(p, s, (L + "F4_w1", (sl, mp)))
                if mp >= 1:
                    g_wait(p, s, (L + "F4_ag", (sl, mp - 1)))
                elif sl >= 1:
                    g_wait(p, s, (L + "F4_ag", (sl - 1, 21)))
                bg = p.bank((mp % 2) * 2 + 1)
                SW(p, s)
                SI(p, s.activation(gt[0:rr, :], bg[0:rr, :], AF.Gelu,
                                   bias=b1sb[0:rr, 22 + mp: 23 + mp]))
            g_set(p, (L + "F4_gelu", (sl, mp)), p.cSc)

            @blk.vector
            def _(v, sl=sl, mp=mp, rr=rr):
                g_wait(p, v, (L + "F4_gelu", (sl, mp)))
                ba = p.bank((mp % 2) * 2)
                VI(p, v.scalar_tensor_tensor(agb[mp][0:rr, :], ba[0:rr, :],
                                             b1sb[0:rr, mp:mp + 1], gt[0:rr, :],
                                             op0=OP.add, op1=OP.mult))
            g_set(p, (L + "F4_ag", (sl, mp)), p.cVc)

        for oc in range(NCH):
            @blk.tensor
            def _(t, sl=sl, oc=oc):
                g_wait(p, t, (L + "F4_ag", (sl, 21)))
                if sl == 0 and oc == 0:
                    g_wait(p, t, (L, "w2resld"))
                k = sl * 8 + oc
                if k >= 2:
                    g_wait(p, t, (L + "F4_w2ev", k - 2))
                bk = p.bank(4 + oc % 2)
                buf = w2res[oc]
                for i in range(22):
                    rr = min(128, FHID - i * 128)
                    ins = t.matmul(bk, buf[0:rr, i * 128: (i + 1) * 128],
                                   agb[i][0:rr, :], start=(i == 0), stop=(i == 21))
                cT.inc(ins, 1)
            g_set(p, (L + "F4_w2", (sl, oc)), cT)

            @blk.vector
            def _(v, sl=sl, oc=oc):
                g_wait(p, v, (L + "F4_w2", (sl, oc)))
                bk = p.bank(4 + oc % 2)
                VW(p, v)
                VI(p, v.tensor_scalar(at[:, :], bk, b2sb[:, oc:oc + 1], None, op0=OP.add))
                VW(p, v)
                VI(p, v.tensor_tensor(fm13[oc][:, 1:513], at[:, :],
                                      fm13[oc][:, 1:513], op=OP.add))
            g_set(p, ((L + "F4_w2ev", sl * 8 + oc)) if False else (L + "F4_w2ev", sl * 8 + oc), p.cVc)
        g_set(p, (L + "F4_fod", sl), p.cVc)
        g_set(p, (L + "F4_w2evdone", sl), p.cVc)

        if l == 1:
            @blk.vector
            def _(v, sl=sl):
                VW(p, v)
                for c in range(NCH):
                    if c == 0 and sl >= 1:
                        g_wait(p, v, (L + "F5r_stats", sl - 1))
                    VI(p, v.tensor_copy(xb5[c][:, :], fm13[c][:, 1:513]))
                    VW(p, v)
                    VI(p, v.tensor_tensor(sq5[c][:, :], xb5[c][:, :], xb5[c][:, :], op=OP.mult))
            g_set(p, (L + "F5_xbsq", sl), p.cVc)

            @blk.tensor
            def _(t, sl=sl):
                g_wait(p, t, (L + "F5_xbsq", sl))
                if sl >= 1:
                    g_wait(p, t, ((L + "F5r", sl - 1), "var"))
                for c in range(NCH):
                    t.matmul(p.bank(2)[0:1, :], p.inv_c[:, :], xb5[c][:, :],
                             start=(c == 0), stop=(c == NCH - 1))
                for c in range(NCH):
                    ins = t.matmul(p.bank(3)[0:1, :], p.inv_c[:, :], sq5[c][:, :],
                                   start=(c == 0), stop=(c == NCH - 1))
                cT.inc(ins, 1)
            g_set(p, (L + "F5r_stats", sl), cT)
            extra5 = [(L + "F5_r5sp", sl - 2)] if sl >= 2 else []
            emit_row_finish(p, (L + "F5r", sl), p.row5,
                            slice((sl % 2) * TSL, (sl % 2 + 1) * TSL),
                            extra_scalar=extra5)

            @blk.sync
            def _(s, sl=sl):
                g_wait(p, s, ((L + "F5r", sl), "rows"))
                cs = [p.cW, p.cE[1]][sl % 2]
                rsl5 = slice((sl % 2) * TSL, (sl % 2 + 1) * TSL)
                for j in range(2):
                    cs.inc(s.dma_start(out=p.scr_r5[j:j + 1, sl * TSL:(sl + 1) * TSL],
                                       in_=p.row5[j][0:1, rsl5]), 16)
            g_set(p, (L + "F5_r5sp", sl), [p.cW, p.cE[1]][sl % 2])
            if sl == NSL - 1:
                g_set(p, "R5_done_e", p.cW)
                g_set(p, "R5_done_o", p.cE[1])

        @blk.gpsimd
        def _(g, sl=sl, ssl=ssl):
            g_wait(p, g, (L + "F4_fod", sl))
            if l == 1:
                g_wait(p, g, (L + "F5_xbsq", sl))
            if "d_xf" in p.dbg and l == 0 and sl == 0:
                g_wait(p, g, "DXF_dump")
            if "d_f0" in p.dbg and l == 1 and sl == 0:
                g_wait(p, g, "DF0_dump")
            for c in range(NCH):
                p.cD[4].inc(g.dma_start(out=p.scr_fA[c, :, ssl], in_=fm13[c][:, 1:513]), 16)
        g_set(p, (L + "F4_spill", sl), p.cD[4])
        g_set(p, (L + "F4_fod", sl), p.cVc) if False else None
    g_set(p, L + "F4_done", p.cD[4])
    g_set(p, L + "F4_done_d1", p.cD[1])
    g_set(p, L + "F4_rows", p.cVc)
    if "d_f0" in p.dbg and l == 0:
        @blk.sync
        def _(s):
            barrier(p, s)
            for c in range(NCH):
                p.cO.inc(s.dma_start(out=p.dbg["d_f0"][c, :, :], in_=p.scr_fA[c, :, :]), 16)
        g_set(p, "DF0_dump", p.cO)
    p.close_scope(L + "D3")
    p.close_scope(L + "D4")
    p.close_scope(L + "LN")


def emit_final(p):
    blk = p.blk
    cT, cA, cG = p.cT, p.cA, p.cG
    r5Z = [p.sbS("Z", f"r5Z{j}", [1, T], side="right") for j in range(2)]
    src2 = [[p.sbS("Z", f"srcZ{j}_{c}", [128, TSL]) for c in range(NCH)] for j in range(2)]
    aux2 = [[p.sbS("Z", f"auxZ{j}_{c}", [128, TSL]) for c in range(NCH)] for j in range(2)]
    tmout2 = [[p.sbS("Z", f"tmoutZ{j}_{i}", [128, C]) for i in range(4)] for j in range(2)]

    for sl in range(NSL):
        ssl = slice(sl * TSL, (sl + 1) * TSL)
        src = src2[sl % 2]
        aux = aux2[sl % 2]
        tmout = tmout2[sl % 2]

        @blk.sync
        def _(s, sl=sl, ssl=ssl, src=src, aux=aux):
            if sl == 0:
                g_wait(p, s, "L1_F4_done")
                g_wait(p, s, "L1_F4_rows")
                g_wait(p, s, "X1_done_e")
                g_wait(p, s, "X1_done_o")
                g_wait(p, s, "R5_done_e")
                g_wait(p, s, "R5_done_o")
                for j in range(2):
                    p.cA.inc(s.dma_start(out=r5Z[j][:, :], in_=p.scr_r5[j:j + 1, :]), 16)
            if sl == 1:
                g_wait(p, s, "L1_F4_done_d1")
            if sl >= 2:
                g_wait(p, s, ("Z_trdone", sl - 2))
            cs = [p.cA, p.cE[0]][sl % 2]
            for c in range(NCH):
                cs.inc(s.dma_start(out=src[c][:, :], in_=p.scr_fA[c, :, ssl]), 16)
                cs.inc(s.dma_start(out=aux[c][:, :], in_=p.scr_x1[c, :, ssl]), 16)
        g_set(p, ("Z_ld", sl), [p.cA, p.cE[0]][sl % 2])

        @blk.tensor
        def _(t, sl=sl, ssl=ssl):
            if sl == 0:
                g_wait(p, t, ("Z_ld", 0))
            if sl >= 1:
                g_wait(p, t, ("Z_apply", sl - 1))
            t.matmul(p.bank(4), p.ones_rf[:, :], r5Z[0][0:1, ssl], start=True, stop=True)
            ins = t.matmul(p.bank(5), p.ones_rf[:, :], r5Z[1][0:1, ssl], start=True, stop=True)
            cT.inc(ins, 1)
        g_set(p, ("Z_planes", sl), cT)

        @blk.vector
        def _(v, sl=sl, src=src, aux=aux):
            g_wait(p, v, ("Z_planes", sl))
            g_wait(p, v, ("Z_ld", sl))
            for c in range(NCH):
                VI(p, v.tensor_tensor(src[c][:, :], src[c][:, :], p.bank(4), op=OP.mult))
                VW(p, v)
                VI(p, v.tensor_tensor(src[c][:, :], src[c][:, :], p.bank(5), op=OP.add))
                VW(p, v)
                VI(p, v.tensor_tensor(src[c][:, :], src[c][:, :], aux[c][:, :], op=OP.add))
        g_set(p, ("Z_apply", sl), p.cVc)

        for r in range(4):
            for c in range(NCH):
                k = sl * 32 + r * 8 + c

                @blk.tensor
                def _(t, sl=sl, r=r, c=c, k=k, src=src):
                    if r == 0 and c == 0:
                        g_wait(p, t, ("Z_apply", sl))
                    if k >= 2:
                        g_wait(p, t, ("Z_ev", k - 2))
                    pt = p.bank(k % 2)[:, 0:128]
                    cT.inc(t.transpose(pt, src[c][:, r * 128:(r + 1) * 128],
                                       p.ident_f[:, :]), 1)
                g_set(p, ("Z_tr", k), cT)

                @blk.vector
                def _(v, sl=sl, r=r, c=c, k=k, tmout=tmout):
                    g_wait(p, v, ("Z_tr", k))
                    if r == 0 and c == 0 and sl >= 2:
                        g_wait(p, v, ("Z_st", sl - 2))
                    pt = p.bank(k % 2)[:, 0:128]
                    VI(p, v.tensor_copy(tmout[r][:, c * 128:(c + 1) * 128], pt))
                g_set(p, ("Z_ev", k), p.cVc)
        g_set(p, ("Z_trdone", sl), p.cVc)

        @blk.sync
        def _(s, sl=sl, tmout=tmout):
            g_wait(p, s, ("Z_trdone", sl))
            cs = [p.cO, p.cE[3]][sl % 2]
            for r in range(4):
                cs.inc(s.dma_start(
                    out=p.out_t[sl * TSL + r * 128: sl * TSL + (r + 1) * 128, :],
                    in_=tmout[r][:, :]), 16)
        g_set(p, ("Z_st", sl), [p.cO, p.cE[3]][sl % 2])

    @blk.sync
    def _(s):
        s.wait_ge(p.cO.h, p.cO.n)
        s.wait_ge(p.cE[3].h, p.cE[3].n)


def prep_inputs(inputs, batch_idx, cst):
    m = {
        "x": np.ascontiguousarray(inputs["L0_attn_x"][batch_idx]),
        "w_in": np.ascontiguousarray(inputs["L0_attn_w_in"]),
        "w_out": np.ascontiguousarray(inputs["L0_attn_w_out"]),
        "gamma_t": np.ascontiguousarray(
            np.asarray(inputs["L0_attn_ln_gamma"]).reshape(NCH, 128).T),
        "beta_t": np.ascontiguousarray(
            np.asarray(inputs["L0_attn_ln_beta"]).reshape(NCH, 128).T),
        "ident": cst["ident"],
        "sel2": cst["sel2"], "brd2": cst["brd2"],
        "rp_cos": cst["rp_cos"], "rp_sin": cst["rp_sin"],
        "xp_qc": cst["xp_qc"], "xp_qs": cst["xp_qs"],
        "xp_kc": cst["xp_kc"], "xp_ks": cst["xp_ks"],
        "lmask": cst["lmask"],
    }
    for l in range(2):
        b1 = np.asarray(inputs["L0_ffn_b1"][l])
        b1p = np.zeros((44, 128), dtype=np.float32)
        for mm in range(22):
            a = b1[128 * mm: min(128 * (mm + 1), FHID)]
            b1p[mm, :len(a)] = a
            g = b1[FHID + 128 * mm: FHID + min(128 * (mm + 1), FHID)]
            b1p[22 + mm, :len(g)] = g
        m[f"b1p{l}"] = np.ascontiguousarray(b1p.T)
        m[f"b2t{l}"] = np.ascontiguousarray(np.asarray(inputs["L0_ffn_b2"][l]).reshape(NCH, 128).T)
        m[f"lgt{l}"] = np.ascontiguousarray(np.asarray(inputs["L0_ffn_ln_g"][l]).reshape(NCH, 128).T)
        m[f"lbt{l}"] = np.ascontiguousarray(np.asarray(inputs["L0_ffn_ln_b"][l]).reshape(NCH, 128).T)
        m[f"qsc{l}"] = np.ascontiguousarray(np.asarray(inputs["L0_ffn_q_scale"][l]).reshape(AD, 1))
        m[f"ksc{l}"] = np.ascontiguousarray(np.asarray(inputs["L0_ffn_k_scale"][l]).reshape(AD, 1))
        m[f"wqkv{l}"] = np.ascontiguousarray(inputs["L0_ffn_wqkv"][l])
        m[f"wo{l}"] = np.ascontiguousarray(inputs["L0_ffn_wo"][l])
        m[f"w1{l}"] = np.ascontiguousarray(inputs["L0_ffn_w1"][l])
        m[f"w2{l}"] = np.ascontiguousarray(inputs["L0_ffn_w2"][l])
    return m


def kernel(**inputs):
    cst = host_constants()
    nc, p = build_program()
    in_maps = [prep_inputs(inputs, b, cst) for b in range(8)]
    res = run_bass_kernel_spmd(nc, in_maps, core_ids=list(range(8)))
    return np.stack([res.results[b]["out"] for b in range(8)], axis=0)


def run_debug(inputs, cores=(0,), trace=False):
    cst = host_constants()
    nc, p = build_program()
    in_maps = [prep_inputs(inputs, b, cst) for b in cores]
    res = run_bass_kernel_spmd(nc, in_maps, core_ids=list(range(len(cores))), trace=trace)
    return res


# version 17
# speedup vs baseline: 1.0308x; 1.0053x over previous
import contextlib
import numpy as np
import ml_dtypes
import concourse.bass as bass
import concourse.mybir as mybir
from concourse.bass_utils import run_bass_kernel_spmd
from concourse.alu_op_type import AluOpType

F32 = mybir.dt.float32
BF16 = mybir.dt.bfloat16
AF = mybir.ActivationFunctionType
AX = mybir.AxisListType
OP = AluOpType

T = 4096
C = 1024
NCH = 8
TSL = 512
NSL = 8
AH = 16
AD = 64
FH = 8
FHID = 2730
F2H = 5460
LN_EPS = 1e-5

DEBUG = []
bfdt = ml_dtypes.bfloat16


def host_constants():
    cst = {}
    cst["ident"] = np.eye(128, dtype=np.float32)
    sel2 = np.zeros((128, 2), np.float32)
    sel2[0:64, 0] = 1.0
    sel2[64:128, 1] = 1.0
    cst["sel2"] = sel2.astype(bfdt)
    brd2 = np.zeros((2, 128), np.float32)
    brd2[0, 0:64] = 1.0
    brd2[1, 64:128] = 1.0
    cst["brd2"] = brd2
    i = np.arange(1, AD // 2 + 1, dtype=np.float64)
    thetas = np.repeat(1000.0 ** (-2.0 * i / AD), 2)
    pos = np.arange(1, T + 1, dtype=np.float64)
    args = pos[:, None] * thetas[None, :]
    perm = np.concatenate([np.arange(0, AD, 2), np.arange(1, AD, 2)])
    cst["rp_cos"] = np.ascontiguousarray(np.cos(args)[:, perm].T).astype(bfdt)
    cst["rp_sin"] = np.ascontiguousarray(np.sin(args)[:, perm].T).astype(bfdt)
    half = np.arange(0, AD, 2, dtype=np.float64)
    inv_freq = 1.0 / 10000.0 ** (half / AD)
    t = np.arange(T, dtype=np.float64)
    freqs = np.concatenate([t[:, None] * inv_freq[None, :]] * 2, axis=-1)
    base = (half + 0.4 * AD) / (1.4 * AD)
    power = (t - T // 2) / 256.0
    scale = np.concatenate([base[None, :] ** power[:, None]] * 2, axis=-1)
    for k, v in [("xp_qc", np.cos(freqs) * scale), ("xp_qs", np.sin(freqs) * scale),
                 ("xp_kc", np.cos(freqs) / scale), ("xp_ks", np.sin(freqs) / scale)]:
        cst[k] = np.ascontiguousarray(v.T).astype(bfdt)
    i_idx = np.arange(128)[:, None]
    j_idx = np.arange(512)[None, :]
    m = np.zeros((8, 128, 512), dtype=np.float32)
    for a, r in enumerate(range(-512, 385, 128)):
        vis = (j_idx - 511 <= r + i_idx) & (r + i_idx <= j_idx)
        m[a] = np.where(vis, 0.0, -30000.0)
    cst["lmask"] = m.astype(bfdt)
    return cst


class Ctr:
    __slots__ = ("h", "n")

    def __init__(self, h):
        self.h = h
        self.n = 0

    def inc(self, instr, k=1):
        instr.then_inc(self.h, k)
        self.n += k
        return instr


class P:
    pass


def build_program():
    nc = bass.Bass("TRN2", target_bir_lowering=False)
    p = P()
    p.nc = nc
    p.gates = {}
    p.rope_slot = {}
    p.plane_slot = {}
    p.plain_slot = {}
    p.dnT_bank = {}
    p.dnT_slot = {}
    p.avhist = {}

    def gate(key):
        pass

    din = lambda name, shape, dt=F32: nc.dram_tensor(name, shape, dt, kind="ExternalInput").ap()
    scr = lambda name, shape, dt=F32: nc.dram_tensor(name, shape, dt, kind="Internal").ap()

    p.x_in = din("x", [T, C])
    p.w_inI = din("w_in", [3 * C, C])
    p.w_outI = din("w_out", [C, C])
    p.gamma_t = din("gamma_t", [128, NCH])
    p.beta_t = din("beta_t", [128, NCH])
    p.wqkvI = [din(f"wqkv{l}", [C, 3 * 512]) for l in range(2)]
    p.woI = [din(f"wo{l}", [512, C]) for l in range(2)]
    p.w1I = [din(f"w1{l}", [C, F2H]) for l in range(2)]
    p.w2I = [din(f"w2{l}", [FHID, C]) for l in range(2)]
    p.b1pI = [din(f"b1p{l}", [128, 44]) for l in range(2)]
    p.b2tI = [din(f"b2t{l}", [128, NCH]) for l in range(2)]
    p.lgtI = [din(f"lgt{l}", [128, NCH]) for l in range(2)]
    p.lbtI = [din(f"lbt{l}", [128, NCH]) for l in range(2)]
    p.qscI = [din(f"qsc{l}", [AD, 1]) for l in range(2)]
    p.kscI = [din(f"ksc{l}", [AD, 1]) for l in range(2)]
    p.identI = din("ident", [128, 128])
    p.sel2I = din("sel2", [128, 2], BF16)
    p.brd2I = din("brd2", [2, 128])
    p.rp_cosI = din("rp_cos", [AD, T], BF16)
    p.rp_sinI = din("rp_sin", [AD, T], BF16)
    p.xpI = {k: din(k, [AD, T], BF16) for k in ["xp_qc", "xp_qs", "xp_kc", "xp_ks"]}
    p.lmaskI = din("lmask", [8, 128, TSL], BF16)

    p.out_t = nc.dram_tensor("out", [T, C], F32, kind="ExternalOutput").ap()
    p.dbg = {}
    for name, shape, dt in DEBUG:
        p.dbg[name] = nc.dram_tensor(name, shape, dt, kind="ExternalOutput").ap()

    p.w1bfI = [scr(f"w1bf{l}", [C, F2H], BF16) for l in range(2)]
    p.w2bfI = [scr(f"w2bf{l}", [FHID, C], BF16) for l in range(2)]
    p.scr_r5 = scr("scr_r5", [2, T])
    p.scr_x0 = scr("scr_x0", [NCH, 128, T])
    p.scr_xh = scr("scr_xh", [NCH, 128, T])
    p.scr_hbq = scr("scr_hbq", [AH, 128, T], BF16)
    p.scr_hbk = scr("scr_hbk", [AH, 128, T], BF16)
    p.scr_v = scr("scr_v", [8, 128, T], BF16)
    p.scr_h = scr("scr_h", [NCH, 128, T])
    p.scr_ao = scr("scr_ao", [NCH, 128, T])
    p.scr_x1 = scr("scr_x1", [NCH, 128, T])
    p.scr_fA = scr("scr_fA", [NCH, 128, T])
    p.scr_fB = scr("scr_fB", [NCH, 128, T])
    p.scr_dnb = scr("scr_dnb", [AH, 32, 128])
    p.scr_bb = scr("scr_bb", [FH, 32, 128])

    stack = contextlib.ExitStack()
    p.stack = stack
    p.sb = lambda nm, shp, dt=F32: stack.enter_context(nc.sbuf_tensor(nm, shp, dt))
    mksem = lambda nm: Ctr(stack.enter_context(nc.semaphore(nm)))
    p.scopes = {}

    def sbS(scope, nm, shp, dt=F32, side=None):
        if scope not in p.scopes:
            p.scopes[scope] = contextlib.ExitStack()
        return p.scopes[scope].enter_context(nc.sbuf_tensor(nm, shp, dt, side=side))

    def close_scope(scope):
        if scope in p.scopes:
            p.scopes.pop(scope).close()

    p.sbS = sbS
    p.close_scope = close_scope

    with stack:
        p.cW = mksem("cW")
        p.cA = mksem("cA")
        p.cG = mksem("cG")
        p.cB = mksem("cB")
        p.cO = mksem("cO")
        p.cT = mksem("cT")
        p.cV = mksem("cV")
        p.cS = mksem("cS")
        p.cX = mksem("cX")
        p.cY = mksem("cY")
        p.cZ = mksem("cZ")
        p.cD = [mksem(f"cD{i}") for i in range(6)]
        p.cE = [mksem(f"cE{i}") for i in range(4)]
        p.cF = [mksem(f"cF{i}") for i in range(4)]
        p.cSL = mksem("cSL")
        p.cRX = mksem("cRX")
        p.cVc = mksem("cVc")
        p.cSc = mksem("cSc")
        p.all_ctrs = [p.cW, p.cA, p.cG, p.cB, p.cO, p.cT, p.cV, p.cS, p.cX, p.cY,
                      p.cZ, p.cVc, p.cSc, p.cSL, p.cRX] + p.cD + p.cE + p.cF

        p.ident_f = p.sb("ident_f", [128, 128])
        p.ident_b = p.sb("ident_b", [128, 128], BF16)
        p.inv_c = p.sb("inv_c", [128, 1], BF16)
        p.ones_rf = p.sb("ones_rf", [1, 128])
        p.ones_rb = p.sb("ones_rb", [1, 128], BF16)
        p.ones128 = p.sb("ones128", [128, 1], BF16)
        p.eps_col = p.sb("eps_col", [128, 1])
        p.epsl2_col = p.sb("epsl2_col", [128, 1])
        p.gam = p.sb("gam", [128, NCH])
        p.bet = p.sb("bet", [128, NCH])
        p.sel2 = p.sb("sel2_sb", [128, 2], BF16)
        p.brd2 = p.sb("brd2_sb", [2, 128])


        p.psF = stack.enter_context(nc.psum_tensor("psF", [128, 3072], F32))
        p.psB = stack.enter_context(nc.psum_tensor("psB", [128, 2048], BF16))
        p.bank = lambda i, w=1: p.psF[:, i * 512:(i + w) * 512]
        p.bslot = lambda i: p.psB[:, (i % 2) * 1024:(i % 2) * 1024 + 128]

        p.srow = [p.sb(f"srow{i}", [1, TSL]) for i in range(4)]

        p.rp_cos = p.sbS("AB", "rp_cos_sb", [AD, T], BF16)
        p.rp_sin = p.sbS("AB", "rp_sin_sb", [AD, T], BF16)
        p.w_inT = [p.sbS("AB", f"w_inT{c}", [128, 3 * C], BF16) for c in range(NCH)]
        p.xhb = [p.sbS("AB", f"xhb{c}", [128, T], BF16) for c in range(NCH)]

        with nc.Block("init") as blk:
            @blk.sync
            def _(s):
                for c0 in p.all_ctrs:
                    s.sem_clear(c0.h)

        with nc.Block("main") as blk:
            p.blk = blk
            emit_const_loads(p)
            emit_winT(p)
            emit_phaseA(p)
            emit_phaseB(p)
            emit_attention(p)
            emit_wout_lnchain(p)
            for l in range(2):
                emit_ffn_layer(p, l)
            emit_final(p)

        for sc in reversed(list(p.scopes)):
            p.close_scope(sc)

    return nc, p


def VI(p, ins):
    p.cVc.inc(ins, 1)
    return ins


def VW(p, v):
    v.wait_ge(p.cVc.h, p.cVc.n)


def SI(p, ins):
    p.cSc.inc(ins, 1)
    return ins


def SW(p, s):
    s.wait_ge(p.cSc.h, p.cSc.n)


def g_set(p, key, ctr):
    p.gates[key] = (ctr, ctr.n)


def g_wait(p, eng, key):
    ctr, v = p.gates[key]
    eng.wait_ge(ctr.h, v)


def barrier(p, eng):
    for c in p.all_ctrs:
        eng.wait_ge(c.h, c.n)


def emit_const_loads(p):
    blk = p.blk
    cW, cG = p.cW, p.cG

    @blk.sync
    def _(s):
        for dst, src in [(p.ident_f, p.identI), (p.gam, p.gamma_t), (p.bet, p.beta_t),
                         (p.rp_cos, p.rp_cosI), (p.rp_sin, p.rp_sinI),
                         (p.sel2, p.sel2I), (p.brd2, p.brd2I)]:
            cW.inc(s.dma_start(out=dst[:, :], in_=src[:, :]), 16)


    @blk.gpsimd
    def _(g):
        p.cD[4].inc(g.dma_start(out=p.ident_b[:, :], in_=p.identI[:, :]), 16)
        g.memset(p.inv_c[:, :], 1.0 / C)
        g.memset(p.ones_rf[:, :], 1.0)
        g.memset(p.ones_rb[:, :], 1.0)
        g.memset(p.ones128[:, :], 1.0)
        g.memset(p.eps_col[:, :], LN_EPS)
        p.cZ.inc(g.memset(p.epsl2_col[:, :], 1e-18), 1)
        for l in range(2):
            p.cY.inc(g.dma_start(out=p.w1bfI[l][:, :], in_=p.w1I[l][:, :]), 16)
            p.cY.inc(g.dma_start(out=p.w2bfI[l][:, :], in_=p.w2I[l][:, :]), 16)
    g_set(p, "wcast", p.cY)
    g_set(p, "consts", p.cZ)
    g_set(p, "identb", p.cD[4])
    g_set(p, "constsW", cW)


def emit_winT(p):
    blk = p.blk
    stg = [p.sbS("W", f"winstg{i}", [128, C], BF16) for i in range(4)]
    cG, cT, cV = p.cG, p.cT, p.cV

    for ch in range(24):
        r0 = ch * 128
        dst = stg[ch % 4]

        @blk.gpsimd
        def _(g, ch=ch, r0=r0, dst=dst):
            cs = p.cD[ch % 4]
            if ch >= 4:
                g_wait(p, g, ("winT_ev", ch - 4))
            if ch < 16:
                for hb in range(2):
                    rr = r0 + hb * 64
                    cs.inc(g.dma_start(out=dst[hb * 64:hb * 64 + 32, :],
                                       in_=p.w_inI[rr:rr + 64:2, :]), 16)
                    cs.inc(g.dma_start(out=dst[hb * 64 + 32:hb * 64 + 64, :],
                                       in_=p.w_inI[rr + 1:rr + 64:2, :]), 16)
            else:
                cs.inc(g.dma_start(out=dst[:, :], in_=p.w_inI[r0:r0 + 128, :]), 16)
        g_set(p, ("winT_ld", ch), p.cD[ch % 4])

        for c in range(NCH):
            k = ch * 8 + c

            @blk.tensor
            def _(t, ch=ch, dst=dst, c=c, k=k):
                if c == 0:
                    g_wait(p, t, ("winT_ld", ch))
                if ch == 0 and c == 0:
                    g_wait(p, t, "consts")
                    g_wait(p, t, "identb")
                if k >= 2:
                    g_wait(p, t, ("winT_evc", k - 2))
                cT.inc(t.transpose(p.bslot(k), dst[:, c * 128:(c + 1) * 128],
                                   p.ident_b[:, :]), 1)
            g_set(p, ("winT_tr", k), cT)

            @blk.vector
            def _(v, ch=ch, c=c, k=k):
                g_wait(p, v, ("winT_tr", k))
                VI(p, v.tensor_copy(p.w_inT[c][:, ch * 128:(ch + 1) * 128], p.bslot(k)))
            g_set(p, ("winT_evc", k), p.cVc)
        g_set(p, ("winT_ev", ch), p.cVc)
    g_set(p, "winT_done", p.cVc)
    p.close_scope("W")


def emit_phaseA(p):
    blk = p.blk
    tm_in = [p.sbS("A", f"tm_in{i}", [128, C]) for i in range(4)]
    x0T = [p.sbS("A", f"x0T{c}", [128, TSL]) for c in range(NCH)]
    xb = [p.sbS("A", f"xbA{c}", [128, TSL], BF16) for c in range(NCH)]
    sq = [p.sbS("A", f"sqA{c}", [128, TSL], BF16) for c in range(NCH)]
    cA, cT, cV, cS, cG = p.cA, p.cT, p.cV, p.cS, p.cG

    for sl in range(NSL):
        ssl = slice(sl * TSL, (sl + 1) * TSL)

        @blk.sync
        def _(s, sl=sl):
            if sl == 0:
                barrier(p, s)
            else:
                g_wait(p, s, ("A_tr_done", sl - 1))
            for r in range(4):
                cA.inc(s.dma_start(out=tm_in[r][:, :],
                                   in_=p.x_in[sl * TSL + r * 128: sl * TSL + (r + 1) * 128, :]), 16)
        g_set(p, ("A_ld", sl), cA)

        for c in range(NCH):
            for r in range(4):
                k = c * 4 + r
                kg = sl * 32 + k

                @blk.tensor
                def _(t, sl=sl, c=c, r=r, k=k, kg=kg):
                    if k == 0:
                        g_wait(p, t, ("A_ld", sl))
                        if sl == 0:
                            g_wait(p, t, "winT_done")
                    if kg >= 2:
                        g_wait(p, t, ("A_ev", kg - 2))
                    pt = p.bank(k % 2)[:, 0:128]
                    cT.inc(t.transpose(pt, tm_in[r][:, c * 128:(c + 1) * 128],
                                       p.ident_f[:, :]), 1)
                g_set(p, ("A_tr", kg), cT)

                @blk.vector
                def _(v, sl=sl, c=c, r=r, k=k, kg=kg):
                    g_wait(p, v, ("A_tr", kg))
                    if k == 0 and sl >= 1:
                        g_wait(p, v, ("A_spill_xh", sl - 1))
                    pt = p.bank(k % 2)[:, 0:128]
                    VI(p, v.tensor_copy(x0T[c][:, r * 128:(r + 1) * 128], pt))
                g_set(p, ("A_ev", kg), p.cVc)
        g_set(p, ("A_tr_done", sl), cT)
        g_set(p, ("A_ev_done", sl), p.cVc)

        @blk.gpsimd
        def _(g, sl=sl, ssl=ssl):
            g_wait(p, g, ("A_ev_done", sl))
            for c in range(NCH):
                cG.inc(g.dma_start(out=p.scr_x0[c, :, ssl], in_=x0T[c][:, :]), 16)
        g_set(p, ("A_spill_x0", sl), cG)

        @blk.vector
        def _(v, sl=sl):
            VW(p, v)
            for c in range(NCH):
                if c == 0 and sl >= 1:
                    g_wait(p, v, ("A_stats", sl - 1))
                VI(p, v.tensor_copy(xb[c][:, :], x0T[c][:, :]))
                VW(p, v)
                VI(p, v.tensor_tensor(sq[c][:, :], xb[c][:, :], xb[c][:, :], op=OP.mult))
        g_set(p, ("A_xbsq", sl), p.cVc)

        @blk.tensor
        def _(t, sl=sl):
            g_wait(p, t, ("A_xbsq", sl))
            if sl >= 1:
                g_wait(p, t, ("A_var", sl - 1))
            for c in range(NCH):
                t.matmul(p.bank(2)[0:1, :], p.inv_c[:, :], xb[c][:, :],
                         start=(c == 0), stop=(c == NCH - 1))
            for c in range(NCH):
                ins = t.matmul(p.bank(3)[0:1, :], p.inv_c[:, :], sq[c][:, :],
                               start=(c == 0), stop=(c == NCH - 1))
            cT.inc(ins, 1)
        g_set(p, ("A_stats", sl), cT)

        @blk.vector
        def _(v, sl=sl):
            g_wait(p, v, ("A_stats", sl))
            VI(p, v.tensor_copy(p.srow[0][:, :], p.bank(2)[0:1, :]))
            VW(p, v)
            VI(p, v.tensor_tensor(p.srow[1][:, :], p.srow[0][:, :], p.srow[0][:, :],
                                  op=OP.mult))
            VW(p, v)
            VI(p, v.tensor_tensor(p.srow[1][:, :], p.bank(3)[0:1, :], p.srow[1][:, :],
                                  op=OP.subtract))
        g_set(p, ("A_var", sl), p.cVc)

        @blk.scalar
        def _(s, sl=sl):
            g_wait(p, s, ("A_var", sl))
            SI(p, s.activation(p.srow[1][:, :], p.srow[1][:, :], AF.Ln, bias=p.eps_col[0:1, :]))
            SW(p, s)
            SI(p, s.activation(p.srow[2][:, :], p.srow[1][:, :], AF.Exp, scale=-0.5))
        g_set(p, ("A_rstd", sl), p.cSc)

        @blk.vector
        def _(v, sl=sl):
            g_wait(p, v, ("A_rstd", sl))
            VI(p, v.scalar_tensor_tensor(p.srow[3][:, :], p.srow[0][:, :], -1.0,
                                         p.srow[2][:, :], op0=OP.mult, op1=OP.mult))
        g_set(p, ("A_mr", sl), p.cVc)

        @blk.tensor
        def _(t, sl=sl):
            g_wait(p, t, ("A_mr", sl))
            if sl >= 1:
                g_wait(p, t, ("A_apply", sl - 1))
            t.matmul(p.bank(4), p.ones_rf[:, :], p.srow[2][:, :], start=True, stop=True)
            ins = t.matmul(p.bank(5), p.ones_rf[:, :], p.srow[3][:, :], start=True, stop=True)
            cT.inc(ins, 1)
        g_set(p, ("A_planes", sl), cT)

        @blk.vector
        def _(v, sl=sl, ssl=ssl):
            g_wait(p, v, ("A_planes", sl))
            g_wait(p, v, ("A_spill_x0", sl))
            for c in range(NCH):
                VI(p, v.tensor_tensor(x0T[c][:, :], x0T[c][:, :], p.bank(4), op=OP.mult))
                VW(p, v)
                VI(p, v.tensor_tensor(x0T[c][:, :], x0T[c][:, :], p.bank(5), op=OP.add))
                VW(p, v)
                VI(p, v.tensor_copy(p.xhb[c][:, ssl], x0T[c][:, :]))
        g_set(p, ("A_apply", sl), p.cVc)

        @blk.gpsimd
        def _(g, sl=sl, ssl=ssl):
            g_wait(p, g, ("A_apply", sl))
            for c in range(NCH):
                cG.inc(g.dma_start(out=p.scr_xh[c, :, ssl], in_=x0T[c][:, :]), 16)
        g_set(p, ("A_spill_xh", sl), cG)
    g_set(p, "A_done", cG)
    p.close_scope("A")

    if "d_xhb" in p.dbg:
        @blk.sync
        def _(s):
            barrier(p, s)
            for c in range(NCH):
                p.cO.inc(s.dma_start(out=p.dbg["d_xhb"][c, :, :], in_=p.xhb[c][:, :]), 16)


def emit_phaseB(p):
    blk = p.blk
    hb = [p.sbS("B", f"hbB{i}", [128, TSL], BF16) for i in range(4)]
    vst = [p.sbS("B", f"vstB{i}", [128, C], BF16) for i in range(8)]
    rsc = p.sbS("B", "rscB", [128, TSL], BF16)
    rsc2 = p.sbS("B", "rsc2B", [64, TSL], BF16)
    cT, cV, cG = p.cT, p.cV, p.cG

    for sl in range(NSL):
        ssl = slice(sl * TSL, (sl + 1) * TSL)
        for fc in range(16):
            k = sl * 16 + fc
            par = k % 2

            @blk.tensor
            def _(t, sl=sl, fc=fc, k=k, par=par, ssl=ssl):
                if k == 0:
                    g_wait(p, t, "A_done")
                if k >= 2:
                    g_wait(p, t, ("B_rope", k - 2))
                bk = p.bank(par)
                for c in range(NCH):
                    ins = t.matmul(bk, p.w_inT[c][:, fc * 128:(fc + 1) * 128],
                                   p.xhb[c][:, ssl], start=(c == 0), stop=(c == NCH - 1))
                cT.inc(ins, 1)
            g_set(p, ("B_qk", k), cT)

            @blk.vector
            def _(v, sl=sl, fc=fc, k=k, par=par, ssl=ssl):
                g_wait(p, v, ("B_qk", k))
                if k == 0:
                    g_wait(p, v, "constsW")
                if k >= 2:
                    g_wait(p, v, ("B_hspill", k - 2))
                bk = p.bank(par)
                A, B = hb[par * 2], hb[par * 2 + 1]
                VI(p, v.tensor_copy(A[64:128, :], bk[0:64, :]))
                VI(p, v.tensor_copy(B[64:128, :], bk[64:128, :]))
                for X in (A, B):
                    VW(p, v)
                    VI(p, v.tensor_copy(rsc[0:64, :], X[64:128, :]))
                    VW(p, v)
                    VI(p, v.tensor_scalar(rsc2[0:32, :], rsc[32:64, :], -1.0, None, op0=OP.mult))
                    VI(p, v.tensor_copy(rsc2[32:64, :], rsc[0:32, :]))
                    VW(p, v)
                    VI(p, v.tensor_tensor(X[0:64, :], rsc[0:64, :], p.rp_cos[0:64, ssl], op=OP.mult))
                    VI(p, v.tensor_tensor(rsc2[0:64, :], rsc2[0:64, :], p.rp_sin[0:64, ssl], op=OP.mult))
                    VW(p, v)
                    VI(p, v.tensor_tensor(X[0:64, :], X[0:64, :], rsc2[0:64, :], op=OP.add))
            g_set(p, ("B_rope", k), p.cVc)

            @blk.gpsimd
            def _(g, sl=sl, fc=fc, k=k, par=par, ssl=ssl):
                g_wait(p, g, ("B_rope", k))
                cs = p.cD[par]
                A, B = hb[par * 2], hb[par * 2 + 1]
                tgt = p.scr_hbq if fc < 8 else p.scr_hbk
                h2 = (fc % 8) * 2
                cs.inc(g.dma_start(out=tgt[h2, :, ssl], in_=A[:, :]), 16)
                cs.inc(g.dma_start(out=tgt[h2 + 1, :, ssl], in_=B[:, :]), 16)
            g_set(p, ("B_hspill", k), p.cD[par])

        for tsub in range(4):
            kv = sl * 4 + tsub

            @blk.tensor
            def _(t, sl=sl, tsub=tsub, kv=kv, ssl=ssl):
                if kv >= 2:
                    g_wait(p, t, ("B_vev", kv - 2))
                for nn in range(2):
                    bk = p.bank(2 + (kv % 2) * 2 + nn)
                    for c in range(NCH):
                        ins = t.matmul(
                            bk,
                            p.xhb[c][:, sl * TSL + tsub * 128: sl * TSL + (tsub + 1) * 128],
                            p.w_inT[c][:, 2048 + nn * 512: 2048 + (nn + 1) * 512],
                            start=(c == 0), stop=(c == NCH - 1))
                cT.inc(ins, 1)
            g_set(p, ("B_v", kv), cT)

            @blk.vector
            def _(v, sl=sl, tsub=tsub, kv=kv):
                g_wait(p, v, ("B_v", kv))
                if kv >= 2:
                    g_wait(p, v, ("B_vspill", kv - 2))
                dst = vst[(kv % 2) * 4 + tsub]
                VI(p, v.tensor_copy(dst[:, 0:512], p.bank(2 + (kv % 2) * 2)))
                VI(p, v.tensor_copy(dst[:, 512:1024], p.bank(3 + (kv % 2) * 2)))
            g_set(p, ("B_vev", kv), p.cVc)

            @blk.gpsimd
            def _(g, sl=sl, tsub=tsub, kv=kv):
                g_wait(p, g, ("B_vev", kv))
                cs = p.cD[2 + kv % 2]
                src = vst[(kv % 2) * 4 + tsub]
                for pr in range(8):
                    cs.inc(g.dma_start(out=p.scr_v[pr, :, kv * 128:(kv + 1) * 128],
                                       in_=src[:, pr * 128:(pr + 1) * 128]), 16)
            g_set(p, ("B_vspill", kv), p.cD[2 + kv % 2])
    for i in range(4):
        g_set(p, f"B_done{i}", p.cD[i])
    g_set(p, "B_done", p.cVc)
    p.close_scope("B")
    p.close_scope("AB")

    if "d_hbq" in p.dbg:
        @blk.sync
        def _(s):
            barrier(p, s)
            for h in range(AH):
                p.cO.inc(s.dma_start(out=p.dbg["d_hbq"][h, :, :], in_=p.scr_hbq[h, :, :]), 16)
                p.cO.inc(s.dma_start(out=p.dbg["d_hbk"][h, :, :], in_=p.scr_hbk[h, :, :]), 16)
            for i in range(T // 128):
                p.cO.inc(s.dma_start(out=p.dbg["d_v"][i, :, :], in_=p.scr_v[i, :, :]), 16)


def emit_attention(p):
    blk = p.blk
    cT, cG, cA, cB = p.cT, p.cG, p.cA, p.cB
    cSL, cRX = p.cSL, p.cRX

    p.ybuf = [p.sbS("SY", f"ybuf{i}", [128, T], BF16) for i in range(8)]
    hq = [p.sbS("S", f"hqS{i}", [128, T], BF16) for i in range(4)]
    hk = [p.sbS("S", f"hkS{i}", [128, T], BF16) for i in range(4)]
    vpair = [p.sbS("S", f"vpairS{i}", [128, T], BF16) for i in range(2)]
    attb = [p.sbS("S", f"attbS{i}", [128, 1024], BF16) for i in range(4)]
    ytmp = p.sbS("S", "ytmpS", [128, TSL])
    denp = [p.sbS("S", f"denpS{i}", [128, 128]) for i in range(2)]
    den = [p.sbS("S", f"denS{i}", [128, 32]) for i in range(2)]
    invd = [p.sbS("S", f"invdS{i}", [128, 32]) for i in range(2)]
    dnstg = [p.sbS("S", f"dnstgS{i}", [32, 128]) for i in range(2)]
    invsl = [p.sbS("S", f"invslS{i}", [1, TSL]) for i in range(2)]

    g = [0]
    p.slot_pair = {}
    pair_last = {}

    gp = [0]

    def next_slot(t, pair=None, plane=False):
        if plane:
            gp[0] += 1
            ordv, ctr = gp[0], cRX
            if pair is None:
                pair = gp[0] % 2
        else:
            g[0] += 1
            ordv, ctr = g[0], cSL
            if pair is None:
                pair = g[0] % 2
        last = pair_last.get(pair)
        if last is not None:
            t.wait_ge(last[0].h, last[1])
        pair_last[pair] = (ctr, ordv)
        if not plane:
            p.slot_pair[g[0]] = pair
        return p.bank(pair * 2, 2), ordv, pair

    for hp in range(8):
        h0, h1 = 2 * hp, 2 * hp + 1

        @blk.sync
        def _(s, hp=hp, h0=h0, h1=h1):
            if hp == 0:
                for i in range(4):
                    g_wait(p, s, f"B_done{i}")
            if hp >= 2:
                g_wait(p, s, ("S_pair_done", hp - 2))
            cs = p.cE[hp % 2]
            par = (hp % 2) * 2
            cs.inc(s.dma_start(out=hq[par][:, :], in_=p.scr_hbq[h0, :, :]), 16)
            cs.inc(s.dma_start(out=hq[par + 1][:, :], in_=p.scr_hbq[h1, :, :]), 16)
            cs.inc(s.dma_start(out=hk[par][:, :], in_=p.scr_hbk[h0, :, :]), 16)
            cs.inc(s.dma_start(out=hk[par + 1][:, :], in_=p.scr_hbk[h1, :, :]), 16)
            cs.inc(s.dma_start(out=vpair[hp % 2][:, :], in_=p.scr_v[hp, :, :]), 16)
        g_set(p, ("S_ld", hp), p.cE[hp % 2])

        slot0 = {}
        for hh in range(2):
            Q = hq[(hp % 2) * 2 + hh]
            K = hk[(hp % 2) * 2 + hh]

            @blk.tensor
            def _(t, hp=hp, hh=hh, Q=Q, K=K):
                if hh == 0:
                    g_wait(p, t, ("S_ld", hp))
                    if hp > 0:
                        g_wait(p, t, ("S_ysc", (hp - 1, 7)))
                else:
                    g_wait(p, t, ("S_dnev", (hp, 0)))
                for qt in range(32):
                    for ks in range(4):
                        bk, _, _ = next_slot(t, pair=g[0] % 3)
                        if qt == 0 and ks == 0:
                            slot0[hh] = g[0]
                        for j in range(2):
                            ins = t.matmul(
                                bk[:, j * 512:(j + 1) * 512],
                                Q[64:128, qt * 128:(qt + 1) * 128],
                                K[64:128, ks * 1024 + j * 512: ks * 1024 + (j + 1) * 512],
                                start=True, stop=True)
                        cT.inc(ins, 1)
            g_set(p, ("S_pmm", (hp, hh)), cT)

            @blk.scalar
            def _(s, hp=hp, hh=hh):
                base_t = p.gates[("S_pmm", (hp, hh))][1] - 128
                g0 = slot0[hh]
                if hh == 0 and hp >= 2:
                    g_wait(p, s, ("S_invd", (hp - 2, 1)))
                for i in range(128):
                    if i % 2 == 0:
                        s.wait_ge(cT.h, base_t + min(i + 2, 128))
                    gg = g0 + i
                    bk = p.bank(p.slot_pair[gg] * 2, 2)
                    cSL.inc(s.activation(bk, bk, AF.Exp, scale=0.125,
                                         accum_out=denp[hh][:, i:i + 1]), 1)
            g_set(p, ("S_pexp", (hp, hh)), cSL)

            @blk.vector
            def _(v, hp=hp, hh=hh):
                g_wait(p, v, ("S_pexp", (hp, hh)))
                VW(p, v)
                VI(p, v.tensor_tensor(den[hh][:, :], denp[hh][:, 0:128:4],
                                      denp[hh][:, 1:128:4], op=OP.add))
                VI(p, v.tensor_tensor(invd[hh][:, :], denp[hh][:, 2:128:4],
                                      denp[hh][:, 3:128:4], op=OP.add))
                VW(p, v)
                VI(p, v.tensor_tensor(den[hh][:, :], den[hh][:, :], invd[hh][:, :], op=OP.add))
                VW(p, v)
                VI(p, v.reciprocal(invd[hh][:, :], den[hh][:, :]))
            g_set(p, ("S_invd", (hp, hh)), p.cVc)

            @blk.tensor
            def _(t, hp=hp, hh=hh):
                g_wait(p, t, ("S_invd", (hp, hh)))
                bk, _, _ = next_slot(t)
                p.dnT_slot[(hp, hh)] = g[0]
                cT.inc(t.transpose(bk[0:32, 0:128], invd[hh][:, :], p.ident_f[:, :]), 1)
            g_set(p, ("S_dnT", (hp, hh)), cT)

            @blk.vector
            def _(v, hp=hp, hh=hh):
                g_wait(p, v, ("S_dnT", (hp, hh)))
                gg = p.dnT_slot[(hp, hh)]
                bk = p.bank(p.slot_pair[gg] * 2, 2)
                cSL.inc(v.tensor_copy(dnstg[hh][:, :], bk[0:32, 0:128]), 1)
            g_set(p, ("S_dnev", (hp, hh)), cSL)

            @blk.gpsimd
            def _(gp, hp=hp, hh=hh):
                g_wait(p, gp, ("S_dnev", (hp, hh)))
                cbs = [cB, p.cD[4], p.cD[5], p.cD[0]][(hp % 2) * 2 + hh]
                cbs.inc(gp.dma_start(out=p.scr_dnb[2 * hp + hh, :, :], in_=dnstg[hh][:, :]), 16)
            g_set(p, ("S_bounceout", (hp, hh)),
                  [cB, p.cD[4], p.cD[5], p.cD[0]][(hp % 2) * 2 + hh])

        for qb in range(8):
            qsl = slice(qb * 512, (qb + 1) * 512)

            @blk.sync
            def _(s, hp=hp, qb=qb):
                if qb > 0:
                    g_wait(p, s, ("S_plane", (hp, qb - 1)))
                elif hp > 0:
                    g_wait(p, s, ("S_plane", (hp - 1, 7)))
                for hh in range(2):
                    g_wait(p, s, ("S_bounceout", (hp, hh)))
                    rowv = p.scr_dnb[2 * hp + hh].rearrange("a b -> (a b)")
                    p.cE[2].inc(s.dma_start(out=invsl[hh][:, :],
                                in_=rowv[None, qb * 512:(qb + 1) * 512]), 16)
            g_set(p, ("S_invsl", (hp, qb)), p.cE[2])

            slot_of = {}

            @blk.tensor
            def _(t, hp=hp, qb=qb, qsl=qsl):
                K0, K1 = hk[(hp % 2) * 2], hk[(hp % 2) * 2 + 1]
                Q0, Q1 = hq[(hp % 2) * 2], hq[(hp % 2) * 2 + 1]
                V = vpair[hp % 2]
                if qb == 0:
                    g_wait(p, t, ("S_dnev", (hp, 1)))
                    if hp > 0:
                        g_wait(p, t, ("S_yev", (hp - 1, 7)))
                for kt in range(33):
                    if kt < 32:
                        bk, _, _ = next_slot(t)
                        slot_of[kt] = g[0]
                        t.matmul(bk[:, 0:512], K0[0:64, kt * 128:(kt + 1) * 128],
                                 Q0[0:64, qsl], start=True, stop=True)
                        ins = t.matmul(bk[:, 512:1024], K1[0:64, kt * 128:(kt + 1) * 128],
                                       Q1[0:64, qsl], start=True, stop=True)
                        cT.inc(ins, 1)
                    if kt >= 1:
                        k2 = kt - 1
                        t.wait_ge(cSL.h, slot_of[k2])
                        if k2 == 0 and (qb >= 2 or (hp > 0 and qb < 2)):
                            prev = (hp, qb - 2) if qb >= 2 else (hp - 1, 6 + qb)
                            g_wait(p, t, ("S_yev", prev))
                        avb = p.bank(4 + qb % 2)
                        att = attb[k2 % 4]
                        t.matmul(avb[0:64, :], V[:, k2 * 128: k2 * 128 + 64],
                                 att[:, 0:512], start=(k2 == 0), stop=(k2 == 31),
                                 tile_position=(0, 0), skip_group_check=True)
                        ins = t.matmul(avb[64:128, :], V[:, k2 * 128 + 64: k2 * 128 + 128],
                                       att[:, 512:1024], start=(k2 == 0), stop=(k2 == 31),
                                       tile_position=(0, 64), skip_group_check=True)
                        cT.inc(ins, 1)
                        g_set(p, ("S_avk", (hp, qb, k2)), cT)
            g_set(p, ("S_av", (hp, qb)), cT)

            @blk.scalar
            def _(s, hp=hp, qb=qb):
                base_t = p.gates[("S_av", (hp, qb))][1] - 64
                for kt in range(32):
                    nmm = base_t + (1 if kt == 0 else 2 * kt)
                    s.wait_ge(cT.h, nmm)
                    if kt >= 4:
                        g_wait(p, s, ("S_avk", (hp, qb, kt - 4)))
                    elif qb > 0:
                        g_wait(p, s, ("S_avk", (hp, qb - 1, 28 + kt)))
                    elif hp > 0:
                        g_wait(p, s, ("S_avk", (hp - 1, 7, 28 + kt)))
                    gg = slot_of[kt]
                    bk = p.bank(p.slot_pair[gg] * 2, 2)
                    cSL.inc(s.activation(attb[kt % 4][:, :], bk, AF.Exp, scale=0.125), 1)
            g_set(p, ("S_rexp", (hp, qb)), cSL)

            @blk.tensor
            def _(t, hp=hp, qb=qb):
                g_wait(p, t, ("S_invsl", (hp, qb)))
                bk, ordv, pr = next_slot(t, plane=True)
                p.plane_slot[(hp, qb)] = (ordv, pr)
                t.matmul(bk[:, 0:512], p.ones_rf[:, :], invsl[0][:, :], start=True, stop=True)
                ins = t.matmul(bk[:, 512:1024], p.ones_rf[:, :], invsl[1][:, :],
                               start=True, stop=True)
                cT.inc(ins, 1)
            g_set(p, ("S_plane", (hp, qb)), cT)

            @blk.vector
            def _(v, hp=hp, qb=qb, qsl=qsl):
                g_wait(p, v, ("S_av", (hp, qb)))
                avb = p.bank(4 + qb % 2)
                VI(p, v.tensor_copy(ytmp[:, :], avb))
                g_set(p, ("S_yev", (hp, qb)), p.cVc)
                VW(p, v)
                g_wait(p, v, ("S_plane", (hp, qb)))
                gg, pr = p.plane_slot[(hp, qb)]
                bk = p.bank(pr * 2, 2)
                VI(p, v.tensor_tensor(p.ybuf[hp][0:64, qsl], ytmp[0:64, :],
                                      bk[0:64, 0:512], op=OP.mult))
                cRX.inc(v.tensor_tensor(p.ybuf[hp][64:128, qsl], ytmp[64:128, :],
                                        bk[64:128, 512:1024], op=OP.mult), 1)
            g_set(p, ("S_ysc", (hp, qb)), cRX)
        g_set(p, ("S_pair_done", hp), cRX)
    g_set(p, "S_done", cRX)
    g_set(p, "S_doneB", cSL)
    p.close_scope("S")

    if "d_ybuf" in p.dbg:
        @blk.sync
        def _(s):
            barrier(p, s)
            for i in range(8):
                p.cO.inc(s.dma_start(out=p.dbg["d_ybuf"][i, :, :], in_=p.ybuf[i][:, :]), 16)
    if "d_invd" in p.dbg:
        @blk.sync
        def _(s):
            barrier(p, s)
            for h in range(AH):
                p.cO.inc(s.dma_start(out=p.dbg["d_invd"][h, :, :], in_=p.scr_dnb[h, :, :]), 16)


def emit_wout_lnchain(p):
    blk = p.blk
    cT, cA, cG = p.cT, p.cA, p.cG
    p.rowA = [p.sbS("LN", "rowA_rstd", [1, 2 * TSL], side="right"),
              p.sbS("LN", "rowA_mr", [1, 2 * TSL], side="right")]
    p.rowB = [p.sbS("LN", "rowB_rstd", [1, 2 * TSL], side="right"),
              p.sbS("LN", "rowB_mr", [1, 2 * TSL], side="right")]
    p.w_outT = [p.sbS("SO", f"w_outT{c}", [128, C], BF16) for c in range(NCH)]
    wstg = [p.sbS("SO", f"wostg{i}", [128, C], BF16) for i in range(2)]
    main2 = [[p.sbS("SO", f"mainW{j}_{c}", [128, TSL]) for c in range(NCH)]
             for j in range(2)]
    aux2 = [[p.sbS("SO", f"auxW{j}_{c}", [128, TSL]) for c in range(NCH)]
            for j in range(2)]
    xb = [p.sbS("SO", f"xbW{c}", [128, TSL], BF16) for c in range(NCH)]
    sq = [p.sbS("SO", f"sqW{c}", [128, TSL], BF16) for c in range(NCH)]

    for ch in range(8):
        @blk.gpsimd
        def _(g, ch=ch):
            if ch == 0:
                g_wait(p, g, "S_done")
                g_wait(p, g, "S_doneB")
            if ch >= 2:
                g_wait(p, g, ("WO_tr", ch - 2))
            p.cD[ch % 2].inc(g.dma_start(out=wstg[ch % 2][:, :],
                                         in_=p.w_outI[ch * 128:(ch + 1) * 128, :]), 16)
        g_set(p, ("WO_ld", ch), p.cD[ch % 2])

        for c in range(NCH):
            k = ch * 8 + c

            @blk.tensor
            def _(t, ch=ch, c=c, k=k):
                if c == 0:
                    g_wait(p, t, ("WO_ld", ch))
                if k == 0:
                    g_wait(p, t, "S_done")
                    g_wait(p, t, "S_doneB")
                if k >= 2:
                    g_wait(p, t, ("WO_evc", k - 2))
                cT.inc(t.transpose(p.bslot(k), wstg[ch % 2][:, c * 128:(c + 1) * 128],
                                   p.ident_b[:, :]), 1)
            g_set(p, ("WO_trk", k), cT)

            @blk.vector
            def _(v, ch=ch, c=c, k=k):
                g_wait(p, v, ("WO_trk", k))
                VI(p, v.tensor_copy(p.w_outT[c][:, ch * 128:(ch + 1) * 128], p.bslot(k)))
            g_set(p, ("WO_evc", k), p.cVc)
        g_set(p, ("WO_tr", ch), p.cVc)
    g_set(p, "WO_done", p.cVc)

    stages = ["2", "3", "4"]
    rows_of = {"2": p.rowA, "3": p.rowB, "4": p.rowA}

    for sl in range(NSL):
        ssl = slice(sl * TSL, (sl + 1) * TSL)
        rsl = slice((sl % 2) * TSL, (sl % 2 + 1) * TSL)
        main = main2[sl % 2]
        aux = aux2[sl % 2]

        @blk.sync
        def _(s, sl=sl, ssl=ssl, main=main):
            if sl == 0:
                g_wait(p, s, "S_done")
                g_wait(p, s, "S_doneB")
            if sl >= 2:
                g_wait(p, s, ("W_x1sp", sl - 2))
                g_wait(p, s, ("W_apl", (sl - 2, "4")))
            cs = [cA, p.cE[0]][sl % 2]
            for c in range(NCH):
                cs.inc(s.dma_start(out=main[c][:, :], in_=p.scr_xh[c, :, ssl]), 16)
        g_set(p, ("W_ldm", sl), [cA, p.cE[0]][sl % 2])

        @blk.scalar
        def _(s, sl=sl, ssl=ssl, aux=aux):
            if sl >= 2:
                g_wait(p, s, ("W_fasp", sl - 2))
            cs = [p.cE[1], p.cE[2]][sl % 2]
            for c in range(NCH):
                cs.inc(s.dma_start(out=aux[c][:, :], in_=p.scr_x0[c, :, ssl]), 16)
        g_set(p, ("W_ldx", sl), [p.cE[1], p.cE[2]][sl % 2])

        for oc in range(NCH):
            k = sl * 8 + oc

            @blk.tensor
            def _(t, sl=sl, oc=oc, k=k, ssl=ssl):
                if sl == 0 and oc == 0:
                    g_wait(p, t, "WO_done")
                if k >= 2:
                    g_wait(p, t, ("W_hev", k - 2))
                bk = p.bank(k % 2)
                for c in range(NCH):
                    ins = t.matmul(bk, p.w_outT[c][:, oc * 128:(oc + 1) * 128],
                                   p.ybuf[c][:, ssl], start=(c == 0), stop=(c == NCH - 1))
                cT.inc(ins, 1)
            g_set(p, ("W_mm", k), cT)

            @blk.vector
            def _(v, sl=sl, oc=oc, k=k, main=main):
                g_wait(p, v, ("W_mm", k))
                if oc == 0:
                    g_wait(p, v, ("W_ldm", sl))
                VI(p, v.tensor_tensor(main[oc][:, :], p.bank(k % 2), main[oc][:, :],
                                      op=OP.add))
            g_set(p, ("W_hev", k), p.cVc)

        for si, st in enumerate(stages):
            @blk.vector
            def _(v, sl=sl, st=st, si=si, main=main):
                VW(p, v)
                prev = (sl, stages[si - 1]) if si > 0 else \
                    ((sl - 1, "4") if sl > 0 else None)
                if prev is not None:
                    g_wait(p, v, ("W_st", prev))
                for c in range(NCH):
                    VI(p, v.tensor_copy(xb[c][:, :], main[c][:, :]))
                    VW(p, v)
                    VI(p, v.tensor_tensor(sq[c][:, :], xb[c][:, :], xb[c][:, :],
                                          op=OP.mult))
            g_set(p, ("W_xb", (sl, st)), p.cVc)

            @blk.tensor
            def _(t, sl=sl, st=st, si=si):
                g_wait(p, t, ("W_xb", (sl, st)))
                prev = (sl, stages[si - 1]) if si > 0 else \
                    ((sl - 1, "4") if sl > 0 else None)
                if prev is not None:
                    g_wait(p, t, (("W" + prev[1], prev[0]), "rows"))
                for c in range(NCH):
                    t.matmul(p.bank(2)[0:1, :], p.inv_c[:, :], xb[c][:, :],
                             start=(c == 0), stop=(c == NCH - 1))
                for c in range(NCH):
                    ins = t.matmul(p.bank(3)[0:1, :], p.inv_c[:, :], sq[c][:, :],
                                   start=(c == 0), stop=(c == NCH - 1))
                cT.inc(ins, 1)
            g_set(p, ("W_st", (sl, st)), cT)
            g_set(p, ("W" + st + "_stats", sl), cT)
            extra = []
            if st == "4":
                extra.append(("W_pl", (sl, "2")))
            if sl >= 2:
                extra.append(("W_pl", (sl - 2, "4" if st == "2" else st)))
            emit_row_finish(p, ("W" + st, sl), rows_of[st], rsl, extra_scalar=extra)

            @blk.tensor
            def _(t, sl=sl, st=st, si=si, ssl=ssl):
                g_wait(p, t, (("W" + st, sl), "rows"))
                prevu = (sl, stages[si - 1]) if si > 0 else \
                    ((sl - 1, "4") if sl > 0 else None)
                if prevu is not None:
                    g_wait(p, t, ("W_apl", prevu))
                rows = rows_of[st]
                rsl = slice((sl % 2) * TSL, (sl % 2 + 1) * TSL)
                t.matmul(p.bank(4), p.ones_rf[:, :], rows[0][0:1, rsl],
                         start=True, stop=True)
                ins = t.matmul(p.bank(5), p.ones_rf[:, :], rows[1][0:1, rsl],
                               start=True, stop=True)
                cT.inc(ins, 1)
            g_set(p, ("W_pl", (sl, st)), cT)

            @blk.vector
            def _(v, sl=sl, st=st, main=main, aux=aux):
                g_wait(p, v, ("W_pl", (sl, st)))
                if st == "3":
                    g_wait(p, v, ("W_ldx", sl))
                for c in range(NCH):
                    if st == "4":
                        VI(p, v.tensor_tensor(aux[c][:, :], main[c][:, :], p.bank(4),
                                              op=OP.mult))
                        VW(p, v)
                        VI(p, v.tensor_tensor(aux[c][:, :], aux[c][:, :], p.bank(5),
                                              op=OP.add))
                        continue
                    VI(p, v.tensor_tensor(main[c][:, :], main[c][:, :], p.bank(4),
                                          op=OP.mult))
                    VW(p, v)
                    VI(p, v.tensor_tensor(main[c][:, :], main[c][:, :], p.bank(5),
                                          op=OP.add))
                    VW(p, v)
                    if st == "2":
                        VI(p, v.tensor_scalar(main[c][:, :], main[c][:, :],
                                              p.gam[:, c:c + 1], p.bet[:, c:c + 1],
                                              op0=OP.mult, op1=OP.add))
                    else:
                        VI(p, v.tensor_tensor(main[c][:, :], main[c][:, :],
                                              aux[c][:, :], op=OP.add))
            g_set(p, ("W_apl", (sl, st)), p.cVc)

            if st == "3":
                @blk.gpsimd
                def _(g, sl=sl, ssl=ssl, main=main):
                    g_wait(p, g, ("W_apl", (sl, "3")))
                    cs = [p.cD[2], p.cD[3]][sl % 2]
                    for c in range(NCH):
                        cs.inc(g.dma_start(out=p.scr_x1[c, :, ssl],
                                           in_=main[c][:, :]), 16)
                g_set(p, ("W_x1sp", sl), [p.cD[2], p.cD[3]][sl % 2])

        @blk.gpsimd
        def _(g, sl=sl, ssl=ssl, aux=aux):
            g_wait(p, g, ("W_apl", (sl, "4")))
            cs = [cG, p.cD[1]][sl % 2]
            for c in range(NCH):
                cs.inc(g.dma_start(out=p.scr_fA[c, :, ssl], in_=aux[c][:, :]), 16)
        g_set(p, ("W_fasp", sl), [cG, p.cD[1]][sl % 2])

    g_set(p, "LC_done", cG)
    g_set(p, "LC_done_odd", p.cD[1])
    g_set(p, "X1_done_e", p.cD[2])
    g_set(p, "X1_done_o", p.cD[3])
    p.close_scope("SO")
    p.close_scope("SY")
    p.close_scope("LN")

    if "d_xf" in p.dbg:
        @blk.sync
        def _(s):
            barrier(p, s)
            for c in range(NCH):
                p.cO.inc(s.dma_start(out=p.dbg["d_xf"][c, :, :], in_=p.scr_fA[c, :, :]), 16)
            for c in range(NCH):
                p.cO.inc(s.dma_start(out=p.dbg["d_x1"][c, :, :], in_=p.scr_x1[c, :, :]), 16)
        g_set(p, "DXF_dump", p.cO)


def emit_row_finish(p, tag, rows, ssl, extra_scalar=()):
    blk = p.blk

    @blk.vector
    def _(v):
        g_wait(p, v, (tag[0] + "_stats", tag[1]) if isinstance(tag, tuple) else tag)
        VI(p, v.tensor_copy(p.srow[0][:, :], p.bank(2)[0:1, :]))
        VW(p, v)
        VI(p, v.tensor_tensor(p.srow[1][:, :], p.srow[0][:, :], p.srow[0][:, :], op=OP.mult))
        VW(p, v)
        VI(p, v.tensor_tensor(p.srow[1][:, :], p.bank(3)[0:1, :], p.srow[1][:, :],
                              op=OP.subtract))
    g_set(p, (tag, "var"), p.cVc)

    @blk.scalar
    def _(s):
        g_wait(p, s, (tag, "var"))
        for eg in extra_scalar:
            g_wait(p, s, eg)
        SI(p, s.activation(p.srow[1][:, :], p.srow[1][:, :], AF.Ln, bias=p.eps_col[0:1, :]))
        SW(p, s)
        SI(p, s.activation(rows[0][0:1, ssl], p.srow[1][:, :], AF.Exp, scale=-0.5))
    g_set(p, (tag, "rstd"), p.cSc)

    @blk.vector
    def _(v):
        g_wait(p, v, (tag, "rstd"))
        VI(p, v.scalar_tensor_tensor(rows[1][0:1, ssl], p.srow[0][:, :], -1.0,
                                     rows[0][0:1, ssl], op0=OP.mult, op1=OP.mult))
    g_set(p, (tag, "rows"), p.cVc)


def emit_ffn_layer(p, l):
    blk = p.blk
    cT, cA, cG, cB = p.cT, p.cA, p.cG, p.cB
    L = f"L{l}_"
    vis_k = lambda w: list(range(max(0, 4 * w - 4), 4 * w + 4))

    qT = [p.sbS(L + "B1", L + f"qT{i}", [128, T], BF16) for i in range(4)]
    kT = [p.sbS(L + "B1", L + f"kT{i}", [128, T], BF16) for i in range(4)]
    vtm = [p.sbS(L + "B1", L + f"vtm{i}", [128, 520], BF16) for i in range(32)]

    wq = [p.sbS(L + "A", L + f"wq{c}", [128, 1536], BF16) for c in range(NCH)]
    fin13 = [p.sbS(L + "A", L + f"fin13{c}", [128, 513]) for c in range(NCH)]
    finb2 = [[p.sbS(L + "A", L + f"finb{j}_{c}", [128, 513], BF16) for c in range(NCH)]
             for j in range(2)]
    xps = [p.sbS(L + "A", L + f"xps{i}", [64, TSL], BF16) for i in range(4)]
    l2a8 = [p.sbS(L + "A", L + f"l2a{i}", [64, TSL], BF16) for i in range(8)]
    sqf = [p.sbS(L + "A", L + f"sqf{i}", [128, TSL], BF16) for i in range(4)]
    srw = [p.sbS(L + "A", L + f"srw{i}", [2, TSL]) for i in range(4)]
    l2r = p.sbS(L + "A", L + "l2r", [64, TSL], BF16)
    l2t = p.sbS(L + "A", L + "l2t", [64, TSL], BF16)
    qscol = p.sbS(L + "A", L + "qscol", [128, 1])
    kscol = p.sbS(L + "A", L + "kscol", [128, 1])
    prev_done = "LC_done" if l == 0 else "L0_F4_done"

    @blk.sync
    def _(s):
        g_wait(p, s, prev_done)
        if l == 0:
            g_wait(p, s, "LC_done_odd")
            g_wait(p, s, "X1_done_e")
            g_wait(p, s, "X1_done_o")
        p.cE[3].inc(s.dma_start(out=qscol[0:64, :], in_=p.qscI[l][:, :]), 16)
        p.cE[3].inc(s.dma_start(out=qscol[64:128, :], in_=p.qscI[l][:, :]), 16)
        p.cE[3].inc(s.dma_start(out=kscol[0:64, :], in_=p.kscI[l][:, :]), 16)
        p.cE[3].inc(s.dma_start(out=kscol[64:128, :], in_=p.kscI[l][:, :]), 16)
    g_set(p, (L, "qsld"), p.cE[3])

    @blk.gpsimd
    def _(g):
        g_wait(p, g, prev_done)
        if l == 0:
            g_wait(p, g, "LC_done_odd")
            g_wait(p, g, "X1_done_e")
            g_wait(p, g, "X1_done_o")
        else:
            g_wait(p, g, "L0_F4_done_d1")
        for c in range(NCH):
            p.cD[1].inc(g.dma_start(out=wq[c][:, :],
                                    in_=p.wqkvI[l][c * 128:(c + 1) * 128, :]), 16)
        for i in range(32):
            ins = g.memset(vtm[i][:, :], 1.0)
        for j in range(2):
            for c in range(NCH):
                ins = g.memset(finb2[j][c][:, 0:1], 0.0)
        for c in range(NCH):
            ins = g.memset(fin13[c][:, 0:1], 0.0)
        p.cZ.inc(ins, 1)
    g_set(p, (L, "wqld"), p.cD[1])
    g_set(p, (L, "vtmz"), p.cZ)

    for sl in range(NSL):
        ssl = slice(sl * TSL, (sl + 1) * TSL)

        finb = finb2[sl % 2]

        @blk.sync
        def _(s, sl=sl):
            if sl == 0:
                g_wait(p, s, prev_done)
            if sl >= 1:
                g_wait(p, s, (L + "F1_cp", sl - 1))
            for c in range(NCH):
                if sl == 0:
                    p.cA.inc(s.dma_start(out=fin13[c][:, 1:513],
                                         in_=p.scr_fA[c, :, 0:512]), 16)
                else:
                    p.cA.inc(s.dma_start(out=fin13[c][:, 0:513],
                                         in_=p.scr_fA[c, :, sl * 512 - 1: (sl + 1) * 512]), 16)
        g_set(p, (L + "F1_ldf", sl), p.cA)

        @blk.vector
        def _(v, sl=sl, finb=finb):
            g_wait(p, v, (L + "F1_ldf", sl))
            if sl == 0:
                g_wait(p, v, (L, "vtmz"))
            if sl >= 2:
                g_wait(p, v, (L + "F1_vdone", sl - 2))
            for c in range(NCH):
                VI(p, v.tensor_copy(finb[c][:, :], fin13[c][:, :]))
        g_set(p, (L + "F1_cp", sl), p.cVc)
        g_set(p, (L + "F1_ld", sl), p.cVc)

        @blk.sync
        def _(s, sl=sl):
            if sl >= 1:
                g_wait(p, s, (L + "F1_xpos", sl * 16 - 1))
            for i, nm in enumerate(["xp_qc", "xp_qs", "xp_kc", "xp_ks"]):
                p.cE[1].inc(s.dma_start(out=xps[i][:, :],
                                        in_=p.xpI[nm][:, sl * TSL:(sl + 1) * TSL]), 16)
        g_set(p, (L + "F1_xpld", sl), p.cE[1])

        def rhsv(c, w=TSL, t0=0):
            off = (1 if c < 4 else 0) + t0
            return finb[c][:, off:off + w]

        for fc in range(8):
            @blk.tensor
            def _(t, sl=sl, fc=fc):
                k = sl * 8 + fc
                if k == 0:
                    g_wait(p, t, (L, "wqld"))
                    g_wait(p, t, "LC_done" if l == 0 else "L0_F4_done")
                g_wait(p, t, (L + "F1_ld", sl))
                if k >= 2:
                    g_wait(p, t, (L + "F1_qkev", k - 2))
                bk = p.bank(fc % 2)
                for c in range(NCH):
                    ins = t.matmul(bk, wq[c][:, fc * 128:(fc + 1) * 128],
                                   rhsv(c), start=(c == 0), stop=(c == NCH - 1))
                cT.inc(ins, 1)
            g_set(p, (L + "F1_qk", sl * 8 + fc), cT)

            @blk.vector
            def _(v, sl=sl, fc=fc, ssl=ssl):
                k = sl * 8 + fc
                g_wait(p, v, (L + "F1_qk", k))
                dst = qT[fc] if fc < 4 else kT[fc - 4]
                VI(p, v.tensor_copy(dst[:, ssl], p.bank(fc % 2)))
            g_set(p, (L + "F1_qkev", sl * 8 + fc), p.cVc)

        for tsub in range(4):
            @blk.tensor
            def _(t, sl=sl, tsub=tsub):
                kv = sl * 4 + tsub
                if kv >= 2:
                    g_wait(p, t, (L + "F1_vev", kv - 2))
                bk = p.bank(2 + kv % 2)
                for c in range(NCH):
                    ins = t.matmul(bk, rhsv(c, 128, tsub * 128),
                                   wq[c][:, 1024:1536], start=(c == 0), stop=(c == NCH - 1))
                cT.inc(ins, 1)
            g_set(p, (L + "F1_v", sl * 4 + tsub), cT)

            @blk.vector
            def _(v, sl=sl, tsub=tsub):
                kv = sl * 4 + tsub
                g_wait(p, v, (L + "F1_v", kv))
                bk = p.bank(2 + kv % 2)
                for h in range(FH):
                    VI(p, v.tensor_copy(vtm[kv][:, h * 65: h * 65 + 64],
                                        bk[:, h * 64:(h + 1) * 64]))
            g_set(p, (L + "F1_vev", sl * 4 + tsub), p.cVc)

        for wg in range(2):
            prs = [wg * 4 + j for j in range(4)]

            @blk.vector
            def _(v, sl=sl, wg=wg, prs=prs, ssl=ssl):
                if wg == 0:
                    g_wait(p, v, (L + "F1_qkev", sl * 8 + 7))
                    g_wait(p, v, (L + "F1_xpld", sl))
                    if sl == 0:
                        g_wait(p, v, (L, "qsld"))
                for j, pr in enumerate(prs):
                    src = qT[pr] if pr < 4 else kT[pr - 4]
                    scol = qscol if pr < 4 else kscol
                    if sl > 0 or wg > 0:
                        pw = (sl, wg - 1) if wg > 0 else (sl - 1, 1)
                        g_wait(p, v, (L + "F1_stmm", (pw[0], pw[1], j)))
                    VW(p, v)
                    VI(p, v.tensor_scalar(l2a8[2 * j][:, :], src[0:64, ssl],
                                          scol[0:64, :], None, op0=OP.mult))
                    VI(p, v.tensor_scalar(l2a8[2 * j + 1][:, :], src[64:128, ssl],
                                          scol[64:128, :], None, op0=OP.mult))
                    VW(p, v)
                    VI(p, v.tensor_tensor(sqf[j][0:64, :], l2a8[2 * j][:, :],
                                          l2a8[2 * j][:, :], op=OP.mult))
                    VI(p, v.tensor_tensor(sqf[j][64:128, :], l2a8[2 * j + 1][:, :],
                                          l2a8[2 * j + 1][:, :], op=OP.mult))
            g_set(p, (L + "F1_sqw", (sl, wg)), p.cVc)

            @blk.tensor
            def _(t, sl=sl, wg=wg, prs=prs):
                g_wait(p, t, (L + "F1_sqw", (sl, wg)))
                if l == 0 and sl == 0 and wg == 0:
                    g_wait(p, t, "consts")
                    g_wait(p, t, "constsW")
                for j, pr in enumerate(prs):
                    if sl > 0 or wg > 0:
                        pw = (sl, wg - 1) if wg > 0 else (sl - 1, 1)
                        g_wait(p, t, (L + "F1_xposw", pw))
                    cT.inc(t.matmul(p.bank(2 + j)[0:2, :], p.sel2[:, :],
                                    sqf[j][:, :], start=True, stop=True), 1)
                    g_set(p, (L + "F1_stmm", (sl, wg, j)), cT)

            @blk.scalar
            def _(s, sl=sl, wg=wg, prs=prs):
                for j, pr in enumerate(prs):
                    g_wait(p, s, (L + "F1_stmm", (sl, wg, j)))
                    if sl > 0 or wg > 0:
                        pw = (sl, wg - 1) if wg > 0 else (sl - 1, 1)
                        g_wait(p, s, (L + "F1_plw", pw))
                    SW(p, s)
                    SI(p, s.activation(srw[j][:, :], p.bank(2 + j)[0:2, :], AF.Ln,
                                       bias=p.epsl2_col[0:2, :]))
                    SW(p, s)
                    SI(p, s.activation(srw[j][:, :], srw[j][:, :], AF.Exp, scale=-0.5))
            g_set(p, (L + "F1_rstdw", (sl, wg)), p.cSc)

            @blk.tensor
            def _(t, sl=sl, wg=wg, prs=prs):
                g_wait(p, t, (L + "F1_rstdw", (sl, wg)))
                for j, pr in enumerate(prs):
                    cT.inc(t.matmul(p.bank(2 + j), p.brd2[:, :], srw[j][:, :],
                                    start=True, stop=True), 1)
            g_set(p, (L + "F1_plw", (sl, wg)), cT)

            @blk.vector
            def _(v, sl=sl, wg=wg, prs=prs, ssl=ssl):
                g_wait(p, v, (L + "F1_plw", (sl, wg)))
                for j, pr in enumerate(prs):
                    src = qT[pr] if pr < 4 else kT[pr - 4]
                    xc, xs = (xps[0], xps[1]) if pr < 4 else (xps[2], xps[3])
                    bk = p.bank(2 + j)
                    for h in range(2):
                        a = l2a8[2 * j + h]
                        r0 = h * 64
                        VW(p, v)
                        VI(p, v.tensor_scalar(l2r[0:32, :], a[32:64, :], -1.0, None,
                                              op0=OP.mult))
                        VI(p, v.tensor_copy(l2r[32:64, :], a[0:32, :]))
                        VW(p, v)
                        VI(p, v.tensor_tensor(l2t[:, :], a[:, :], xc[:, :], op=OP.mult))
                        VI(p, v.tensor_tensor(l2r[:, :], l2r[:, :], xs[:, :], op=OP.mult))
                        VW(p, v)
                        VI(p, v.tensor_tensor(l2t[:, :], l2t[:, :], l2r[:, :], op=OP.add))
                        VW(p, v)
                        VI(p, v.tensor_tensor(src[r0:r0 + 64, ssl], l2t[:, :],
                                              bk[r0:r0 + 64, :], op=OP.mult))
            g_set(p, (L + "F1_xposw", (sl, wg)), p.cVc)
        g_set(p, (L + "F1_xpos", sl * 16 + 15), p.cVc)
        @blk.vector
        def _(v, sl=sl):
            VW(p, v)
            VI(p, v.tensor_copy(l2t[0:1, 0:1], l2t[0:1, 0:1]))
        g_set(p, (L + "F1_vdone", sl), p.cVc)
    g_set(p, L + "F1_done", p.cVc)
    p.close_scope(L + "A")

    attf = [p.sbS(L + "C", L + f"attf{i}", [128, TSL], BF16) for i in range(4)]
    p.lmask_sb = [p.sbS(L + "C", L + f"lmask{i}", [128, TSL], BF16) for i in range(8)]
    negB = [p.sbS(L + "C", L + f"negB{i}", [1, T], BF16) for i in range(2)]
    Bbuf = [p.sbS(L + "C", L + f"Bbuf{i}", [128, 32]) for i in range(2)]
    bstg = [p.sbS(L + "C", L + f"bstg{i}", [32, 128]) for i in range(2)]
    yt65 = p.sbS(L + "C", L + "yt65", [128, TSL])
    invloc = p.sbS(L + "C", L + "invloc", [1, TSL])
    ostg = [p.sbS(L + "C", L + f"ostg{i}", [64, TSL], BF16) for i in range(2)]

    @blk.sync
    def _(s):
        g_wait(p, s, L + "F1_done")
        for i in range(8):
            p.cE[3].inc(s.dma_start(out=p.lmask_sb[i][:, :], in_=p.lmaskI[i, :, :]), 16)
    g_set(p, (L, "lmaskld"), p.cE[3])

    g2 = [p.cSL.n]
    f2slot = {}
    f2avval = {}
    Bbuf2 = [p.sbS(L + "C", L + f"Bbuf2_{i}", [128, 64]) for i in range(2)]

    F2B = [2, 3, 0]

    def nslot2(t):
        g2[0] += 1
        t.wait_ge(p.cSL.h, g2[0] - 3)
        return p.bank(F2B[g2[0] % 3])

    gp2 = [p.cRX.n]

    def nslotP(t):
        gp2[0] += 1
        if gp2[0] - 1 > 0:
            t.wait_ge(p.cRX.h, gp2[0] - 1)
        return p.bank(1)

    def run_P1(h):
        Qs, Ks, r0 = qT[h // 2], kT[h // 2], (h % 2) * 64
        for w in range(8):
            kk = vis_k(w)
            nb = len(kk) // 4
            slots_w = {}

            @blk.tensor
            def _(t, h=h, w=w, kk=kk, nb=nb, Qs=Qs, Ks=Ks, r0=r0, slots_w=slots_w):
                if h == 0 and w == 0:
                    g_wait(p, t, L + "F1_done")
                for qt in range(4):
                    for j in range(nb):
                        bk = nslot2(t)
                        slots_w[(qt, j)] = g2[0]
                        cT.inc(t.matmul(
                            bk,
                            Qs[r0:r0 + 64, w * 512 + qt * 128: w * 512 + (qt + 1) * 128],
                            Ks[r0:r0 + 64, kk[0] * 128 + j * 512: kk[0] * 128 + (j + 1) * 512],
                            start=True, stop=True), 1)
                        g_set(p, (L + "F2_p1mm", (h, w, qt, j)), cT)

            @blk.vector
            def _(v, h=h, w=w, nb=nb, slots_w=slots_w):
                for qt in range(4):
                    for j in range(nb):
                        g_wait(p, v, (L + "F2_p1mm", (h, w, qt, j)))
                        if w == 0 and qt == 0 and j == 0 and h >= 2:
                            g_wait(p, v, (L + "F2_bev", h - 2))
                        gg = slots_w[(qt, j)]
                        bk = p.bank(F2B[gg % 3])
                        dstcol = Bbuf2[h % 2][:, (w * 4 + qt) * 2 + j: (w * 4 + qt) * 2 + j + 1]
                        p.cSL.inc(v.reduce_max(dstcol, bk, axis=AX.X), 1)
                    if nb == 1:
                        v.wait_ge(p.cSL.h, p.cSL.n)
                        VI(p, v.tensor_copy(
                            Bbuf2[h % 2][:, (w * 4 + qt) * 2 + 1: (w * 4 + qt) * 2 + 2],
                            Bbuf2[h % 2][:, (w * 4 + qt) * 2: (w * 4 + qt) * 2 + 1]))
            g_set(p, (L + "F2_maxw", (h, w)), p.cSL)

    def emit_bfinish(h):
        @blk.vector
        def _(v, h=h):
            g_wait(p, v, (L + "F2_maxw", (h, 7)))
            VW(p, v)
            VI(p, v.tensor_tensor(Bbuf[h % 2][:, :], Bbuf2[h % 2][:, 0:64:2],
                                  Bbuf2[h % 2][:, 1:64:2], op=OP.max))
            VW(p, v)
            VI(p, v.tensor_scalar(Bbuf[h % 2][:, :], Bbuf[h % 2][:, :], -1.0, None,
                                  op0=OP.mult))
        g_set(p, (L + "F2_bneg", h), p.cVc)

        @blk.tensor
        def _(t, h=h):
            g_wait(p, t, (L + "F2_bneg", h))
            bk = nslotP(t)
            cT.inc(t.transpose(bk[0:32, 0:128], Bbuf[h % 2][:, :], p.ident_f[:, :]), 1)
        g_set(p, (L + "F2_bT", h), cT)

        @blk.vector
        def _(v, h=h):
            g_wait(p, v, (L + "F2_bT", h))
            bk = p.bank(1)
            p.cRX.inc(v.tensor_copy(bstg[h % 2][:, :], bk[0:32, 0:128]), 1)
        g_set(p, (L + "F2_bev", h), p.cRX)

        @blk.gpsimd
        def _(g, h=h):
            g_wait(p, g, (L + "F2_bev", h))
            cc = p.cD[2 + h % 2]
            cc.inc(g.dma_start(out=p.scr_bb[h, :, :], in_=bstg[h % 2][:, :]), 16)
            g.wait_ge(cc.h, cc.n)
            cc.inc(g.dma_start(
                out=negB[h % 2][0:1, :],
                in_=p.scr_bb[h].rearrange("a b -> (a b)")[None, :]), 16)
        g_set(p, (L + "F2_brow", h), p.cD[2 + h % 2])

    def emit_P2(h):
        Qs, Ks, r0 = qT[h // 2], kT[h // 2], (h % 2) * 64
        for w in range(8):
            kk = vis_k(w)
            nk = len(kk)

            @blk.tensor
            def _(t, h=h, w=w, kk=kk, nk=nk, Qs=Qs, Ks=Ks, r0=r0):
                if w == 0:
                    g_wait(p, t, (L + "F2_brow", h))
                    if h == 0:
                        g_wait(p, t, (L, "lmaskld"))
                    if h + 2 < FH:
                        g_wait(p, t, (L + "F2_maxw", (h + 2, 7)))
                for ki in range(nk + 1):
                    if ki < nk:
                        ktg = kk[ki]
                        bk = nslot2(t)
                        f2slot[(h, w, ki)] = g2[0]
                        t.matmul(bk, Ks[r0:r0 + 64, ktg * 128:(ktg + 1) * 128],
                                 Qs[r0:r0 + 64, w * 512:(w + 1) * 512],
                                 start=True, stop=False)
                        t.matmul(bk, p.ones_rb[0:1, :],
                                 negB[h % 2][0:1, w * 512:(w + 1) * 512],
                                 start=False, stop=False)
                        ridx = (ktg * 128 - w * 512 + 512) // 128
                        ins = t.matmul(bk, p.ident_b[:, :], p.lmask_sb[ridx][:, :],
                                       start=False, stop=True)
                        cT.inc(ins, 1)
                        g_set(p, (L + "F2_p2mm", (h, w, ki)), cT)
                    if ki >= 1:
                        k2 = ki - 1
                        t.wait_ge(p.cSL.h, f2slot[(h, w, k2)])
                        if k2 == 0 and not (h == 0 and w <= 1):
                            pw = (h, w - 2) if w >= 2 else (h - 1, 6 + w)
                            g_wait(p, t, (L + "F2_yev", pw))
                        avb = p.bank(4 + w % 2)
                        ins = t.matmul(avb[0:65, :], vtm[kk[k2]][:, h * 65: h * 65 + 65],
                                       attf[f2slot[(h, w, k2)] % 4][:, :],
                                       start=(k2 == 0), stop=(k2 == nk - 1))
                        cT.inc(ins, 1)
                        g_set(p, (L + "F2_avk", (h, w, k2)), cT)
            g_set(p, (L + "F2_av", (h, w)), cT)

            @blk.scalar
            def _(s, h=h, w=w, nk=nk):
                for ki in range(nk):
                    gg = f2slot[(h, w, ki)]
                    g_wait(p, s, (L + "F2_p2mm", (h, w, ki)))
                    pk = (h, w, ki - 4)
                    if ki < 4:
                        if w > 0:
                            nkp = len(vis_k(w - 1))
                            pk = (h, w - 1, nkp - 4 + ki)
                        elif h > 0:
                            pk = (h - 1, 7, 4 + ki)
                        else:
                            pk = None
                    if pk is not None:
                        g_wait(p, s, (L + "F2_avk", pk))
                    bk = p.bank(F2B[gg % 3])
                    p.cSL.inc(s.activation(attf[gg % 4][:, :], bk, AF.Exp, scale=8.0), 1)
            g_set(p, (L + "F2_exp", (h, w)), p.cSL)

            @blk.vector
            def _(v, h=h, w=w):
                g_wait(p, v, (L + "F2_av", (h, w)))
                if w > 0:
                    g_wait(p, v, (L + "F2_ysc", (h, w - 1)))
                elif h > 0:
                    g_wait(p, v, (L + "F2_ysc", (h - 1, 7)))
                avb = p.bank(4 + w % 2)
                VI(p, v.tensor_copy(yt65[0:65, :], avb[0:65, :]))
                g_set(p, (L + "F2_yev", (h, w)), p.cVc)
                VW(p, v)
                VI(p, v.reciprocal(invloc[:, :], yt65[64:65, :]))
            g_set(p, (L + "F2_invl", (h, w)), p.cVc)

            @blk.tensor
            def _(t, h=h, w=w):
                g_wait(p, t, (L + "F2_invl", (h, w)))
                bk = nslotP(t)
                cT.inc(t.matmul(bk, p.ones_rf[:, :], invloc[:, :], start=True, stop=True), 1)
            g_set(p, (L + "F2_ypl", (h, w)), cT)

            @blk.vector
            def _(v, h=h, w=w, r0=r0):
                g_wait(p, v, (L + "F2_ypl", (h, w)))
                nw = h * 8 + w
                if nw >= 2:
                    g_wait(p, v, (L + "F2_osp", ((nw - 2) // 8, (nw - 2) % 8)))
                bk = p.bank(1)
                p.cRX.inc(v.tensor_tensor(ostg[nw % 2][:, :],
                                          yt65[0:64, :], bk[0:64, :], op=OP.mult), 1)
            g_set(p, (L + "F2_ysc", (h, w)), p.cRX)

            @blk.gpsimd
            def _(g, h=h, w=w, r0=r0):
                g_wait(p, g, (L + "F2_ysc", (h, w)))
                nw = h * 8 + w
                cs = p.cD[nw % 2]
                cs.inc(g.dma_start(
                    out=p.scr_v[h // 2, r0:r0 + 64, w * 512:(w + 1) * 512],
                    in_=ostg[nw % 2][:, :]), 16)
            g_set(p, (L + "F2_osp", (h, w)), p.cD[nw % 2] if False else p.cD[(h * 8 + w) % 2])

    p.f2_yslot = {}
    p.f2_bslot = {}
    run_P1(0)
    run_P1(1)
    for h in range(FH):
        emit_bfinish(h)
        if h + 2 < FH:
            run_P1(h + 2)
        emit_P2(h)
    g_set(p, L + "F2_done", p.cSL)

    @blk.gpsimd
    def _(g):
        for h in range(FH):
            for w in range(8):
                g_wait(p, g, (L + "F2_osp", (h, w)))
        p.cD[0].inc(g.dma_start(out=bstg[0][0:1, 0:1].rearrange("a b -> (a b)")[None, :]
                                if False else p.scr_bb[0, 0:1, 0:1], in_=bstg[0][0:1, 0:1]), 16)
    g_set(p, (L, "oTspill"), p.cD[0])
    p.close_scope(L + "C")
    p.close_scope(L + "B1")

    if l == 1:
        p.row5 = [p.sbS("LN5", "row5_rstd", [1, 2 * TSL], side="right"),
                  p.sbS("LN5", "row5_mr", [1, 2 * TSL], side="right")]
    p.rowF = [p.sbS(L + "LN", L + "rowF_rstd", [1, 2 * TSL], side="right"),
              p.sbS(L + "LN", L + "rowF_mr", [1, 2 * TSL], side="right")]
    hb16 = [p.sbS(L + "D4", L + f"hb16{c}", [128, TSL], BF16) for c in range(NCH)]
    agb = [p.sbS(L + "D4", L + f"agb{i}", [128, TSL], BF16) for i in range(22)]
    gt2 = [p.sbS(L + "D4", L + f"gt{i}", [128, TSL], BF16) for i in range(2)]
    at2 = [p.sbS(L + "D4", L + f"at{i}", [128, TSL]) for i in range(2)]
    fm13 = [p.sbS(L + "D4", L + f"fm13{c}", [128, 513]) for c in range(NCH)]
    w1sb = [p.sbS(L + "D4", L + f"w1sb{i}", [128, 8 * 128], BF16) for i in range(4)]
    w2res = [p.sbS(L + "D4", L + f"w2res{i}", [128, 22 * 128], BF16) for i in range(NCH)]
    b1sb = p.sbS(L + "D4", L + "b1sb", [128, 44])
    b2sb = p.sbS(L + "D4", L + "b2sb", [128, NCH])
    lgsb = p.sbS(L + "D4", L + "lgsb", [128, NCH])
    lbsb = p.sbS(L + "D4", L + "lbsb", [128, NCH])
    xb5 = [p.sbS(L + "D4", L + f"xb5{c}", [128, TSL], BF16) for c in range(NCH)]
    sq5 = [p.sbS(L + "D4", L + f"sq5{c}", [128, TSL], BF16) for c in range(NCH)]
    wo_sb = [p.sbS(L + "D3", L + f"wo{i}", [128, C], BF16) for i in range(4)]
    oTs2 = [p.sbS(L + "D3", L + f"oTs{i}", [128, 4 * TSL], BF16) for i in range(1)]
    finD2 = [[p.sbS(L + "D3", L + f"finD{j}_{c}", [128, TSL]) for c in range(NCH)]
             for j in range(1)]
    carry = [p.sbS(L + "D3", L + f"carry{c}", [128, 1]) for c in range(NCH)]
    carryF = [p.sbS(L + "D3", L + f"carryF{c}", [128, 1]) for c in range(NCH)]
    xbD = [p.sbS(L + "D3", L + f"xbD{c}", [128, TSL], BF16) for c in range(NCH)]
    sqD = [p.sbS(L + "D3", L + f"sqD{c}", [128, TSL], BF16) for c in range(NCH)]

    @blk.sync
    def _(s):
        g_wait(p, s, L + "F2_done")
        for dst, srcI in [(b1sb, p.b1pI[l]), (b2sb, p.b2tI[l]),
                          (lgsb, p.lgtI[l]), (lbsb, p.lbtI[l])]:
            p.cE[3].inc(s.dma_start(out=dst[:, :], in_=srcI[:, :]), 16)
    g_set(p, (L, "bld"), p.cE[3])

    @blk.gpsimd
    def _(g):
        g_wait(p, g, L + "F2_done")
        g_wait(p, g, (L, "oTspill"))
        for i in range(4):
            p.cD[1].inc(g.dma_start(out=wo_sb[i][:, :],
                                    in_=p.woI[l][i * 128:(i + 1) * 128, :]), 16)
        for c in range(NCH):
            ins = g.memset(carry[c][:, :], 0.0)
        p.cZ.inc(ins, 1)
    g_set(p, (L, "wold"), p.cD[1])

    @blk.gpsimd
    def _(g):
        g_wait(p, g, "wcast")
        for oc in range(NCH):
            p.cF[2].inc(g.dma_start(
                out=w2res[oc].rearrange("p (c n) -> p c n", c=22)[0:128, 0:21, :],
                in_=p.w2bfI[l][0:2688, oc * 128:(oc + 1) * 128].rearrange(
                    "(c p) n -> p c n", p=128)), 16)
            p.cF[2].inc(g.dma_start(
                out=w2res[oc][0:42, 21 * 128: 22 * 128],
                in_=p.w2bfI[l][2688:2730, oc * 128:(oc + 1) * 128]), 16)
    g_set(p, (L, "w2resld"), p.cF[2])

    @blk.gpsimd
    def _(g):
        for c in range(NCH):
            ins = g.memset(fm13[c][:, 0:1], 0.0)
        p.cZ.inc(ins, 1)
    g_set(p, (L, "fm13z"), p.cZ)
    g_set(p, (L, "carryz"), p.cZ)

    for sl in range(NSL):
        ssl = slice(sl * TSL, (sl + 1) * TSL)
        finD = finD2[0]
        fmD = finD
        oTs = oTs2[0]

        @blk.sync
        def _(s, sl=sl, ssl=ssl, finD=finD, oTs=oTs):
            g_wait(p, s, L + "F2_done")
            g_wait(p, s, (L, "oTspill"))
            if sl >= 1:
                g_wait(p, s, (L + "F3_mmdone", sl - 1))
                g_wait(p, s, (L + "F3_xbsq", sl - 1))
                g_wait(p, s, (L + "F3_fm13", sl - 1))
            cs = [p.cA, p.cE[2]][sl % 2]
            for c in range(NCH):
                cs.inc(s.dma_start(out=finD[c][:, :], in_=p.scr_fA[c, :, ssl]), 16)
            for i in range(4):
                cs.inc(s.dma_start(out=oTs[0 * 0][:, i * TSL:(i + 1) * TSL]
                                   if False else oTs[:, i * TSL:(i + 1) * TSL],
                                   in_=p.scr_v[i, :, ssl]), 16)
        g_set(p, (L + "F3_ld", sl), [p.cA, p.cE[2]][sl % 2])

        for oc in range(NCH):
            @blk.tensor
            def _(t, sl=sl, oc=oc, ssl=ssl, oTs=oTs):
                k = sl * 8 + oc
                if k == 0:
                    g_wait(p, t, (L, "wold"))
                if oc == 0 and sl >= 1:
                    g_wait(p, t, (L + "F4_ag", (sl - 1, 21)))
                if k >= 2:
                    g_wait(p, t, (L + "F3_hev", k - 2))
                g_wait(p, t, (L + "F3_ld", sl))
                bk = p.bank(k % 2)
                for i in range(4):
                    ins = t.matmul(bk, wo_sb[i][:, oc * 128:(oc + 1) * 128],
                                   oTs[:, i * TSL:(i + 1) * TSL],
                                   start=(i == 0), stop=(i == 3))
                cT.inc(ins, 1)
            g_set(p, (L + "F3_mm", sl * 8 + oc), cT)
            if oc == NCH - 1:
                g_set(p, (L + "F3_mmdone", sl), cT)

            @blk.vector
            def _(v, sl=sl, oc=oc, finD=finD):
                k = sl * 8 + oc
                g_wait(p, v, (L + "F3_mm", k))
                if oc == 0:
                    g_wait(p, v, (L + "F3_ld", sl))
                VI(p, v.tensor_tensor(finD[oc][:, :], p.bank(k % 2), finD[oc][:, :], op=OP.add))
            g_set(p, (L + "F3_hev", sl * 8 + oc), p.cVc)
        g_set(p, (L + "F3_h", sl), p.cVc)

        @blk.vector
        def _(v, sl=sl, fmD=fmD):
            VW(p, v)
            if sl == 0:
                g_wait(p, v, (L, "carryz"))
            for c in range(NCH):
                if c == 0 and sl >= 1:
                    g_wait(p, v, (L + "F3r_stats", sl - 1))
                if c < 4:
                    VI(p, v.tensor_copy(xbD[c][:, :], fmD[c][:, :]))
                else:
                    VI(p, v.tensor_copy(xbD[c][:, 0:1], carry[c][:, :]))
                    VI(p, v.tensor_copy(xbD[c][:, 1:TSL], fmD[c][:, 0:TSL - 1]))
                    VW(p, v)
                    VI(p, v.tensor_copy(carry[c][:, :], fmD[c][:, TSL - 1:TSL]))
                VW(p, v)
                VI(p, v.tensor_tensor(sqD[c][:, :], xbD[c][:, :], xbD[c][:, :], op=OP.mult))
        g_set(p, (L + "F3_xbsq", sl), p.cVc)

        @blk.tensor
        def _(t, sl=sl):
            g_wait(p, t, (L + "F3_xbsq", sl))
            if sl >= 1:
                g_wait(p, t, ((L + "F3r", sl - 1), "var"))
            for c in range(NCH):
                t.matmul(p.bank(2)[0:1, :], p.inv_c[:, :], xbD[c][:, :],
                         start=(c == 0), stop=(c == NCH - 1))
            for c in range(NCH):
                ins = t.matmul(p.bank(3)[0:1, :], p.inv_c[:, :], sqD[c][:, :],
                               start=(c == 0), stop=(c == NCH - 1))
            cT.inc(ins, 1)
        g_set(p, (L + "F3r_stats", sl), cT)
        extraF = [(L + "F4_planes", sl - 2)] if sl >= 2 else []
        emit_row_finish(p, (L + "F3r", sl), p.rowF,
                        slice((sl % 2) * TSL, (sl % 2 + 1) * TSL), extra_scalar=extraF)

        @blk.vector
        def _(v, sl=sl, fmD=fmD):
            VW(p, v)
            if sl >= 1:
                g_wait(p, v, (L + "F4_fod", sl - 1))
                g_wait(p, v, (L + "F4_spill", sl - 1))
                if l == 1:
                    g_wait(p, v, (L + "F5_xbsq", sl - 1))
            for c in range(NCH):
                if c >= 4 and sl >= 1:
                    VI(p, v.tensor_copy(fm13[c][:, 0:1], carryF[c][:, :]))
                VI(p, v.tensor_copy(fm13[c][:, 1:513], fmD[c][:, :]))
                if c >= 4:
                    VW(p, v)
                    VI(p, v.tensor_copy(carryF[c][:, :], fmD[c][:, TSL - 1:TSL]))
                    VW(p, v)
        g_set(p, (L + "F3_fm13", sl), p.cVc)

        @blk.tensor
        def _(t, sl=sl, ssl=ssl):
            g_wait(p, t, ((L + "F3r", sl), "rows"))
            if sl >= 1:
                g_wait(p, t, (L + "F4_w2evdone", sl - 1))
            rslF = slice((sl % 2) * TSL, (sl % 2 + 1) * TSL)
            t.matmul(p.bank(4), p.ones_rf[:, :], p.rowF[0][0:1, rslF], start=True, stop=True)
            ins = t.matmul(p.bank(5), p.ones_rf[:, :], p.rowF[1][0:1, rslF], start=True, stop=True)
            cT.inc(ins, 1)
        g_set(p, (L + "F4_planes", sl), cT)

        @blk.vector
        def _(v, sl=sl):
            g_wait(p, v, (L + "F4_planes", sl))
            g_wait(p, v, (L + "F3_fm13", sl))
            if sl == 0:
                g_wait(p, v, (L, "fm13z"))
                g_wait(p, v, (L, "bld"))
            if sl >= 1:
                g_wait(p, v, (L + "F4_w1", (sl - 1, 21)))
            for c in range(NCH):
                off = 1 if c < 4 else 0
                sh = fm13[c][:, off:off + TSL]
                VI(p, v.tensor_tensor(hb16[c][:, :], sh, p.bank(4), op=OP.mult))
                VW(p, v)
                VI(p, v.tensor_tensor(hb16[c][:, :], hb16[c][:, :], p.bank(5), op=OP.add))
                VW(p, v)
                VI(p, v.tensor_scalar(hb16[c][:, :], hb16[c][:, :], lgsb[:, c:c + 1],
                                      lbsb[:, c:c + 1], op0=OP.mult, op1=OP.add))
        g_set(p, (L + "F4_apply", sl), p.cVc)

        for mp in range(22):
            rr = min(128, FHID - mp * 128)
            kk = sl * 22 + mp

            @blk.gpsimd
            def _(g, sl=sl, mp=mp, rr=rr, kk=kk):
                if kk >= 2:
                    g_wait(p, g, (L + "F4_w1", (sl, mp - 2) if mp >= 2 else (sl - 1, 20 + mp)))
                if kk < 2:
                    g_wait(p, g, "wcast")
                buf_a = w1sb[(kk % 2) * 2]
                buf_g = w1sb[(kk % 2) * 2 + 1]
                srcA = p.w1bfI[l][:, mp * 128: mp * 128 + rr].rearrange(
                    "(c p) n -> p c n", p=128)
                srcG = p.w1bfI[l][:, FHID + mp * 128: FHID + mp * 128 + rr].rearrange(
                    "(c p) n -> p c n", p=128)
                cw1 = p.cF[kk % 2]
                cw1.inc(g.dma_start(
                    out=buf_a.rearrange("p (c n) -> p c n", c=NCH)[:, :, 0:rr], in_=srcA), 16)
                cw1.inc(g.dma_start(
                    out=buf_g.rearrange("p (c n) -> p c n", c=NCH)[:, :, 0:rr], in_=srcG), 16)
            g_set(p, (L + "F4_w1ld", (sl, mp)), p.cF[kk % 2])

            @blk.tensor
            def _(t, sl=sl, mp=mp, rr=rr, kk=kk):
                g_wait(p, t, (L + "F4_apply", sl))
                g_wait(p, t, (L + "F4_w1ld", (sl, mp)))
                if mp >= 2:
                    g_wait(p, t, (L + "F4_ag", (sl, mp - 2)))
                elif sl >= 1:
                    g_wait(p, t, (L + "F4_ag", (sl - 1, 20 + mp)))
                ba = p.bank((mp % 2) * 2)
                bg = p.bank((mp % 2) * 2 + 1)
                buf_a = w1sb[(kk % 2) * 2]
                buf_g = w1sb[(kk % 2) * 2 + 1]
                for c in range(NCH):
                    t.matmul(ba[0:rr, :], buf_a[:, c * 128: c * 128 + rr],
                             hb16[c][:, :], start=(c == 0), stop=(c == NCH - 1))
                for c in range(NCH):
                    ins = t.matmul(bg[0:rr, :], buf_g[:, c * 128: c * 128 + rr],
                                   hb16[c][:, :], start=(c == 0), stop=(c == NCH - 1))
                cT.inc(ins, 1)
            g_set(p, (L + "F4_w1", (sl, mp)), cT)

            @blk.scalar
            def _(s, sl=sl, mp=mp, rr=rr):
                g_wait(p, s, (L + "F4_w1", (sl, mp)))
                if mp >= 2:
                    g_wait(p, s, (L + "F4_ag", (sl, mp - 2)))
                elif sl >= 1:
                    g_wait(p, s, (L + "F4_ag", (sl - 1, 20 + mp)))
                bg = p.bank((mp % 2) * 2 + 1)
                gt = gt2[mp % 2]
                SW(p, s)
                SI(p, s.activation(gt[0:rr, :], bg[0:rr, :], AF.Gelu,
                                   bias=b1sb[0:rr, 22 + mp: 23 + mp]))
            g_set(p, (L + "F4_gelu", (sl, mp)), p.cSc)

            @blk.vector
            def _(v, sl=sl, mp=mp, rr=rr):
                g_wait(p, v, (L + "F4_gelu", (sl, mp)))
                ba = p.bank((mp % 2) * 2)
                gt = gt2[mp % 2]
                VI(p, v.scalar_tensor_tensor(agb[mp][0:rr, :], ba[0:rr, :],
                                             b1sb[0:rr, mp:mp + 1], gt[0:rr, :],
                                             op0=OP.add, op1=OP.mult))
            g_set(p, (L + "F4_ag", (sl, mp)), p.cVc)

        for oc in range(NCH):
            @blk.tensor
            def _(t, sl=sl, oc=oc):
                g_wait(p, t, (L + "F4_ag", (sl, 21)))
                if sl == 0 and oc == 0:
                    g_wait(p, t, (L, "w2resld"))
                k = sl * 8 + oc
                if k >= 2:
                    g_wait(p, t, (L + "F4_w2ev", k - 2))
                bk = p.bank(4 + oc % 2)
                buf = w2res[oc]
                for i in range(22):
                    rr = min(128, FHID - i * 128)
                    ins = t.matmul(bk, buf[0:rr, i * 128: (i + 1) * 128],
                                   agb[i][0:rr, :], start=(i == 0), stop=(i == 21))
                cT.inc(ins, 1)
            g_set(p, (L + "F4_w2", (sl, oc)), cT)

            @blk.vector
            def _(v, sl=sl, oc=oc):
                g_wait(p, v, (L + "F4_w2", (sl, oc)))
                bk = p.bank(4 + oc % 2)
                at = at2[oc % 2]
                VW(p, v)
                VI(p, v.tensor_scalar(at[:, :], bk, b2sb[:, oc:oc + 1], None, op0=OP.add))
                VW(p, v)
                VI(p, v.tensor_tensor(fm13[oc][:, 1:513], at[:, :],
                                      fm13[oc][:, 1:513], op=OP.add))
            g_set(p, ((L + "F4_w2ev", sl * 8 + oc)) if False else (L + "F4_w2ev", sl * 8 + oc), p.cVc)
        g_set(p, (L + "F4_fod", sl), p.cVc)
        g_set(p, (L + "F4_w2evdone", sl), p.cVc)

        if l == 1:
            @blk.vector
            def _(v, sl=sl):
                VW(p, v)
                for c in range(NCH):
                    if c == 0 and sl >= 1:
                        g_wait(p, v, (L + "F5r_stats", sl - 1))
                    VI(p, v.tensor_copy(xb5[c][:, :], fm13[c][:, 1:513]))
                    VW(p, v)
                    VI(p, v.tensor_tensor(sq5[c][:, :], xb5[c][:, :], xb5[c][:, :], op=OP.mult))
            g_set(p, (L + "F5_xbsq", sl), p.cVc)

            @blk.tensor
            def _(t, sl=sl):
                g_wait(p, t, (L + "F5_xbsq", sl))
                if sl >= 1:
                    g_wait(p, t, ((L + "F5r", sl - 1), "var"))
                for c in range(NCH):
                    t.matmul(p.bank(2)[0:1, :], p.inv_c[:, :], xb5[c][:, :],
                             start=(c == 0), stop=(c == NCH - 1))
                for c in range(NCH):
                    ins = t.matmul(p.bank(3)[0:1, :], p.inv_c[:, :], sq5[c][:, :],
                                   start=(c == 0), stop=(c == NCH - 1))
                cT.inc(ins, 1)
            g_set(p, (L + "F5r_stats", sl), cT)
            extra5 = [(L + "F5_r5sp", sl - 2)] if sl >= 2 else []
            emit_row_finish(p, (L + "F5r", sl), p.row5,
                            slice((sl % 2) * TSL, (sl % 2 + 1) * TSL),
                            extra_scalar=extra5)

            @blk.sync
            def _(s, sl=sl):
                g_wait(p, s, ((L + "F5r", sl), "rows"))
                cs = [p.cW, p.cE[1]][sl % 2]
                rsl5 = slice((sl % 2) * TSL, (sl % 2 + 1) * TSL)
                for j in range(2):
                    cs.inc(s.dma_start(out=p.scr_r5[j:j + 1, sl * TSL:(sl + 1) * TSL],
                                       in_=p.row5[j][0:1, rsl5]), 16)
            g_set(p, (L + "F5_r5sp", sl), [p.cW, p.cE[1]][sl % 2])
            if sl == NSL - 1:
                g_set(p, "R5_done_e", p.cW)
                g_set(p, "R5_done_o", p.cE[1])

        @blk.gpsimd
        def _(g, sl=sl, ssl=ssl):
            g_wait(p, g, (L + "F4_fod", sl))
            if l == 1:
                g_wait(p, g, (L + "F5_xbsq", sl))
            if "d_xf" in p.dbg and l == 0 and sl == 0:
                g_wait(p, g, "DXF_dump")
            if "d_f0" in p.dbg and l == 1 and sl == 0:
                g_wait(p, g, "DF0_dump")
            for c in range(NCH):
                p.cD[4].inc(g.dma_start(out=p.scr_fA[c, :, ssl], in_=fm13[c][:, 1:513]), 16)
        g_set(p, (L + "F4_spill", sl), p.cD[4])
        g_set(p, (L + "F4_fod", sl), p.cVc) if False else None
    g_set(p, L + "F4_done", p.cD[4])
    g_set(p, L + "F4_done_d1", p.cD[1])
    g_set(p, L + "F4_rows", p.cVc)
    if "d_f0" in p.dbg and l == 0:
        @blk.sync
        def _(s):
            barrier(p, s)
            for c in range(NCH):
                p.cO.inc(s.dma_start(out=p.dbg["d_f0"][c, :, :], in_=p.scr_fA[c, :, :]), 16)
        g_set(p, "DF0_dump", p.cO)
    p.close_scope(L + "D3")
    p.close_scope(L + "D4")
    p.close_scope(L + "LN")


def emit_final(p):
    blk = p.blk
    cT, cA, cG = p.cT, p.cA, p.cG
    r5Z = [p.sbS("Z", f"r5Z{j}", [1, T], side="right") for j in range(2)]
    src2 = [[p.sbS("Z", f"srcZ{j}_{c}", [128, TSL]) for c in range(NCH)] for j in range(2)]
    aux2 = [[p.sbS("Z", f"auxZ{j}_{c}", [128, TSL]) for c in range(NCH)] for j in range(2)]
    tmout2 = [[p.sbS("Z", f"tmoutZ{j}_{i}", [128, C]) for i in range(4)] for j in range(2)]

    for sl in range(NSL):
        ssl = slice(sl * TSL, (sl + 1) * TSL)
        src = src2[sl % 2]
        aux = aux2[sl % 2]
        tmout = tmout2[sl % 2]

        @blk.sync
        def _(s, sl=sl, ssl=ssl, src=src, aux=aux):
            if sl == 0:
                g_wait(p, s, "L1_F4_done")
                g_wait(p, s, "L1_F4_rows")
                g_wait(p, s, "X1_done_e")
                g_wait(p, s, "X1_done_o")
                g_wait(p, s, "R5_done_e")
                g_wait(p, s, "R5_done_o")
                for j in range(2):
                    p.cA.inc(s.dma_start(out=r5Z[j][:, :], in_=p.scr_r5[j:j + 1, :]), 16)
            if sl == 1:
                g_wait(p, s, "L1_F4_done_d1")
            if sl >= 2:
                g_wait(p, s, ("Z_trdone", sl - 2))
            cs = [p.cA, p.cE[0]][sl % 2]
            for c in range(NCH):
                cs.inc(s.dma_start(out=src[c][:, :], in_=p.scr_fA[c, :, ssl]), 16)
                cs.inc(s.dma_start(out=aux[c][:, :], in_=p.scr_x1[c, :, ssl]), 16)
        g_set(p, ("Z_ld", sl), [p.cA, p.cE[0]][sl % 2])

        @blk.tensor
        def _(t, sl=sl, ssl=ssl):
            if sl == 0:
                g_wait(p, t, ("Z_ld", 0))
            if sl >= 2:
                g_wait(p, t, ("Z_apply", sl - 2))
            pb = 4 if sl % 2 == 0 else 2
            t.matmul(p.bank(pb), p.ones_rf[:, :], r5Z[0][0:1, ssl], start=True, stop=True)
            ins = t.matmul(p.bank(pb + 1), p.ones_rf[:, :], r5Z[1][0:1, ssl],
                           start=True, stop=True)
            cT.inc(ins, 1)
        g_set(p, ("Z_planes", sl), cT)

        @blk.vector
        def _(v, sl=sl, src=src, aux=aux):
            g_wait(p, v, ("Z_planes", sl))
            g_wait(p, v, ("Z_ld", sl))
            pb = 4 if sl % 2 == 0 else 2
            for c in range(NCH):
                VI(p, v.tensor_tensor(src[c][:, :], src[c][:, :], p.bank(pb), op=OP.mult))
                VW(p, v)
                VI(p, v.tensor_tensor(src[c][:, :], src[c][:, :], p.bank(pb + 1), op=OP.add))
                VW(p, v)
                VI(p, v.tensor_tensor(src[c][:, :], src[c][:, :], aux[c][:, :], op=OP.add))
        g_set(p, ("Z_apply", sl), p.cVc)

        for r in range(4):
            for c in range(NCH):
                k = sl * 32 + r * 8 + c

                @blk.tensor
                def _(t, sl=sl, r=r, c=c, k=k, src=src):
                    if r == 0 and c == 0:
                        g_wait(p, t, ("Z_apply", sl))
                    if k >= 2:
                        g_wait(p, t, ("Z_ev", k - 2))
                    pt = p.bank(k % 2)[:, 0:128]
                    cT.inc(t.transpose(pt, src[c][:, r * 128:(r + 1) * 128],
                                       p.ident_f[:, :]), 1)
                g_set(p, ("Z_tr", k), cT)

                @blk.vector
                def _(v, sl=sl, r=r, c=c, k=k, tmout=tmout):
                    g_wait(p, v, ("Z_tr", k))
                    if r == 0 and c == 0 and sl >= 2:
                        g_wait(p, v, ("Z_st", sl - 2))
                    pt = p.bank(k % 2)[:, 0:128]
                    VI(p, v.tensor_copy(tmout[r][:, c * 128:(c + 1) * 128], pt))
                g_set(p, ("Z_ev", k), p.cVc)
        g_set(p, ("Z_trdone", sl), p.cVc)

        @blk.sync
        def _(s, sl=sl, tmout=tmout):
            g_wait(p, s, ("Z_trdone", sl))
            cs = [p.cO, p.cE[3]][sl % 2]
            for r in range(4):
                cs.inc(s.dma_start(
                    out=p.out_t[sl * TSL + r * 128: sl * TSL + (r + 1) * 128, :],
                    in_=tmout[r][:, :]), 16)
        g_set(p, ("Z_st", sl), [p.cO, p.cE[3]][sl % 2])

    @blk.sync
    def _(s):
        s.wait_ge(p.cO.h, p.cO.n)
        s.wait_ge(p.cE[3].h, p.cE[3].n)


def prep_inputs(inputs, batch_idx, cst):
    m = {
        "x": np.ascontiguousarray(inputs["L0_attn_x"][batch_idx]),
        "w_in": np.ascontiguousarray(inputs["L0_attn_w_in"]),
        "w_out": np.ascontiguousarray(inputs["L0_attn_w_out"]),
        "gamma_t": np.ascontiguousarray(
            np.asarray(inputs["L0_attn_ln_gamma"]).reshape(NCH, 128).T),
        "beta_t": np.ascontiguousarray(
            np.asarray(inputs["L0_attn_ln_beta"]).reshape(NCH, 128).T),
        "ident": cst["ident"],
        "sel2": cst["sel2"], "brd2": cst["brd2"],
        "rp_cos": cst["rp_cos"], "rp_sin": cst["rp_sin"],
        "xp_qc": cst["xp_qc"], "xp_qs": cst["xp_qs"],
        "xp_kc": cst["xp_kc"], "xp_ks": cst["xp_ks"],
        "lmask": cst["lmask"],
    }
    for l in range(2):
        b1 = np.asarray(inputs["L0_ffn_b1"][l])
        b1p = np.zeros((44, 128), dtype=np.float32)
        for mm in range(22):
            a = b1[128 * mm: min(128 * (mm + 1), FHID)]
            b1p[mm, :len(a)] = a
            g = b1[FHID + 128 * mm: FHID + min(128 * (mm + 1), FHID)]
            b1p[22 + mm, :len(g)] = g
        m[f"b1p{l}"] = np.ascontiguousarray(b1p.T)
        m[f"b2t{l}"] = np.ascontiguousarray(np.asarray(inputs["L0_ffn_b2"][l]).reshape(NCH, 128).T)
        m[f"lgt{l}"] = np.ascontiguousarray(np.asarray(inputs["L0_ffn_ln_g"][l]).reshape(NCH, 128).T)
        m[f"lbt{l}"] = np.ascontiguousarray(np.asarray(inputs["L0_ffn_ln_b"][l]).reshape(NCH, 128).T)
        m[f"qsc{l}"] = np.ascontiguousarray(np.asarray(inputs["L0_ffn_q_scale"][l]).reshape(AD, 1))
        m[f"ksc{l}"] = np.ascontiguousarray(np.asarray(inputs["L0_ffn_k_scale"][l]).reshape(AD, 1))
        m[f"wqkv{l}"] = np.ascontiguousarray(inputs["L0_ffn_wqkv"][l])
        m[f"wo{l}"] = np.ascontiguousarray(inputs["L0_ffn_wo"][l])
        m[f"w1{l}"] = np.ascontiguousarray(inputs["L0_ffn_w1"][l])
        m[f"w2{l}"] = np.ascontiguousarray(inputs["L0_ffn_w2"][l])
    return m


def kernel(**inputs):
    cst = host_constants()
    nc, p = build_program()
    in_maps = [prep_inputs(inputs, b, cst) for b in range(8)]
    res = run_bass_kernel_spmd(nc, in_maps, core_ids=list(range(8)))
    return np.stack([res.results[b]["out"] for b in range(8)], axis=0)


def run_debug(inputs, cores=(0,), trace=False):
    cst = host_constants()
    nc, p = build_program()
    in_maps = [prep_inputs(inputs, b, cst) for b in cores]
    res = run_bass_kernel_spmd(nc, in_maps, core_ids=list(range(len(cores))), trace=trace)
    return res
